# version 07
# speedup vs baseline: 1.5658x; 1.0003x over previous
import numpy as np
import ml_dtypes
from contextlib import ExitStack
import concourse.bass as bass
import concourse.mybir as mybir
from concourse.bass_utils import run_bass_kernel_spmd
from concourse.alu_op_type import AluOpType as Alu

F32 = mybir.dt.float32
BF16 = mybir.dt.bfloat16
AF = mybir.ActivationFunctionType
AX = mybir.AxisListType.X
BF = ml_dtypes.bfloat16

NC = 8
B, S, D = 2, 8192, 1024
TPC = 2048
CPB = 4



class Ch:
    def __init__(self):
        self.plan = []

    def c(self, eng, fn):
        self.plan.append((eng, 1, fn))

    def d(self, eng, fn):
        self.plan.append((eng, 16, fn))

    SEG = 24000

    def n(self, eng, fn):
        self.plan.append((eng, 0, fn))

    def cc(self, fn, ccsem):
        self.plan.append(("gpsimd", 0, lambda e, fn=fn, s=ccsem: fn(e).then_inc(s)))
        self.plan.append(("gpsimd", 1, lambda e, s=ccsem: (e.wait_ge(s, 1), e.nop())[1]))

    def flush(self, nc, sems):
        if not hasattr(self, "_emitted"):
            self._emitted = 0
            self._seg = 0
            self._v = 0
            self._prev = (None, 0)
        pend = self.plan[self._emitted:]
        self._emitted = len(self.plan)
        if not pend:
            return
        segs = []
        if not hasattr(self, "_preng"):
            self._preng = None
        for eng, inc, fn in pend:
            if self._v + inc > Ch.SEG:
                self._seg += 1
                self._v = 0
            skip = (self._preng is not None and self._preng[0] == eng
                    and not self._preng[1]
                    and eng in ("tensor", "scalar"))
            if skip:
                segs.append((self._seg, None, 0))
            else:
                segs.append((self._seg, self._prev[0], self._prev[1]))
            self._v += inc
            self._prev = (self._seg, self._v)
            self._preng = (eng, inc >= 16)
        assert self._seg < len(sems), "need more chain semaphores"
        with nc.Block() as blk:
            def emit_for(name):
                def f(eng):
                    for (en, inc, fn), (sg, wsg, wv) in zip(pend, segs):
                        if en != name:
                            continue
                        if wsg is not None and wv:
                            eng.wait_ge(sems[wsg], wv)
                        ins = fn(eng)
                        if inc:
                            ins.then_inc(sems[sg], inc)
                return f
            blk.sync(emit_for("sync"))
            blk.gpsimd(emit_for("gpsimd"))
            blk.vector(emit_for("vector"))
            blk.scalar(emit_for("scalar"))
            blk.tensor(emit_for("tensor"))

    def final(self, nc, sems):
        self.flush(nc, sems)
        fs, fv = self._prev
        with nc.Block() as blk:
            def f(eng):
                eng.wait_ge(sems[fs], fv)
            blk.sync(f)

    def emit(self, nc, block, sems):
        segs = []
        seg, v = 0, 0
        prev = (None, 0)
        for eng, inc, fn in self.plan:
            if v + inc > Ch.SEG:
                seg, v = seg + 1, 0
            segs.append((seg, prev[0], prev[1]))
            v += inc
            prev = (seg, v)
        nseg = seg + 1
        assert nseg <= len(sems), f"need {nseg} semaphores"
        final_seg, final_val = prev

        def emit_for(name):
            def f(eng):
                for (en, inc, fn), (sg, wsg, wv) in zip(self.plan, segs):
                    if en != name:
                        continue
                    if wsg is not None and wv:
                        eng.wait_ge(sems[wsg], wv)
                    ins = fn(eng)
                    if inc:
                        ins.then_inc(sems[sg], inc)
                if name == "sync":
                    eng.wait_ge(sems[final_seg], final_val)
            return f

        block.sync(emit_for("sync"))
        block.gpsimd(emit_for("gpsimd"))
        block.vector(emit_for("vector"))
        block.scalar(emit_for("scalar"))
        block.tensor(emit_for("tensor"))


class _FlushStack(ExitStack):
    def __init__(self, ch, nc, sems):
        super().__init__()
        self._c = (ch, nc, sems)

    def __exit__(self, *a):
        ch, nc, sems = self._c
        ch.flush(nc, sems)
        return super().__exit__(*a)


def _ln_tile(ch, x_ap, out_ap, tmp_dx, tmp_sq, st1, st2, Dn, eps=1e-5):
    ch.c("vector", lambda e, a=x_ap, o=st1: e.tensor_reduce(out=o, in_=a, axis=AX, op=Alu.add))
    ch.c("vector", lambda e, o=st1: e.tensor_scalar(out=o, in0=o, scalar1=1.0 / Dn, scalar2=None, op0=Alu.mult))
    ch.c("vector", lambda e, a=x_ap, o=tmp_dx, s=st1: e.tensor_scalar(out=o, in0=a, scalar1=s, scalar2=None, op0=Alu.subtract))
    ch.c("scalar", lambda e, a=tmp_dx, o=tmp_sq: e.activation(out=o, in_=a, func=AF.Square))
    ch.c("vector", lambda e, a=tmp_sq, o=st2: e.tensor_reduce(out=o, in_=a, axis=AX, op=Alu.add))
    ch.c("vector", lambda e, o=st2: e.tensor_scalar(out=o, in0=o, scalar1=1.0 / Dn, scalar2=eps, op0=Alu.mult, op1=Alu.add))
    ch.c("scalar", lambda e, o=st2: e.activation(out=o, in_=o, func=AF.Sqrt))
    ch.c("vector", lambda e, o=st2: e.reciprocal(out=o, in_=o))
    ch.c("vector", lambda e, a=tmp_dx, o=out_ap, s=st2: e.tensor_scalar(out=o, in0=a, scalar1=s, scalar2=None, op0=Alu.mult))



def build_ln0():
    nc = bass.Bass("TRN2", target_bir_lowering=False)
    x = nc.dram_tensor("x", [TPC, D], F32, kind="ExternalInput").ap()
    gam = nc.dram_tensor("gam", [128, D], F32, kind="ExternalInput").ap()
    bet = nc.dram_tensor("bet", [128, D], F32, kind="ExternalInput").ap()
    xn = nc.dram_tensor("xn", [TPC, D], F32, kind="ExternalOutput").ap()
    xnA = nc.dram_tensor("xnA", [TPC, D], F32, kind="ExternalOutput").ap()

    ch = Ch()
    with (
        nc.sbuf_tensor("xt", [128, D], F32) as xt,
        nc.sbuf_tensor("dx", [128, D], F32) as dx,
        nc.sbuf_tensor("sq", [128, D], F32) as sq,
        nc.sbuf_tensor("xnt", [128, D], F32) as xnt,
        nc.sbuf_tensor("xat", [128, D], F32) as xat,
        nc.sbuf_tensor("gt", [128, D], F32) as gt,
        nc.sbuf_tensor("bt", [128, D], F32) as bt,
        nc.sbuf_tensor("s1", [128, 1], F32) as s1,
        nc.sbuf_tensor("s2", [128, 1], F32) as s2,
        nc.Block() as block,
    ):
        sems = [nc.semaphore(f"S{i}").__enter__() for i in range(8)]
        ch.d("sync", lambda e: e.dma_start(out=gt[:], in_=gam[:, :]))
        ch.d("sync", lambda e: e.dma_start(out=bt[:], in_=bet[:, :]))
        for t in range(TPC // 128):
            r0 = t * 128
            ch.d("sync", lambda e, r0=r0: e.dma_start(out=xt[:], in_=x[r0:r0 + 128, :]))
            _ln_tile(ch, xt[:], xnt[:], dx[:], sq[:], s1[:], s2[:], D)
            ch.d("sync", lambda e, r0=r0: e.dma_start(out=xn[r0:r0 + 128, :], in_=xnt[:]))
            _ln_tile(ch, xnt[:], xat[:], dx[:], sq[:], s1[:], s2[:], D)
            ch.c("vector", lambda e: e.tensor_tensor(out=xat[:], in0=xat[:], in1=gt[:], op=Alu.mult))
            ch.c("vector", lambda e: e.tensor_tensor(out=xat[:], in0=xat[:], in1=bt[:], op=Alu.add))
            ch.d("sync", lambda e, r0=r0: e.dma_start(out=xnA[r0:r0 + 128, :], in_=xat[:]))
        ch.emit(nc, block, sems)
    return nc



def build_l0qkv():
    nc = bass.Bass("TRN2", target_bir_lowering=False)
    xnAT = nc.dram_tensor("xnAT", [D, S], F32, kind="ExternalInput").ap()
    Wq = nc.dram_tensor("Wq", [D, D], F32, kind="ExternalInput").ap()
    Wk = nc.dram_tensor("Wk", [D, D], F32, kind="ExternalInput").ap()
    Wv = nc.dram_tensor("Wv", [D, D], F32, kind="ExternalInput").ap()
    q_out = nc.dram_tensor("q_out", [8, 128, TPC], BF16, kind="ExternalOutput").ap()
    k_out = nc.dram_tensor("k_out", [8, 128, S], BF16, kind="ExternalOutput").ap()
    v_out = nc.dram_tensor("v_out", [16, 128, 4096], BF16, kind="ExternalOutput").ap()

    xnATl = nc.dram_tensor("xnATl", [D, TPC], F32, kind="ExternalInput").ap()

    ch = Ch()
    with (
        nc.sbuf_tensor("xa", [128, 8 * S], BF16) as xa,
        nc.sbuf_tensor("xl", [128, 8 * TPC], BF16) as xl,
        nc.sbuf_tensor("wq", [128, 8 * 128], BF16) as wq,
        nc.sbuf_tensor("wk", [128, 8 * 128], BF16) as wk,
        nc.sbuf_tensor("wv", [128, 8 * 128], BF16) as wv,
        nc.sbuf_tensor("qs", [128, TPC], BF16) as qs,
        nc.sbuf_tensor("ks", [128, S], BF16) as ks,
        nc.sbuf_tensor("vs", [128, 2 * 4096], BF16) as vs,
        nc.psum_tensor("ps", [128, 512], F32) as ps,
        nc.psum_tensor("pv", [128, 128], F32) as pv,
        nc.Block() as block,
    ):
        sems = [nc.semaphore(f"S{i}").__enter__() for i in range(8)]
        for c in range(8):
            ch.d("gpsimd", lambda e, c=c: e.dma_start(out=xa[:, c * S:(c + 1) * S], in_=xnAT[c * 128:(c + 1) * 128, :]))
            ch.d("gpsimd", lambda e, c=c: e.dma_start(out=xl[:, c * TPC:(c + 1) * TPC], in_=xnATl[c * 128:(c + 1) * 128, :]))
        for p in range(8):
            for c in range(8):
                for W, ws in ((Wq, wq), (Wk, wk), (Wv, wv)):
                    ch.d("gpsimd", lambda e, c=c, p=p, W=W, ws=ws: e.dma_start(
                        out=ws[:, c * 128:(c + 1) * 128], in_=W[c * 128:(c + 1) * 128, p * 128:(p + 1) * 128]))
            for tg in range(TPC // 512):
                for c in range(8):
                    ch.c("tensor", lambda e, c=c, tg=tg: e.matmul(
                        ps[:], wq[:, c * 128:(c + 1) * 128], xl[:, c * TPC + tg * 512: c * TPC + tg * 512 + 512],
                        start=(c == 0), stop=(c == 7)))
                ch.c("vector", lambda e, tg=tg: e.tensor_copy(out=qs[:, tg * 512:(tg + 1) * 512], in_=ps[:]))
            ch.d("sync", lambda e, p=p: e.dma_start(out=q_out[p, :, :], in_=qs[:]))
            for kt in range(S // 512):
                for c in range(8):
                    ch.c("tensor", lambda e, c=c, kt=kt: e.matmul(
                        ps[:], wk[:, c * 128:(c + 1) * 128], xa[:, c * S + kt * 512: c * S + kt * 512 + 512],
                        start=(c == 0), stop=(c == 7)))
                ch.c("vector", lambda e, kt=kt: e.tensor_copy(out=ks[:, kt * 512:(kt + 1) * 512], in_=ps[:]))
            ch.d("sync", lambda e, p=p: e.dma_start(out=k_out[p, :, :], in_=ks[:]))
            for tc in range(S // 128):
                for c in range(8):
                    ch.c("tensor", lambda e, c=c, tc=tc: e.matmul(
                        pv[:], xa[:, c * S + tc * 128: c * S + tc * 128 + 128], wv[:, c * 128:(c + 1) * 128],
                        start=(c == 0), stop=(c == 7)))
                ch.c("vector", lambda e, tc=tc: e.tensor_copy(out=vs[:, tc * 64:(tc + 1) * 64], in_=pv[:, 0:64]))
                ch.c("vector", lambda e, tc=tc: e.tensor_copy(out=vs[:, 4096 + tc * 64:4096 + (tc + 1) * 64], in_=pv[:, 64:128]))
            ch.d("sync", lambda e, p=p: e.dma_start(out=v_out[2 * p, :, :], in_=vs[:, 0:4096]))
            ch.d("sync", lambda e, p=p: e.dma_start(out=v_out[2 * p + 1, :, :], in_=vs[:, 4096:8192]))
        ch.emit(nc, block, sems)
    return nc



SLOPES = [2.0 ** (-8.0 / n) for n in range(16, 0, -1)]
RAMPLEN = 10240

def build_l0attn():
    nc = bass.Bass("TRN2", target_bir_lowering=False)
    q_in = nc.dram_tensor("q_in", [8, 128, TPC], BF16, kind="ExternalInput").ap()
    k_in = nc.dram_tensor("k_in", [8, 128, S], BF16, kind="ExternalInput").ap()
    v_in = nc.dram_tensor("v_in", [16, 128, 4096], BF16, kind="ExternalInput").ap()
    Wo = nc.dram_tensor("Wo", [D, D], F32, kind="ExternalInput").ap()
    dtab = nc.dram_tensor("dtab", [76, 128, 512], BF16, kind="ExternalInput").ap()
    attn0T = nc.dram_tensor("attn0T", [D, TPC], F32, kind="ExternalOutput").ap()

    NQT = TPC // 512
    NKC = S // 128

    ch = Ch()
    with ExitStack() as st:
        sb = lambda n, sh, dt=BF16: st.enter_context(nc.sbuf_tensor(n, sh, dt))
        ka = sb("ka", [66, S]); vsb = sb("vsb", [128, 4096]); qp = sb("qp", [128, TPC])
        qa = sb("qa", [66, TPC]); phat = sb("phat", [128, TPC]); dts = sb("dts", [128, 76 * 512])
        ctx = sb("ctx", [128, 8 * TPC]); wo = sb("wo", [128, 8 * 1024])
        sqk = sb("sqk", [64, 512]); sqq = sb("sqq", [64, 512])
        kn512 = sb("kn512", [1, 512], F32); kmx = sb("kmx", [1, 1], F32); qn2 = sb("qn2", [1, TPC], F32)
        khat = sb("khat", [1, 1], F32); vqr = sb("vqr", [1, TPC], F32)
        stgh = sb("stgh", [1, TPC]); stgl = sb("stgl", [1, TPC])
        sx = sb("sx", [128, 512], F32); lsb = sb("lsb", [1, 512], F32)
        rhi = sb("rhi", [1, 512]); rlo = sb("rlo", [1, 512]); rlf = sb("rlf", [1, 512], F32)
        ctxf = sb("ctxf", [64, 512], F32)
        o64 = sb("o64", [64, 1]); o128 = sb("o128", [128, 1]); o1x64 = sb("o1x64", [1, 64])
        outf = sb("outf", [128, 512], F32)
        sps = st.enter_context(nc.psum_tensor("sps", [128, 2048], F32))
        cps = st.enter_context(nc.psum_tensor("cps", [66, 2048], F32))
        sems = [st.enter_context(nc.semaphore(f"S{i}")) for i in range(8)]
        block = st.enter_context(nc.Block())
        for c in range(8):
            ch.d("gpsimd", lambda e, c=c: e.dma_start(out=wo[:, c * 1024:(c + 1) * 1024], in_=Wo[c * 128:(c + 1) * 128, :]))
        for dd in range(76):
            ch.d("sync", lambda e, dd=dd: e.dma_start(out=dts[:, dd * 512:(dd + 1) * 512], in_=dtab[dd, :, :]))
        ch.c("gpsimd", lambda e: e.memset(o64[:], 1.0))
        ch.c("gpsimd", lambda e: e.memset(o128[:], 1.0))
        ch.c("gpsimd", lambda e: e.memset(o1x64[:], 1.0))
        ch.c("gpsimd", lambda e: e.memset(ka[64:66, :], 1.0))

        for h in range(16):
            p, r = h // 2, h % 2
            slope = SLOPES[h]
            if r == 0:
                ch.d("sync", lambda e, p=p: e.dma_start(out=qp[:], in_=q_in[p, :, :]))
            ch.d("sync", lambda e, p=p, r=r: e.dma_start(out=ka[0:64, :], in_=k_in[p, r * 64:r * 64 + 64, :]))
            ch.d("sync", lambda e, h=h: e.dma_start(out=vsb[:], in_=v_in[h, :, :]))
            ch.c("gpsimd", lambda e: e.memset(khat[:], 0.0))
            for kt in range(S // 512):
                ch.c("scalar", lambda e, kt=kt: e.activation(out=sqk[:], in_=ka[0:64, kt * 512:(kt + 1) * 512], func=AF.Square))
                ch.c("tensor", lambda e: e.matmul(sps[0:1, 0:512], o64[:], sqk[:], start=True, stop=True))
                ch.c("vector", lambda e: e.tensor_copy(out=kn512[:], in_=sps[0:1, 0:512]))
                ch.c("vector", lambda e: e.tensor_reduce(out=kmx[:], in_=kn512[:], axis=AX, op=Alu.max))
                ch.c("vector", lambda e: e.tensor_tensor(out=khat[:], in0=khat[:], in1=kmx[:], op=Alu.max))
            ch.c("scalar", lambda e: e.activation(out=khat[:], in_=khat[:], func=AF.Sqrt))
            for tg in range(NQT):
                ch.c("scalar", lambda e, r=r, tg=tg: e.activation(out=sqq[:], in_=qp[r * 64:r * 64 + 64, tg * 512:(tg + 1) * 512], func=AF.Square))
                ch.c("tensor", lambda e: e.matmul(sps[0:1, 0:512], o64[:], sqq[:], start=True, stop=True))
                ch.c("vector", lambda e, tg=tg: e.tensor_copy(out=qn2[:, tg * 512:(tg + 1) * 512], in_=sps[0:1, 0:512]))
            ch.c("scalar", lambda e: e.activation(out=qn2[:], in_=qn2[:], func=AF.Sqrt))
            ch.c("vector", lambda e: e.tensor_scalar(out=vqr[:], in0=qn2[:], scalar1=khat[0:1, 0:1], scalar2=-1.03, op0=Alu.mult, op1=Alu.mult))
            ch.c("vector", lambda e, r=r: e.tensor_copy(out=qa[0:64, :], in_=qp[r * 64:r * 64 + 64, :]))
            ch.c("vector", lambda e: e.tensor_copy(out=stgh[:], in_=vqr[:]))
            ch.c("vector", lambda e: e.tensor_tensor(out=qn2[:], in0=vqr[:], in1=stgh[:], op=Alu.subtract))
            ch.c("vector", lambda e: e.tensor_copy(out=stgl[:], in_=qn2[:]))
            ch.d("sync", lambda e: e.dma_start(out=qa[64:65, :], in_=stgh[:]))
            ch.d("sync", lambda e: e.dma_start(out=qa[65:66, :], in_=stgl[:]))
            for kc in range(NKC):
                kb = kc * 128
                for qt in range(NQT):
                    ch.c("tensor", lambda e, kb=kb, qt=qt: e.matmul(
                        sps[:, qt * 512:(qt + 1) * 512], ka[:, kb:kb + 128], qa[:, qt * 512:(qt + 1) * 512], start=True, stop=True))
                for qt in range(NQT):
                    dd = (qt * 512 - kb + 8064) // 128
                    ch.c("vector", lambda e, qt=qt, dd=dd, slope=slope: e.scalar_tensor_tensor(
                        out=sx[:], in0=dts[:, dd * 512:(dd + 1) * 512], scalar=slope, in1=sps[:, qt * 512:(qt + 1) * 512],
                        op0=Alu.mult, op1=Alu.add))
                    ch.c("scalar", lambda e, qt=qt: e.activation(
                        out=phat[:, qt * 512:(qt + 1) * 512], in_=sx[:], func=AF.Exp, bias=0.0, scale=0.125))
                for qt in range(NQT):
                    ch.c("tensor", lambda e, kc=kc, qt=qt: e.matmul(
                        cps[0:64, qt * 512:(qt + 1) * 512], vsb[:, kc * 64:(kc + 1) * 64], phat[:, qt * 512:(qt + 1) * 512],
                        start=(kc == 0), stop=(kc == NKC - 1)))
                    ch.c("tensor", lambda e, kc=kc, qt=qt: e.matmul(
                        cps[64:65, qt * 512:(qt + 1) * 512], o128[:], phat[:, qt * 512:(qt + 1) * 512],
                        start=(kc == 0), stop=(kc == NKC - 1)))
            for qt in range(NQT):
                ch.c("vector", lambda e, qt=qt: e.tensor_copy(out=lsb[:], in_=cps[64:65, qt * 512:(qt + 1) * 512]))
                ch.c("vector", lambda e: e.reciprocal(out=rlf[:], in_=lsb[:]))
                ch.c("vector", lambda e: e.tensor_copy(out=rhi[:], in_=rlf[:]))
                ch.c("vector", lambda e: e.tensor_tensor(out=lsb[:], in0=rlf[:], in1=rhi[:], op=Alu.subtract))
                ch.c("vector", lambda e: e.tensor_copy(out=rlo[:], in_=lsb[:]))
                ch.c("tensor", lambda e: e.matmul(sps[0:64, 0:512], o1x64[:], rhi[:], start=True, stop=False))
                ch.c("tensor", lambda e: e.matmul(sps[0:64, 0:512], o1x64[:], rlo[:], start=False, stop=True))
                ch.c("vector", lambda e, qt=qt: e.tensor_copy(out=ctxf[:], in_=cps[0:64, qt * 512:(qt + 1) * 512]))
                ch.c("vector", lambda e, qt=qt, p=p, r=r: e.tensor_tensor(
                    out=ctx[r * 64:r * 64 + 64, p * TPC + qt * 512: p * TPC + (qt + 1) * 512], in0=ctxf[:], in1=sps[0:64, 0:512], op=Alu.mult))
        for dmc in range(8):
            for qt in range(NQT):
                for c in range(8):
                    ch.c("tensor", lambda e, c=c, dmc=dmc, qt=qt: e.matmul(
                        sps[:, 0:512], wo[:, c * 1024 + dmc * 128: c * 1024 + dmc * 128 + 128],
                        ctx[:, c * TPC + qt * 512: c * TPC + (qt + 1) * 512], start=(c == 0), stop=(c == 7)))
                ch.c("vector", lambda e: e.tensor_copy(out=outf[:], in_=sps[:, 0:512]))
                ch.d("sync", lambda e, dmc=dmc, qt=qt: e.dma_start(
                    out=attn0T[dmc * 128:(dmc + 1) * 128, qt * 512:(qt + 1) * 512], in_=outf[:]))
        ch.emit(nc, block, sems)
    return nc



def _quant_tile(ch, xt, outb, W, m, u, g, t, sg):
    nb = W // 16
    x3 = xt.rearrange("p (n b) -> p n b", b=16)
    mw = int(m.shape[1])
    m3 = bass.AP(m, 0, [[mw, 128], [1, nb], [1, 1]])
    mb = bass.AP(m, 0, [[mw, 128], [1, nb], [0, 16]])
    ch.c("vector", lambda e: e.tensor_reduce(out=m3, in_=x3, axis=AX, op=Alu.max, apply_absolute_value=True))
    ch.c("vector", lambda e: e.tensor_scalar(out=m[0:128, 0:nb], in0=m[0:128, 0:nb], scalar1=1.0 / 6.0, scalar2=1e-12, op0=Alu.mult, op1=Alu.max))
    ch.c("vector", lambda e: e.reciprocal(out=g[0:128, 0:nb], in_=m[0:128, 0:nb]))
    gb = bass.AP(g, 0, [[int(g.shape[1]), 128], [1, nb], [0, 16]])
    ch.c("vector", lambda e: e.tensor_tensor(out=u[0:128, 0:W].rearrange("p (n b) -> p n b", b=16), in0=x3, in1=gb, op=Alu.mult))
    ch.c("scalar", lambda e: e.activation(out=sg[0:128, 0:W], in_=u[0:128, 0:W], func=AF.Sign))
    ch.c("vector", lambda e: e.scalar_tensor_tensor(out=t[0:128, 0:W], in0=u[0:128, 0:W], scalar=-1.0, in1=u[0:128, 0:W], op0=Alu.mult, op1=Alu.max))
    ch.c("vector", lambda e: e.tensor_scalar(out=g[0:128, 0:W], in0=t[0:128, 0:W], scalar1=0.25, scalar2=0.5, op0=Alu.is_gt, op1=Alu.mult))
    for thr, w in ((0.75, 0.5), (1.25, 0.5), (1.75, 0.5), (2.5, 1.0), (3.5, 1.0), (5.0, 2.0)):
        ch.c("vector", lambda e, thr=thr, w=w: e.tensor_scalar(out=u[0:128, 0:W], in0=t[0:128, 0:W], scalar1=thr, scalar2=w, op0=Alu.is_gt, op1=Alu.mult))
        ch.c("vector", lambda e: e.tensor_tensor(out=g[0:128, 0:W], in0=g[0:128, 0:W], in1=u[0:128, 0:W], op=Alu.add))
    ch.c("vector", lambda e: e.tensor_tensor(out=g[0:128, 0:W], in0=g[0:128, 0:W], in1=sg[0:128, 0:W], op=Alu.mult))
    ch.c("vector", lambda e: e.tensor_tensor(out=outb.rearrange("p (n b) -> p n b", b=16),
                                             in0=g[0:128, 0:W].rearrange("p (n b) -> p n b", b=16), in1=mb, op=Alu.mult))


def _transp(ch, src_ap, dst_ap, pt, idn):
    ch.c("tensor", lambda e: e.transpose(out=pt[:], in_=src_ap, identity=idn[:]))
    ch.c("vector", lambda e: e.tensor_copy(out=dst_ap, in_=pt[:]))


IH = 2816

def build_l0ffn():
    nc = bass.Bass("TRN2", target_bir_lowering=False)
    xn_i = nc.dram_tensor("xn_i", [TPC, D], F32, kind="ExternalInput").ap()
    x_i = nc.dram_tensor("x_i", [TPC, D], F32, kind="ExternalInput").ap()
    at_i = nc.dram_tensor("at_i", [TPC, D], F32, kind="ExternalInput").ap()
    wg_i = nc.dram_tensor("wg_i", [IH, D], F32, kind="ExternalInput").ap()
    wu_i = nc.dram_tensor("wu_i", [IH, D], F32, kind="ExternalInput").ap()
    wd_i = nc.dram_tensor("wd_i", [D, IH], F32, kind="ExternalInput").ap()
    x1_o = nc.dram_tensor("x1_o", [TPC, D], F32, kind="ExternalOutput").ap()
    xn1_o = nc.dram_tensor("xn1_o", [TPC, D], F32, kind="ExternalOutput").ap()
    wgT_d = nc.dram_tensor("wgT_d", [D, IH], BF16).ap()
    wuT_d = nc.dram_tensor("wuT_d", [D, IH], BF16).ap()
    wdT_d = nc.dram_tensor("wdT_d", [IH, D], BF16).ap()

    NTG = TPC // 128
    HTS = [(0, 512), (512, 512), (1024, 512), (1536, 512), (2048, 512), (2560, 256)]

    ch = Ch()
    with ExitStack() as st:
        sb = lambda n, sh, dt=BF16: st.enter_context(nc.sbuf_tensor(n, sh, dt))
        xqT = sb("xqT", [128, 8 * TPC])
        wdT = sb("wdT", [128, 22 * 1024])
        wgs = sb("wgs", [128, 8 * 512]); wus = sb("wus", [128, 8 * 512])
        ld = sb("ld", [128, IH], F32)
        qb = sb("qb", [128, IH])
        mq = sb("mq", [128, IH // 16], F32)
        uq = sb("uq", [128, IH], F32); gq = sb("gq", [128, IH], F32)
        tq = sb("tq", [128, IH], F32); sgq = sb("sgq", [128, IH], F32)
        hqT = sb("hqT", [128, 22 * 128])
        ts0 = sb("ts0", [128, 128])
        idn = sb("idn", [128, 128])
        hs = sb("hs", [128, 512], F32)
        hf = sb("hf", [128, 512], F32)
        xt = sb("xt", [128, D], F32); att = sb("att", [128, D], F32)
        x1t = sb("x1t", [128, D], F32); xn1t = sb("xn1t", [128, D], F32)
        dxs = sb("dxs", [128, D], F32); sqs = sb("sqs", [128, D], F32)
        st1 = sb("st1", [128, 1], F32); st2 = sb("st2", [128, 1], F32)
        pg = st.enter_context(nc.psum_tensor("pg", [128, 512], F32))
        pu = st.enter_context(nc.psum_tensor("pu", [128, 512], F32))
        pd = st.enter_context(nc.psum_tensor("pd", [128, 512], F32))
        pt = st.enter_context(nc.psum_tensor("pt", [128, 128], BF16))
        sems = [st.enter_context(nc.semaphore(f"S{i}")) for i in range(8)]
        block = st.enter_context(nc.Block())

        ch.c("gpsimd", lambda e: e.memset(ts0[:], 1.0))
        ch.c("gpsimd", lambda e: e.affine_select(out=idn[:], in_=ts0[:], pattern=[[-1, 128]], compare_op=Alu.is_equal, fill=0.0, base=0, channel_multiplier=1))

        for W_in, T_out, nt in ((wg_i, wgT_d, 22), (wu_i, wuT_d, 22)):
            for ttile in range(nt):
                ch.d("sync", lambda e, W_in=W_in, tt=ttile: e.dma_start(out=ld[:, 0:D], in_=W_in[tt * 128:(tt + 1) * 128, :]))
                _quant_tile(ch, ld[:, 0:D], qb[:, 0:D], D, mq, uq, gq, tq, sgq)
                for fc in range(8):
                    _transp(ch, qb[:, fc * 128:(fc + 1) * 128], ts0[:], pt, idn)
                    ch.d("sync", lambda e, T_out=T_out, fc=fc, tt=ttile: e.dma_start(
                        out=T_out[fc * 128:(fc + 1) * 128, tt * 128:(tt + 1) * 128], in_=ts0[:]))
        for ttile in range(8):
            ch.d("sync", lambda e, tt=ttile: e.dma_start(out=ld[:], in_=wd_i[tt * 128:(tt + 1) * 128, :]))
            _quant_tile(ch, ld[:], qb[:], IH, mq, uq, gq, tq, sgq)
            for hc in range(22):
                _transp(ch, qb[:, hc * 128:(hc + 1) * 128], ts0[:], pt, idn)
                ch.d("sync", lambda e, hc=hc, tt=ttile: e.dma_start(
                    out=wdT_d[hc * 128:(hc + 1) * 128, tt * 128:(tt + 1) * 128], in_=ts0[:]))

        for tg in range(NTG):
            ch.d("sync", lambda e, tg=tg: e.dma_start(out=ld[:, 0:D], in_=xn_i[tg * 128:(tg + 1) * 128, :]))
            _quant_tile(ch, ld[:, 0:D], qb[:, 0:D], D, mq, uq, gq, tq, sgq)
            for fc in range(8):
                _transp(ch, qb[:, fc * 128:(fc + 1) * 128], xqT[:, fc * TPC + tg * 128: fc * TPC + (tg + 1) * 128], pt, idn)

        for hc in range(22):
            ch.d("sync", lambda e, hc=hc: e.dma_start(out=wdT[:, hc * 1024:(hc + 1) * 1024], in_=wdT_d[hc * 128:(hc + 1) * 128, :]))

        for tg in range(NTG):
            ch.d("sync", lambda e, tg=tg: e.dma_start(out=xt[:], in_=x_i[tg * 128:(tg + 1) * 128, :]))
            ch.d("sync", lambda e, tg=tg: e.dma_start(out=att[:], in_=at_i[tg * 128:(tg + 1) * 128, :]))
            for hti, (h0, hw) in enumerate(HTS):
                for fc in range(8):
                    ch.d("sync", lambda e, fc=fc, h0=h0, hw=hw: e.dma_start(out=wgs[:, fc * 512: fc * 512 + hw], in_=wgT_d[fc * 128:(fc + 1) * 128, h0:h0 + hw]))
                    ch.d("sync", lambda e, fc=fc, h0=h0, hw=hw: e.dma_start(out=wus[:, fc * 512: fc * 512 + hw], in_=wuT_d[fc * 128:(fc + 1) * 128, h0:h0 + hw]))
                for fc in range(8):
                    ch.c("tensor", lambda e, fc=fc, tg=tg, hw=hw: e.matmul(
                        pg[:, 0:hw], xqT[:, fc * TPC + tg * 128: fc * TPC + (tg + 1) * 128], wgs[:, fc * 512: fc * 512 + hw],
                        start=(fc == 0), stop=(fc == 7)))
                for fc in range(8):
                    ch.c("tensor", lambda e, fc=fc, tg=tg, hw=hw: e.matmul(
                        pu[:, 0:hw], xqT[:, fc * TPC + tg * 128: fc * TPC + (tg + 1) * 128], wus[:, fc * 512: fc * 512 + hw],
                        start=(fc == 0), stop=(fc == 7)))
                ch.c("scalar", lambda e, hw=hw: e.activation(out=hs[:, 0:hw], in_=pg[:, 0:hw], func=AF.Silu))
                ch.c("vector", lambda e, hw=hw: e.tensor_tensor(out=hf[:, 0:hw], in0=hs[:, 0:hw], in1=pu[:, 0:hw], op=Alu.mult))
                _quant_tile(ch, hf[:, 0:hw], qb[:, 0:hw], hw, mq, uq, gq, tq, sgq)
                for sc in range(hw // 128):
                    _transp(ch, qb[:, sc * 128:(sc + 1) * 128],
                            hqT[:, (h0 // 128 + sc) * 128:(h0 // 128 + sc + 1) * 128], pt, idn)
            for dmt in range(2):
                for hc in range(22):
                    ch.c("tensor", lambda e, hc=hc, dmt=dmt: e.matmul(
                        pd[:], hqT[:, hc * 128:(hc + 1) * 128], wdT[:, hc * 1024 + dmt * 512: hc * 1024 + dmt * 512 + 512],
                        start=(hc == 0), stop=(hc == 21)))
                ch.c("vector", lambda e, dmt=dmt: e.tensor_tensor(out=x1t[:, dmt * 512:(dmt + 1) * 512], in0=att[:, dmt * 512:(dmt + 1) * 512], in1=pd[:], op=Alu.add))
            ch.c("vector", lambda e: e.tensor_tensor(out=x1t[:], in0=x1t[:], in1=xt[:], op=Alu.add))
            _ln_tile(ch, x1t[:], xn1t[:], dxs[:], sqs[:], st1[:], st2[:], D)
            ch.d("sync", lambda e, tg=tg: e.dma_start(out=x1_o[tg * 128:(tg + 1) * 128, :], in_=x1t[:]))
            ch.d("sync", lambda e, tg=tg: e.dma_start(out=xn1_o[tg * 128:(tg + 1) * 128, :], in_=xn1t[:]))
        ch.emit(nc, block, sems)
    return nc



def _rms_norm(ch, src_ps, dst_ap, W, sqb, nps, rr, rhi2, rlo2, bps, evf, o64, o1x64, wcol, eps=1.1920929e-07):
    ch.c("scalar", lambda e: e.activation(out=sqb[:, 0:W], in_=src_ps, func=AF.Square))
    ch.c("tensor", lambda e: e.matmul(nps[0:1, 0:W], o64[:], sqb[:, 0:W], start=True, stop=True))
    ch.c("vector", lambda e: e.tensor_copy(out=rr[:, 0:W], in_=nps[0:1, 0:W]))
    ch.c("vector", lambda e: e.tensor_scalar(out=rr[:, 0:W], in0=rr[:, 0:W], scalar1=1.0 / 64.0, scalar2=eps, op0=Alu.mult, op1=Alu.add))
    ch.c("scalar", lambda e: e.activation(out=rr[:, 0:W], in_=rr[:, 0:W], func=AF.Sqrt))
    ch.c("vector", lambda e: e.reciprocal(out=rr[:, 0:W], in_=rr[:, 0:W]))
    ch.c("vector", lambda e: e.tensor_copy(out=rhi2[:, 0:W], in_=rr[:, 0:W]))
    ch.c("vector", lambda e: e.tensor_tensor(out=rr[:, 0:W], in0=rr[:, 0:W], in1=rhi2[:, 0:W], op=Alu.subtract))
    ch.c("vector", lambda e: e.tensor_copy(out=rlo2[:, 0:W], in_=rr[:, 0:W]))
    ch.c("tensor", lambda e: e.matmul(bps[0:64, 0:W], o1x64[:], rhi2[:, 0:W], start=True, stop=False))
    ch.c("tensor", lambda e: e.matmul(bps[0:64, 0:W], o1x64[:], rlo2[:, 0:W], start=False, stop=True))
    ch.c("vector", lambda e: e.tensor_copy(out=evf[:, 0:W], in_=src_ps))
    ch.c("vector", lambda e: e.tensor_tensor(out=evf[:, 0:W], in0=evf[:, 0:W], in1=bps[0:64, 0:W], op=Alu.mult))
    ch.c("vector", lambda e: e.tensor_scalar(out=dst_ap, in0=evf[:, 0:W], scalar1=wcol, scalar2=None, op0=Alu.mult))


def build_l1kqv():
    nc = bass.Bass("TRN2", target_bir_lowering=False)
    xn1T = nc.dram_tensor("xn1T", [D, S], F32, kind="ExternalInput").ap()
    wdq = nc.dram_tensor("wdq", [64, 512], F32, kind="ExternalInput").ap()
    wuq = nc.dram_tensor("wuq", [512, 64], F32, kind="ExternalInput").ap()
    wdkv = nc.dram_tensor("wdkv", [64, 1024], F32, kind="ExternalInput").ap()
    wukv = nc.dram_tensor("wukv", [1024, 128], F32, kind="ExternalInput").ap()
    wqr = nc.dram_tensor("wqr", [64, 64], F32, kind="ExternalInput").ap()
    wkr = nc.dram_tensor("wkr", [64, 64], F32, kind="ExternalInput").ap()
    qnw = nc.dram_tensor("qnw", [64, 1], F32, kind="ExternalInput").ap()
    knw = nc.dram_tensor("knw", [64, 1], F32, kind="ExternalInput").ap()
    rw = nc.dram_tensor("rw", [D, 8], F32, kind="ExternalInput").ap()

    qc_o = nc.dram_tensor("qc_o", [4, 64, S], BF16, kind="ExternalOutput").ap()
    qr_o = nc.dram_tensor("qr_o", [4, 64, S], BF16, kind="ExternalOutput").ap()
    vq_o = nc.dram_tensor("vq_o", [4, 1, S], F32, kind="ExternalOutput").ap()
    kc_o = nc.dram_tensor("kc_o", [4, 64, S], BF16, kind="ExternalOutput").ap()
    kr_o = nc.dram_tensor("kr_o", [4, 64, S], BF16, kind="ExternalOutput").ap()
    v_o = nc.dram_tensor("v_o", [4, 128, 4096], BF16, kind="ExternalOutput").ap()
    rsc_o = nc.dram_tensor("rsc_o", [S, 8], F32, kind="ExternalOutput").ap()
    thr_o = nc.dram_tensor("thr_o", [8, 4], F32, kind="ExternalOutput").ap()


    ch = Ch()
    with ExitStack() as st:
        sb = lambda n, sh, dt=BF16: st.enter_context(nc.sbuf_tensor(n, sh, dt))
        xh = sb("xh", [64, S])
        cq = sb("cq", [128, 4 * TPC])
        ckv = sb("ckv", [128, 8 * 512])
        kcs = sb("kcs", [64, S]); krs = sb("krs", [64, S])
        vstage = sb("vstage", [128, 4096]); vtmp = sb("vtmp", [64, 512])
        qcs = sb("qcs", [64, TPC]); qrs = sb("qrs", [64, TPC])
        w1 = sb("w1", [64, 512])
        w2t = sb("w2t", [128, 4 * 64])
        w3 = sb("w3", [64, 1024])
        w4 = sb("w4", [128, 8 * 128])
        w5 = sb("w5", [64, 64]); w6 = sb("w6", [64, 64])
        qnwc = sb("qnwc", [64, 1], F32); knwc = sb("knwc", [64, 1], F32)
        rwT = sb("rwT", [128, 8 * 8], F32)
        xcs = sb("xcs", [128, 8 * 128], F32)
        sqb = sb("sqb", [64, 512])
        rr = sb("rr", [1, 512], F32); rhi2 = sb("rhi2", [1, 512]); rlo2 = sb("rlo2", [1, 512])
        evf = sb("evf", [64, 512], F32)
        o64 = sb("o64", [64, 1]); o1x64 = sb("o1x64", [1, 64]); o1x128 = sb("o1x128", [1, 128])
        kn512 = sb("kn512", [1, 512], F32); kmx = sb("kmx", [1, 1], F32); khat = sb("khat", [1, 1], F32)
        qn2 = sb("qn2", [1, TPC], F32); vqr = sb("vqr", [1, TPC], F32)
        rsf = sb("rsf", [128, 8], F32); rmax = sb("rmax", [128, 1], F32); rsum = sb("rsum", [128, 1], F32)
        scT = sb("scT", [8, S], F32)
        cmpb = sb("cmpb", [8, S], F32)
        lo = sb("lo", [8, 1], F32); hiv = sb("hiv", [8, 1], F32); mid = sb("mid", [8, 1], F32)
        cnt = sb("cnt", [8, 1], F32); sel = sb("sel", [8, 1], F32); nsel = sb("nsel", [8, 1], F32)
        tv = sb("tv", [8, 1], F32); need = sb("need", [8, 1], F32)
        rlo_i = sb("rlo_i", [8, 1], F32); rhi_i = sb("rhi_i", [8, 1], F32)
        thrs = sb("thrs", [8, 4], F32)
        o1x128m = sb("o1x128m", [1, 128])
        idn_kqv = sb("idn_kqv", [128, 128]); tsk = sb("tsk", [128, 128])
        idnf_kqv = sb("idnf_kqv", [128, 128], F32); tskf = sb("tskf", [128, 128], F32)
        rsbf_kqv = sb("rsbf_kqv", [128, 8], F32)
        ps = st.enter_context(nc.psum_tensor("ps", [128, 512], F32))
        pkv = st.enter_context(nc.psum_tensor("pkv", [128, 512], F32))
        nps = st.enter_context(nc.psum_tensor("nps", [1, 512], F32))
        bps = st.enter_context(nc.psum_tensor("bps", [64, 512], F32))
        pr = st.enter_context(nc.psum_tensor("pr", [128, 8], F32))
        ptv = st.enter_context(nc.psum_tensor("ptv", [128, 64], BF16))
        ptr = st.enter_context(nc.psum_tensor("ptr", [8, 128], F32))
        sems = [st.enter_context(nc.semaphore(f"S{i}")) for i in range(8)]
        block = st.enter_context(nc.Block())

        ch.c("gpsimd", lambda e: e.memset(o64[:], 1.0))
        ch.c("gpsimd", lambda e: e.memset(o1x64[:], 1.0))
        ch.c("gpsimd", lambda e: e.memset(o1x128[:], 1.0))
        ch.c("gpsimd", lambda e: e.memset(tsk[:], 1.0))
        ch.c("gpsimd", lambda e: e.affine_select(out=idn_kqv[:], in_=tsk[:], pattern=[[-1, 128]], compare_op=Alu.is_equal, fill=0.0, base=0, channel_multiplier=1))
        ch.c("gpsimd", lambda e: e.memset(tskf[:], 1.0))
        ch.c("gpsimd", lambda e: e.affine_select(out=idnf_kqv[:], in_=tskf[:], pattern=[[-1, 128]], compare_op=Alu.is_equal, fill=0.0, base=0, channel_multiplier=1))
        ch.d("gpsimd", lambda e: e.dma_start(out=w1[:], in_=wdq[:, :]))
        for lc in range(4):
            ch.d("gpsimd", lambda e, lc=lc: e.dma_start(out=w2t[:, lc * 64:(lc + 1) * 64], in_=wuq[lc * 128:(lc + 1) * 128, :]))
        ch.d("gpsimd", lambda e: e.dma_start(out=w3[:], in_=wdkv[:, :]))
        for lc in range(8):
            ch.d("gpsimd", lambda e, lc=lc: e.dma_start(out=w4[:, lc * 128:(lc + 1) * 128], in_=wukv[lc * 128:(lc + 1) * 128, :]))
        ch.d("gpsimd", lambda e: e.dma_start(out=w5[:], in_=wqr[:, :]))
        ch.d("gpsimd", lambda e: e.dma_start(out=w6[:], in_=wkr[:, :]))
        ch.d("sync", lambda e: e.dma_start(out=qnwc[:], in_=qnw[:, :]))
        ch.d("sync", lambda e: e.dma_start(out=knwc[:], in_=knw[:, :]))
        for fc in range(8):
            ch.d("gpsimd", lambda e, fc=fc: e.dma_start(out=rwT[:, fc * 8:(fc + 1) * 8], in_=rw[fc * 128:(fc + 1) * 128, :]))

        for hi in range(4):
            ch.d("gpsimd", lambda e, hi=hi: e.dma_start(out=xh[:], in_=xn1T[hi * 64:(hi + 1) * 64, :]))
            ch.c("gpsimd", lambda e: e.memset(khat[:], 0.0))
            for tt in range(S // 512):
                t0 = tt * 512
                for lc in range(8):
                    ch.c("tensor", lambda e, lc=lc, t0=t0: e.matmul(ps[:], w3[:, lc * 128:(lc + 1) * 128], xh[:, t0:t0 + 512], start=True, stop=True))
                    ch.c("vector", lambda e, lc=lc: e.tensor_copy(out=ckv[:, lc * 512:(lc + 1) * 512], in_=ps[:]))
                for lc in range(8):
                    ch.c("tensor", lambda e, lc=lc: e.matmul(pkv[:], w4[:, lc * 128:(lc + 1) * 128], ckv[:, lc * 512:(lc + 1) * 512], start=(lc == 0), stop=(lc == 7)))
                _rms_norm(ch, pkv[0:64, 0:512], kcs[0:64, t0:t0 + 512], 512, sqb, nps, rr, rhi2, rlo2, bps, evf, o64, o1x64, knwc[:])
                ch.c("vector", lambda e: e.tensor_copy(out=vtmp[:], in_=pkv[64:128, 0:512]))
                for j in range(4):
                    ch.c("tensor", lambda e, j=j: e.transpose(out=ptv[:], in_=vtmp[:, j * 128:(j + 1) * 128], identity=idn_kqv[0:64, 0:64]))
                    ch.c("vector", lambda e, j=j, tt=tt: e.tensor_copy(out=vstage[:, (tt * 4 + j) * 64:(tt * 4 + j + 1) * 64], in_=ptv[:, 0:64]))
                ch.c("tensor", lambda e, t0=t0: e.matmul(ps[0:64, 0:512], w6[:], xh[:, t0:t0 + 512], start=True, stop=True))
                ch.c("vector", lambda e, t0=t0: e.tensor_copy(out=krs[:, t0:t0 + 512], in_=ps[0:64, 0:512]))
                ch.c("scalar", lambda e, t0=t0: e.activation(out=sqb[:], in_=kcs[:, t0:t0 + 512], func=AF.Square))
                ch.c("tensor", lambda e: e.matmul(nps[:], o64[:], sqb[:], start=True, stop=False))
                ch.c("scalar", lambda e, t0=t0: e.activation(out=sqb[:], in_=krs[:, t0:t0 + 512], func=AF.Square))
                ch.c("tensor", lambda e: e.matmul(nps[:], o64[:], sqb[:], start=False, stop=True))
                ch.c("vector", lambda e: e.tensor_copy(out=kn512[:], in_=nps[:]))
                ch.c("vector", lambda e: e.tensor_reduce(out=kmx[:], in_=kn512[:], axis=AX, op=Alu.max))
                ch.c("vector", lambda e: e.tensor_tensor(out=khat[:], in0=khat[:], in1=kmx[:], op=Alu.max))
            ch.c("scalar", lambda e: e.activation(out=khat[:], in_=khat[:], func=AF.Sqrt))
            ch.d("sync", lambda e, hi=hi: e.dma_start(out=kc_o[hi, :, :], in_=kcs[:]))
            ch.d("sync", lambda e, hi=hi: e.dma_start(out=kr_o[hi, :, :], in_=krs[:]))
            ch.d("sync", lambda e, hi=hi: e.dma_start(out=v_o[hi, :, :], in_=vstage[:]))
            for tb in range(S // TPC):
                b0 = tb * TPC
                for lc in range(4):
                    for ttt in range(4):
                        ch.c("tensor", lambda e, lc=lc, ttt=ttt, b0=b0: e.matmul(
                            ps[:], w1[:, lc * 128:(lc + 1) * 128], xh[:, b0 + ttt * 512: b0 + (ttt + 1) * 512], start=True, stop=True))
                        ch.c("vector", lambda e, lc=lc, ttt=ttt: e.tensor_copy(out=cq[:, lc * TPC + ttt * 512: lc * TPC + (ttt + 1) * 512], in_=ps[:]))
                for ttt in range(4):
                    for lc in range(4):
                        ch.c("tensor", lambda e, lc=lc, ttt=ttt: e.matmul(
                            pkv[0:64, 0:512], w2t[:, lc * 64:(lc + 1) * 64], cq[:, lc * TPC + ttt * 512: lc * TPC + (ttt + 1) * 512],
                            start=(lc == 0), stop=(lc == 3)))
                    _rms_norm(ch, pkv[0:64, 0:512], qcs[0:64, ttt * 512:(ttt + 1) * 512], 512, sqb, nps, rr, rhi2, rlo2, bps, evf, o64, o1x64, qnwc[:])
                    ch.c("tensor", lambda e, ttt=ttt: e.matmul(ps[0:64, 0:512], w5[:], qcs[:, ttt * 512:(ttt + 1) * 512], start=True, stop=True))
                    ch.c("vector", lambda e, ttt=ttt: e.tensor_copy(out=qrs[:, ttt * 512:(ttt + 1) * 512], in_=ps[0:64, 0:512]))
                    ch.c("scalar", lambda e, ttt=ttt: e.activation(out=sqb[:], in_=qcs[:, ttt * 512:(ttt + 1) * 512], func=AF.Square))
                    ch.c("tensor", lambda e: e.matmul(nps[:], o64[:], sqb[:], start=True, stop=False))
                    ch.c("scalar", lambda e, ttt=ttt: e.activation(out=sqb[:], in_=qrs[:, ttt * 512:(ttt + 1) * 512], func=AF.Square))
                    ch.c("tensor", lambda e: e.matmul(nps[:], o64[:], sqb[:], start=False, stop=True))
                    ch.c("vector", lambda e, ttt=ttt: e.tensor_copy(out=qn2[:, ttt * 512:(ttt + 1) * 512], in_=nps[:]))
                ch.c("scalar", lambda e: e.activation(out=qn2[:], in_=qn2[:], func=AF.Sqrt))
                ch.c("vector", lambda e: e.tensor_scalar(out=vqr[:], in0=qn2[:], scalar1=khat[0:1, 0:1], scalar2=-1.03, op0=Alu.mult, op1=Alu.mult))
                ch.d("sync", lambda e, hi=hi, b0=b0: e.dma_start(out=qc_o[hi, :, b0:b0 + TPC], in_=qcs[:]))
                ch.d("sync", lambda e, hi=hi, b0=b0: e.dma_start(out=qr_o[hi, :, b0:b0 + TPC], in_=qrs[:]))
                ch.d("sync", lambda e, hi=hi, b0=b0: e.dma_start(out=vq_o[hi, :, b0:b0 + TPC], in_=vqr[:]))

        for tc in range(S // 128):
            for fc in range(8):
                ch.d("sync", lambda e, fc=fc, tc=tc: e.dma_start(out=xcs[:, fc * 128:(fc + 1) * 128], in_=xn1T[fc * 128:(fc + 1) * 128, tc * 128:(tc + 1) * 128]))
            for fc in range(8):
                ch.c("tensor", lambda e, fc=fc: e.matmul(pr[:], xcs[:, fc * 128:(fc + 1) * 128], rwT[:, fc * 8:(fc + 1) * 8], start=(fc == 0), stop=(fc == 7)))
            ch.c("vector", lambda e: e.tensor_reduce(out=rmax[:], in_=pr[:], axis=AX, op=Alu.max, negate=True))
            ch.c("scalar", lambda e: e.activation(out=rsf[:], in_=pr[:], func=AF.Exp, bias=rmax[:], scale=1.0))
            ch.c("vector", lambda e: e.tensor_reduce(out=rsum[:], in_=rsf[:], axis=AX, op=Alu.add))
            ch.c("vector", lambda e: e.reciprocal(out=rsum[:], in_=rsum[:]))
            ch.c("vector", lambda e: e.tensor_scalar(out=rsf[:], in0=rsf[:], scalar1=rsum[:], scalar2=None, op0=Alu.mult))
            ch.d("sync", lambda e, tc=tc: e.dma_start(out=rsc_o[tc * 128:(tc + 1) * 128, :], in_=rsf[:]))
            ch.c("vector", lambda e: e.tensor_copy(out=rsbf_kqv[:, 0:8], in_=rsf[:]))
            ch.c("tensor", lambda e: e.transpose(out=ptr[:], in_=rsbf_kqv[:, 0:8], identity=idnf_kqv[:]))
            ch.c("vector", lambda e, tc=tc: e.tensor_copy(out=scT[:, tc * 128:(tc + 1) * 128], in_=ptr[:]))

        KCAP = 2049.0
        ch.c("gpsimd", lambda e: e.memset(lo[:], -1.0))
        ch.c("gpsimd", lambda e: e.memset(hiv[:], 2.0))
        for it in range(64):
            ch.c("vector", lambda e: e.tensor_tensor(out=mid[:], in0=lo[:], in1=hiv[:], op=Alu.add))
            ch.c("vector", lambda e: e.tensor_scalar(out=mid[:], in0=mid[:], scalar1=0.5, scalar2=None, op0=Alu.mult))
            ch.c("vector", lambda e: e.tensor_scalar(out=cmpb[:], in0=scT[:], scalar1=mid[:], scalar2=None, op0=Alu.is_gt))
            ch.c("vector", lambda e: e.tensor_reduce(out=cnt[:], in_=cmpb[:], axis=AX, op=Alu.add))
            ch.c("vector", lambda e: e.tensor_scalar(out=sel[:], in0=cnt[:], scalar1=KCAP, scalar2=None, op0=Alu.is_ge))
            ch.c("vector", lambda e: e.tensor_scalar(out=nsel[:], in0=sel[:], scalar1=-1.0, scalar2=1.0, op0=Alu.mult, op1=Alu.add))
            ch.c("vector", lambda e: e.tensor_tensor(out=tv[:], in0=mid[:], in1=lo[:], op=Alu.subtract))
            ch.c("vector", lambda e: e.tensor_tensor(out=tv[:], in0=sel[:], in1=tv[:], op=Alu.mult))
            ch.c("vector", lambda e: e.tensor_tensor(out=lo[:], in0=lo[:], in1=tv[:], op=Alu.add))
            ch.c("vector", lambda e: e.tensor_tensor(out=tv[:], in0=mid[:], in1=hiv[:], op=Alu.subtract))
            ch.c("vector", lambda e: e.tensor_tensor(out=tv[:], in0=nsel[:], in1=tv[:], op=Alu.mult))
            ch.c("vector", lambda e: e.tensor_tensor(out=hiv[:], in0=hiv[:], in1=tv[:], op=Alu.add))
        ch.c("vector", lambda e: e.tensor_scalar(out=cmpb[:], in0=scT[:], scalar1=hiv[:], scalar2=None, op0=Alu.is_gt))
        ch.c("vector", lambda e: e.tensor_reduce(out=cnt[:], in_=cmpb[:], axis=AX, op=Alu.add))
        ch.c("vector", lambda e: e.tensor_scalar(out=need[:], in0=cnt[:], scalar1=-1.0, scalar2=KCAP, op0=Alu.mult, op1=Alu.add))
        ch.c("vector", lambda e: e.tensor_scalar(out=scT[:], in0=scT[:], scalar1=hiv[:], scalar2=None, op0=Alu.is_equal))
        ch.c("gpsimd", lambda e: e.memset(rlo_i[:], -1.0))
        ch.c("gpsimd", lambda e: e.memset(rhi_i[:], 8191.0))
        for it in range(30):
            ch.c("vector", lambda e: e.tensor_tensor(out=mid[:], in0=rlo_i[:], in1=rhi_i[:], op=Alu.add))
            ch.c("vector", lambda e: e.tensor_scalar(out=mid[:], in0=mid[:], scalar1=0.5, scalar2=None, op0=Alu.mult))
            ch.c("gpsimd", lambda e: e.iota(cmpb[:], [[1, S]], base=0, channel_multiplier=0, allow_small_or_imprecise_dtypes=True))
            ch.c("vector", lambda e: e.tensor_scalar(out=cmpb[:], in0=cmpb[:], scalar1=mid[:], scalar2=None, op0=Alu.is_le))
            ch.c("vector", lambda e: e.tensor_tensor(out=cmpb[:], in0=cmpb[:], in1=scT[:], op=Alu.mult))
            ch.c("vector", lambda e: e.tensor_reduce(out=cnt[:], in_=cmpb[:], axis=AX, op=Alu.add))
            ch.c("vector", lambda e: e.tensor_scalar(out=sel[:], in0=cnt[:], scalar1=need[:], scalar2=None, op0=Alu.is_ge))
            ch.c("vector", lambda e: e.tensor_scalar(out=nsel[:], in0=sel[:], scalar1=-1.0, scalar2=1.0, op0=Alu.mult, op1=Alu.add))
            ch.c("vector", lambda e: e.tensor_tensor(out=tv[:], in0=mid[:], in1=rhi_i[:], op=Alu.subtract))
            ch.c("vector", lambda e: e.tensor_tensor(out=tv[:], in0=sel[:], in1=tv[:], op=Alu.mult))
            ch.c("vector", lambda e: e.tensor_tensor(out=rhi_i[:], in0=rhi_i[:], in1=tv[:], op=Alu.add))
            ch.c("vector", lambda e: e.tensor_tensor(out=tv[:], in0=mid[:], in1=rlo_i[:], op=Alu.subtract))
            ch.c("vector", lambda e: e.tensor_tensor(out=tv[:], in0=nsel[:], in1=tv[:], op=Alu.mult))
            ch.c("vector", lambda e: e.tensor_tensor(out=rlo_i[:], in0=rlo_i[:], in1=tv[:], op=Alu.add))
        ch.c("vector", lambda e: e.tensor_copy(out=thrs[:, 0:1], in_=lo[:]))
        ch.c("vector", lambda e: e.tensor_copy(out=thrs[:, 1:2], in_=hiv[:]))
        ch.c("vector", lambda e: e.tensor_copy(out=thrs[:, 2:3], in_=rhi_i[:]))
        ch.c("vector", lambda e: e.tensor_copy(out=thrs[:, 3:4], in_=need[:]))
        ch.d("sync", lambda e: e.dma_start(out=thr_o[:, :], in_=thrs[:]))
        ch.emit(nc, block, sems)
    return nc



def build_l1attn():
    nc = bass.Bass("TRN2", target_bir_lowering=False)
    qc_i = nc.dram_tensor("qc_i", [4, 64, S], BF16, kind="ExternalInput").ap()
    qr_i = nc.dram_tensor("qr_i", [4, 64, S], BF16, kind="ExternalInput").ap()
    vq_i = nc.dram_tensor("vq_i", [4, 1, S], F32, kind="ExternalInput").ap()
    kc_i = nc.dram_tensor("kc_i", [4, 64, S], BF16, kind="ExternalInput").ap()
    kr_i = nc.dram_tensor("kr_i", [4, 64, S], BF16, kind="ExternalInput").ap()
    v_i = nc.dram_tensor("v_i", [4, 128, 4096], BF16, kind="ExternalInput").ap()
    wo = nc.dram_tensor("wo", [64, 64], F32, kind="ExternalInput").ap()
    mtab = nc.dram_tensor("mtab", [4, 128, 512], F32, kind="ExternalInput").ap()
    wol1 = nc.dram_tensor("wol1", [64, 64], F32, kind="ExternalInput").ap()
    at1_o = nc.dram_tensor("at1_o", [4, 64, S], F32, kind="ExternalOutput").ap()

    NQT = S // 512

    ch = Ch()
    with ExitStack() as st:
        sb = lambda n, sh, dt=BF16: st.enter_context(nc.sbuf_tensor(n, sh, dt))
        kcs = sb("kcs", [64, S]); kra = sb("kra", [66, S])
        vsb = sb("vsb", [128, 4096])
        qcs = sb("qcs", [64, S]); qra = sb("qra", [66, S])
        vqf = sb("vqf", [1, S], F32)
        stgh = sb("stgh", [1, S]); stgl = sb("stgl", [1, S]); losq = sb("losq", [1, S], F32)
        mts = sb("mts", [128, 4 * 512], F32)
        phat = sb("phat", [128, 512])
        sx = sb("sx", [128, 512], F32)
        wos = sb("wos", [64, 64])
        lsb = sb("lsb", [1, 512], F32); rhi = sb("rhi", [1, 512]); rlo = sb("rlo", [1, 512]); rlf = sb("rlf", [1, 512], F32)
        ctxf = sb("ctxf", [64, 512], F32)
        ctxn = sb("ctxn", [64, 512])
        o128 = sb("o128", [128, 1]); o1x64 = sb("o1x64", [1, 64])
        outf = sb("outf", [64, 512], F32)
        sps = st.enter_context(nc.psum_tensor("sps", [128, 512], F32))
        cps = st.enter_context(nc.psum_tensor("cps", [66, 512], F32))
        bps = st.enter_context(nc.psum_tensor("bps", [64, 512], F32))
        pso = st.enter_context(nc.psum_tensor("pso", [64, 512], F32))
        sems = [st.enter_context(nc.semaphore(f"S{i}")) for i in range(8)]
        block = st.enter_context(nc.Block())

        ch.c("gpsimd", lambda e: e.memset(o128[:], 1.0))
        ch.c("gpsimd", lambda e: e.memset(o1x64[:], 1.0))
        ch.c("gpsimd", lambda e: e.memset(kra[64:66, :], 1.0))
        ch.d("gpsimd", lambda e: e.dma_start(out=wos[:], in_=wo[:, :]))
        for v in range(4):
            ch.d("sync", lambda e, v=v: e.dma_start(out=mts[:, v * 512:(v + 1) * 512], in_=mtab[v, :, :]))

        for hi in range(4):
            ch.d("sync", lambda e, hi=hi: e.dma_start(out=kcs[:], in_=kc_i[hi, :, :]))
            ch.d("sync", lambda e, hi=hi: e.dma_start(out=kra[0:64, :], in_=kr_i[hi, :, :]))
            ch.d("sync", lambda e, hi=hi: e.dma_start(out=vsb[:], in_=v_i[hi, :, :]))
            ch.d("sync", lambda e, hi=hi: e.dma_start(out=qcs[:], in_=qc_i[hi, :, :]))
            ch.d("sync", lambda e, hi=hi: e.dma_start(out=qra[0:64, :], in_=qr_i[hi, :, :]))
            ch.d("sync", lambda e, hi=hi: e.dma_start(out=vqf[:], in_=vq_i[hi, :, :]))
            ch.c("vector", lambda e: e.tensor_copy(out=stgh[:], in_=vqf[:]))
            ch.c("vector", lambda e: e.tensor_tensor(out=losq[:], in0=vqf[:], in1=stgh[:], op=Alu.subtract))
            ch.c("vector", lambda e: e.tensor_copy(out=stgl[:], in_=losq[:]))
            ch.d("sync", lambda e: e.dma_start(out=qra[64:65, :], in_=stgh[:]))
            ch.d("sync", lambda e: e.dma_start(out=qra[65:66, :], in_=stgl[:]))
            for qt in range(NQT):
                q0 = qt * 512
                nkc = 4 * qt + 4
                for kc in range(nkc):
                    kb = kc * 128
                    ch.c("tensor", lambda e, kb=kb, q0=q0: e.matmul(
                        sps[:], kcs[:, kb:kb + 128], qcs[:, q0:q0 + 512], start=True, stop=False))
                    ch.c("tensor", lambda e, kb=kb, q0=q0: e.matmul(
                        sps[:], kra[:, kb:kb + 128], qra[:, q0:q0 + 512], start=False, stop=True))
                    if kc >= 4 * qt:
                        v = kc - 4 * qt
                        ch.c("vector", lambda e, v=v: e.tensor_tensor(out=sx[:], in0=mts[:, v * 512:(v + 1) * 512], in1=sps[:], op=Alu.add))
                        ch.c("scalar", lambda e: e.activation(out=phat[:], in_=sx[:], func=AF.Exp, bias=0.0, scale=0.125))
                    else:
                        ch.c("scalar", lambda e: e.activation(out=phat[:], in_=sps[:], func=AF.Exp, bias=0.0, scale=0.125))
                    ch.c("tensor", lambda e, kc=kc, nkc=nkc: e.matmul(
                        cps[0:64, :], vsb[:, kc * 64:(kc + 1) * 64], phat[:], start=(kc == 0), stop=(kc == nkc - 1)))
                    ch.c("tensor", lambda e, kc=kc, nkc=nkc: e.matmul(
                        cps[64:65, :], o128[:], phat[:], start=(kc == 0), stop=(kc == nkc - 1)))
                ch.c("vector", lambda e: e.tensor_copy(out=lsb[:], in_=cps[64:65, :]))
                ch.c("vector", lambda e: e.reciprocal(out=rlf[:], in_=lsb[:]))
                ch.c("vector", lambda e: e.tensor_copy(out=rhi[:], in_=rlf[:]))
                ch.c("vector", lambda e: e.tensor_tensor(out=lsb[:], in0=rlf[:], in1=rhi[:], op=Alu.subtract))
                ch.c("vector", lambda e: e.tensor_copy(out=rlo[:], in_=lsb[:]))
                ch.c("tensor", lambda e: e.matmul(bps[:], o1x64[:], rhi[:], start=True, stop=False))
                ch.c("tensor", lambda e: e.matmul(bps[:], o1x64[:], rlo[:], start=False, stop=True))
                ch.c("vector", lambda e: e.tensor_copy(out=ctxf[:], in_=cps[0:64, :]))
                ch.c("vector", lambda e: e.tensor_tensor(out=ctxn[:], in0=ctxf[:], in1=bps[:], op=Alu.mult))
                ch.c("tensor", lambda e: e.matmul(pso[:], wos[:], ctxn[:], start=True, stop=True))
                ch.c("vector", lambda e: e.tensor_copy(out=outf[:], in_=pso[:]))
                ch.d("sync", lambda e, hi=hi, q0=q0: e.dma_start(out=at1_o[hi, :, q0:q0 + 512], in_=outf[:]))
        ch.emit(nc, block, sems)
    return nc



def build_l1moe():
    nc = bass.Bass("TRN2", target_bir_lowering=False)
    xn1T = nc.dram_tensor("xn1T", [D, TPC], F32, kind="ExternalInput").ap()
    x1T = nc.dram_tensor("x1T", [D, TPC], F32, kind="ExternalInput").ap()
    at1T = nc.dram_tensor("at1T", [D, TPC], F32, kind="ExternalInput").ap()
    rscT = nc.dram_tensor("rscT", [8, TPC], F32, kind="ExternalInput").ap()
    thr = nc.dram_tensor("thr", [8, 4], F32, kind="ExternalInput").ap()
    idxl = nc.dram_tensor("idxl", [8, TPC], F32, kind="ExternalInput").ap()
    w1 = nc.dram_tensor("w1", [8, D, 4096], F32, kind="ExternalInput").ap()
    b1T = nc.dram_tensor("b1T", [8, 128, 32], F32, kind="ExternalInput").ap()
    w2 = nc.dram_tensor("w2", [8, 4096, D], F32, kind="ExternalInput").ap()
    b2T = nc.dram_tensor("b2T", [8, 128, 8], F32, kind="ExternalInput").ap()
    x2T_o = nc.dram_tensor("x2T_o", [D, TPC], F32, kind="ExternalOutput").ap()

    ch = Ch()
    with ExitStack() as st:
        sb = lambda n, sh, dt=BF16: st.enter_context(nc.sbuf_tensor(n, sh, dt))
        xb = sb("xb", [128, 8 * TPC])
        w1s = sb("w1s", [128, 8 * 128]); w2s = sb("w2s", [128, 32 * 128])
        b1s = sb("b1s", [128, 32], F32); b2s = sb("b2s", [128, 8], F32)
        gsb = sb("gsb", [8, TPC], F32)
        grow = sb("grow", [1, TPC], F32)
        ghi = sb("ghi", [1, TPC]); glo = sb("glo", [1, TPC]); gtm = sb("gtm", [1, TPC], F32)
        thrs = sb("thrs", [8, 4], F32)
        o1x128m = sb("o1x128m", [1, 128])
        t1 = sb("t1", [128, 512], F32)
        xs1 = sb("xs1", [128, 512], F32)
        ph = st.enter_context(nc.psum_tensor("ph", [128, 512], F32))
        po = st.enter_context(nc.psum_tensor("po", [128, 512], F32))
        pb = st.enter_context(nc.psum_tensor("pb", [128, 512], F32))
        sems = [st.enter_context(nc.semaphore(f"S{i}")) for i in range(8)]
        block = st.enter_context(nc.Block())

        TH = TPC // 2

        ch.d("sync", lambda e: e.dma_start(out=thrs[:], in_=thr[:, :]))
        for fc in range(8):
            ch.d("gpsimd", lambda e, fc=fc: e.dma_start(out=xb[:, fc * TPC:(fc + 1) * TPC], in_=xn1T[fc * 128:(fc + 1) * 128, :]))
        ch.c("gpsimd", lambda e: e.memset(o1x128m[:], 1.0))
        with ExitStack() as st2:
            rss = st2.enter_context(nc.sbuf_tensor("rss", [8, TPC], F32))
            idxs = st2.enter_context(nc.sbuf_tensor("idxs", [8, TPC], F32))
            m1 = st2.enter_context(nc.sbuf_tensor("m1", [8, TPC], F32))
            ch.d("sync", lambda e: e.dma_start(out=rss[:], in_=rscT[:, :]))
            ch.d("sync", lambda e: e.dma_start(out=idxs[:], in_=idxl[:, :]))
            ch.c("vector", lambda e: e.tensor_scalar(out=idxs[:], in0=idxs[:], scalar1=thrs[:, 2:3], scalar2=None, op0=Alu.is_le))
            ch.c("vector", lambda e: e.tensor_scalar(out=m1[:], in0=rss[:], scalar1=thrs[:, 1:2], scalar2=None, op0=Alu.is_equal))
            ch.c("vector", lambda e: e.tensor_tensor(out=m1[:], in0=m1[:], in1=idxs[:], op=Alu.mult))
            ch.c("vector", lambda e: e.tensor_scalar(out=gsb[:], in0=rss[:], scalar1=thrs[:, 1:2], scalar2=None, op0=Alu.is_gt))
            ch.c("vector", lambda e: e.tensor_tensor(out=gsb[:], in0=gsb[:], in1=m1[:], op=Alu.add))
            ch.c("vector", lambda e: e.tensor_tensor(out=gsb[:], in0=gsb[:], in1=rss[:], op=Alu.mult))
        hsb = sb("hsb", [128, 32 * 1024])
        x2a = sb("x2a", [128, 8 * TPC], F32)
        ch.c("gpsimd", lambda e: e.memset(x2a[:], 0.0))

        for ex in range(8):
            ch.d("sync", lambda e, ex=ex: e.dma_start(out=b1s[:], in_=b1T[ex, :, :]))
            ch.d("sync", lambda e, ex=ex: e.dma_start(out=b2s[:], in_=b2T[ex, :, :]))
            ch.d("sync", lambda e, ex=ex: e.dma_start(out=grow[:], in_=gsb[ex:ex + 1, :]))
            ch.c("vector", lambda e: e.tensor_copy(out=ghi[:], in_=grow[:]))
            ch.c("vector", lambda e: e.tensor_tensor(out=gtm[:], in0=grow[:], in1=ghi[:], op=Alu.subtract))
            ch.c("vector", lambda e: e.tensor_copy(out=glo[:], in_=gtm[:]))
            for th in range(2):
                tb = th * TH
                for hc in range(32):
                    for fc in range(8):
                        ch.d("gpsimd", lambda e, ex=ex, hc=hc, fc=fc: e.dma_start(
                            out=w1s[:, fc * 128:(fc + 1) * 128], in_=w1[ex, fc * 128:(fc + 1) * 128, hc * 128:(hc + 1) * 128]))
                    for t5 in range(2):
                        for fc in range(8):
                            ch.c("tensor", lambda e, fc=fc, tb=tb, t5=t5: e.matmul(
                                ph[:], w1s[:, fc * 128:(fc + 1) * 128], xb[:, fc * TPC + tb + t5 * 512: fc * TPC + tb + t5 * 512 + 512],
                                start=(fc == 0), stop=(fc == 7)))
                        ch.c("scalar", lambda e, hc=hc, t5=t5: e.activation(
                            out=hsb[:, hc * 1024 + t5 * 512: hc * 1024 + (t5 + 1) * 512], in_=ph[:], func=AF.Gelu,
                            bias=b1s[:, hc:hc + 1], scale=1.0))
                for dmc in range(8):
                    for hc in range(32):
                        ch.d("gpsimd", lambda e, ex=ex, hc=hc, dmc=dmc: e.dma_start(
                            out=w2s[:, hc * 128:(hc + 1) * 128], in_=w2[ex, hc * 128:(hc + 1) * 128, dmc * 128:(dmc + 1) * 128]))
                    for t5 in range(2):
                        for hc in range(32):
                            ch.c("tensor", lambda e, hc=hc, t5=t5: e.matmul(
                                po[:], w2s[:, hc * 128:(hc + 1) * 128], hsb[:, hc * 1024 + t5 * 512: hc * 1024 + (t5 + 1) * 512],
                                start=(hc == 0), stop=(hc == 31)))
                        ch.c("tensor", lambda e, tb=tb, t5=t5: e.matmul(pb[:], o1x128m[:], ghi[0:1, tb + t5 * 512: tb + (t5 + 1) * 512], start=True, stop=False))
                        ch.c("tensor", lambda e, tb=tb, t5=t5: e.matmul(pb[:], o1x128m[:], glo[0:1, tb + t5 * 512: tb + (t5 + 1) * 512], start=False, stop=True))
                        ch.c("vector", lambda e, dmc=dmc: e.tensor_scalar(out=t1[:], in0=po[:], scalar1=b2s[:, dmc:dmc + 1], scalar2=None, op0=Alu.add))
                        ch.c("vector", lambda e: e.tensor_tensor(out=t1[:], in0=t1[:], in1=pb[:], op=Alu.mult))
                        ch.c("vector", lambda e, dmc=dmc, tb=tb, t5=t5: e.tensor_tensor(
                            out=x2a[:, dmc * TPC + tb + t5 * 512: dmc * TPC + tb + (t5 + 1) * 512],
                            in0=x2a[:, dmc * TPC + tb + t5 * 512: dmc * TPC + tb + (t5 + 1) * 512], in1=t1[:], op=Alu.add))
        for fc in range(8):
            for t5 in range(4):
                ch.d("sync", lambda e, fc=fc, t5=t5: e.dma_start(out=xs1[:], in_=x1T[fc * 128:(fc + 1) * 128, t5 * 512:(t5 + 1) * 512]))
                ch.d("sync", lambda e, fc=fc, t5=t5: e.dma_start(out=t1[:], in_=at1T[fc * 128:(fc + 1) * 128, t5 * 512:(t5 + 1) * 512]))
                ch.c("vector", lambda e: e.tensor_tensor(out=xs1[:], in0=xs1[:], in1=t1[:], op=Alu.add))
                ch.c("vector", lambda e, fc=fc, t5=t5: e.tensor_tensor(
                    out=xs1[:], in0=xs1[:], in1=x2a[:, fc * TPC + t5 * 512: fc * TPC + (t5 + 1) * 512], op=Alu.add))
                ch.d("sync", lambda e, fc=fc, t5=t5: e.dma_start(out=x2T_o[fc * 128:(fc + 1) * 128, t5 * 512:(t5 + 1) * 512], in_=xs1[:]))
        ch.emit(nc, block, sems)
    return nc



def build_fused(upto=99):
    nc = bass.Bass("TRN2", target_bir_lowering=False)
    x_i = nc.dram_tensor("x_i", [TPC, D], F32, kind="ExternalInput").ap()
    gam = nc.dram_tensor("gam", [128, D], F32, kind="ExternalInput").ap()
    bet = nc.dram_tensor("bet", [128, D], F32, kind="ExternalInput").ap()
    Wq = nc.dram_tensor("Wq", [D, D], F32, kind="ExternalInput").ap()
    Wk = nc.dram_tensor("Wk", [D, D], F32, kind="ExternalInput").ap()
    Wv = nc.dram_tensor("Wv", [D, D], F32, kind="ExternalInput").ap()
    Wo = nc.dram_tensor("Wo", [D, D], F32, kind="ExternalInput").ap()
    dtab = nc.dram_tensor("dtab", [76, 128, 512], BF16, kind="ExternalInput").ap()
    wg_i = nc.dram_tensor("wg_i", [IH, D], F32, kind="ExternalInput").ap()
    wu_i = nc.dram_tensor("wu_i", [IH, D], F32, kind="ExternalInput").ap()
    wd_i = nc.dram_tensor("wd_i", [D, IH], F32, kind="ExternalInput").ap()
    wdq = nc.dram_tensor("wdq", [64, 512], F32, kind="ExternalInput").ap()
    wuq = nc.dram_tensor("wuq", [512, 64], F32, kind="ExternalInput").ap()
    wdkv = nc.dram_tensor("wdkv", [64, 1024], F32, kind="ExternalInput").ap()
    wukv = nc.dram_tensor("wukv", [1024, 128], F32, kind="ExternalInput").ap()
    wqr = nc.dram_tensor("wqr", [64, 64], F32, kind="ExternalInput").ap()
    wkr = nc.dram_tensor("wkr", [64, 64], F32, kind="ExternalInput").ap()
    qnw = nc.dram_tensor("qnw", [64, 1], F32, kind="ExternalInput").ap()
    knw = nc.dram_tensor("knw", [64, 1], F32, kind="ExternalInput").ap()
    rw = nc.dram_tensor("rw", [D, 8], F32, kind="ExternalInput").ap()
    mtab = nc.dram_tensor("mtab", [4, 128, 512], F32, kind="ExternalInput").ap()
    wol1 = nc.dram_tensor("wol1", [64, 64], F32, kind="ExternalInput").ap()
    w1 = nc.dram_tensor("w1", [8, D, 4096], F32, kind="ExternalInput").ap()
    b1T = nc.dram_tensor("b1T", [8, 128, 32], F32, kind="ExternalInput").ap()
    w2 = nc.dram_tensor("w2", [8, 4096, D], F32, kind="ExternalInput").ap()
    b2T = nc.dram_tensor("b2T", [8, 128, 8], F32, kind="ExternalInput").ap()
    idxl = nc.dram_tensor("idxl", [8, TPC], F32, kind="ExternalInput").ap()
    hgi = nc.dram_tensor("hgi", [1, 1], mybir.dt.int32, kind="ExternalInput").ap()
    loci = nc.dram_tensor("loci", [1, 1], mybir.dt.int32, kind="ExternalInput").ap()
    x2T_o = nc.dram_tensor("x2T_o", [D, TPC], F32, kind="ExternalOutput").ap()
    xn_d = nc.dram_tensor("xn_d", [TPC, D], F32)
    e1c = nc.dram_tensor("e1c", [D, TPC], BF16)
    G1 = nc.dram_tensor("G1", [4 * D, TPC], BF16)
    q_d = nc.dram_tensor("q_d", [8, 128, TPC], BF16)
    k_d = nc.dram_tensor("k_d", [8, 128, S], BF16)
    v_d = nc.dram_tensor("v_d", [16, 128, 4096], BF16)
    at0_d = nc.dram_tensor("at0_d", [TPC, D], F32)
    x1_d = nc.dram_tensor("x1_d", [TPC, D], F32)
    x1T_d = nc.dram_tensor("x1T_d", [D, TPC], F32)
    e3c = nc.dram_tensor("e3c", [D, TPC], F32)
    G3 = nc.dram_tensor("G3", [4 * D, TPC], F32)
    wgT_d = nc.dram_tensor("wgT_d", [D, IH], BF16)
    wuT_d = nc.dram_tensor("wuT_d", [D, IH], BF16)
    wdT_d = nc.dram_tensor("wdT_d", [IH, D], BF16)
    qc_d = nc.dram_tensor("qc_d", [4, 64, S], BF16)
    qr_d = nc.dram_tensor("qr_d", [4, 64, S], BF16)
    vq_d = nc.dram_tensor("vq_d", [4, 1, S], F32)
    kc_d = nc.dram_tensor("kc_d", [4, 64, S], BF16)
    kr_d = nc.dram_tensor("kr_d", [4, 64, S], BF16)
    v1_d = nc.dram_tensor("v1_d", [4, 128, 4096], BF16)
    scT_d = nc.dram_tensor("scT_d", [8, S], F32)
    thr_d = nc.dram_tensor("thr_d", [8, 4], F32)
    e5c = nc.dram_tensor("e5c", [256, S], F32)
    G5 = nc.dram_tensor("G5", [4 * 256, S], F32)
    GROUPS = [[0, 1, 2, 3], [4, 5, 6, 7]]

    ch = Ch()
    with ExitStack() as topst:
        sems = [topst.enter_context(nc.semaphore(f"S{i}")) for i in range(16)]
        ccs1 = topst.enter_context(nc.semaphore("cc1"))
        ccs3 = topst.enter_context(nc.semaphore("cc3"))
        ccs5 = topst.enter_context(nc.semaphore("cc5"))
        grA = topst.enter_context(nc.gpsimd.register("grA"))
        grB = topst.enter_context(nc.gpsimd.register("grB"))

        def _SEC1():
            with _FlushStack(ch, nc, sems) as st:
                sb = lambda n, sh, dt=F32: st.enter_context(nc.sbuf_tensor(n, sh, dt))
                xt = sb("s1_xt", [128, D]); dx = sb("s1_dx", [128, D]); sq = sb("s1_sq", [128, D])
                xnt = sb("s1_xnt", [128, D]); xat = sb("s1_xat", [128, D])
                gt = sb("s1_gt", [128, D]); bt = sb("s1_bt", [128, D])
                s1v = sb("s1_s1", [128, 1]); s2v = sb("s1_s2", [128, 1])
                xab = sb("s1_xab", [128, D], BF16); sT = sb("s1_sT", [128, 128], BF16)
                ts0 = sb("s1_ts0", [128, 128], BF16); idn = sb("s1_idn", [128, 128], BF16)
                pt = st.enter_context(nc.psum_tensor("s1_pt", [128, 128], BF16))
                ch.c("gpsimd", lambda e: e.memset(ts0[:], 1.0))
                ch.c("gpsimd", lambda e: e.affine_select(out=idn[:], in_=ts0[:], pattern=[[-1, 128]], compare_op=Alu.is_equal, fill=0.0, base=0, channel_multiplier=1))
                ch.d("sync", lambda e: e.dma_start(out=gt[:], in_=gam[:, :]))
                ch.d("sync", lambda e: e.dma_start(out=bt[:], in_=bet[:, :]))
                for tg in range(TPC // 128):
                    r0 = tg * 128
                    ch.d("sync", lambda e, r0=r0: e.dma_start(out=xt[:], in_=x_i[r0:r0 + 128, :]))
                    _ln_tile(ch, xt[:], xnt[:], dx[:], sq[:], s1v[:], s2v[:], D)
                    ch.d("sync", lambda e, r0=r0: e.dma_start(out=xn_d.ap()[r0:r0 + 128, :], in_=xnt[:]))
                    _ln_tile(ch, xnt[:], xat[:], dx[:], sq[:], s1v[:], s2v[:], D)
                    ch.c("vector", lambda e: e.tensor_tensor(out=xat[:], in0=xat[:], in1=gt[:], op=Alu.mult))
                    ch.c("vector", lambda e: e.tensor_tensor(out=xat[:], in0=xat[:], in1=bt[:], op=Alu.add))
                    ch.c("vector", lambda e: e.tensor_copy(out=xab[:], in_=xat[:]))
                    for fc in range(8):
                        ch.c("tensor", lambda e, fc=fc: e.transpose(out=pt[:], in_=xab[:, fc * 128:(fc + 1) * 128], identity=idn[:]))
                        ch.c("vector", lambda e: e.tensor_copy(out=sT[:], in_=pt[:]))
                        ch.d("sync", lambda e, fc=fc, tg=tg: e.dma_start(out=e1c.ap()[fc * 128:(fc + 1) * 128, tg * 128:(tg + 1) * 128], in_=sT[:]))
            ch.flush(nc, sems)
        if upto >= 1: _SEC1()
        if upto >= 1.5:
            ch.cc(lambda e: e.collective_compute("AllGather", Alu.bypass, replica_groups=GROUPS,
                                                 ins=[e1c.ap().opt()], outs=[G1.ap().opt()]), ccs1)

        def _SEC2():
            with _FlushStack(ch, nc, sems) as st:
                sb = lambda n, sh, dt=BF16: st.enter_context(nc.sbuf_tensor(n, sh, dt))
                xa = sb("s2_xa", [128, 8 * S]); xl = sb("s2_xl", [128, 8 * TPC])
                wqs = sb("s2_wq", [128, 8 * 128]); wks = sb("s2_wk", [128, 8 * 128]); wvs = sb("s2_wv", [128, 8 * 128])
                qs = sb("s2_qs", [128, TPC]); ks = sb("s2_ks", [128, S]); vs = sb("s2_vs", [128, 2 * 4096])
                ps = st.enter_context(nc.psum_tensor("s2_ps", [128, 512], F32))
                pv = st.enter_context(nc.psum_tensor("s2_pv", [128, 128], F32))
                for c in range(8):
                    for r in range(4):
                        ch.d("sync", lambda e, c=c, r=r: e.dma_start(
                            out=xa[:, c * S + r * TPC: c * S + (r + 1) * TPC], in_=G1.ap()[r * D + c * 128: r * D + (c + 1) * 128, :]))
                    ch.d("sync", lambda e, c=c: e.dma_start(out=xl[:, c * TPC:(c + 1) * TPC], in_=e1c.ap()[c * 128:(c + 1) * 128, :]))
                for p in range(8):
                    for c in range(8):
                        for W, ws in ((Wq, wqs), (Wk, wks), (Wv, wvs)):
                            ch.d("gpsimd", lambda e, c=c, p=p, W=W, ws=ws: e.dma_start(
                                out=ws[:, c * 128:(c + 1) * 128], in_=W[c * 128:(c + 1) * 128, p * 128:(p + 1) * 128]))
                    for tg in range(TPC // 512):
                        for c in range(8):
                            ch.c("tensor", lambda e, c=c, tg=tg: e.matmul(
                                ps[:], wqs[:, c * 128:(c + 1) * 128], xl[:, c * TPC + tg * 512: c * TPC + tg * 512 + 512],
                                start=(c == 0), stop=(c == 7)))
                        ch.c("vector", lambda e, tg=tg: e.tensor_copy(out=qs[:, tg * 512:(tg + 1) * 512], in_=ps[:]))
                    ch.d("sync", lambda e, p=p: e.dma_start(out=q_d.ap()[p, :, :], in_=qs[:]))
                    for kt in range(S // 512):
                        for c in range(8):
                            ch.c("tensor", lambda e, c=c, kt=kt: e.matmul(
                                ps[:], wks[:, c * 128:(c + 1) * 128], xa[:, c * S + kt * 512: c * S + kt * 512 + 512],
                                start=(c == 0), stop=(c == 7)))
                        ch.c("vector", lambda e, kt=kt: e.tensor_copy(out=ks[:, kt * 512:(kt + 1) * 512], in_=ps[:]))
                    ch.d("sync", lambda e, p=p: e.dma_start(out=k_d.ap()[p, :, :], in_=ks[:]))
                    for tc in range(S // 128):
                        for c in range(8):
                            ch.c("tensor", lambda e, c=c, tc=tc: e.matmul(
                                pv[:], xa[:, c * S + tc * 128: c * S + tc * 128 + 128], wvs[:, c * 128:(c + 1) * 128],
                                start=(c == 0), stop=(c == 7)))
                        ch.c("vector", lambda e, tc=tc: e.tensor_copy(out=vs[:, tc * 64:(tc + 1) * 64], in_=pv[:, 0:64]))
                        ch.c("vector", lambda e, tc=tc: e.tensor_copy(out=vs[:, 4096 + tc * 64:4096 + (tc + 1) * 64], in_=pv[:, 64:128]))
                    ch.d("sync", lambda e, p=p: e.dma_start(out=v_d.ap()[2 * p, :, :], in_=vs[:, 0:4096]))
                    ch.d("sync", lambda e, p=p: e.dma_start(out=v_d.ap()[2 * p + 1, :, :], in_=vs[:, 4096:8192]))

            ch.flush(nc, sems)
        if upto >= 2: _SEC2()
        def _SEC3():
            NQT = TPC // 512
            NKC = S // 128
            with _FlushStack(ch, nc, sems) as st0:
                sb0 = lambda n, sh, dt=BF16: st0.enter_context(nc.sbuf_tensor(n, sh, dt))
                ctx = sb0("s3_ctx", [128, 8 * TPC])
                wos = sb0("s3_wo", [128, 8 * 1024])
                for c in range(8):
                    ch.d("gpsimd", lambda e, c=c: e.dma_start(out=wos[:, c * 1024:(c + 1) * 1024], in_=Wo[c * 128:(c + 1) * 128, :]))
                with _FlushStack(ch, nc, sems) as st:
                    sb = lambda n, sh, dt=BF16: st.enter_context(nc.sbuf_tensor(n, sh, dt))
                    ka = sb("s3_ka", [66, S]); vsb = sb("s3_vsb", [128, 4096]); qp = sb("s3_qp", [128, TPC])
                    qa = sb("s3_qa", [66, TPC]); phat = sb("s3_phat", [128, TPC]); dts = sb("s3_dts", [128, 76 * 512])
                    sqk = sb("s3_sqk", [64, 512]); sqq = sb("s3_sqq", [64, 512])
                    kn512 = sb("s3_kn512", [1, 512], F32); kmx = sb("s3_kmx", [1, 1], F32)
                    qn2 = sb("s3_qn2", [1, TPC], F32)
                    khat = sb("s3_khat", [1, 1], F32); vqr = sb("s3_vqr", [1, TPC], F32)
                    stgh = sb("s3_stgh", [1, TPC]); stgl = sb("s3_stgl", [1, TPC])
                    sx = sb("s3_sx", [128, 512], F32)
                    lsb = sb("s3_lsb", [1, 512], F32); rhi = sb("s3_rhi", [1, 512]); rlo = sb("s3_rlo", [1, 512])
                    rlf = sb("s3_rlf", [1, 512], F32); ctxf = sb("s3_ctxf", [64, 512], F32)
                    o64 = sb("s3_o64", [64, 1]); o128 = sb("s3_o128", [128, 1]); o1x64 = sb("s3_o1x64", [1, 64])
                    sps = st.enter_context(nc.psum_tensor("s3_sps", [128, 2048], F32))
                    cps = st.enter_context(nc.psum_tensor("s3_cps", [66, 2048], F32))
                    for dd in range(76):
                        ch.d("sync", lambda e, dd=dd: e.dma_start(out=dts[:, dd * 512:(dd + 1) * 512], in_=dtab[dd, :, :]))
                    ch.c("gpsimd", lambda e: e.memset(o64[:], 1.0))
                    ch.c("gpsimd", lambda e: e.memset(o128[:], 1.0))
                    ch.c("gpsimd", lambda e: e.memset(o1x64[:], 1.0))
                    ch.c("gpsimd", lambda e: e.memset(ka[64:66, :], 1.0))
                    for h in range(16):
                        p, r = h // 2, h % 2
                        slope = SLOPES[h]
                        if r == 0:
                            ch.d("sync", lambda e, p=p: e.dma_start(out=qp[:], in_=q_d.ap()[p, :, :]))
                        ch.d("sync", lambda e, p=p, r=r: e.dma_start(out=ka[0:64, :], in_=k_d.ap()[p, r * 64:r * 64 + 64, :]))
                        ch.d("sync", lambda e, h=h: e.dma_start(out=vsb[:], in_=v_d.ap()[h, :, :]))
                        ch.c("gpsimd", lambda e: e.memset(khat[:], 0.0))
                        for kt in range(S // 512):
                            ch.c("scalar", lambda e, kt=kt: e.activation(out=sqk[:], in_=ka[0:64, kt * 512:(kt + 1) * 512], func=AF.Square))
                            ch.c("tensor", lambda e: e.matmul(sps[0:1, 0:512], o64[:], sqk[:], start=True, stop=True))
                            ch.c("vector", lambda e: e.tensor_copy(out=kn512[:], in_=sps[0:1, 0:512]))
                            ch.c("vector", lambda e: e.tensor_reduce(out=kmx[:], in_=kn512[:], axis=AX, op=Alu.max))
                            ch.c("vector", lambda e: e.tensor_tensor(out=khat[:], in0=khat[:], in1=kmx[:], op=Alu.max))
                        ch.c("scalar", lambda e: e.activation(out=khat[:], in_=khat[:], func=AF.Sqrt))
                        for tg in range(NQT):
                            ch.c("scalar", lambda e, r=r, tg=tg: e.activation(out=sqq[:], in_=qp[r * 64:r * 64 + 64, tg * 512:(tg + 1) * 512], func=AF.Square))
                            ch.c("tensor", lambda e: e.matmul(sps[0:1, 0:512], o64[:], sqq[:], start=True, stop=True))
                            ch.c("vector", lambda e, tg=tg: e.tensor_copy(out=qn2[:, tg * 512:(tg + 1) * 512], in_=sps[0:1, 0:512]))
                        ch.c("scalar", lambda e: e.activation(out=qn2[:], in_=qn2[:], func=AF.Sqrt))
                        ch.c("vector", lambda e: e.tensor_scalar(out=vqr[:], in0=qn2[:], scalar1=khat[0:1, 0:1], scalar2=-1.03, op0=Alu.mult, op1=Alu.mult))
                        ch.c("vector", lambda e, r=r: e.tensor_copy(out=qa[0:64, :], in_=qp[r * 64:r * 64 + 64, :]))
                        ch.c("vector", lambda e: e.tensor_copy(out=stgh[:], in_=vqr[:]))
                        ch.c("vector", lambda e: e.tensor_tensor(out=qn2[:], in0=vqr[:], in1=stgh[:], op=Alu.subtract))
                        ch.c("vector", lambda e: e.tensor_copy(out=stgl[:], in_=qn2[:]))
                        ch.d("sync", lambda e: e.dma_start(out=qa[64:65, :], in_=stgh[:]))
                        ch.d("sync", lambda e: e.dma_start(out=qa[65:66, :], in_=stgl[:]))
                        for kc in range(NKC):
                            kb = kc * 128
                            for qt in range(NQT):
                                ch.c("tensor", lambda e, kb=kb, qt=qt: e.matmul(
                                    sps[:, qt * 512:(qt + 1) * 512], ka[:, kb:kb + 128], qa[:, qt * 512:(qt + 1) * 512], start=True, stop=True))
                            for qt in range(NQT):
                                dd = (qt * 512 - kb + 8064) // 128
                                ch.c("vector", lambda e, qt=qt, dd=dd, slope=slope: e.scalar_tensor_tensor(
                                    out=sx[:], in0=dts[:, dd * 512:(dd + 1) * 512], scalar=slope, in1=sps[:, qt * 512:(qt + 1) * 512],
                                    op0=Alu.mult, op1=Alu.add))
                                ch.c("scalar", lambda e, qt=qt: e.activation(
                                    out=phat[:, qt * 512:(qt + 1) * 512], in_=sx[:], func=AF.Exp, bias=0.0, scale=0.125))
                            for qt in range(NQT):
                                ch.c("tensor", lambda e, kc=kc, qt=qt: e.matmul(
                                    cps[0:64, qt * 512:(qt + 1) * 512], vsb[:, kc * 64:(kc + 1) * 64], phat[:, qt * 512:(qt + 1) * 512],
                                    start=(kc == 0), stop=(kc == NKC - 1)))
                                ch.c("tensor", lambda e, kc=kc, qt=qt: e.matmul(
                                    cps[64:65, qt * 512:(qt + 1) * 512], o128[:], phat[:, qt * 512:(qt + 1) * 512],
                                    start=(kc == 0), stop=(kc == NKC - 1)))
                        for qt in range(NQT):
                            ch.c("vector", lambda e, qt=qt: e.tensor_copy(out=lsb[:], in_=cps[64:65, qt * 512:(qt + 1) * 512]))
                            ch.c("vector", lambda e: e.reciprocal(out=rlf[:], in_=lsb[:]))
                            ch.c("vector", lambda e: e.tensor_copy(out=rhi[:], in_=rlf[:]))
                            ch.c("vector", lambda e: e.tensor_tensor(out=lsb[:], in0=rlf[:], in1=rhi[:], op=Alu.subtract))
                            ch.c("vector", lambda e: e.tensor_copy(out=rlo[:], in_=lsb[:]))
                            ch.c("tensor", lambda e: e.matmul(sps[0:64, 0:512], o1x64[:], rhi[:], start=True, stop=False))
                            ch.c("tensor", lambda e: e.matmul(sps[0:64, 0:512], o1x64[:], rlo[:], start=False, stop=True))
                            ch.c("vector", lambda e, qt=qt: e.tensor_copy(out=ctxf[:], in_=cps[0:64, qt * 512:(qt + 1) * 512]))
                            ch.c("vector", lambda e, qt=qt, p=p, r=r: e.tensor_tensor(
                                out=ctx[r * 64:r * 64 + 64, p * TPC + qt * 512: p * TPC + (qt + 1) * 512], in0=ctxf[:], in1=sps[0:64, 0:512], op=Alu.mult))
                with _FlushStack(ch, nc, sems) as st:
                    sb = lambda n, sh, dt=F32: st.enter_context(nc.sbuf_tensor(n, sh, dt))
                    outf = sb("s3_outf", [128, 512]); otm = sb("s3_otm", [128, 128])
                    tsf = sb("s3_tsf", [128, 128]); idnf = sb("s3_idnf", [128, 128])
                    pso = st.enter_context(nc.psum_tensor("s3_pso", [128, 512], F32))
                    ptf = st.enter_context(nc.psum_tensor("s3_ptf", [128, 128], F32))
                    ch.c("gpsimd", lambda e: e.memset(tsf[:], 1.0))
                    ch.c("gpsimd", lambda e: e.affine_select(out=idnf[:], in_=tsf[:], pattern=[[-1, 128]], compare_op=Alu.is_equal, fill=0.0, base=0, channel_multiplier=1))
                    for dmc in range(8):
                        for qt in range(NQT):
                            for c in range(8):
                                ch.c("tensor", lambda e, c=c, dmc=dmc, qt=qt: e.matmul(
                                    pso[:], wos[:, c * 1024 + dmc * 128: c * 1024 + dmc * 128 + 128],
                                    ctx[:, c * TPC + qt * 512: c * TPC + (qt + 1) * 512], start=(c == 0), stop=(c == 7)))
                            ch.c("vector", lambda e: e.tensor_copy(out=outf[:], in_=pso[:]))
                            for j in range(4):
                                ch.c("tensor", lambda e, j=j: e.transpose(out=ptf[:], in_=outf[:, j * 128:(j + 1) * 128], identity=idnf[:]))
                                ch.c("vector", lambda e: e.tensor_copy(out=otm[:], in_=ptf[:]))
                                ch.d("sync", lambda e, dmc=dmc, qt=qt, j=j: e.dma_start(
                                    out=at0_d.ap()[qt * 512 + j * 128: qt * 512 + (j + 1) * 128, dmc * 128:(dmc + 1) * 128], in_=otm[:]))

            ch.flush(nc, sems)
        if upto >= 3: _SEC3()
        def _SEC4():
            HTS = [(0, 512), (512, 512), (1024, 512), (1536, 512), (2048, 512), (2560, 256)]
            NTG = TPC // 128
            with _FlushStack(ch, nc, sems) as st:
                sb = lambda n, sh, dt=BF16: st.enter_context(nc.sbuf_tensor(n, sh, dt))
                xqT = sb("s4_xqT", [128, 8 * TPC]); wdT = sb("s4_wdT", [128, 22 * 1024])
                wgs = sb("s4_wgs", [128, 8 * 512]); wus = sb("s4_wus", [128, 8 * 512])
                ld = sb("s4_ld", [128, IH], F32); qb = sb("s4_qb", [128, IH])
                mq = sb("s4_mq", [128, IH // 16], F32)
                uq = sb("s4_uq", [128, IH], F32); gq = sb("s4_gq", [128, IH], F32)
                tq = sb("s4_tq", [128, IH], F32); sgq = sb("s4_sgq", [128, IH], F32)
                hqT = sb("s4_hqT", [128, 22 * 128]); ts0 = sb("s4_ts0", [128, 128]); idn = sb("s4_idn", [128, 128])
                tsf = sb("s4_tsf", [128, 128], F32); idnf = sb("s4_idnf", [128, 128], F32); otm = sb("s4_otm", [128, 128], F32)
                hs = sb("s4_hs", [128, 512], F32); hf = sb("s4_hf", [128, 512], F32)
                xt = sb("s4_xt", [128, D], F32); att = sb("s4_att", [128, D], F32)
                x1t = sb("s4_x1t", [128, D], F32); xn1t = sb("s4_xn1t", [128, D], F32)
                dxs = sb("s4_dxs", [128, D], F32); sqs = sb("s4_sqs", [128, D], F32)
                st1 = sb("s4_st1", [128, 1], F32); st2 = sb("s4_st2", [128, 1], F32)
                pg = st.enter_context(nc.psum_tensor("s4_pg", [128, 512], F32))
                pu = st.enter_context(nc.psum_tensor("s4_pu", [128, 512], F32))
                pd = st.enter_context(nc.psum_tensor("s4_pd", [128, 512], F32))
                pt = st.enter_context(nc.psum_tensor("s4_pt", [128, 128], BF16))
                ptf = st.enter_context(nc.psum_tensor("s4_ptf", [128, 128], F32))
                ch.c("gpsimd", lambda e: e.memset(ts0[:], 1.0))
                ch.c("gpsimd", lambda e: e.affine_select(out=idn[:], in_=ts0[:], pattern=[[-1, 128]], compare_op=Alu.is_equal, fill=0.0, base=0, channel_multiplier=1))
                ch.c("gpsimd", lambda e: e.memset(tsf[:], 1.0))
                ch.c("gpsimd", lambda e: e.affine_select(out=idnf[:], in_=tsf[:], pattern=[[-1, 128]], compare_op=Alu.is_equal, fill=0.0, base=0, channel_multiplier=1))
                for W_in, T_out, nt in ((wg_i, wgT_d, 22), (wu_i, wuT_d, 22)):
                    for ttile in range(nt):
                        ch.d("sync", lambda e, W_in=W_in, tt=ttile: e.dma_start(out=ld[:, 0:D], in_=W_in[tt * 128:(tt + 1) * 128, :]))
                        _quant_tile(ch, ld[:, 0:D], qb[:, 0:D], D, mq, uq, gq, tq, sgq)
                        for fc in range(8):
                            _transp(ch, qb[:, fc * 128:(fc + 1) * 128], ts0[:], pt, idn)
                            ch.d("sync", lambda e, T_out=T_out, fc=fc, tt=ttile: e.dma_start(
                                out=T_out.ap()[fc * 128:(fc + 1) * 128, tt * 128:(tt + 1) * 128], in_=ts0[:]))
                for ttile in range(8):
                    ch.d("sync", lambda e, tt=ttile: e.dma_start(out=ld[:], in_=wd_i[tt * 128:(tt + 1) * 128, :]))
                    _quant_tile(ch, ld[:], qb[:], IH, mq, uq, gq, tq, sgq)
                    for hc in range(22):
                        _transp(ch, qb[:, hc * 128:(hc + 1) * 128], ts0[:], pt, idn)
                        ch.d("sync", lambda e, hc=hc, tt=ttile: e.dma_start(
                            out=wdT_d.ap()[hc * 128:(hc + 1) * 128, tt * 128:(tt + 1) * 128], in_=ts0[:]))
                for tg in range(NTG):
                    ch.d("sync", lambda e, tg=tg: e.dma_start(out=ld[:, 0:D], in_=xn_d.ap()[tg * 128:(tg + 1) * 128, :]))
                    _quant_tile(ch, ld[:, 0:D], qb[:, 0:D], D, mq, uq, gq, tq, sgq)
                    for fc in range(8):
                        _transp(ch, qb[:, fc * 128:(fc + 1) * 128], xqT[:, fc * TPC + tg * 128: fc * TPC + (tg + 1) * 128], pt, idn)
                for hc in range(22):
                    ch.d("sync", lambda e, hc=hc: e.dma_start(out=wdT[:, hc * 1024:(hc + 1) * 1024], in_=wdT_d.ap()[hc * 128:(hc + 1) * 128, :]))
                for tg in range(NTG):
                    ch.d("sync", lambda e, tg=tg: e.dma_start(out=xt[:], in_=x_i[tg * 128:(tg + 1) * 128, :]))
                    ch.d("sync", lambda e, tg=tg: e.dma_start(out=att[:], in_=at0_d.ap()[tg * 128:(tg + 1) * 128, :]))
                    for hti, (h0, hw) in enumerate(HTS):
                        ch.d("sync", lambda e, h0=h0, hw=hw: e.dma_start(
                            out=wgs[:].rearrange("p (f i) -> p f i", i=512)[:, :, 0:hw],
                            in_=bass.AP(wgT_d, h0, [[IH, 128], [128 * IH, 8], [1, hw]])))
                        ch.d("sync", lambda e, h0=h0, hw=hw: e.dma_start(
                            out=wus[:].rearrange("p (f i) -> p f i", i=512)[:, :, 0:hw],
                            in_=bass.AP(wuT_d, h0, [[IH, 128], [128 * IH, 8], [1, hw]])))
                        for fc in range(8):
                            ch.c("tensor", lambda e, fc=fc, tg=tg, hw=hw: e.matmul(
                                pg[:, 0:hw], xqT[:, fc * TPC + tg * 128: fc * TPC + (tg + 1) * 128], wgs[:, fc * 512: fc * 512 + hw],
                                start=(fc == 0), stop=(fc == 7)))
                        for fc in range(8):
                            ch.c("tensor", lambda e, fc=fc, tg=tg, hw=hw: e.matmul(
                                pu[:, 0:hw], xqT[:, fc * TPC + tg * 128: fc * TPC + (tg + 1) * 128], wus[:, fc * 512: fc * 512 + hw],
                                start=(fc == 0), stop=(fc == 7)))
                        ch.c("scalar", lambda e, hw=hw: e.activation(out=hs[:, 0:hw], in_=pg[:, 0:hw], func=AF.Silu))
                        ch.c("vector", lambda e, hw=hw: e.tensor_tensor(out=hf[:, 0:hw], in0=hs[:, 0:hw], in1=pu[:, 0:hw], op=Alu.mult))
                        _quant_tile(ch, hf[:, 0:hw], qb[:, 0:hw], hw, mq, uq, gq, tq, sgq)
                        for sc in range(hw // 128):
                            _transp(ch, qb[:, sc * 128:(sc + 1) * 128],
                                    hqT[:, (h0 // 128 + sc) * 128:(h0 // 128 + sc + 1) * 128], pt, idn)
                    for dmt in range(2):
                        for hc in range(22):
                            ch.c("tensor", lambda e, hc=hc, dmt=dmt: e.matmul(
                                pd[:], hqT[:, hc * 128:(hc + 1) * 128], wdT[:, hc * 1024 + dmt * 512: hc * 1024 + dmt * 512 + 512],
                                start=(hc == 0), stop=(hc == 21)))
                        ch.c("vector", lambda e, dmt=dmt: e.tensor_tensor(out=x1t[:, dmt * 512:(dmt + 1) * 512], in0=att[:, dmt * 512:(dmt + 1) * 512], in1=pd[:], op=Alu.add))
                    ch.c("vector", lambda e: e.tensor_tensor(out=x1t[:], in0=x1t[:], in1=xt[:], op=Alu.add))
                    _ln_tile(ch, x1t[:], xn1t[:], dxs[:], sqs[:], st1[:], st2[:], D)
                    ch.d("sync", lambda e, tg=tg: e.dma_start(out=x1_d.ap()[tg * 128:(tg + 1) * 128, :], in_=x1t[:]))
                    for fc in range(8):
                        ch.c("tensor", lambda e, fc=fc: e.transpose(out=ptf[:], in_=x1t[:, fc * 128:(fc + 1) * 128], identity=idnf[:]))
                        ch.c("vector", lambda e: e.tensor_copy(out=otm[:], in_=ptf[:]))
                        ch.d("sync", lambda e, fc=fc, tg=tg: e.dma_start(out=x1T_d.ap()[fc * 128:(fc + 1) * 128, tg * 128:(tg + 1) * 128], in_=otm[:]))
                    for fc in range(8):
                        ch.c("tensor", lambda e, fc=fc: e.transpose(out=ptf[:], in_=xn1t[:, fc * 128:(fc + 1) * 128], identity=idnf[:]))
                        ch.c("vector", lambda e: e.tensor_copy(out=otm[:], in_=ptf[:]))
                        ch.d("sync", lambda e, fc=fc, tg=tg: e.dma_start(out=e3c.ap()[fc * 128:(fc + 1) * 128, tg * 128:(tg + 1) * 128], in_=otm[:]))
            ch.flush(nc, sems)
        if upto >= 4: _SEC4()
        if upto >= 4.5:
            ch.cc(lambda e: e.collective_compute("AllGather", Alu.bypass, replica_groups=GROUPS,
                                                 ins=[e3c.ap().opt()], outs=[G3.ap().opt()]), ccs3)

        def _SEC5():
            with _FlushStack(ch, nc, sems) as st:
                sb = lambda n, sh, dt=BF16: st.enter_context(nc.sbuf_tensor(n, sh, dt))
                xh = sb("s5_xh", [64, S]); cq = sb("s5_cq", [128, 4 * TPC]); ckv = sb("s5_ckv", [128, 8 * 512])
                kcs = sb("s5_kcs", [64, S]); krs = sb("s5_krs", [64, S])
                vstage = sb("s5_vst", [128, 4096]); vtmp = sb("s5_vtmp", [64, 512])
                qcs = sb("s5_qcs", [64, TPC]); qrs = sb("s5_qrs", [64, TPC])
                w1s = sb("s5_w1", [64, 512]); w2t = sb("s5_w2t", [128, 4 * 64]); w3s = sb("s5_w3", [64, 1024])
                w4s = sb("s5_w4", [128, 8 * 128]); w5s = sb("s5_w5", [64, 64]); w6s = sb("s5_w6", [64, 64])
                qnwc = sb("s5_qnw", [64, 1], F32); knwc = sb("s5_knw", [64, 1], F32)
                rwT = sb("s5_rwT", [128, 8 * 8], F32); xcs = sb("s5_xcs", [128, 8 * 128], F32)
                sqb = sb("s5_sqb", [64, 512])
                rr = sb("s5_rr", [1, 512], F32); rhi2 = sb("s5_rhi2", [1, 512]); rlo2 = sb("s5_rlo2", [1, 512])
                evf = sb("s5_evf", [64, 512], F32)
                o64 = sb("s5_o64", [64, 1]); o1x64 = sb("s5_o1x64", [1, 64]); o1x128 = sb("s5_o1x128", [1, 128])
                kn512 = sb("s5_kn512", [1, 512], F32); kmx = sb("s5_kmx", [1, 1], F32); khat = sb("s5_khat", [1, 1], F32)
                qn2 = sb("s5_qn2", [1, TPC], F32); vqr = sb("s5_vqr", [1, TPC], F32)
                rsf = sb("s5_rsf", [128, 8], F32); rmax = sb("s5_rmax", [128, 1], F32); rsum = sb("s5_rsum", [128, 1], F32)
                scT = sb("s5_scT", [8, S], F32); cmpb = sb("s5_cmpb", [8, S], F32)
                lo = sb("s5_lo", [8, 1], F32); hiv = sb("s5_hiv", [8, 1], F32); mid = sb("s5_mid", [8, 1], F32)
                cnt = sb("s5_cnt", [8, 1], F32); sel = sb("s5_sel", [8, 1], F32); nsel = sb("s5_nsel", [8, 1], F32)
                tv = sb("s5_tv", [8, 1], F32); need = sb("s5_need", [8, 1], F32)
                rlo_i = sb("s5_rloi", [8, 1], F32); rhi_i = sb("s5_rhii", [8, 1], F32)
                thrs = sb("s5_thrs", [8, 4], F32)
                idn_k = sb("s5_idn", [128, 128]); tsk = sb("s5_tsk", [128, 128])
                idnf_k = sb("s5_idnf", [128, 128], F32); tskf = sb("s5_tskf", [128, 128], F32)
                rsbf = sb("s5_rsbf", [128, 8], F32)
                hgs = sb("s5_hgs", [1, 1], mybir.dt.int32)
                ps = st.enter_context(nc.psum_tensor("s5_ps", [128, 512], F32))
                pkv = st.enter_context(nc.psum_tensor("s5_pkv", [128, 512], F32))
                nps = st.enter_context(nc.psum_tensor("s5_nps", [1, 512], F32))
                bps = st.enter_context(nc.psum_tensor("s5_bps", [64, 512], F32))
                pr = st.enter_context(nc.psum_tensor("s5_pr", [128, 8], F32))
                ptv = st.enter_context(nc.psum_tensor("s5_ptv", [128, 64], BF16))
                ptr = st.enter_context(nc.psum_tensor("s5_ptr", [8, 128], F32))
                ch.c("gpsimd", lambda e: e.memset(o64[:], 1.0))
                ch.c("gpsimd", lambda e: e.memset(o1x64[:], 1.0))
                ch.c("gpsimd", lambda e: e.memset(o1x128[:], 1.0))
                ch.c("gpsimd", lambda e: e.memset(tsk[:], 1.0))
                ch.c("gpsimd", lambda e: e.affine_select(out=idn_k[:], in_=tsk[:], pattern=[[-1, 128]], compare_op=Alu.is_equal, fill=0.0, base=0, channel_multiplier=1))
                ch.c("gpsimd", lambda e: e.memset(tskf[:], 1.0))
                ch.c("gpsimd", lambda e: e.affine_select(out=idnf_k[:], in_=tskf[:], pattern=[[-1, 128]], compare_op=Alu.is_equal, fill=0.0, base=0, channel_multiplier=1))
                ch.d("gpsimd", lambda e: e.dma_start(out=w1s[:], in_=wdq[:, :]))
                for lc in range(4):
                    ch.d("gpsimd", lambda e, lc=lc: e.dma_start(out=w2t[:, lc * 64:(lc + 1) * 64], in_=wuq[lc * 128:(lc + 1) * 128, :]))
                ch.d("gpsimd", lambda e: e.dma_start(out=w3s[:], in_=wdkv[:, :]))
                for lc in range(8):
                    ch.d("gpsimd", lambda e, lc=lc: e.dma_start(out=w4s[:, lc * 128:(lc + 1) * 128], in_=wukv[lc * 128:(lc + 1) * 128, :]))
                ch.d("gpsimd", lambda e: e.dma_start(out=w5s[:], in_=wqr[:, :]))
                ch.d("gpsimd", lambda e: e.dma_start(out=w6s[:], in_=wkr[:, :]))
                ch.d("sync", lambda e: e.dma_start(out=qnwc[:], in_=qnw[:, :]))
                ch.d("sync", lambda e: e.dma_start(out=knwc[:], in_=knw[:, :]))
                for fc in range(8):
                    ch.d("sync", lambda e, fc=fc: e.dma_start(out=rwT[:, fc * 8:(fc + 1) * 8], in_=rw[fc * 128:(fc + 1) * 128, :]))
                ch.d("sync", lambda e: e.dma_start(out=hgs[:], in_=hgi[:, :]))
                ch.n("gpsimd", lambda e: e.reg_load(grA, hgs[:1, :1]))
                ch.n("gpsimd", lambda e: e.reg_mul(grA, grA, 256 * TPC))
                for hi in range(4):
                    ch.n("gpsimd", lambda e, o=(hi * 64) * TPC: e.reg_mov(grB, o))
                    ch.n("gpsimd", lambda e: e.reg_add(grB, grB, grA))
                    ch.d("gpsimd", lambda e: e.dma_start(
                        out=xh[:].rearrange("p (r t) -> p r t", t=TPC),
                        in_=bass.AP(G3, grB, [[TPC, 64], [D * TPC, 4], [1, TPC]])))
                    ch.c("gpsimd", lambda e: e.memset(khat[:], 0.0))
                    for tt in range(S // 512):
                        t0 = tt * 512
                        for lc in range(8):
                            ch.c("tensor", lambda e, lc=lc, t0=t0: e.matmul(ps[:], w3s[:, lc * 128:(lc + 1) * 128], xh[:, t0:t0 + 512], start=True, stop=True))
                            ch.c("vector", lambda e, lc=lc: e.tensor_copy(out=ckv[:, lc * 512:(lc + 1) * 512], in_=ps[:]))
                        for lc in range(8):
                            ch.c("tensor", lambda e, lc=lc: e.matmul(pkv[:], w4s[:, lc * 128:(lc + 1) * 128], ckv[:, lc * 512:(lc + 1) * 512], start=(lc == 0), stop=(lc == 7)))
                        _rms_norm(ch, pkv[0:64, 0:512], kcs[0:64, t0:t0 + 512], 512, sqb, nps, rr, rhi2, rlo2, bps, evf, o64, o1x64, knwc[:])
                        ch.c("vector", lambda e: e.tensor_copy(out=vtmp[:], in_=pkv[64:128, 0:512]))
                        for j in range(4):
                            ch.c("tensor", lambda e, j=j: e.transpose(out=ptv[:], in_=vtmp[:, j * 128:(j + 1) * 128], identity=idn_k[0:64, 0:64]))
                            ch.c("vector", lambda e, j=j, tt=tt: e.tensor_copy(out=vstage[:, (tt * 4 + j) * 64:(tt * 4 + j + 1) * 64], in_=ptv[:, 0:64]))
                        ch.c("tensor", lambda e, t0=t0: e.matmul(ps[0:64, 0:512], w6s[:], xh[:, t0:t0 + 512], start=True, stop=True))
                        ch.c("vector", lambda e, t0=t0: e.tensor_copy(out=krs[:, t0:t0 + 512], in_=ps[0:64, 0:512]))
                        ch.c("scalar", lambda e, t0=t0: e.activation(out=sqb[:], in_=kcs[:, t0:t0 + 512], func=AF.Square))
                        ch.c("tensor", lambda e: e.matmul(nps[:], o64[:], sqb[:], start=True, stop=False))
                        ch.c("scalar", lambda e, t0=t0: e.activation(out=sqb[:], in_=krs[:, t0:t0 + 512], func=AF.Square))
                        ch.c("tensor", lambda e: e.matmul(nps[:], o64[:], sqb[:], start=False, stop=True))
                        ch.c("vector", lambda e: e.tensor_copy(out=kn512[:], in_=nps[:]))
                        ch.c("vector", lambda e: e.tensor_reduce(out=kmx[:], in_=kn512[:], axis=AX, op=Alu.max))
                        ch.c("vector", lambda e: e.tensor_tensor(out=khat[:], in0=khat[:], in1=kmx[:], op=Alu.max))
                    ch.c("scalar", lambda e: e.activation(out=khat[:], in_=khat[:], func=AF.Sqrt))
                    ch.d("sync", lambda e, hi=hi: e.dma_start(out=kc_d.ap()[hi, :, :], in_=kcs[:]))
                    ch.d("sync", lambda e, hi=hi: e.dma_start(out=kr_d.ap()[hi, :, :], in_=krs[:]))
                    ch.d("sync", lambda e, hi=hi: e.dma_start(out=v1_d.ap()[hi, :, :], in_=vstage[:]))
                    for tb in range(S // TPC):
                        b0 = tb * TPC
                        for lc in range(4):
                            for ttt in range(4):
                                ch.c("tensor", lambda e, lc=lc, ttt=ttt, b0=b0: e.matmul(
                                    ps[:], w1s[:, lc * 128:(lc + 1) * 128], xh[:, b0 + ttt * 512: b0 + (ttt + 1) * 512], start=True, stop=True))
                                ch.c("vector", lambda e, lc=lc, ttt=ttt: e.tensor_copy(out=cq[:, lc * TPC + ttt * 512: lc * TPC + (ttt + 1) * 512], in_=ps[:]))
                        for ttt in range(4):
                            for lc in range(4):
                                ch.c("tensor", lambda e, lc=lc, ttt=ttt: e.matmul(
                                    pkv[0:64, 0:512], w2t[:, lc * 64:(lc + 1) * 64], cq[:, lc * TPC + ttt * 512: lc * TPC + (ttt + 1) * 512],
                                    start=(lc == 0), stop=(lc == 3)))
                            _rms_norm(ch, pkv[0:64, 0:512], qcs[0:64, ttt * 512:(ttt + 1) * 512], 512, sqb, nps, rr, rhi2, rlo2, bps, evf, o64, o1x64, qnwc[:])
                            ch.c("tensor", lambda e, ttt=ttt: e.matmul(ps[0:64, 0:512], w5s[:], qcs[:, ttt * 512:(ttt + 1) * 512], start=True, stop=True))
                            ch.c("vector", lambda e, ttt=ttt: e.tensor_copy(out=qrs[:, ttt * 512:(ttt + 1) * 512], in_=ps[0:64, 0:512]))
                            ch.c("scalar", lambda e, ttt=ttt: e.activation(out=sqb[:], in_=qcs[:, ttt * 512:(ttt + 1) * 512], func=AF.Square))
                            ch.c("tensor", lambda e: e.matmul(nps[:], o64[:], sqb[:], start=True, stop=False))
                            ch.c("scalar", lambda e, ttt=ttt: e.activation(out=sqb[:], in_=qrs[:, ttt * 512:(ttt + 1) * 512], func=AF.Square))
                            ch.c("tensor", lambda e: e.matmul(nps[:], o64[:], sqb[:], start=False, stop=True))
                            ch.c("vector", lambda e, ttt=ttt: e.tensor_copy(out=qn2[:, ttt * 512:(ttt + 1) * 512], in_=nps[:]))
                        ch.c("scalar", lambda e: e.activation(out=qn2[:], in_=qn2[:], func=AF.Sqrt))
                        ch.c("vector", lambda e: e.tensor_scalar(out=vqr[:], in0=qn2[:], scalar1=khat[0:1, 0:1], scalar2=-1.03, op0=Alu.mult, op1=Alu.mult))
                        ch.d("sync", lambda e, hi=hi, b0=b0: e.dma_start(out=qc_d.ap()[hi, :, b0:b0 + TPC], in_=qcs[:]))
                        ch.d("sync", lambda e, hi=hi, b0=b0: e.dma_start(out=qr_d.ap()[hi, :, b0:b0 + TPC], in_=qrs[:]))
                        ch.d("sync", lambda e, hi=hi, b0=b0: e.dma_start(out=vq_d.ap()[hi, :, b0:b0 + TPC], in_=vqr[:]))
                for tc in range(S // 128):
                    rk, ti = tc // 16, tc % 16
                    for fc in range(8):
                        ch.d("sync", lambda e, fc=fc, rk=rk, ti=ti: e.dma_start(
                            out=xcs[:, fc * 128:(fc + 1) * 128], in_=G3.ap()[rk * D + fc * 128: rk * D + (fc + 1) * 128, ti * 128:(ti + 1) * 128]))
                    for fc in range(8):
                        ch.c("tensor", lambda e, fc=fc: e.matmul(pr[:], xcs[:, fc * 128:(fc + 1) * 128], rwT[:, fc * 8:(fc + 1) * 8], start=(fc == 0), stop=(fc == 7)))
                    ch.c("vector", lambda e: e.tensor_reduce(out=rmax[:], in_=pr[:], axis=AX, op=Alu.max, negate=True))
                    ch.c("scalar", lambda e: e.activation(out=rsf[:], in_=pr[:], func=AF.Exp, bias=rmax[:], scale=1.0))
                    ch.c("vector", lambda e: e.tensor_reduce(out=rsum[:], in_=rsf[:], axis=AX, op=Alu.add))
                    ch.c("vector", lambda e: e.reciprocal(out=rsum[:], in_=rsum[:]))
                    ch.c("vector", lambda e: e.tensor_scalar(out=rsf[:], in0=rsf[:], scalar1=rsum[:], scalar2=None, op0=Alu.mult))
                    ch.c("vector", lambda e: e.tensor_copy(out=rsbf[:, 0:8], in_=rsf[:]))
                    ch.c("tensor", lambda e: e.transpose(out=ptr[:], in_=rsbf[:, 0:8], identity=idnf_k[:]))
                    ch.c("vector", lambda e, tc=tc: e.tensor_copy(out=scT[:, tc * 128:(tc + 1) * 128], in_=ptr[:]))
                KCAP = 2049.0
                ch.c("gpsimd", lambda e: e.memset(lo[:], -1.0))
                ch.c("gpsimd", lambda e: e.memset(hiv[:], 2.0))
                for it in range(64):
                    ch.c("vector", lambda e: e.tensor_tensor(out=mid[:], in0=lo[:], in1=hiv[:], op=Alu.add))
                    ch.c("vector", lambda e: e.tensor_scalar(out=mid[:], in0=mid[:], scalar1=0.5, scalar2=None, op0=Alu.mult))
                    ch.c("vector", lambda e: e.tensor_scalar(out=cmpb[:], in0=scT[:], scalar1=mid[:], scalar2=None, op0=Alu.is_gt))
                    ch.c("vector", lambda e: e.tensor_reduce(out=cnt[:], in_=cmpb[:], axis=AX, op=Alu.add))
                    ch.c("vector", lambda e: e.tensor_scalar(out=sel[:], in0=cnt[:], scalar1=KCAP, scalar2=None, op0=Alu.is_ge))
                    ch.c("vector", lambda e: e.tensor_scalar(out=nsel[:], in0=sel[:], scalar1=-1.0, scalar2=1.0, op0=Alu.mult, op1=Alu.add))
                    ch.c("vector", lambda e: e.tensor_tensor(out=tv[:], in0=mid[:], in1=lo[:], op=Alu.subtract))
                    ch.c("vector", lambda e: e.tensor_tensor(out=tv[:], in0=sel[:], in1=tv[:], op=Alu.mult))
                    ch.c("vector", lambda e: e.tensor_tensor(out=lo[:], in0=lo[:], in1=tv[:], op=Alu.add))
                    ch.c("vector", lambda e: e.tensor_tensor(out=tv[:], in0=mid[:], in1=hiv[:], op=Alu.subtract))
                    ch.c("vector", lambda e: e.tensor_tensor(out=tv[:], in0=nsel[:], in1=tv[:], op=Alu.mult))
                    ch.c("vector", lambda e: e.tensor_tensor(out=hiv[:], in0=hiv[:], in1=tv[:], op=Alu.add))
                ch.c("vector", lambda e: e.tensor_scalar(out=cmpb[:], in0=scT[:], scalar1=hiv[:], scalar2=None, op0=Alu.is_gt))
                ch.c("vector", lambda e: e.tensor_reduce(out=cnt[:], in_=cmpb[:], axis=AX, op=Alu.add))
                ch.c("vector", lambda e: e.tensor_scalar(out=need[:], in0=cnt[:], scalar1=-1.0, scalar2=KCAP, op0=Alu.mult, op1=Alu.add))
                ch.d("sync", lambda e: e.dma_start(out=scT_d.ap()[:, :], in_=scT[:]))
                ch.c("vector", lambda e: e.tensor_scalar(out=scT[:], in0=scT[:], scalar1=hiv[:], scalar2=None, op0=Alu.is_equal))
                ch.c("gpsimd", lambda e: e.memset(rlo_i[:], -1.0))
                ch.c("gpsimd", lambda e: e.memset(rhi_i[:], 8191.0))
                for it in range(30):
                    ch.c("vector", lambda e: e.tensor_tensor(out=mid[:], in0=rlo_i[:], in1=rhi_i[:], op=Alu.add))
                    ch.c("vector", lambda e: e.tensor_scalar(out=mid[:], in0=mid[:], scalar1=0.5, scalar2=None, op0=Alu.mult))
                    ch.c("gpsimd", lambda e: e.iota(cmpb[:], [[1, S]], base=0, channel_multiplier=0, allow_small_or_imprecise_dtypes=True))
                    ch.c("vector", lambda e: e.tensor_scalar(out=cmpb[:], in0=cmpb[:], scalar1=mid[:], scalar2=None, op0=Alu.is_le))
                    ch.c("vector", lambda e: e.tensor_tensor(out=cmpb[:], in0=cmpb[:], in1=scT[:], op=Alu.mult))
                    ch.c("vector", lambda e: e.tensor_reduce(out=cnt[:], in_=cmpb[:], axis=AX, op=Alu.add))
                    ch.c("vector", lambda e: e.tensor_scalar(out=sel[:], in0=cnt[:], scalar1=need[:], scalar2=None, op0=Alu.is_ge))
                    ch.c("vector", lambda e: e.tensor_scalar(out=nsel[:], in0=sel[:], scalar1=-1.0, scalar2=1.0, op0=Alu.mult, op1=Alu.add))
                    ch.c("vector", lambda e: e.tensor_tensor(out=tv[:], in0=mid[:], in1=rhi_i[:], op=Alu.subtract))
                    ch.c("vector", lambda e: e.tensor_tensor(out=tv[:], in0=sel[:], in1=tv[:], op=Alu.mult))
                    ch.c("vector", lambda e: e.tensor_tensor(out=rhi_i[:], in0=rhi_i[:], in1=tv[:], op=Alu.add))
                    ch.c("vector", lambda e: e.tensor_tensor(out=tv[:], in0=mid[:], in1=rlo_i[:], op=Alu.subtract))
                    ch.c("vector", lambda e: e.tensor_tensor(out=tv[:], in0=nsel[:], in1=tv[:], op=Alu.mult))
                    ch.c("vector", lambda e: e.tensor_tensor(out=rlo_i[:], in0=rlo_i[:], in1=tv[:], op=Alu.add))
                ch.c("vector", lambda e: e.tensor_copy(out=thrs[:, 0:1], in_=lo[:]))
                ch.c("vector", lambda e: e.tensor_copy(out=thrs[:, 1:2], in_=hiv[:]))
                ch.c("vector", lambda e: e.tensor_copy(out=thrs[:, 2:3], in_=rhi_i[:]))
                ch.c("vector", lambda e: e.tensor_copy(out=thrs[:, 3:4], in_=need[:]))
                ch.d("sync", lambda e: e.dma_start(out=thr_d.ap()[:, :], in_=thrs[:]))

            ch.flush(nc, sems)
        if upto >= 5: _SEC5()
        def _SEC6():
            NQT6 = S // 512
            with _FlushStack(ch, nc, sems) as st:
                sb = lambda n, sh, dt=BF16: st.enter_context(nc.sbuf_tensor(n, sh, dt))
                kcs = sb("s6_kcs", [64, S]); kra = sb("s6_kra", [66, S])
                vsb = sb("s6_vsb", [128, 4096])
                qcs = sb("s6_qcs", [64, S]); qra = sb("s6_qra", [66, S])
                vqf = sb("s6_vqf", [1, S], F32)
                stgh = sb("s6_stgh", [1, S]); stgl = sb("s6_stgl", [1, S]); losq = sb("s6_losq", [1, S], F32)
                mts = sb("s6_mts", [128, 4 * 512], F32)
                phat = sb("s6_phat", [128, 512]); sx = sb("s6_sx", [128, 512], F32)
                wosb = sb("s6_wos", [64, 64])
                lsb = sb("s6_lsb", [1, 512], F32); rhi = sb("s6_rhi", [1, 512]); rlo = sb("s6_rlo", [1, 512]); rlf = sb("s6_rlf", [1, 512], F32)
                ctxf = sb("s6_ctxf", [64, 512], F32); ctxn = sb("s6_ctxn", [64, 512])
                o128 = sb("s6_o128", [128, 1]); o1x64 = sb("s6_o1x64", [1, 64])
                outf = sb("s6_outf", [64, 512], F32)
                sps = st.enter_context(nc.psum_tensor("s6_sps", [128, 512], F32))
                cps = st.enter_context(nc.psum_tensor("s6_cps", [66, 512], F32))
                bps = st.enter_context(nc.psum_tensor("s6_bps", [64, 512], F32))
                pso = st.enter_context(nc.psum_tensor("s6_pso", [64, 512], F32))
                ch.c("gpsimd", lambda e: e.memset(o128[:], 1.0))
                ch.c("gpsimd", lambda e: e.memset(o1x64[:], 1.0))
                ch.c("gpsimd", lambda e: e.memset(kra[64:66, :], 1.0))
                ch.d("gpsimd", lambda e: e.dma_start(out=wosb[:], in_=wol1[:, :]))
                for v in range(4):
                    ch.d("sync", lambda e, v=v: e.dma_start(out=mts[:, v * 512:(v + 1) * 512], in_=mtab[v, :, :]))
                for hi in range(4):
                    ch.d("sync", lambda e, hi=hi: e.dma_start(out=kcs[:], in_=kc_d.ap()[hi, :, :]))
                    ch.d("sync", lambda e, hi=hi: e.dma_start(out=kra[0:64, :], in_=kr_d.ap()[hi, :, :]))
                    ch.d("sync", lambda e, hi=hi: e.dma_start(out=vsb[:], in_=v1_d.ap()[hi, :, :]))
                    ch.d("sync", lambda e, hi=hi: e.dma_start(out=qcs[:], in_=qc_d.ap()[hi, :, :]))
                    ch.d("sync", lambda e, hi=hi: e.dma_start(out=qra[0:64, :], in_=qr_d.ap()[hi, :, :]))
                    ch.d("sync", lambda e, hi=hi: e.dma_start(out=vqf[:], in_=vq_d.ap()[hi, :, :]))
                    ch.c("vector", lambda e: e.tensor_copy(out=stgh[:], in_=vqf[:]))
                    ch.c("vector", lambda e: e.tensor_tensor(out=losq[:], in0=vqf[:], in1=stgh[:], op=Alu.subtract))
                    ch.c("vector", lambda e: e.tensor_copy(out=stgl[:], in_=losq[:]))
                    ch.d("sync", lambda e: e.dma_start(out=qra[64:65, :], in_=stgh[:]))
                    ch.d("sync", lambda e: e.dma_start(out=qra[65:66, :], in_=stgl[:]))
                    for qt in range(NQT6):
                        q0 = qt * 512
                        nkc = 4 * qt + 4
                        for kc in range(nkc):
                            kb = kc * 128
                            ch.c("tensor", lambda e, kb=kb, q0=q0: e.matmul(sps[:], kcs[:, kb:kb + 128], qcs[:, q0:q0 + 512], start=True, stop=False))
                            ch.c("tensor", lambda e, kb=kb, q0=q0: e.matmul(sps[:], kra[:, kb:kb + 128], qra[:, q0:q0 + 512], start=False, stop=True))
                            if kc >= 4 * qt:
                                v = kc - 4 * qt
                                ch.c("vector", lambda e, v=v: e.tensor_tensor(out=sx[:], in0=mts[:, v * 512:(v + 1) * 512], in1=sps[:], op=Alu.add))
                                ch.c("scalar", lambda e: e.activation(out=phat[:], in_=sx[:], func=AF.Exp, bias=0.0, scale=0.125))
                            else:
                                ch.c("scalar", lambda e: e.activation(out=phat[:], in_=sps[:], func=AF.Exp, bias=0.0, scale=0.125))
                            ch.c("tensor", lambda e, kc=kc, nkc=nkc: e.matmul(cps[0:64, :], vsb[:, kc * 64:(kc + 1) * 64], phat[:], start=(kc == 0), stop=(kc == nkc - 1)))
                            ch.c("tensor", lambda e, kc=kc, nkc=nkc: e.matmul(cps[64:65, :], o128[:], phat[:], start=(kc == 0), stop=(kc == nkc - 1)))
                        ch.c("vector", lambda e: e.tensor_copy(out=lsb[:], in_=cps[64:65, :]))
                        ch.c("vector", lambda e: e.reciprocal(out=rlf[:], in_=lsb[:]))
                        ch.c("vector", lambda e: e.tensor_copy(out=rhi[:], in_=rlf[:]))
                        ch.c("vector", lambda e: e.tensor_tensor(out=lsb[:], in0=rlf[:], in1=rhi[:], op=Alu.subtract))
                        ch.c("vector", lambda e: e.tensor_copy(out=rlo[:], in_=lsb[:]))
                        ch.c("tensor", lambda e: e.matmul(bps[:], o1x64[:], rhi[:], start=True, stop=False))
                        ch.c("tensor", lambda e: e.matmul(bps[:], o1x64[:], rlo[:], start=False, stop=True))
                        ch.c("vector", lambda e: e.tensor_copy(out=ctxf[:], in_=cps[0:64, :]))
                        ch.c("vector", lambda e: e.tensor_tensor(out=ctxn[:], in0=ctxf[:], in1=bps[:], op=Alu.mult))
                        ch.c("tensor", lambda e: e.matmul(pso[:], wosb[:], ctxn[:], start=True, stop=True))
                        ch.c("vector", lambda e: e.tensor_copy(out=outf[:], in_=pso[:]))
                        ch.d("sync", lambda e, hi=hi, q0=q0: e.dma_start(out=e5c.ap()[hi * 64:hi * 64 + 64, q0:q0 + 512], in_=outf[:]))
            ch.flush(nc, sems)
        if upto >= 6: _SEC6()
        if upto >= 6.5:
            ch.cc(lambda e: e.collective_compute("AllGather", Alu.bypass, replica_groups=GROUPS,
                                                 ins=[e5c.ap().opt()], outs=[G5.ap().opt()]), ccs5)

        def _SEC7():
            with _FlushStack(ch, nc, sems) as st:
                sb = lambda n, sh, dt=BF16: st.enter_context(nc.sbuf_tensor(n, sh, dt))
                xb = sb("s7_xb", [128, 8 * TPC])
                w1sb = sb("s7_w1", [128, 8 * 128]); w2sb = sb("s7_w2", [128, 16 * 128])
                b1s = sb("s7_b1", [128, 32], F32); b2s = sb("s7_b2", [128, 8], F32)
                gsb = sb("s7_g", [8, TPC], F32); grow = sb("s7_grow", [1, TPC], F32)
                ghi = sb("s7_ghi", [1, TPC]); glo = sb("s7_glo", [1, TPC]); gtm = sb("s7_gtm", [1, TPC], F32)
                thrs = sb("s7_thrs", [8, 4], F32); o1x128m = sb("s7_o1x128", [1, 128])
                t1 = sb("s7_t1", [128, 512], F32); xs1 = sb("s7_xs1", [128, 512], F32)
                locs7 = sb("s7_loc", [1, 1], mybir.dt.int32)
                ph = st.enter_context(nc.psum_tensor("s7_ph", [128, 512], F32))
                po = st.enter_context(nc.psum_tensor("s7_po", [128, 1024], F32))
                pb = st.enter_context(nc.psum_tensor("s7_pb", [128, 512], F32))
                TH = TPC // 2
                ch.d("sync", lambda e: e.dma_start(out=thrs[:], in_=thr_d.ap()[:, :]))
                for fc in range(8):
                    ch.d("gpsimd", lambda e, fc=fc: e.dma_start(out=xb[:, fc * TPC:(fc + 1) * TPC], in_=e3c.ap()[fc * 128:(fc + 1) * 128, :]))
                ch.c("gpsimd", lambda e: e.memset(o1x128m[:], 1.0))
                ch.d("sync", lambda e: e.dma_start(out=locs7[:], in_=loci[:, :]))
                ch.n("gpsimd", lambda e: e.reg_load(grA, locs7[:1, :1]))
                with _FlushStack(ch, nc, sems) as st2:
                    rss = st2.enter_context(nc.sbuf_tensor("s7_rss", [8, TPC], F32))
                    idxs = st2.enter_context(nc.sbuf_tensor("s7_idxs", [8, TPC], F32))
                    m1 = st2.enter_context(nc.sbuf_tensor("s7_m1", [8, TPC], F32))
                    ch.d("gpsimd", lambda e: e.dma_start(out=rss[:], in_=bass.AP(scT_d, grA, [[S, 8], [1, TPC]])))
                    ch.d("sync", lambda e: e.dma_start(out=idxs[:], in_=idxl[:, :]))
                    ch.c("vector", lambda e: e.tensor_scalar(out=idxs[:], in0=idxs[:], scalar1=thrs[:, 2:3], scalar2=None, op0=Alu.is_le))
                    ch.c("vector", lambda e: e.tensor_scalar(out=m1[:], in0=rss[:], scalar1=thrs[:, 1:2], scalar2=None, op0=Alu.is_equal))
                    ch.c("vector", lambda e: e.tensor_tensor(out=m1[:], in0=m1[:], in1=idxs[:], op=Alu.mult))
                    ch.c("vector", lambda e: e.tensor_scalar(out=gsb[:], in0=rss[:], scalar1=thrs[:, 1:2], scalar2=None, op0=Alu.is_gt))
                    ch.c("vector", lambda e: e.tensor_tensor(out=gsb[:], in0=gsb[:], in1=m1[:], op=Alu.add))
                    ch.c("vector", lambda e: e.tensor_tensor(out=gsb[:], in0=gsb[:], in1=rss[:], op=Alu.mult))
                hsb = sb("s7_hsb", [128, 32 * 1024])
                x2a = sb("s7_x2a", [128, 8 * TPC], F32)
                ch.c("gpsimd", lambda e: e.memset(x2a[:], 0.0))
                for ex in (range(8) if upto >= 7.5 else ()):
                    ch.d("sync", lambda e, ex=ex: e.dma_start(out=b1s[:], in_=b1T[ex, :, :]))
                    ch.d("sync", lambda e, ex=ex: e.dma_start(out=b2s[:], in_=b2T[ex, :, :]))
                    ch.d("sync", lambda e, ex=ex: e.dma_start(out=grow[:], in_=gsb[ex:ex + 1, :]))
                    ch.c("vector", lambda e: e.tensor_copy(out=ghi[:], in_=grow[:]))
                    ch.c("vector", lambda e: e.tensor_tensor(out=gtm[:], in0=grow[:], in1=ghi[:], op=Alu.subtract))
                    ch.c("vector", lambda e: e.tensor_copy(out=glo[:], in_=gtm[:]))
                    for th in range(2):
                        tb = th * TH
                        for hc in range(32):
                            ch.d("gpsimd", lambda e, ex=ex, hc=hc: e.dma_start(
                                out=w1sb[:].rearrange("p (f i) -> p f i", i=128),
                                in_=bass.AP(w1.tensor, ex * D * 4096 + hc * 128, [[4096, 128], [128 * 4096, 8], [1, 128]])))
                            for t5 in range(2):
                                for fc in range(8):
                                    ch.c("tensor", lambda e, fc=fc, tb=tb, t5=t5: e.matmul(
                                        ph[:], w1sb[:, fc * 128:(fc + 1) * 128], xb[:, fc * TPC + tb + t5 * 512: fc * TPC + tb + t5 * 512 + 512],
                                        start=(fc == 0), stop=(fc == 7)))
                                ch.c("scalar", lambda e, hc=hc, t5=t5: e.activation(
                                    out=hsb[:, hc * 1024 + t5 * 512: hc * 1024 + (t5 + 1) * 512], in_=ph[:], func=AF.Gelu,
                                    bias=b1s[:, hc:hc + 1], scale=1.0))
                        for dmc in range(8):
                            for hh in range(2):
                                for g4 in range(2):
                                    ch.d("gpsimd", lambda e, ex=ex, dmc=dmc, hh=hh, g4=g4: e.dma_start(
                                        out=w2sb[:].rearrange("p (f i) -> p f i", i=128)[:, g4 * 8:(g4 + 1) * 8, :],
                                        in_=bass.AP(w2.tensor, ex * 4096 * D + dmc * 128 + (hh * 16 + g4 * 8) * 128 * D, [[D, 128], [128 * D, 8], [1, 128]])))
                                for t5 in range(2):
                                    for hc in range(16):
                                        ch.c("tensor", lambda e, hc=hc, hh=hh, t5=t5: e.matmul(
                                            po[:, t5 * 512:(t5 + 1) * 512], w2sb[:, hc * 128:(hc + 1) * 128],
                                            hsb[:, (hh * 16 + hc) * 1024 + t5 * 512: (hh * 16 + hc) * 1024 + (t5 + 1) * 512],
                                            start=(hh == 0 and hc == 0), stop=(hh == 1 and hc == 15)))
                            for t5 in range(2):
                                ch.c("tensor", lambda e, tb=tb, t5=t5: e.matmul(pb[:], o1x128m[:], ghi[0:1, tb + t5 * 512: tb + (t5 + 1) * 512], start=True, stop=False))
                                ch.c("tensor", lambda e, tb=tb, t5=t5: e.matmul(pb[:], o1x128m[:], glo[0:1, tb + t5 * 512: tb + (t5 + 1) * 512], start=False, stop=True))
                                ch.c("vector", lambda e, dmc=dmc, t5=t5: e.tensor_scalar(out=t1[:], in0=po[:, t5 * 512:(t5 + 1) * 512], scalar1=b2s[:, dmc:dmc + 1], scalar2=None, op0=Alu.add))
                                ch.c("vector", lambda e: e.tensor_tensor(out=t1[:], in0=t1[:], in1=pb[:], op=Alu.mult))
                                ch.c("vector", lambda e, dmc=dmc, tb=tb, t5=t5: e.tensor_tensor(
                                    out=x2a[:, dmc * TPC + tb + t5 * 512: dmc * TPC + tb + (t5 + 1) * 512],
                                    in0=x2a[:, dmc * TPC + tb + t5 * 512: dmc * TPC + tb + (t5 + 1) * 512], in1=t1[:], op=Alu.add))
                ch.n("gpsimd", lambda e: e.reg_load(grB, locs7[:1, :1]))
                at1b = sb("s7_at1b", [128, TPC])
                for fc in (range(8) if upto >= 7.8 else ()):
                    off_static = (fc // 2) * 256 * S + (fc % 2) * 128 * S
                    ch.n("gpsimd", lambda e, o=off_static: e.reg_mov(grA, o))
                    ch.n("gpsimd", lambda e: e.reg_add(grA, grA, grB))
                    ch.d("gpsimd", lambda e: e.dma_start(out=at1b[:], in_=bass.AP(G5, grA, [[S, 128], [1, TPC]])))
                    for t5 in range(4):
                        ch.d("sync", lambda e, fc=fc, t5=t5: e.dma_start(out=xs1[:], in_=x1T_d.ap()[fc * 128:(fc + 1) * 128, t5 * 512:(t5 + 1) * 512]))
                        ch.c("vector", lambda e, t5=t5: e.tensor_tensor(out=xs1[:], in0=xs1[:], in1=at1b[:, t5 * 512:(t5 + 1) * 512], op=Alu.add))
                        ch.c("vector", lambda e, fc=fc, t5=t5: e.tensor_tensor(
                            out=xs1[:], in0=xs1[:], in1=x2a[:, fc * TPC + t5 * 512: fc * TPC + (t5 + 1) * 512], op=Alu.add))
                        ch.d("sync", lambda e, fc=fc, t5=t5: e.dma_start(out=x2T_o[fc * 128:(fc + 1) * 128, t5 * 512:(t5 + 1) * 512], in_=xs1[:]))
            ch.flush(nc, sems)
        if upto >= 7: _SEC7()
        ch.final(nc, sems)
    return nc



_built = {}

def _get(name, fn):
    if name not in _built:
        _built[name] = fn()
    return _built[name]


def _run(nc, in_maps):
    res = run_bass_kernel_spmd(nc, in_maps, core_ids=list(range(NC)))
    return res.results


def _f(a):
    return np.ascontiguousarray(a, dtype=np.float32)


def _mk_fused_maps(ins):
    x = _f(ins["L0_attn_x"]).reshape(B * S, D)
    gam_b = _f(np.broadcast_to(np.asarray(ins["L0_attn_ln_gamma"], np.float32), (128, D)))
    bet_b = _f(np.broadcast_to(np.asarray(ins["L0_attn_ln_beta"], np.float32), (128, D)))
    mtab = np.zeros((4, 128, 512), np.float32)
    pcol = np.arange(128)[:, None]
    ccol = np.arange(512)[None, :]
    for v in range(4):
        mtab[v] = np.where(ccol < 128 * v + pcol, -1e9, 0.0)
    com = {
        "gam": gam_b, "bet": bet_b,
        "Wq": _f(ins["L0_attn_Wq"]), "Wk": _f(ins["L0_attn_Wk"]),
        "Wv": _f(ins["L0_attn_Wv"]), "Wo": _f(ins["L0_attn_Wo"]),
        "wg_i": _f(ins["L0_ffn_w_gate"]), "wu_i": _f(ins["L0_ffn_w_up"]), "wd_i": _f(ins["L0_ffn_w_down"]),
        "wdq": _f(np.asarray(ins["L1_attn_W_DQ"]).T), "wuq": _f(np.asarray(ins["L1_attn_W_UQ"]).T),
        "wdkv": _f(np.asarray(ins["L1_attn_W_DKV"]).T), "wukv": _f(np.asarray(ins["L1_attn_W_UKV"]).T),
        "wqr": _f(np.asarray(ins["L1_attn_W_QR"]).T), "wkr": _f(np.asarray(ins["L1_attn_W_KR"]).T),
        "qnw": _f(np.asarray(ins["L1_attn_q_norm_w"]).reshape(64, 1)),
        "knw": _f(np.asarray(ins["L1_attn_k_norm_w"]).reshape(64, 1)),
        "rw": _f(np.asarray(ins["L1_ffn_router_w"]).T), "mtab": mtab,
        "wol1": _f(np.asarray(ins["L1_attn_W_O"]).T),
        "w1": _f(ins["L1_ffn_w1"]),
        "b1T": _f(np.asarray(ins["L1_ffn_b1"]).reshape(8, 32, 128).transpose(0, 2, 1)),
        "w2": _f(ins["L1_ffn_w2"]),
        "b2T": _f(np.asarray(ins["L1_ffn_b2"]).reshape(8, 8, 128).transpose(0, 2, 1)),
    }
    maps = []
    for c in range(NC):
        hg, loc0 = c % CPB, (c % CPB) * TPC
        dlt = np.arange(76) * 128 - 8064 + loc0
        dtab = (-8.0 * np.abs(dlt[:, None, None] + np.arange(512)[None, None, :]
                              - np.arange(128)[None, :, None])).astype(BF)
        m = dict(com)
        m["x_i"] = np.ascontiguousarray(x[c * TPC:(c + 1) * TPC])
        m["dtab"] = dtab
        m["idxl"] = np.ascontiguousarray(np.broadcast_to(np.arange(loc0, loc0 + TPC, dtype=np.float32), (8, TPC)))
        m["hgi"] = np.array([[hg]], dtype=np.int32)
        m["loci"] = np.array([[loc0]], dtype=np.int32)
        maps.append(m)
    return maps


def kernel(**ins):
    maps = _mk_fused_maps(ins)
    r = _run(_get("fused", build_fused), maps)
    x2 = np.concatenate([np.asarray(r[c]["x2T_o"]).T for c in range(NC)], axis=0)
    return x2.reshape(B, S, D)


def _kernel_multilaunch(**ins):

    x = _f(ins["L0_attn_x"]).reshape(B * S, D)
    gam_b = _f(np.broadcast_to(np.asarray(ins["L0_attn_ln_gamma"], np.float32), (128, D)))
    bet_b = _f(np.broadcast_to(np.asarray(ins["L0_attn_ln_beta"], np.float32), (128, D)))

    r1 = _run(_get("ln0", build_ln0),
              [{"x": x[c * TPC:(c + 1) * TPC], "gam": gam_b, "bet": bet_b} for c in range(NC)])
    xn = np.concatenate([r["xn"] for r in r1], axis=0)
    xnA = np.concatenate([r["xnA"] for r in r1], axis=0)

    xnAT = [np.ascontiguousarray(xnA[b * S:(b + 1) * S].T) for b in range(B)]
    maps = []
    for c in range(NC):
        b, loc0 = c // CPB, (c % CPB) * TPC
        maps.append({"xnAT": xnAT[b], "xnATl": np.ascontiguousarray(xnAT[b][:, loc0:loc0 + TPC]),
                     "Wq": _f(ins["L0_attn_Wq"]), "Wk": _f(ins["L0_attn_Wk"]), "Wv": _f(ins["L0_attn_Wv"])})
    r2 = _run(_get("l0qkv", build_l0qkv), maps)

    maps = []
    for c in range(NC):
        loc0 = (c % CPB) * TPC
        dlt = np.arange(76) * 128 - 8064 + loc0
        dtab = (-8.0 * np.abs(dlt[:, None, None] + np.arange(512)[None, None, :]
                              - np.arange(128)[None, :, None])).astype(BF)
        maps.append({"q_in": np.asarray(r2[c]["q_out"]), "k_in": np.asarray(r2[c]["k_out"]),
                     "v_in": np.asarray(r2[c]["v_out"]), "Wo": _f(ins["L0_attn_Wo"]), "dtab": dtab})
    r3 = _run(_get("l0attn", build_l0attn), maps)
    attn0 = np.concatenate([np.asarray(r3[c]["attn0T"]).T for c in range(NC)], axis=0)

    maps = []
    for c in range(NC):
        sl = slice(c * TPC, (c + 1) * TPC)
        maps.append({"xn_i": np.ascontiguousarray(xn[sl]), "x_i": np.ascontiguousarray(x[sl]),
                     "at_i": np.ascontiguousarray(attn0[sl]),
                     "wg_i": _f(ins["L0_ffn_w_gate"]), "wu_i": _f(ins["L0_ffn_w_up"]),
                     "wd_i": _f(ins["L0_ffn_w_down"])})
    r4 = _run(_get("l0ffn", build_l0ffn), maps)
    x1 = np.concatenate([r["x1_o"] for r in r4], axis=0)
    xn1 = np.concatenate([r["xn1_o"] for r in r4], axis=0)

    xn1T = [np.ascontiguousarray(xn1[b * S:(b + 1) * S].T) for b in range(B)]
    maps = []
    for c in range(NC):
        b, hg = c // CPB, c % CPB
        maps.append({"xn1T": np.ascontiguousarray(np.roll(xn1T[b], -hg * 256, axis=0)),
                     "wdq": _f(ins["L1_attn_W_DQ"].T), "wuq": _f(ins["L1_attn_W_UQ"].T),
                     "wdkv": _f(ins["L1_attn_W_DKV"].T), "wukv": _f(ins["L1_attn_W_UKV"].T),
                     "wqr": _f(ins["L1_attn_W_QR"].T), "wkr": _f(ins["L1_attn_W_KR"].T),
                     "qnw": _f(np.asarray(ins["L1_attn_q_norm_w"]).reshape(64, 1)),
                     "knw": _f(np.asarray(ins["L1_attn_k_norm_w"]).reshape(64, 1)),
                     "rw": _f(ins["L1_ffn_router_w"].T)})
    r5 = _run(_get("l1kqv", build_l1kqv), maps)

    mtab = np.zeros((4, 128, 512), np.float32)
    pcol = np.arange(128)[:, None]
    ccol = np.arange(512)[None, :]
    for v in range(4):
        mtab[v] = np.where(ccol < 128 * v + pcol, -1e9, 0.0)
    maps = []
    for c in range(NC):
        maps.append({"qc_i": np.asarray(r5[c]["qc_o"]), "qr_i": np.asarray(r5[c]["qr_o"]),
                     "vq_i": np.asarray(r5[c]["vq_o"]), "kc_i": np.asarray(r5[c]["kc_o"]),
                     "kr_i": np.asarray(r5[c]["kr_o"]), "v_i": np.asarray(r5[c]["v_o"]),
                     "wo": _f(ins["L1_attn_W_O"].T), "mtab": mtab})
    r6 = _run(_get("l1attn", build_l1attn), maps)
    at1T = [np.zeros((D, S), np.float32) for _ in range(B)]
    for c in range(NC):
        b, hg = c // CPB, c % CPB
        at1T[b][hg * 256:(hg + 1) * 256, :] = np.asarray(r6[c]["at1_o"]).reshape(256, S)

    x1T = [np.ascontiguousarray(x1[b * S:(b + 1) * S].T) for b in range(B)]
    maps = []
    for c in range(NC):
        b, loc0 = c // CPB, (c % CPB) * TPC
        sl = slice(loc0, loc0 + TPC)
        rsc = np.asarray(r5[b * CPB]["rsc_o"])
        maps.append({"xn1T": np.ascontiguousarray(xn1T[b][:, sl]),
                     "x1T": np.ascontiguousarray(x1T[b][:, sl]),
                     "at1T": np.ascontiguousarray(at1T[b][:, sl]),
                     "rscT": np.ascontiguousarray(rsc.T[:, sl]),
                     "thr": np.asarray(r5[b * CPB]["thr_o"]),
                     "idxl": np.ascontiguousarray(np.broadcast_to(
                         np.arange(loc0, loc0 + TPC, dtype=np.float32), (8, TPC))),
                     "w1": _f(ins["L1_ffn_w1"]),
                     "b1T": _f(np.asarray(ins["L1_ffn_b1"]).reshape(8, 32, 128).transpose(0, 2, 1)),
                     "w2": _f(ins["L1_ffn_w2"]),
                     "b2T": _f(np.asarray(ins["L1_ffn_b2"]).reshape(8, 8, 128).transpose(0, 2, 1))})
    r7 = _run(_get("l1moe", build_l1moe), maps)
    x2 = np.concatenate([np.asarray(r7[c]["x2T_o"]).T for c in range(NC)], axis=0)
    return x2.reshape(B, S, D)


# version 08
# speedup vs baseline: 1.7566x; 1.1219x over previous
import numpy as np
import ml_dtypes
from contextlib import ExitStack
import concourse.bass as bass
import concourse.mybir as mybir
from concourse.bass_utils import run_bass_kernel_spmd
from concourse.alu_op_type import AluOpType as Alu

F32 = mybir.dt.float32
BF16 = mybir.dt.bfloat16
AF = mybir.ActivationFunctionType
AX = mybir.AxisListType.X
BF = ml_dtypes.bfloat16

NC = 8
B, S, D = 2, 8192, 1024
TPC = 2048
CPB = 4



class Ch:
    def __init__(self):
        self.plan = []

    def c(self, eng, fn):
        self.plan.append((eng, 1, fn))

    def d(self, eng, fn):
        self.plan.append((eng, 16, fn))

    SEG = 24000

    def n(self, eng, fn):
        self.plan.append((eng, 0, fn))

    def raw(self, eng, fn):
        self.plan.append((eng, 0, fn, True))

    def bar(self, eng):
        self.plan.append((eng, 0, lambda e: None))

    def cc(self, fn, ccsem):
        self.plan.append(("gpsimd", 0, lambda e, fn=fn, s=ccsem: fn(e).then_inc(s)))
        self.plan.append(("gpsimd", 1, lambda e, s=ccsem: (e.wait_ge(s, 1), e.nop())[1]))

    def flush(self, nc, sems):
        if not hasattr(self, "_emitted"):
            self._emitted = 0
            self._seg = 0
            self._v = 0
            self._prev = (None, 0)
        pend = self.plan[self._emitted:]
        self._emitted = len(self.plan)
        if not pend:
            return
        segs = []
        if not hasattr(self, "_preng"):
            self._preng = None
        for ent in pend:
            eng, inc, fn = ent[0], ent[1], ent[2]
            nowait = len(ent) > 3 and ent[3]
            if self._v + inc > Ch.SEG:
                self._seg += 1
                self._v = 0
            skip = nowait or (self._preng is not None and self._preng[0] == eng
                              and not self._preng[1]
                              and eng in ("tensor", "scalar"))
            if skip:
                segs.append((self._seg, None, 0))
            else:
                segs.append((self._seg, self._prev[0], self._prev[1]))
            self._v += inc
            self._prev = (self._seg, self._v)
            if not nowait:
                self._preng = (eng, inc >= 16)
        assert self._seg < len(sems), "need more chain semaphores"
        with nc.Block() as blk:
            def emit_for(name):
                def f(eng):
                    for ent, (sg, wsg, wv) in zip(pend, segs):
                        en, inc, fn = ent[0], ent[1], ent[2]
                        if en != name:
                            continue
                        if wsg is not None and wv:
                            eng.wait_ge(sems[wsg], wv)
                        ins = fn(eng)
                        if inc:
                            ins.then_inc(sems[sg], inc)
                return f
            blk.sync(emit_for("sync"))
            blk.gpsimd(emit_for("gpsimd"))
            blk.vector(emit_for("vector"))
            blk.scalar(emit_for("scalar"))
            blk.tensor(emit_for("tensor"))

    def final(self, nc, sems):
        self.flush(nc, sems)
        fs, fv = self._prev
        with nc.Block() as blk:
            def f(eng):
                eng.wait_ge(sems[fs], fv)
            blk.sync(f)

    def emit(self, nc, block, sems):
        segs = []
        seg, v = 0, 0
        prev = (None, 0)
        for eng, inc, fn in self.plan:
            if v + inc > Ch.SEG:
                seg, v = seg + 1, 0
            segs.append((seg, prev[0], prev[1]))
            v += inc
            prev = (seg, v)
        nseg = seg + 1
        assert nseg <= len(sems), f"need {nseg} semaphores"
        final_seg, final_val = prev

        def emit_for(name):
            def f(eng):
                for (en, inc, fn), (sg, wsg, wv) in zip(self.plan, segs):
                    if en != name:
                        continue
                    if wsg is not None and wv:
                        eng.wait_ge(sems[wsg], wv)
                    ins = fn(eng)
                    if inc:
                        ins.then_inc(sems[sg], inc)
                if name == "sync":
                    eng.wait_ge(sems[final_seg], final_val)
            return f

        block.sync(emit_for("sync"))
        block.gpsimd(emit_for("gpsimd"))
        block.vector(emit_for("vector"))
        block.scalar(emit_for("scalar"))
        block.tensor(emit_for("tensor"))


class _FlushStack(ExitStack):
    def __init__(self, ch, nc, sems):
        super().__init__()
        self._c = (ch, nc, sems)

    def __exit__(self, *a):
        ch, nc, sems = self._c
        ch.flush(nc, sems)
        return super().__exit__(*a)


def _ln_tile(ch, x_ap, out_ap, tmp_dx, tmp_sq, st1, st2, Dn, eps=1e-5):
    ch.c("vector", lambda e, a=x_ap, o=st1: e.tensor_reduce(out=o, in_=a, axis=AX, op=Alu.add))
    ch.c("vector", lambda e, o=st1: e.tensor_scalar(out=o, in0=o, scalar1=1.0 / Dn, scalar2=None, op0=Alu.mult))
    ch.c("vector", lambda e, a=x_ap, o=tmp_dx, s=st1: e.tensor_scalar(out=o, in0=a, scalar1=s, scalar2=None, op0=Alu.subtract))
    ch.c("scalar", lambda e, a=tmp_dx, o=tmp_sq: e.activation(out=o, in_=a, func=AF.Square))
    ch.c("vector", lambda e, a=tmp_sq, o=st2: e.tensor_reduce(out=o, in_=a, axis=AX, op=Alu.add))
    ch.c("vector", lambda e, o=st2: e.tensor_scalar(out=o, in0=o, scalar1=1.0 / Dn, scalar2=eps, op0=Alu.mult, op1=Alu.add))
    ch.c("scalar", lambda e, o=st2: e.activation(out=o, in_=o, func=AF.Sqrt))
    ch.c("vector", lambda e, o=st2: e.reciprocal(out=o, in_=o))
    ch.c("vector", lambda e, a=tmp_dx, o=out_ap, s=st2: e.tensor_scalar(out=o, in0=a, scalar1=s, scalar2=None, op0=Alu.mult))



def build_ln0():
    nc = bass.Bass("TRN2", target_bir_lowering=False)
    x = nc.dram_tensor("x", [TPC, D], F32, kind="ExternalInput").ap()
    gam = nc.dram_tensor("gam", [128, D], F32, kind="ExternalInput").ap()
    bet = nc.dram_tensor("bet", [128, D], F32, kind="ExternalInput").ap()
    xn = nc.dram_tensor("xn", [TPC, D], F32, kind="ExternalOutput").ap()
    xnA = nc.dram_tensor("xnA", [TPC, D], F32, kind="ExternalOutput").ap()

    ch = Ch()
    with (
        nc.sbuf_tensor("xt", [128, D], F32) as xt,
        nc.sbuf_tensor("dx", [128, D], F32) as dx,
        nc.sbuf_tensor("sq", [128, D], F32) as sq,
        nc.sbuf_tensor("xnt", [128, D], F32) as xnt,
        nc.sbuf_tensor("xat", [128, D], F32) as xat,
        nc.sbuf_tensor("gt", [128, D], F32) as gt,
        nc.sbuf_tensor("bt", [128, D], F32) as bt,
        nc.sbuf_tensor("s1", [128, 1], F32) as s1,
        nc.sbuf_tensor("s2", [128, 1], F32) as s2,
        nc.Block() as block,
    ):
        sems = [nc.semaphore(f"S{i}").__enter__() for i in range(8)]
        ch.d("sync", lambda e: e.dma_start(out=gt[:], in_=gam[:, :]))
        ch.d("sync", lambda e: e.dma_start(out=bt[:], in_=bet[:, :]))
        for t in range(TPC // 128):
            r0 = t * 128
            ch.d("sync", lambda e, r0=r0: e.dma_start(out=xt[:], in_=x[r0:r0 + 128, :]))
            _ln_tile(ch, xt[:], xnt[:], dx[:], sq[:], s1[:], s2[:], D)
            ch.d("sync", lambda e, r0=r0: e.dma_start(out=xn[r0:r0 + 128, :], in_=xnt[:]))
            _ln_tile(ch, xnt[:], xat[:], dx[:], sq[:], s1[:], s2[:], D)
            ch.c("vector", lambda e: e.tensor_tensor(out=xat[:], in0=xat[:], in1=gt[:], op=Alu.mult))
            ch.c("vector", lambda e: e.tensor_tensor(out=xat[:], in0=xat[:], in1=bt[:], op=Alu.add))
            ch.d("sync", lambda e, r0=r0: e.dma_start(out=xnA[r0:r0 + 128, :], in_=xat[:]))
        ch.emit(nc, block, sems)
    return nc



def build_l0qkv():
    nc = bass.Bass("TRN2", target_bir_lowering=False)
    xnAT = nc.dram_tensor("xnAT", [D, S], F32, kind="ExternalInput").ap()
    Wq = nc.dram_tensor("Wq", [D, D], F32, kind="ExternalInput").ap()
    Wk = nc.dram_tensor("Wk", [D, D], F32, kind="ExternalInput").ap()
    Wv = nc.dram_tensor("Wv", [D, D], F32, kind="ExternalInput").ap()
    q_out = nc.dram_tensor("q_out", [8, 128, TPC], BF16, kind="ExternalOutput").ap()
    k_out = nc.dram_tensor("k_out", [8, 128, S], BF16, kind="ExternalOutput").ap()
    v_out = nc.dram_tensor("v_out", [16, 128, 4096], BF16, kind="ExternalOutput").ap()

    xnATl = nc.dram_tensor("xnATl", [D, TPC], F32, kind="ExternalInput").ap()

    ch = Ch()
    with (
        nc.sbuf_tensor("xa", [128, 8 * S], BF16) as xa,
        nc.sbuf_tensor("xl", [128, 8 * TPC], BF16) as xl,
        nc.sbuf_tensor("wq", [128, 8 * 128], BF16) as wq,
        nc.sbuf_tensor("wk", [128, 8 * 128], BF16) as wk,
        nc.sbuf_tensor("wv", [128, 8 * 128], BF16) as wv,
        nc.sbuf_tensor("qs", [128, TPC], BF16) as qs,
        nc.sbuf_tensor("ks", [128, S], BF16) as ks,
        nc.sbuf_tensor("vs", [128, 2 * 4096], BF16) as vs,
        nc.psum_tensor("ps", [128, 512], F32) as ps,
        nc.psum_tensor("pv", [128, 128], F32) as pv,
        nc.Block() as block,
    ):
        sems = [nc.semaphore(f"S{i}").__enter__() for i in range(8)]
        for c in range(8):
            ch.d("gpsimd", lambda e, c=c: e.dma_start(out=xa[:, c * S:(c + 1) * S], in_=xnAT[c * 128:(c + 1) * 128, :]))
            ch.d("gpsimd", lambda e, c=c: e.dma_start(out=xl[:, c * TPC:(c + 1) * TPC], in_=xnATl[c * 128:(c + 1) * 128, :]))
        for p in range(8):
            for c in range(8):
                for W, ws in ((Wq, wq), (Wk, wk), (Wv, wv)):
                    ch.d("gpsimd", lambda e, c=c, p=p, W=W, ws=ws: e.dma_start(
                        out=ws[:, c * 128:(c + 1) * 128], in_=W[c * 128:(c + 1) * 128, p * 128:(p + 1) * 128]))
            for tg in range(TPC // 512):
                for c in range(8):
                    ch.c("tensor", lambda e, c=c, tg=tg: e.matmul(
                        ps[:], wq[:, c * 128:(c + 1) * 128], xl[:, c * TPC + tg * 512: c * TPC + tg * 512 + 512],
                        start=(c == 0), stop=(c == 7)))
                ch.c("vector", lambda e, tg=tg: e.tensor_copy(out=qs[:, tg * 512:(tg + 1) * 512], in_=ps[:]))
            ch.d("sync", lambda e, p=p: e.dma_start(out=q_out[p, :, :], in_=qs[:]))
            for kt in range(S // 512):
                for c in range(8):
                    ch.c("tensor", lambda e, c=c, kt=kt: e.matmul(
                        ps[:], wk[:, c * 128:(c + 1) * 128], xa[:, c * S + kt * 512: c * S + kt * 512 + 512],
                        start=(c == 0), stop=(c == 7)))
                ch.c("vector", lambda e, kt=kt: e.tensor_copy(out=ks[:, kt * 512:(kt + 1) * 512], in_=ps[:]))
            ch.d("sync", lambda e, p=p: e.dma_start(out=k_out[p, :, :], in_=ks[:]))
            for tc in range(S // 128):
                for c in range(8):
                    ch.c("tensor", lambda e, c=c, tc=tc: e.matmul(
                        pv[:], xa[:, c * S + tc * 128: c * S + tc * 128 + 128], wv[:, c * 128:(c + 1) * 128],
                        start=(c == 0), stop=(c == 7)))
                ch.c("vector", lambda e, tc=tc: e.tensor_copy(out=vs[:, tc * 64:(tc + 1) * 64], in_=pv[:, 0:64]))
                ch.c("vector", lambda e, tc=tc: e.tensor_copy(out=vs[:, 4096 + tc * 64:4096 + (tc + 1) * 64], in_=pv[:, 64:128]))
            ch.d("sync", lambda e, p=p: e.dma_start(out=v_out[2 * p, :, :], in_=vs[:, 0:4096]))
            ch.d("sync", lambda e, p=p: e.dma_start(out=v_out[2 * p + 1, :, :], in_=vs[:, 4096:8192]))
        ch.emit(nc, block, sems)
    return nc



SLOPES = [2.0 ** (-8.0 / n) for n in range(16, 0, -1)]
RAMPLEN = 10240

def build_l0attn():
    nc = bass.Bass("TRN2", target_bir_lowering=False)
    q_in = nc.dram_tensor("q_in", [8, 128, TPC], BF16, kind="ExternalInput").ap()
    k_in = nc.dram_tensor("k_in", [8, 128, S], BF16, kind="ExternalInput").ap()
    v_in = nc.dram_tensor("v_in", [16, 128, 4096], BF16, kind="ExternalInput").ap()
    Wo = nc.dram_tensor("Wo", [D, D], F32, kind="ExternalInput").ap()
    dtab = nc.dram_tensor("dtab", [76, 128, 512], BF16, kind="ExternalInput").ap()
    attn0T = nc.dram_tensor("attn0T", [D, TPC], F32, kind="ExternalOutput").ap()

    NQT = TPC // 512
    NKC = S // 128

    ch = Ch()
    with ExitStack() as st:
        sb = lambda n, sh, dt=BF16: st.enter_context(nc.sbuf_tensor(n, sh, dt))
        ka = sb("ka", [66, S]); vsb = sb("vsb", [128, 4096]); qp = sb("qp", [128, TPC])
        qa = sb("qa", [66, TPC]); phat = sb("phat", [128, TPC]); dts = sb("dts", [128, 76 * 512])
        ctx = sb("ctx", [128, 8 * TPC]); wo = sb("wo", [128, 8 * 1024])
        sqk = sb("sqk", [64, 512]); sqq = sb("sqq", [64, 512])
        kn512 = sb("kn512", [1, 512], F32); kmx = sb("kmx", [1, 1], F32); qn2 = sb("qn2", [1, TPC], F32)
        khat = sb("khat", [1, 1], F32); vqr = sb("vqr", [1, TPC], F32)
        stgh = sb("stgh", [1, TPC]); stgl = sb("stgl", [1, TPC])
        sx = sb("sx", [128, 512], F32); lsb = sb("lsb", [1, 512], F32)
        rhi = sb("rhi", [1, 512]); rlo = sb("rlo", [1, 512]); rlf = sb("rlf", [1, 512], F32)
        ctxf = sb("ctxf", [64, 512], F32)
        o64 = sb("o64", [64, 1]); o128 = sb("o128", [128, 1]); o1x64 = sb("o1x64", [1, 64])
        outf = sb("outf", [128, 512], F32)
        sps = st.enter_context(nc.psum_tensor("sps", [128, 2048], F32))
        cps = st.enter_context(nc.psum_tensor("cps", [66, 2048], F32))
        sems = [st.enter_context(nc.semaphore(f"S{i}")) for i in range(8)]
        block = st.enter_context(nc.Block())
        for c in range(8):
            ch.d("gpsimd", lambda e, c=c: e.dma_start(out=wo[:, c * 1024:(c + 1) * 1024], in_=Wo[c * 128:(c + 1) * 128, :]))
        for dd in range(76):
            ch.d("sync", lambda e, dd=dd: e.dma_start(out=dts[:, dd * 512:(dd + 1) * 512], in_=dtab[dd, :, :]))
        ch.c("gpsimd", lambda e: e.memset(o64[:], 1.0))
        ch.c("gpsimd", lambda e: e.memset(o128[:], 1.0))
        ch.c("gpsimd", lambda e: e.memset(o1x64[:], 1.0))
        ch.c("gpsimd", lambda e: e.memset(ka[64:66, :], 1.0))

        for h in range(16):
            p, r = h // 2, h % 2
            slope = SLOPES[h]
            if r == 0:
                ch.d("sync", lambda e, p=p: e.dma_start(out=qp[:], in_=q_in[p, :, :]))
            ch.d("sync", lambda e, p=p, r=r: e.dma_start(out=ka[0:64, :], in_=k_in[p, r * 64:r * 64 + 64, :]))
            ch.d("sync", lambda e, h=h: e.dma_start(out=vsb[:], in_=v_in[h, :, :]))
            ch.c("gpsimd", lambda e: e.memset(khat[:], 0.0))
            for kt in range(S // 512):
                ch.c("scalar", lambda e, kt=kt: e.activation(out=sqk[:], in_=ka[0:64, kt * 512:(kt + 1) * 512], func=AF.Square))
                ch.c("tensor", lambda e: e.matmul(sps[0:1, 0:512], o64[:], sqk[:], start=True, stop=True))
                ch.c("vector", lambda e: e.tensor_copy(out=kn512[:], in_=sps[0:1, 0:512]))
                ch.c("vector", lambda e: e.tensor_reduce(out=kmx[:], in_=kn512[:], axis=AX, op=Alu.max))
                ch.c("vector", lambda e: e.tensor_tensor(out=khat[:], in0=khat[:], in1=kmx[:], op=Alu.max))
            ch.c("scalar", lambda e: e.activation(out=khat[:], in_=khat[:], func=AF.Sqrt))
            for tg in range(NQT):
                ch.c("scalar", lambda e, r=r, tg=tg: e.activation(out=sqq[:], in_=qp[r * 64:r * 64 + 64, tg * 512:(tg + 1) * 512], func=AF.Square))
                ch.c("tensor", lambda e: e.matmul(sps[0:1, 0:512], o64[:], sqq[:], start=True, stop=True))
                ch.c("vector", lambda e, tg=tg: e.tensor_copy(out=qn2[:, tg * 512:(tg + 1) * 512], in_=sps[0:1, 0:512]))
            ch.c("scalar", lambda e: e.activation(out=qn2[:], in_=qn2[:], func=AF.Sqrt))
            ch.c("vector", lambda e: e.tensor_scalar(out=vqr[:], in0=qn2[:], scalar1=khat[0:1, 0:1], scalar2=-1.03, op0=Alu.mult, op1=Alu.mult))
            ch.c("vector", lambda e, r=r: e.tensor_copy(out=qa[0:64, :], in_=qp[r * 64:r * 64 + 64, :]))
            ch.c("vector", lambda e: e.tensor_copy(out=stgh[:], in_=vqr[:]))
            ch.c("vector", lambda e: e.tensor_tensor(out=qn2[:], in0=vqr[:], in1=stgh[:], op=Alu.subtract))
            ch.c("vector", lambda e: e.tensor_copy(out=stgl[:], in_=qn2[:]))
            ch.d("sync", lambda e: e.dma_start(out=qa[64:65, :], in_=stgh[:]))
            ch.d("sync", lambda e: e.dma_start(out=qa[65:66, :], in_=stgl[:]))
            for kc in range(NKC):
                kb = kc * 128
                for qt in range(NQT):
                    ch.c("tensor", lambda e, kb=kb, qt=qt: e.matmul(
                        sps[:, qt * 512:(qt + 1) * 512], ka[:, kb:kb + 128], qa[:, qt * 512:(qt + 1) * 512], start=True, stop=True))
                for qt in range(NQT):
                    dd = (qt * 512 - kb + 8064) // 128
                    ch.c("vector", lambda e, qt=qt, dd=dd, slope=slope: e.scalar_tensor_tensor(
                        out=sx[:], in0=dts[:, dd * 512:(dd + 1) * 512], scalar=slope, in1=sps[:, qt * 512:(qt + 1) * 512],
                        op0=Alu.mult, op1=Alu.add))
                    ch.c("scalar", lambda e, qt=qt: e.activation(
                        out=phat[:, qt * 512:(qt + 1) * 512], in_=sx[:], func=AF.Exp, bias=0.0, scale=0.125))
                for qt in range(NQT):
                    ch.c("tensor", lambda e, kc=kc, qt=qt: e.matmul(
                        cps[0:64, qt * 512:(qt + 1) * 512], vsb[:, kc * 64:(kc + 1) * 64], phat[:, qt * 512:(qt + 1) * 512],
                        start=(kc == 0), stop=(kc == NKC - 1)))
                    ch.c("tensor", lambda e, kc=kc, qt=qt: e.matmul(
                        cps[64:65, qt * 512:(qt + 1) * 512], o128[:], phat[:, qt * 512:(qt + 1) * 512],
                        start=(kc == 0), stop=(kc == NKC - 1)))
            for qt in range(NQT):
                ch.c("vector", lambda e, qt=qt: e.tensor_copy(out=lsb[:], in_=cps[64:65, qt * 512:(qt + 1) * 512]))
                ch.c("vector", lambda e: e.reciprocal(out=rlf[:], in_=lsb[:]))
                ch.c("vector", lambda e: e.tensor_copy(out=rhi[:], in_=rlf[:]))
                ch.c("vector", lambda e: e.tensor_tensor(out=lsb[:], in0=rlf[:], in1=rhi[:], op=Alu.subtract))
                ch.c("vector", lambda e: e.tensor_copy(out=rlo[:], in_=lsb[:]))
                ch.c("tensor", lambda e: e.matmul(sps[0:64, 0:512], o1x64[:], rhi[:], start=True, stop=False))
                ch.c("tensor", lambda e: e.matmul(sps[0:64, 0:512], o1x64[:], rlo[:], start=False, stop=True))
                ch.c("vector", lambda e, qt=qt: e.tensor_copy(out=ctxf[:], in_=cps[0:64, qt * 512:(qt + 1) * 512]))
                ch.c("vector", lambda e, qt=qt, p=p, r=r: e.tensor_tensor(
                    out=ctx[r * 64:r * 64 + 64, p * TPC + qt * 512: p * TPC + (qt + 1) * 512], in0=ctxf[:], in1=sps[0:64, 0:512], op=Alu.mult))
        for dmc in range(8):
            for qt in range(NQT):
                for c in range(8):
                    ch.c("tensor", lambda e, c=c, dmc=dmc, qt=qt: e.matmul(
                        sps[:, 0:512], wo[:, c * 1024 + dmc * 128: c * 1024 + dmc * 128 + 128],
                        ctx[:, c * TPC + qt * 512: c * TPC + (qt + 1) * 512], start=(c == 0), stop=(c == 7)))
                ch.c("vector", lambda e: e.tensor_copy(out=outf[:], in_=sps[:, 0:512]))
                ch.d("sync", lambda e, dmc=dmc, qt=qt: e.dma_start(
                    out=attn0T[dmc * 128:(dmc + 1) * 128, qt * 512:(qt + 1) * 512], in_=outf[:]))
        ch.emit(nc, block, sems)
    return nc



def _quant_tile(ch, xt, outb, W, m, u, g, t, sg):
    nb = W // 16
    x3 = xt.rearrange("p (n b) -> p n b", b=16)
    mw = int(m.shape[1])
    m3 = bass.AP(m, 0, [[mw, 128], [1, nb], [1, 1]])
    mb = bass.AP(m, 0, [[mw, 128], [1, nb], [0, 16]])
    ch.c("vector", lambda e: e.tensor_reduce(out=m3, in_=x3, axis=AX, op=Alu.max, apply_absolute_value=True))
    ch.c("vector", lambda e: e.tensor_scalar(out=m[0:128, 0:nb], in0=m[0:128, 0:nb], scalar1=1.0 / 6.0, scalar2=1e-12, op0=Alu.mult, op1=Alu.max))
    ch.c("vector", lambda e: e.reciprocal(out=g[0:128, 0:nb], in_=m[0:128, 0:nb]))
    gb = bass.AP(g, 0, [[int(g.shape[1]), 128], [1, nb], [0, 16]])
    ch.c("vector", lambda e: e.tensor_tensor(out=u[0:128, 0:W].rearrange("p (n b) -> p n b", b=16), in0=x3, in1=gb, op=Alu.mult))
    ch.c("scalar", lambda e: e.activation(out=sg[0:128, 0:W], in_=u[0:128, 0:W], func=AF.Sign))
    ch.c("vector", lambda e: e.scalar_tensor_tensor(out=t[0:128, 0:W], in0=u[0:128, 0:W], scalar=-1.0, in1=u[0:128, 0:W], op0=Alu.mult, op1=Alu.max))
    ch.c("vector", lambda e: e.tensor_scalar(out=g[0:128, 0:W], in0=t[0:128, 0:W], scalar1=0.25, scalar2=0.5, op0=Alu.is_gt, op1=Alu.mult))
    for thr, w in ((0.75, 0.5), (1.25, 0.5), (1.75, 0.5), (2.5, 1.0), (3.5, 1.0), (5.0, 2.0)):
        ch.c("vector", lambda e, thr=thr, w=w: e.tensor_scalar(out=u[0:128, 0:W], in0=t[0:128, 0:W], scalar1=thr, scalar2=w, op0=Alu.is_gt, op1=Alu.mult))
        ch.c("vector", lambda e: e.tensor_tensor(out=g[0:128, 0:W], in0=g[0:128, 0:W], in1=u[0:128, 0:W], op=Alu.add))
    ch.c("vector", lambda e: e.tensor_tensor(out=g[0:128, 0:W], in0=g[0:128, 0:W], in1=sg[0:128, 0:W], op=Alu.mult))
    ch.c("vector", lambda e: e.tensor_tensor(out=outb.rearrange("p (n b) -> p n b", b=16),
                                             in0=g[0:128, 0:W].rearrange("p (n b) -> p n b", b=16), in1=mb, op=Alu.mult))


def _transp(ch, src_ap, dst_ap, pt, idn):
    ch.c("tensor", lambda e: e.transpose(out=pt[:], in_=src_ap, identity=idn[:]))
    ch.c("vector", lambda e: e.tensor_copy(out=dst_ap, in_=pt[:]))


IH = 2816

def build_l0ffn():
    nc = bass.Bass("TRN2", target_bir_lowering=False)
    xn_i = nc.dram_tensor("xn_i", [TPC, D], F32, kind="ExternalInput").ap()
    x_i = nc.dram_tensor("x_i", [TPC, D], F32, kind="ExternalInput").ap()
    at_i = nc.dram_tensor("at_i", [TPC, D], F32, kind="ExternalInput").ap()
    wg_i = nc.dram_tensor("wg_i", [IH, D], F32, kind="ExternalInput").ap()
    wu_i = nc.dram_tensor("wu_i", [IH, D], F32, kind="ExternalInput").ap()
    wd_i = nc.dram_tensor("wd_i", [D, IH], F32, kind="ExternalInput").ap()
    x1_o = nc.dram_tensor("x1_o", [TPC, D], F32, kind="ExternalOutput").ap()
    xn1_o = nc.dram_tensor("xn1_o", [TPC, D], F32, kind="ExternalOutput").ap()
    wgT_d = nc.dram_tensor("wgT_d", [D, IH], BF16).ap()
    wuT_d = nc.dram_tensor("wuT_d", [D, IH], BF16).ap()
    wdT_d = nc.dram_tensor("wdT_d", [IH, D], BF16).ap()

    NTG = TPC // 128
    HTS = [(0, 512), (512, 512), (1024, 512), (1536, 512), (2048, 512), (2560, 256)]

    ch = Ch()
    with ExitStack() as st:
        sb = lambda n, sh, dt=BF16: st.enter_context(nc.sbuf_tensor(n, sh, dt))
        xqT = sb("xqT", [128, 8 * TPC])
        wdT = sb("wdT", [128, 22 * 1024])
        wgs = sb("wgs", [128, 8 * 512]); wus = sb("wus", [128, 8 * 512])
        ld = sb("ld", [128, IH], F32)
        qb = sb("qb", [128, IH])
        mq = sb("mq", [128, IH // 16], F32)
        uq = sb("uq", [128, IH], F32); gq = sb("gq", [128, IH], F32)
        tq = sb("tq", [128, IH], F32); sgq = sb("sgq", [128, IH], F32)
        hqT = sb("hqT", [128, 22 * 128])
        ts0 = sb("ts0", [128, 128])
        idn = sb("idn", [128, 128])
        hs = sb("hs", [128, 512], F32)
        hf = sb("hf", [128, 512], F32)
        xt = sb("xt", [128, D], F32); att = sb("att", [128, D], F32)
        x1t = sb("x1t", [128, D], F32); xn1t = sb("xn1t", [128, D], F32)
        dxs = sb("dxs", [128, D], F32); sqs = sb("sqs", [128, D], F32)
        st1 = sb("st1", [128, 1], F32); st2 = sb("st2", [128, 1], F32)
        pg = st.enter_context(nc.psum_tensor("pg", [128, 512], F32))
        pu = st.enter_context(nc.psum_tensor("pu", [128, 512], F32))
        pd = st.enter_context(nc.psum_tensor("pd", [128, 512], F32))
        pt = st.enter_context(nc.psum_tensor("pt", [128, 128], BF16))
        sems = [st.enter_context(nc.semaphore(f"S{i}")) for i in range(8)]
        block = st.enter_context(nc.Block())

        ch.c("gpsimd", lambda e: e.memset(ts0[:], 1.0))
        ch.c("gpsimd", lambda e: e.affine_select(out=idn[:], in_=ts0[:], pattern=[[-1, 128]], compare_op=Alu.is_equal, fill=0.0, base=0, channel_multiplier=1))

        for W_in, T_out, nt in ((wg_i, wgT_d, 22), (wu_i, wuT_d, 22)):
            for ttile in range(nt):
                ch.d("sync", lambda e, W_in=W_in, tt=ttile: e.dma_start(out=ld[:, 0:D], in_=W_in[tt * 128:(tt + 1) * 128, :]))
                _quant_tile(ch, ld[:, 0:D], qb[:, 0:D], D, mq, uq, gq, tq, sgq)
                for fc in range(8):
                    _transp(ch, qb[:, fc * 128:(fc + 1) * 128], ts0[:], pt, idn)
                    ch.d("sync", lambda e, T_out=T_out, fc=fc, tt=ttile: e.dma_start(
                        out=T_out[fc * 128:(fc + 1) * 128, tt * 128:(tt + 1) * 128], in_=ts0[:]))
        for ttile in range(8):
            ch.d("sync", lambda e, tt=ttile: e.dma_start(out=ld[:], in_=wd_i[tt * 128:(tt + 1) * 128, :]))
            _quant_tile(ch, ld[:], qb[:], IH, mq, uq, gq, tq, sgq)
            for hc in range(22):
                _transp(ch, qb[:, hc * 128:(hc + 1) * 128], ts0[:], pt, idn)
                ch.d("sync", lambda e, hc=hc, tt=ttile: e.dma_start(
                    out=wdT_d[hc * 128:(hc + 1) * 128, tt * 128:(tt + 1) * 128], in_=ts0[:]))

        for tg in range(NTG):
            ch.d("sync", lambda e, tg=tg: e.dma_start(out=ld[:, 0:D], in_=xn_i[tg * 128:(tg + 1) * 128, :]))
            _quant_tile(ch, ld[:, 0:D], qb[:, 0:D], D, mq, uq, gq, tq, sgq)
            for fc in range(8):
                _transp(ch, qb[:, fc * 128:(fc + 1) * 128], xqT[:, fc * TPC + tg * 128: fc * TPC + (tg + 1) * 128], pt, idn)

        for hc in range(22):
            ch.d("sync", lambda e, hc=hc: e.dma_start(out=wdT[:, hc * 1024:(hc + 1) * 1024], in_=wdT_d[hc * 128:(hc + 1) * 128, :]))

        for tg in range(NTG):
            ch.d("sync", lambda e, tg=tg: e.dma_start(out=xt[:], in_=x_i[tg * 128:(tg + 1) * 128, :]))
            ch.d("sync", lambda e, tg=tg: e.dma_start(out=att[:], in_=at_i[tg * 128:(tg + 1) * 128, :]))
            for hti, (h0, hw) in enumerate(HTS):
                for fc in range(8):
                    ch.d("sync", lambda e, fc=fc, h0=h0, hw=hw: e.dma_start(out=wgs[:, fc * 512: fc * 512 + hw], in_=wgT_d[fc * 128:(fc + 1) * 128, h0:h0 + hw]))
                    ch.d("sync", lambda e, fc=fc, h0=h0, hw=hw: e.dma_start(out=wus[:, fc * 512: fc * 512 + hw], in_=wuT_d[fc * 128:(fc + 1) * 128, h0:h0 + hw]))
                for fc in range(8):
                    ch.c("tensor", lambda e, fc=fc, tg=tg, hw=hw: e.matmul(
                        pg[:, 0:hw], xqT[:, fc * TPC + tg * 128: fc * TPC + (tg + 1) * 128], wgs[:, fc * 512: fc * 512 + hw],
                        start=(fc == 0), stop=(fc == 7)))
                for fc in range(8):
                    ch.c("tensor", lambda e, fc=fc, tg=tg, hw=hw: e.matmul(
                        pu[:, 0:hw], xqT[:, fc * TPC + tg * 128: fc * TPC + (tg + 1) * 128], wus[:, fc * 512: fc * 512 + hw],
                        start=(fc == 0), stop=(fc == 7)))
                ch.c("scalar", lambda e, hw=hw: e.activation(out=hs[:, 0:hw], in_=pg[:, 0:hw], func=AF.Silu))
                ch.c("vector", lambda e, hw=hw: e.tensor_tensor(out=hf[:, 0:hw], in0=hs[:, 0:hw], in1=pu[:, 0:hw], op=Alu.mult))
                _quant_tile(ch, hf[:, 0:hw], qb[:, 0:hw], hw, mq, uq, gq, tq, sgq)
                for sc in range(hw // 128):
                    _transp(ch, qb[:, sc * 128:(sc + 1) * 128],
                            hqT[:, (h0 // 128 + sc) * 128:(h0 // 128 + sc + 1) * 128], pt, idn)
            for dmt in range(2):
                for hc in range(22):
                    ch.c("tensor", lambda e, hc=hc, dmt=dmt: e.matmul(
                        pd[:], hqT[:, hc * 128:(hc + 1) * 128], wdT[:, hc * 1024 + dmt * 512: hc * 1024 + dmt * 512 + 512],
                        start=(hc == 0), stop=(hc == 21)))
                ch.c("vector", lambda e, dmt=dmt: e.tensor_tensor(out=x1t[:, dmt * 512:(dmt + 1) * 512], in0=att[:, dmt * 512:(dmt + 1) * 512], in1=pd[:], op=Alu.add))
            ch.c("vector", lambda e: e.tensor_tensor(out=x1t[:], in0=x1t[:], in1=xt[:], op=Alu.add))
            _ln_tile(ch, x1t[:], xn1t[:], dxs[:], sqs[:], st1[:], st2[:], D)
            ch.d("sync", lambda e, tg=tg: e.dma_start(out=x1_o[tg * 128:(tg + 1) * 128, :], in_=x1t[:]))
            ch.d("sync", lambda e, tg=tg: e.dma_start(out=xn1_o[tg * 128:(tg + 1) * 128, :], in_=xn1t[:]))
        ch.emit(nc, block, sems)
    return nc



def _rms_norm(ch, src_ps, dst_ap, W, sqb, nps, rr, rhi2, rlo2, bps, evf, o64, o1x64, wcol, eps=1.1920929e-07):
    ch.c("scalar", lambda e: e.activation(out=sqb[:, 0:W], in_=src_ps, func=AF.Square))
    ch.c("tensor", lambda e: e.matmul(nps[0:1, 0:W], o64[:], sqb[:, 0:W], start=True, stop=True))
    ch.c("vector", lambda e: e.tensor_copy(out=rr[:, 0:W], in_=nps[0:1, 0:W]))
    ch.c("vector", lambda e: e.tensor_scalar(out=rr[:, 0:W], in0=rr[:, 0:W], scalar1=1.0 / 64.0, scalar2=eps, op0=Alu.mult, op1=Alu.add))
    ch.c("scalar", lambda e: e.activation(out=rr[:, 0:W], in_=rr[:, 0:W], func=AF.Sqrt))
    ch.c("vector", lambda e: e.reciprocal(out=rr[:, 0:W], in_=rr[:, 0:W]))
    ch.c("vector", lambda e: e.tensor_copy(out=rhi2[:, 0:W], in_=rr[:, 0:W]))
    ch.c("vector", lambda e: e.tensor_tensor(out=rr[:, 0:W], in0=rr[:, 0:W], in1=rhi2[:, 0:W], op=Alu.subtract))
    ch.c("vector", lambda e: e.tensor_copy(out=rlo2[:, 0:W], in_=rr[:, 0:W]))
    ch.c("tensor", lambda e: e.matmul(bps[0:64, 0:W], o1x64[:], rhi2[:, 0:W], start=True, stop=False))
    ch.c("tensor", lambda e: e.matmul(bps[0:64, 0:W], o1x64[:], rlo2[:, 0:W], start=False, stop=True))
    ch.c("vector", lambda e: e.tensor_copy(out=evf[:, 0:W], in_=src_ps))
    ch.c("vector", lambda e: e.tensor_tensor(out=evf[:, 0:W], in0=evf[:, 0:W], in1=bps[0:64, 0:W], op=Alu.mult))
    ch.c("vector", lambda e: e.tensor_scalar(out=dst_ap, in0=evf[:, 0:W], scalar1=wcol, scalar2=None, op0=Alu.mult))


def build_l1kqv():
    nc = bass.Bass("TRN2", target_bir_lowering=False)
    xn1T = nc.dram_tensor("xn1T", [D, S], F32, kind="ExternalInput").ap()
    wdq = nc.dram_tensor("wdq", [64, 512], F32, kind="ExternalInput").ap()
    wuq = nc.dram_tensor("wuq", [512, 64], F32, kind="ExternalInput").ap()
    wdkv = nc.dram_tensor("wdkv", [64, 1024], F32, kind="ExternalInput").ap()
    wukv = nc.dram_tensor("wukv", [1024, 128], F32, kind="ExternalInput").ap()
    wqr = nc.dram_tensor("wqr", [64, 64], F32, kind="ExternalInput").ap()
    wkr = nc.dram_tensor("wkr", [64, 64], F32, kind="ExternalInput").ap()
    qnw = nc.dram_tensor("qnw", [64, 1], F32, kind="ExternalInput").ap()
    knw = nc.dram_tensor("knw", [64, 1], F32, kind="ExternalInput").ap()
    rw = nc.dram_tensor("rw", [D, 8], F32, kind="ExternalInput").ap()

    qc_o = nc.dram_tensor("qc_o", [4, 64, S], BF16, kind="ExternalOutput").ap()
    qr_o = nc.dram_tensor("qr_o", [4, 64, S], BF16, kind="ExternalOutput").ap()
    vq_o = nc.dram_tensor("vq_o", [4, 1, S], F32, kind="ExternalOutput").ap()
    kc_o = nc.dram_tensor("kc_o", [4, 64, S], BF16, kind="ExternalOutput").ap()
    kr_o = nc.dram_tensor("kr_o", [4, 64, S], BF16, kind="ExternalOutput").ap()
    v_o = nc.dram_tensor("v_o", [4, 128, 4096], BF16, kind="ExternalOutput").ap()
    rsc_o = nc.dram_tensor("rsc_o", [S, 8], F32, kind="ExternalOutput").ap()
    thr_o = nc.dram_tensor("thr_o", [8, 4], F32, kind="ExternalOutput").ap()


    ch = Ch()
    with ExitStack() as st:
        sb = lambda n, sh, dt=BF16: st.enter_context(nc.sbuf_tensor(n, sh, dt))
        xh = sb("xh", [64, S])
        cq = sb("cq", [128, 4 * TPC])
        ckv = sb("ckv", [128, 8 * 512])
        kcs = sb("kcs", [64, S]); krs = sb("krs", [64, S])
        vstage = sb("vstage", [128, 4096]); vtmp = sb("vtmp", [64, 512])
        qcs = sb("qcs", [64, TPC]); qrs = sb("qrs", [64, TPC])
        w1 = sb("w1", [64, 512])
        w2t = sb("w2t", [128, 4 * 64])
        w3 = sb("w3", [64, 1024])
        w4 = sb("w4", [128, 8 * 128])
        w5 = sb("w5", [64, 64]); w6 = sb("w6", [64, 64])
        qnwc = sb("qnwc", [64, 1], F32); knwc = sb("knwc", [64, 1], F32)
        rwT = sb("rwT", [128, 8 * 8], F32)
        xcs = sb("xcs", [128, 8 * 128], F32)
        sqb = sb("sqb", [64, 512])
        rr = sb("rr", [1, 512], F32); rhi2 = sb("rhi2", [1, 512]); rlo2 = sb("rlo2", [1, 512])
        evf = sb("evf", [64, 512], F32)
        o64 = sb("o64", [64, 1]); o1x64 = sb("o1x64", [1, 64]); o1x128 = sb("o1x128", [1, 128])
        kn512 = sb("kn512", [1, 512], F32); kmx = sb("kmx", [1, 1], F32); khat = sb("khat", [1, 1], F32)
        qn2 = sb("qn2", [1, TPC], F32); vqr = sb("vqr", [1, TPC], F32)
        rsf = sb("rsf", [128, 8], F32); rmax = sb("rmax", [128, 1], F32); rsum = sb("rsum", [128, 1], F32)
        scT = sb("scT", [8, S], F32)
        cmpb = sb("cmpb", [8, S], F32)
        lo = sb("lo", [8, 1], F32); hiv = sb("hiv", [8, 1], F32); mid = sb("mid", [8, 1], F32)
        cnt = sb("cnt", [8, 1], F32); sel = sb("sel", [8, 1], F32); nsel = sb("nsel", [8, 1], F32)
        tv = sb("tv", [8, 1], F32); need = sb("need", [8, 1], F32)
        rlo_i = sb("rlo_i", [8, 1], F32); rhi_i = sb("rhi_i", [8, 1], F32)
        thrs = sb("thrs", [8, 4], F32)
        o1x128m = sb("o1x128m", [1, 128])
        idn_kqv = sb("idn_kqv", [128, 128]); tsk = sb("tsk", [128, 128])
        idnf_kqv = sb("idnf_kqv", [128, 128], F32); tskf = sb("tskf", [128, 128], F32)
        rsbf_kqv = sb("rsbf_kqv", [128, 8], F32)
        ps = st.enter_context(nc.psum_tensor("ps", [128, 512], F32))
        pkv = st.enter_context(nc.psum_tensor("pkv", [128, 512], F32))
        nps = st.enter_context(nc.psum_tensor("nps", [1, 512], F32))
        bps = st.enter_context(nc.psum_tensor("bps", [64, 512], F32))
        pr = st.enter_context(nc.psum_tensor("pr", [128, 8], F32))
        ptv = st.enter_context(nc.psum_tensor("ptv", [128, 64], BF16))
        ptr = st.enter_context(nc.psum_tensor("ptr", [8, 128], F32))
        sems = [st.enter_context(nc.semaphore(f"S{i}")) for i in range(8)]
        block = st.enter_context(nc.Block())

        ch.c("gpsimd", lambda e: e.memset(o64[:], 1.0))
        ch.c("gpsimd", lambda e: e.memset(o1x64[:], 1.0))
        ch.c("gpsimd", lambda e: e.memset(o1x128[:], 1.0))
        ch.c("gpsimd", lambda e: e.memset(tsk[:], 1.0))
        ch.c("gpsimd", lambda e: e.affine_select(out=idn_kqv[:], in_=tsk[:], pattern=[[-1, 128]], compare_op=Alu.is_equal, fill=0.0, base=0, channel_multiplier=1))
        ch.c("gpsimd", lambda e: e.memset(tskf[:], 1.0))
        ch.c("gpsimd", lambda e: e.affine_select(out=idnf_kqv[:], in_=tskf[:], pattern=[[-1, 128]], compare_op=Alu.is_equal, fill=0.0, base=0, channel_multiplier=1))
        ch.d("gpsimd", lambda e: e.dma_start(out=w1[:], in_=wdq[:, :]))
        for lc in range(4):
            ch.d("gpsimd", lambda e, lc=lc: e.dma_start(out=w2t[:, lc * 64:(lc + 1) * 64], in_=wuq[lc * 128:(lc + 1) * 128, :]))
        ch.d("gpsimd", lambda e: e.dma_start(out=w3[:], in_=wdkv[:, :]))
        for lc in range(8):
            ch.d("gpsimd", lambda e, lc=lc: e.dma_start(out=w4[:, lc * 128:(lc + 1) * 128], in_=wukv[lc * 128:(lc + 1) * 128, :]))
        ch.d("gpsimd", lambda e: e.dma_start(out=w5[:], in_=wqr[:, :]))
        ch.d("gpsimd", lambda e: e.dma_start(out=w6[:], in_=wkr[:, :]))
        ch.d("sync", lambda e: e.dma_start(out=qnwc[:], in_=qnw[:, :]))
        ch.d("sync", lambda e: e.dma_start(out=knwc[:], in_=knw[:, :]))
        for fc in range(8):
            ch.d("gpsimd", lambda e, fc=fc: e.dma_start(out=rwT[:, fc * 8:(fc + 1) * 8], in_=rw[fc * 128:(fc + 1) * 128, :]))

        for hi in range(4):
            ch.d("gpsimd", lambda e, hi=hi: e.dma_start(out=xh[:], in_=xn1T[hi * 64:(hi + 1) * 64, :]))
            ch.c("gpsimd", lambda e: e.memset(khat[:], 0.0))
            for tt in range(S // 512):
                t0 = tt * 512
                for lc in range(8):
                    ch.c("tensor", lambda e, lc=lc, t0=t0: e.matmul(ps[:], w3[:, lc * 128:(lc + 1) * 128], xh[:, t0:t0 + 512], start=True, stop=True))
                    ch.c("vector", lambda e, lc=lc: e.tensor_copy(out=ckv[:, lc * 512:(lc + 1) * 512], in_=ps[:]))
                for lc in range(8):
                    ch.c("tensor", lambda e, lc=lc: e.matmul(pkv[:], w4[:, lc * 128:(lc + 1) * 128], ckv[:, lc * 512:(lc + 1) * 512], start=(lc == 0), stop=(lc == 7)))
                _rms_norm(ch, pkv[0:64, 0:512], kcs[0:64, t0:t0 + 512], 512, sqb, nps, rr, rhi2, rlo2, bps, evf, o64, o1x64, knwc[:])
                ch.c("vector", lambda e: e.tensor_copy(out=vtmp[:], in_=pkv[64:128, 0:512]))
                for j in range(4):
                    ch.c("tensor", lambda e, j=j: e.transpose(out=ptv[:], in_=vtmp[:, j * 128:(j + 1) * 128], identity=idn_kqv[0:64, 0:64]))
                    ch.c("vector", lambda e, j=j, tt=tt: e.tensor_copy(out=vstage[:, (tt * 4 + j) * 64:(tt * 4 + j + 1) * 64], in_=ptv[:, 0:64]))
                ch.c("tensor", lambda e, t0=t0: e.matmul(ps[0:64, 0:512], w6[:], xh[:, t0:t0 + 512], start=True, stop=True))
                ch.c("vector", lambda e, t0=t0: e.tensor_copy(out=krs[:, t0:t0 + 512], in_=ps[0:64, 0:512]))
                ch.c("scalar", lambda e, t0=t0: e.activation(out=sqb[:], in_=kcs[:, t0:t0 + 512], func=AF.Square))
                ch.c("tensor", lambda e: e.matmul(nps[:], o64[:], sqb[:], start=True, stop=False))
                ch.c("scalar", lambda e, t0=t0: e.activation(out=sqb[:], in_=krs[:, t0:t0 + 512], func=AF.Square))
                ch.c("tensor", lambda e: e.matmul(nps[:], o64[:], sqb[:], start=False, stop=True))
                ch.c("vector", lambda e: e.tensor_copy(out=kn512[:], in_=nps[:]))
                ch.c("vector", lambda e: e.tensor_reduce(out=kmx[:], in_=kn512[:], axis=AX, op=Alu.max))
                ch.c("vector", lambda e: e.tensor_tensor(out=khat[:], in0=khat[:], in1=kmx[:], op=Alu.max))
            ch.c("scalar", lambda e: e.activation(out=khat[:], in_=khat[:], func=AF.Sqrt))
            ch.d("sync", lambda e, hi=hi: e.dma_start(out=kc_o[hi, :, :], in_=kcs[:]))
            ch.d("sync", lambda e, hi=hi: e.dma_start(out=kr_o[hi, :, :], in_=krs[:]))
            ch.d("sync", lambda e, hi=hi: e.dma_start(out=v_o[hi, :, :], in_=vstage[:]))
            for tb in range(S // TPC):
                b0 = tb * TPC
                for lc in range(4):
                    for ttt in range(4):
                        ch.c("tensor", lambda e, lc=lc, ttt=ttt, b0=b0: e.matmul(
                            ps[:], w1[:, lc * 128:(lc + 1) * 128], xh[:, b0 + ttt * 512: b0 + (ttt + 1) * 512], start=True, stop=True))
                        ch.c("vector", lambda e, lc=lc, ttt=ttt: e.tensor_copy(out=cq[:, lc * TPC + ttt * 512: lc * TPC + (ttt + 1) * 512], in_=ps[:]))
                for ttt in range(4):
                    for lc in range(4):
                        ch.c("tensor", lambda e, lc=lc, ttt=ttt: e.matmul(
                            pkv[0:64, 0:512], w2t[:, lc * 64:(lc + 1) * 64], cq[:, lc * TPC + ttt * 512: lc * TPC + (ttt + 1) * 512],
                            start=(lc == 0), stop=(lc == 3)))
                    _rms_norm(ch, pkv[0:64, 0:512], qcs[0:64, ttt * 512:(ttt + 1) * 512], 512, sqb, nps, rr, rhi2, rlo2, bps, evf, o64, o1x64, qnwc[:])
                    ch.c("tensor", lambda e, ttt=ttt: e.matmul(ps[0:64, 0:512], w5[:], qcs[:, ttt * 512:(ttt + 1) * 512], start=True, stop=True))
                    ch.c("vector", lambda e, ttt=ttt: e.tensor_copy(out=qrs[:, ttt * 512:(ttt + 1) * 512], in_=ps[0:64, 0:512]))
                    ch.c("scalar", lambda e, ttt=ttt: e.activation(out=sqb[:], in_=qcs[:, ttt * 512:(ttt + 1) * 512], func=AF.Square))
                    ch.c("tensor", lambda e: e.matmul(nps[:], o64[:], sqb[:], start=True, stop=False))
                    ch.c("scalar", lambda e, ttt=ttt: e.activation(out=sqb[:], in_=qrs[:, ttt * 512:(ttt + 1) * 512], func=AF.Square))
                    ch.c("tensor", lambda e: e.matmul(nps[:], o64[:], sqb[:], start=False, stop=True))
                    ch.c("vector", lambda e, ttt=ttt: e.tensor_copy(out=qn2[:, ttt * 512:(ttt + 1) * 512], in_=nps[:]))
                ch.c("scalar", lambda e: e.activation(out=qn2[:], in_=qn2[:], func=AF.Sqrt))
                ch.c("vector", lambda e: e.tensor_scalar(out=vqr[:], in0=qn2[:], scalar1=khat[0:1, 0:1], scalar2=-1.03, op0=Alu.mult, op1=Alu.mult))
                ch.d("sync", lambda e, hi=hi, b0=b0: e.dma_start(out=qc_o[hi, :, b0:b0 + TPC], in_=qcs[:]))
                ch.d("sync", lambda e, hi=hi, b0=b0: e.dma_start(out=qr_o[hi, :, b0:b0 + TPC], in_=qrs[:]))
                ch.d("sync", lambda e, hi=hi, b0=b0: e.dma_start(out=vq_o[hi, :, b0:b0 + TPC], in_=vqr[:]))

        for tc in range(S // 128):
            for fc in range(8):
                ch.d("sync", lambda e, fc=fc, tc=tc: e.dma_start(out=xcs[:, fc * 128:(fc + 1) * 128], in_=xn1T[fc * 128:(fc + 1) * 128, tc * 128:(tc + 1) * 128]))
            for fc in range(8):
                ch.c("tensor", lambda e, fc=fc: e.matmul(pr[:], xcs[:, fc * 128:(fc + 1) * 128], rwT[:, fc * 8:(fc + 1) * 8], start=(fc == 0), stop=(fc == 7)))
            ch.c("vector", lambda e: e.tensor_reduce(out=rmax[:], in_=pr[:], axis=AX, op=Alu.max, negate=True))
            ch.c("scalar", lambda e: e.activation(out=rsf[:], in_=pr[:], func=AF.Exp, bias=rmax[:], scale=1.0))
            ch.c("vector", lambda e: e.tensor_reduce(out=rsum[:], in_=rsf[:], axis=AX, op=Alu.add))
            ch.c("vector", lambda e: e.reciprocal(out=rsum[:], in_=rsum[:]))
            ch.c("vector", lambda e: e.tensor_scalar(out=rsf[:], in0=rsf[:], scalar1=rsum[:], scalar2=None, op0=Alu.mult))
            ch.d("sync", lambda e, tc=tc: e.dma_start(out=rsc_o[tc * 128:(tc + 1) * 128, :], in_=rsf[:]))
            ch.c("vector", lambda e: e.tensor_copy(out=rsbf_kqv[:, 0:8], in_=rsf[:]))
            ch.c("tensor", lambda e: e.transpose(out=ptr[:], in_=rsbf_kqv[:, 0:8], identity=idnf_kqv[:]))
            ch.c("vector", lambda e, tc=tc: e.tensor_copy(out=scT[:, tc * 128:(tc + 1) * 128], in_=ptr[:]))

        KCAP = 2049.0
        ch.c("gpsimd", lambda e: e.memset(lo[:], -1.0))
        ch.c("gpsimd", lambda e: e.memset(hiv[:], 2.0))
        for it in range(64):
            ch.c("vector", lambda e: e.tensor_tensor(out=mid[:], in0=lo[:], in1=hiv[:], op=Alu.add))
            ch.c("vector", lambda e: e.tensor_scalar(out=mid[:], in0=mid[:], scalar1=0.5, scalar2=None, op0=Alu.mult))
            ch.c("vector", lambda e: e.tensor_scalar(out=cmpb[:], in0=scT[:], scalar1=mid[:], scalar2=None, op0=Alu.is_gt))
            ch.c("vector", lambda e: e.tensor_reduce(out=cnt[:], in_=cmpb[:], axis=AX, op=Alu.add))
            ch.c("vector", lambda e: e.tensor_scalar(out=sel[:], in0=cnt[:], scalar1=KCAP, scalar2=None, op0=Alu.is_ge))
            ch.c("vector", lambda e: e.tensor_scalar(out=nsel[:], in0=sel[:], scalar1=-1.0, scalar2=1.0, op0=Alu.mult, op1=Alu.add))
            ch.c("vector", lambda e: e.tensor_tensor(out=tv[:], in0=mid[:], in1=lo[:], op=Alu.subtract))
            ch.c("vector", lambda e: e.tensor_tensor(out=tv[:], in0=sel[:], in1=tv[:], op=Alu.mult))
            ch.c("vector", lambda e: e.tensor_tensor(out=lo[:], in0=lo[:], in1=tv[:], op=Alu.add))
            ch.c("vector", lambda e: e.tensor_tensor(out=tv[:], in0=mid[:], in1=hiv[:], op=Alu.subtract))
            ch.c("vector", lambda e: e.tensor_tensor(out=tv[:], in0=nsel[:], in1=tv[:], op=Alu.mult))
            ch.c("vector", lambda e: e.tensor_tensor(out=hiv[:], in0=hiv[:], in1=tv[:], op=Alu.add))
        ch.c("vector", lambda e: e.tensor_scalar(out=cmpb[:], in0=scT[:], scalar1=hiv[:], scalar2=None, op0=Alu.is_gt))
        ch.c("vector", lambda e: e.tensor_reduce(out=cnt[:], in_=cmpb[:], axis=AX, op=Alu.add))
        ch.c("vector", lambda e: e.tensor_scalar(out=need[:], in0=cnt[:], scalar1=-1.0, scalar2=KCAP, op0=Alu.mult, op1=Alu.add))
        ch.c("vector", lambda e: e.tensor_scalar(out=scT[:], in0=scT[:], scalar1=hiv[:], scalar2=None, op0=Alu.is_equal))
        ch.c("gpsimd", lambda e: e.memset(rlo_i[:], -1.0))
        ch.c("gpsimd", lambda e: e.memset(rhi_i[:], 8191.0))
        for it in range(30):
            ch.c("vector", lambda e: e.tensor_tensor(out=mid[:], in0=rlo_i[:], in1=rhi_i[:], op=Alu.add))
            ch.c("vector", lambda e: e.tensor_scalar(out=mid[:], in0=mid[:], scalar1=0.5, scalar2=None, op0=Alu.mult))
            ch.c("gpsimd", lambda e: e.iota(cmpb[:], [[1, S]], base=0, channel_multiplier=0, allow_small_or_imprecise_dtypes=True))
            ch.c("vector", lambda e: e.tensor_scalar(out=cmpb[:], in0=cmpb[:], scalar1=mid[:], scalar2=None, op0=Alu.is_le))
            ch.c("vector", lambda e: e.tensor_tensor(out=cmpb[:], in0=cmpb[:], in1=scT[:], op=Alu.mult))
            ch.c("vector", lambda e: e.tensor_reduce(out=cnt[:], in_=cmpb[:], axis=AX, op=Alu.add))
            ch.c("vector", lambda e: e.tensor_scalar(out=sel[:], in0=cnt[:], scalar1=need[:], scalar2=None, op0=Alu.is_ge))
            ch.c("vector", lambda e: e.tensor_scalar(out=nsel[:], in0=sel[:], scalar1=-1.0, scalar2=1.0, op0=Alu.mult, op1=Alu.add))
            ch.c("vector", lambda e: e.tensor_tensor(out=tv[:], in0=mid[:], in1=rhi_i[:], op=Alu.subtract))
            ch.c("vector", lambda e: e.tensor_tensor(out=tv[:], in0=sel[:], in1=tv[:], op=Alu.mult))
            ch.c("vector", lambda e: e.tensor_tensor(out=rhi_i[:], in0=rhi_i[:], in1=tv[:], op=Alu.add))
            ch.c("vector", lambda e: e.tensor_tensor(out=tv[:], in0=mid[:], in1=rlo_i[:], op=Alu.subtract))
            ch.c("vector", lambda e: e.tensor_tensor(out=tv[:], in0=nsel[:], in1=tv[:], op=Alu.mult))
            ch.c("vector", lambda e: e.tensor_tensor(out=rlo_i[:], in0=rlo_i[:], in1=tv[:], op=Alu.add))
        ch.c("vector", lambda e: e.tensor_copy(out=thrs[:, 0:1], in_=lo[:]))
        ch.c("vector", lambda e: e.tensor_copy(out=thrs[:, 1:2], in_=hiv[:]))
        ch.c("vector", lambda e: e.tensor_copy(out=thrs[:, 2:3], in_=rhi_i[:]))
        ch.c("vector", lambda e: e.tensor_copy(out=thrs[:, 3:4], in_=need[:]))
        ch.d("sync", lambda e: e.dma_start(out=thr_o[:, :], in_=thrs[:]))
        ch.emit(nc, block, sems)
    return nc



def build_l1attn():
    nc = bass.Bass("TRN2", target_bir_lowering=False)
    qc_i = nc.dram_tensor("qc_i", [4, 64, S], BF16, kind="ExternalInput").ap()
    qr_i = nc.dram_tensor("qr_i", [4, 64, S], BF16, kind="ExternalInput").ap()
    vq_i = nc.dram_tensor("vq_i", [4, 1, S], F32, kind="ExternalInput").ap()
    kc_i = nc.dram_tensor("kc_i", [4, 64, S], BF16, kind="ExternalInput").ap()
    kr_i = nc.dram_tensor("kr_i", [4, 64, S], BF16, kind="ExternalInput").ap()
    v_i = nc.dram_tensor("v_i", [4, 128, 4096], BF16, kind="ExternalInput").ap()
    wo = nc.dram_tensor("wo", [64, 64], F32, kind="ExternalInput").ap()
    mtab = nc.dram_tensor("mtab", [4, 128, 512], F32, kind="ExternalInput").ap()
    wol1 = nc.dram_tensor("wol1", [64, 64], F32, kind="ExternalInput").ap()
    at1_o = nc.dram_tensor("at1_o", [4, 64, S], F32, kind="ExternalOutput").ap()

    NQT = S // 512

    ch = Ch()
    with ExitStack() as st:
        sb = lambda n, sh, dt=BF16: st.enter_context(nc.sbuf_tensor(n, sh, dt))
        kcs = sb("kcs", [64, S]); kra = sb("kra", [66, S])
        vsb = sb("vsb", [128, 4096])
        qcs = sb("qcs", [64, S]); qra = sb("qra", [66, S])
        vqf = sb("vqf", [1, S], F32)
        stgh = sb("stgh", [1, S]); stgl = sb("stgl", [1, S]); losq = sb("losq", [1, S], F32)
        mts = sb("mts", [128, 4 * 512], F32)
        phat = sb("phat", [128, 512])
        sx = sb("sx", [128, 512], F32)
        wos = sb("wos", [64, 64])
        lsb = sb("lsb", [1, 512], F32); rhi = sb("rhi", [1, 512]); rlo = sb("rlo", [1, 512]); rlf = sb("rlf", [1, 512], F32)
        ctxf = sb("ctxf", [64, 512], F32)
        ctxn = sb("ctxn", [64, 512])
        o128 = sb("o128", [128, 1]); o1x64 = sb("o1x64", [1, 64])
        outf = sb("outf", [64, 512], F32)
        sps = st.enter_context(nc.psum_tensor("sps", [128, 512], F32))
        cps = st.enter_context(nc.psum_tensor("cps", [66, 512], F32))
        bps = st.enter_context(nc.psum_tensor("bps", [64, 512], F32))
        pso = st.enter_context(nc.psum_tensor("pso", [64, 512], F32))
        sems = [st.enter_context(nc.semaphore(f"S{i}")) for i in range(8)]
        block = st.enter_context(nc.Block())

        ch.c("gpsimd", lambda e: e.memset(o128[:], 1.0))
        ch.c("gpsimd", lambda e: e.memset(o1x64[:], 1.0))
        ch.c("gpsimd", lambda e: e.memset(kra[64:66, :], 1.0))
        ch.d("gpsimd", lambda e: e.dma_start(out=wos[:], in_=wo[:, :]))
        for v in range(4):
            ch.d("sync", lambda e, v=v: e.dma_start(out=mts[:, v * 512:(v + 1) * 512], in_=mtab[v, :, :]))

        for hi in range(4):
            ch.d("sync", lambda e, hi=hi: e.dma_start(out=kcs[:], in_=kc_i[hi, :, :]))
            ch.d("sync", lambda e, hi=hi: e.dma_start(out=kra[0:64, :], in_=kr_i[hi, :, :]))
            ch.d("sync", lambda e, hi=hi: e.dma_start(out=vsb[:], in_=v_i[hi, :, :]))
            ch.d("sync", lambda e, hi=hi: e.dma_start(out=qcs[:], in_=qc_i[hi, :, :]))
            ch.d("sync", lambda e, hi=hi: e.dma_start(out=qra[0:64, :], in_=qr_i[hi, :, :]))
            ch.d("sync", lambda e, hi=hi: e.dma_start(out=vqf[:], in_=vq_i[hi, :, :]))
            ch.c("vector", lambda e: e.tensor_copy(out=stgh[:], in_=vqf[:]))
            ch.c("vector", lambda e: e.tensor_tensor(out=losq[:], in0=vqf[:], in1=stgh[:], op=Alu.subtract))
            ch.c("vector", lambda e: e.tensor_copy(out=stgl[:], in_=losq[:]))
            ch.d("sync", lambda e: e.dma_start(out=qra[64:65, :], in_=stgh[:]))
            ch.d("sync", lambda e: e.dma_start(out=qra[65:66, :], in_=stgl[:]))
            for qt in range(NQT):
                q0 = qt * 512
                nkc = 4 * qt + 4
                for kc in range(nkc):
                    kb = kc * 128
                    ch.c("tensor", lambda e, kb=kb, q0=q0: e.matmul(
                        sps[:], kcs[:, kb:kb + 128], qcs[:, q0:q0 + 512], start=True, stop=False))
                    ch.c("tensor", lambda e, kb=kb, q0=q0: e.matmul(
                        sps[:], kra[:, kb:kb + 128], qra[:, q0:q0 + 512], start=False, stop=True))
                    if kc >= 4 * qt:
                        v = kc - 4 * qt
                        ch.c("vector", lambda e, v=v: e.tensor_tensor(out=sx[:], in0=mts[:, v * 512:(v + 1) * 512], in1=sps[:], op=Alu.add))
                        ch.c("scalar", lambda e: e.activation(out=phat[:], in_=sx[:], func=AF.Exp, bias=0.0, scale=0.125))
                    else:
                        ch.c("scalar", lambda e: e.activation(out=phat[:], in_=sps[:], func=AF.Exp, bias=0.0, scale=0.125))
                    ch.c("tensor", lambda e, kc=kc, nkc=nkc: e.matmul(
                        cps[0:64, :], vsb[:, kc * 64:(kc + 1) * 64], phat[:], start=(kc == 0), stop=(kc == nkc - 1)))
                    ch.c("tensor", lambda e, kc=kc, nkc=nkc: e.matmul(
                        cps[64:65, :], o128[:], phat[:], start=(kc == 0), stop=(kc == nkc - 1)))
                ch.c("vector", lambda e: e.tensor_copy(out=lsb[:], in_=cps[64:65, :]))
                ch.c("vector", lambda e: e.reciprocal(out=rlf[:], in_=lsb[:]))
                ch.c("vector", lambda e: e.tensor_copy(out=rhi[:], in_=rlf[:]))
                ch.c("vector", lambda e: e.tensor_tensor(out=lsb[:], in0=rlf[:], in1=rhi[:], op=Alu.subtract))
                ch.c("vector", lambda e: e.tensor_copy(out=rlo[:], in_=lsb[:]))
                ch.c("tensor", lambda e: e.matmul(bps[:], o1x64[:], rhi[:], start=True, stop=False))
                ch.c("tensor", lambda e: e.matmul(bps[:], o1x64[:], rlo[:], start=False, stop=True))
                ch.c("vector", lambda e: e.tensor_copy(out=ctxf[:], in_=cps[0:64, :]))
                ch.c("vector", lambda e: e.tensor_tensor(out=ctxn[:], in0=ctxf[:], in1=bps[:], op=Alu.mult))
                ch.c("tensor", lambda e: e.matmul(pso[:], wos[:], ctxn[:], start=True, stop=True))
                ch.c("vector", lambda e: e.tensor_copy(out=outf[:], in_=pso[:]))
                ch.d("sync", lambda e, hi=hi, q0=q0: e.dma_start(out=at1_o[hi, :, q0:q0 + 512], in_=outf[:]))
        ch.emit(nc, block, sems)
    return nc



def build_l1moe():
    nc = bass.Bass("TRN2", target_bir_lowering=False)
    xn1T = nc.dram_tensor("xn1T", [D, TPC], F32, kind="ExternalInput").ap()
    x1T = nc.dram_tensor("x1T", [D, TPC], F32, kind="ExternalInput").ap()
    at1T = nc.dram_tensor("at1T", [D, TPC], F32, kind="ExternalInput").ap()
    rscT = nc.dram_tensor("rscT", [8, TPC], F32, kind="ExternalInput").ap()
    thr = nc.dram_tensor("thr", [8, 4], F32, kind="ExternalInput").ap()
    idxl = nc.dram_tensor("idxl", [8, TPC], F32, kind="ExternalInput").ap()
    w1 = nc.dram_tensor("w1", [8, D, 4096], F32, kind="ExternalInput").ap()
    b1T = nc.dram_tensor("b1T", [8, 128, 32], F32, kind="ExternalInput").ap()
    w2 = nc.dram_tensor("w2", [8, 4096, D], F32, kind="ExternalInput").ap()
    b2T = nc.dram_tensor("b2T", [8, 128, 8], F32, kind="ExternalInput").ap()
    x2T_o = nc.dram_tensor("x2T_o", [D, TPC], F32, kind="ExternalOutput").ap()

    ch = Ch()
    with ExitStack() as st:
        sb = lambda n, sh, dt=BF16: st.enter_context(nc.sbuf_tensor(n, sh, dt))
        xb = sb("xb", [128, 8 * TPC])
        w1s = sb("w1s", [128, 8 * 128]); w2s = sb("w2s", [128, 32 * 128])
        b1s = sb("b1s", [128, 32], F32); b2s = sb("b2s", [128, 8], F32)
        gsb = sb("gsb", [8, TPC], F32)
        grow = sb("grow", [1, TPC], F32)
        ghi = sb("ghi", [1, TPC]); glo = sb("glo", [1, TPC]); gtm = sb("gtm", [1, TPC], F32)
        thrs = sb("thrs", [8, 4], F32)
        o1x128m = sb("o1x128m", [1, 128])
        t1 = sb("t1", [128, 512], F32)
        xs1 = sb("xs1", [128, 512], F32)
        ph = st.enter_context(nc.psum_tensor("ph", [128, 512], F32))
        po = st.enter_context(nc.psum_tensor("po", [128, 512], F32))
        pb = st.enter_context(nc.psum_tensor("pb", [128, 512], F32))
        sems = [st.enter_context(nc.semaphore(f"S{i}")) for i in range(8)]
        block = st.enter_context(nc.Block())

        TH = TPC // 2

        ch.d("sync", lambda e: e.dma_start(out=thrs[:], in_=thr[:, :]))
        for fc in range(8):
            ch.d("gpsimd", lambda e, fc=fc: e.dma_start(out=xb[:, fc * TPC:(fc + 1) * TPC], in_=xn1T[fc * 128:(fc + 1) * 128, :]))
        ch.c("gpsimd", lambda e: e.memset(o1x128m[:], 1.0))
        with ExitStack() as st2:
            rss = st2.enter_context(nc.sbuf_tensor("rss", [8, TPC], F32))
            idxs = st2.enter_context(nc.sbuf_tensor("idxs", [8, TPC], F32))
            m1 = st2.enter_context(nc.sbuf_tensor("m1", [8, TPC], F32))
            ch.d("sync", lambda e: e.dma_start(out=rss[:], in_=rscT[:, :]))
            ch.d("sync", lambda e: e.dma_start(out=idxs[:], in_=idxl[:, :]))
            ch.c("vector", lambda e: e.tensor_scalar(out=idxs[:], in0=idxs[:], scalar1=thrs[:, 2:3], scalar2=None, op0=Alu.is_le))
            ch.c("vector", lambda e: e.tensor_scalar(out=m1[:], in0=rss[:], scalar1=thrs[:, 1:2], scalar2=None, op0=Alu.is_equal))
            ch.c("vector", lambda e: e.tensor_tensor(out=m1[:], in0=m1[:], in1=idxs[:], op=Alu.mult))
            ch.c("vector", lambda e: e.tensor_scalar(out=gsb[:], in0=rss[:], scalar1=thrs[:, 1:2], scalar2=None, op0=Alu.is_gt))
            ch.c("vector", lambda e: e.tensor_tensor(out=gsb[:], in0=gsb[:], in1=m1[:], op=Alu.add))
            ch.c("vector", lambda e: e.tensor_tensor(out=gsb[:], in0=gsb[:], in1=rss[:], op=Alu.mult))
        hsb = sb("hsb", [128, 32 * 1024])
        x2a = sb("x2a", [128, 8 * TPC], F32)
        ch.c("gpsimd", lambda e: e.memset(x2a[:], 0.0))

        for ex in range(8):
            ch.d("sync", lambda e, ex=ex: e.dma_start(out=b1s[:], in_=b1T[ex, :, :]))
            ch.d("sync", lambda e, ex=ex: e.dma_start(out=b2s[:], in_=b2T[ex, :, :]))
            ch.d("sync", lambda e, ex=ex: e.dma_start(out=grow[:], in_=gsb[ex:ex + 1, :]))
            ch.c("vector", lambda e: e.tensor_copy(out=ghi[:], in_=grow[:]))
            ch.c("vector", lambda e: e.tensor_tensor(out=gtm[:], in0=grow[:], in1=ghi[:], op=Alu.subtract))
            ch.c("vector", lambda e: e.tensor_copy(out=glo[:], in_=gtm[:]))
            for th in range(2):
                tb = th * TH
                for hc in range(32):
                    for fc in range(8):
                        ch.d("gpsimd", lambda e, ex=ex, hc=hc, fc=fc: e.dma_start(
                            out=w1s[:, fc * 128:(fc + 1) * 128], in_=w1[ex, fc * 128:(fc + 1) * 128, hc * 128:(hc + 1) * 128]))
                    for t5 in range(2):
                        for fc in range(8):
                            ch.c("tensor", lambda e, fc=fc, tb=tb, t5=t5: e.matmul(
                                ph[:], w1s[:, fc * 128:(fc + 1) * 128], xb[:, fc * TPC + tb + t5 * 512: fc * TPC + tb + t5 * 512 + 512],
                                start=(fc == 0), stop=(fc == 7)))
                        ch.c("scalar", lambda e, hc=hc, t5=t5: e.activation(
                            out=hsb[:, hc * 1024 + t5 * 512: hc * 1024 + (t5 + 1) * 512], in_=ph[:], func=AF.Gelu,
                            bias=b1s[:, hc:hc + 1], scale=1.0))
                for dmc in range(8):
                    for hc in range(32):
                        ch.d("gpsimd", lambda e, ex=ex, hc=hc, dmc=dmc: e.dma_start(
                            out=w2s[:, hc * 128:(hc + 1) * 128], in_=w2[ex, hc * 128:(hc + 1) * 128, dmc * 128:(dmc + 1) * 128]))
                    for t5 in range(2):
                        for hc in range(32):
                            ch.c("tensor", lambda e, hc=hc, t5=t5: e.matmul(
                                po[:], w2s[:, hc * 128:(hc + 1) * 128], hsb[:, hc * 1024 + t5 * 512: hc * 1024 + (t5 + 1) * 512],
                                start=(hc == 0), stop=(hc == 31)))
                        ch.c("tensor", lambda e, tb=tb, t5=t5: e.matmul(pb[:], o1x128m[:], ghi[0:1, tb + t5 * 512: tb + (t5 + 1) * 512], start=True, stop=False))
                        ch.c("tensor", lambda e, tb=tb, t5=t5: e.matmul(pb[:], o1x128m[:], glo[0:1, tb + t5 * 512: tb + (t5 + 1) * 512], start=False, stop=True))
                        ch.c("vector", lambda e, dmc=dmc: e.tensor_scalar(out=t1[:], in0=po[:], scalar1=b2s[:, dmc:dmc + 1], scalar2=None, op0=Alu.add))
                        ch.c("vector", lambda e: e.tensor_tensor(out=t1[:], in0=t1[:], in1=pb[:], op=Alu.mult))
                        ch.c("vector", lambda e, dmc=dmc, tb=tb, t5=t5: e.tensor_tensor(
                            out=x2a[:, dmc * TPC + tb + t5 * 512: dmc * TPC + tb + (t5 + 1) * 512],
                            in0=x2a[:, dmc * TPC + tb + t5 * 512: dmc * TPC + tb + (t5 + 1) * 512], in1=t1[:], op=Alu.add))
        for fc in range(8):
            for t5 in range(4):
                ch.d("sync", lambda e, fc=fc, t5=t5: e.dma_start(out=xs1[:], in_=x1T[fc * 128:(fc + 1) * 128, t5 * 512:(t5 + 1) * 512]))
                ch.d("sync", lambda e, fc=fc, t5=t5: e.dma_start(out=t1[:], in_=at1T[fc * 128:(fc + 1) * 128, t5 * 512:(t5 + 1) * 512]))
                ch.c("vector", lambda e: e.tensor_tensor(out=xs1[:], in0=xs1[:], in1=t1[:], op=Alu.add))
                ch.c("vector", lambda e, fc=fc, t5=t5: e.tensor_tensor(
                    out=xs1[:], in0=xs1[:], in1=x2a[:, fc * TPC + t5 * 512: fc * TPC + (t5 + 1) * 512], op=Alu.add))
                ch.d("sync", lambda e, fc=fc, t5=t5: e.dma_start(out=x2T_o[fc * 128:(fc + 1) * 128, t5 * 512:(t5 + 1) * 512], in_=xs1[:]))
        ch.emit(nc, block, sems)
    return nc



def build_fused(upto=99):
    nc = bass.Bass("TRN2", target_bir_lowering=False)
    x_i = nc.dram_tensor("x_i", [TPC, D], F32, kind="ExternalInput").ap()
    gam = nc.dram_tensor("gam", [128, D], F32, kind="ExternalInput").ap()
    bet = nc.dram_tensor("bet", [128, D], F32, kind="ExternalInput").ap()
    Wq = nc.dram_tensor("Wq", [D, D], F32, kind="ExternalInput").ap()
    Wk = nc.dram_tensor("Wk", [D, D], F32, kind="ExternalInput").ap()
    Wv = nc.dram_tensor("Wv", [D, D], F32, kind="ExternalInput").ap()
    Wo = nc.dram_tensor("Wo", [D, D], F32, kind="ExternalInput").ap()
    dtab = nc.dram_tensor("dtab", [76, 128, 512], BF16, kind="ExternalInput").ap()
    wg_i = nc.dram_tensor("wg_i", [IH, D], F32, kind="ExternalInput").ap()
    wu_i = nc.dram_tensor("wu_i", [IH, D], F32, kind="ExternalInput").ap()
    wd_i = nc.dram_tensor("wd_i", [D, IH], F32, kind="ExternalInput").ap()
    wdq = nc.dram_tensor("wdq", [64, 512], F32, kind="ExternalInput").ap()
    wuq = nc.dram_tensor("wuq", [512, 64], F32, kind="ExternalInput").ap()
    wdkv = nc.dram_tensor("wdkv", [64, 1024], F32, kind="ExternalInput").ap()
    wukv = nc.dram_tensor("wukv", [1024, 128], F32, kind="ExternalInput").ap()
    wqr = nc.dram_tensor("wqr", [64, 64], F32, kind="ExternalInput").ap()
    wkr = nc.dram_tensor("wkr", [64, 64], F32, kind="ExternalInput").ap()
    qnw = nc.dram_tensor("qnw", [64, 1], F32, kind="ExternalInput").ap()
    knw = nc.dram_tensor("knw", [64, 1], F32, kind="ExternalInput").ap()
    rw = nc.dram_tensor("rw", [D, 8], F32, kind="ExternalInput").ap()
    mtab = nc.dram_tensor("mtab", [4, 128, 512], F32, kind="ExternalInput").ap()
    wol1 = nc.dram_tensor("wol1", [64, 64], F32, kind="ExternalInput").ap()
    w1 = nc.dram_tensor("w1", [8, D, 4096], F32, kind="ExternalInput").ap()
    b1T = nc.dram_tensor("b1T", [8, 128, 32], F32, kind="ExternalInput").ap()
    w2 = nc.dram_tensor("w2", [8, 4096, D], F32, kind="ExternalInput").ap()
    b2T = nc.dram_tensor("b2T", [8, 128, 8], F32, kind="ExternalInput").ap()
    idxl = nc.dram_tensor("idxl", [8, TPC], F32, kind="ExternalInput").ap()
    hgi = nc.dram_tensor("hgi", [1, 1], mybir.dt.int32, kind="ExternalInput").ap()
    loci = nc.dram_tensor("loci", [1, 1], mybir.dt.int32, kind="ExternalInput").ap()
    x2T_o = nc.dram_tensor("x2T_o", [D, TPC], F32, kind="ExternalOutput").ap()
    xn_d = nc.dram_tensor("xn_d", [TPC, D], F32)
    e1c = nc.dram_tensor("e1c", [D, TPC], BF16)
    G1 = nc.dram_tensor("G1", [4 * D, TPC], BF16)
    q_d = nc.dram_tensor("q_d", [8, 128, TPC], BF16)
    k_d = nc.dram_tensor("k_d", [8, 128, S], BF16)
    v_d = nc.dram_tensor("v_d", [16, 128, 4096], BF16)
    at0_d = nc.dram_tensor("at0_d", [TPC, D], F32)
    x1_d = nc.dram_tensor("x1_d", [TPC, D], F32)
    x1T_d = nc.dram_tensor("x1T_d", [D, TPC], F32)
    e3c = nc.dram_tensor("e3c", [D, TPC], F32)
    G3 = nc.dram_tensor("G3", [4 * D, TPC], F32)
    wgT_d = nc.dram_tensor("wgT_d", [D, IH], BF16)
    wuT_d = nc.dram_tensor("wuT_d", [D, IH], BF16)
    wdT_d = nc.dram_tensor("wdT_d", [IH, D], BF16)
    qc_d = nc.dram_tensor("qc_d", [4, 64, S], BF16)
    qr_d = nc.dram_tensor("qr_d", [4, 64, S], BF16)
    vq_d = nc.dram_tensor("vq_d", [4, 1, S], F32)
    kc_d = nc.dram_tensor("kc_d", [4, 64, S], BF16)
    kr_d = nc.dram_tensor("kr_d", [4, 64, S], BF16)
    v1_d = nc.dram_tensor("v1_d", [4, 128, 4096], BF16)
    scT_d = nc.dram_tensor("scT_d", [8, S], F32)
    thr_d = nc.dram_tensor("thr_d", [8, 4], F32)
    e5c = nc.dram_tensor("e5c", [256, S], F32)
    G5 = nc.dram_tensor("G5", [4 * 256, S], F32)
    GROUPS = [[0, 1, 2, 3], [4, 5, 6, 7]]

    ch = Ch()
    with ExitStack() as topst:
        sems = [topst.enter_context(nc.semaphore(f"S{i}")) for i in range(16)]
        ccs1 = topst.enter_context(nc.semaphore("cc1"))
        ccs3 = topst.enter_context(nc.semaphore("cc3"))
        ccs5 = topst.enter_context(nc.semaphore("cc5"))
        grA = topst.enter_context(nc.gpsimd.register("grA"))
        grB = topst.enter_context(nc.gpsimd.register("grB"))
        ppa = topst.enter_context(nc.semaphore("ppa"))
        ppv = topst.enter_context(nc.semaphore("ppv"))
        pps = topst.enter_context(nc.semaphore("pps"))
        ppc = topst.enter_context(nc.semaphore("ppc"))
        pcnt = {"a": 0, "v": 0, "s": 0, "c": 0}

        def _SEC1():
            with _FlushStack(ch, nc, sems) as st:
                sb = lambda n, sh, dt=F32: st.enter_context(nc.sbuf_tensor(n, sh, dt))
                xt = sb("s1_xt", [128, D]); dx = sb("s1_dx", [128, D]); sq = sb("s1_sq", [128, D])
                xnt = sb("s1_xnt", [128, D]); xat = sb("s1_xat", [128, D])
                gt = sb("s1_gt", [128, D]); bt = sb("s1_bt", [128, D])
                s1v = sb("s1_s1", [128, 1]); s2v = sb("s1_s2", [128, 1])
                xab = sb("s1_xab", [128, D], BF16); sT = sb("s1_sT", [128, 128], BF16)
                ts0 = sb("s1_ts0", [128, 128], BF16); idn = sb("s1_idn", [128, 128], BF16)
                pt = st.enter_context(nc.psum_tensor("s1_pt", [128, 128], BF16))
                ch.c("gpsimd", lambda e: e.memset(ts0[:], 1.0))
                ch.c("gpsimd", lambda e: e.affine_select(out=idn[:], in_=ts0[:], pattern=[[-1, 128]], compare_op=Alu.is_equal, fill=0.0, base=0, channel_multiplier=1))
                ch.d("sync", lambda e: e.dma_start(out=gt[:], in_=gam[:, :]))
                ch.d("sync", lambda e: e.dma_start(out=bt[:], in_=bet[:, :]))
                for tg in range(TPC // 128):
                    r0 = tg * 128
                    ch.d("sync", lambda e, r0=r0: e.dma_start(out=xt[:], in_=x_i[r0:r0 + 128, :]))
                    _ln_tile(ch, xt[:], xnt[:], dx[:], sq[:], s1v[:], s2v[:], D)
                    ch.d("sync", lambda e, r0=r0: e.dma_start(out=xn_d.ap()[r0:r0 + 128, :], in_=xnt[:]))
                    _ln_tile(ch, xnt[:], xat[:], dx[:], sq[:], s1v[:], s2v[:], D)
                    ch.c("vector", lambda e: e.tensor_tensor(out=xat[:], in0=xat[:], in1=gt[:], op=Alu.mult))
                    ch.c("vector", lambda e: e.tensor_tensor(out=xat[:], in0=xat[:], in1=bt[:], op=Alu.add))
                    ch.c("vector", lambda e: e.tensor_copy(out=xab[:], in_=xat[:]))
                    for fc in range(8):
                        ch.c("tensor", lambda e, fc=fc: e.transpose(out=pt[:], in_=xab[:, fc * 128:(fc + 1) * 128], identity=idn[:]))
                        ch.c("vector", lambda e: e.tensor_copy(out=sT[:], in_=pt[:]))
                        ch.d("sync", lambda e, fc=fc, tg=tg: e.dma_start(out=e1c.ap()[fc * 128:(fc + 1) * 128, tg * 128:(tg + 1) * 128], in_=sT[:]))
            ch.flush(nc, sems)
        if upto >= 1: _SEC1()
        if upto >= 1.5:
            ch.cc(lambda e: e.collective_compute("AllGather", Alu.bypass, replica_groups=GROUPS,
                                                 ins=[e1c.ap().opt()], outs=[G1.ap().opt()]), ccs1)

        def _SEC2():
            with _FlushStack(ch, nc, sems) as st:
                sb = lambda n, sh, dt=BF16: st.enter_context(nc.sbuf_tensor(n, sh, dt))
                xa = sb("s2_xa", [128, 8 * S]); xl = sb("s2_xl", [128, 8 * TPC])
                wqs = sb("s2_wq", [128, 8 * 128]); wks = sb("s2_wk", [128, 8 * 128]); wvs = sb("s2_wv", [128, 8 * 128])
                qs = sb("s2_qs", [128, TPC]); ks = sb("s2_ks", [128, S]); vs = sb("s2_vs", [128, 2 * 4096])
                ps = st.enter_context(nc.psum_tensor("s2_ps", [128, 512], F32))
                pv = st.enter_context(nc.psum_tensor("s2_pv", [128, 128], F32))
                for c in range(8):
                    for r in range(4):
                        ch.d("sync", lambda e, c=c, r=r: e.dma_start(
                            out=xa[:, c * S + r * TPC: c * S + (r + 1) * TPC], in_=G1.ap()[r * D + c * 128: r * D + (c + 1) * 128, :]))
                    ch.d("sync", lambda e, c=c: e.dma_start(out=xl[:, c * TPC:(c + 1) * TPC], in_=e1c.ap()[c * 128:(c + 1) * 128, :]))
                for p in range(8):
                    for c in range(8):
                        for W, ws in ((Wq, wqs), (Wk, wks), (Wv, wvs)):
                            ch.d("gpsimd", lambda e, c=c, p=p, W=W, ws=ws: e.dma_start(
                                out=ws[:, c * 128:(c + 1) * 128], in_=W[c * 128:(c + 1) * 128, p * 128:(p + 1) * 128]))
                    for tg in range(TPC // 512):
                        for c in range(8):
                            ch.c("tensor", lambda e, c=c, tg=tg: e.matmul(
                                ps[:], wqs[:, c * 128:(c + 1) * 128], xl[:, c * TPC + tg * 512: c * TPC + tg * 512 + 512],
                                start=(c == 0), stop=(c == 7)))
                        ch.c("vector", lambda e, tg=tg: e.tensor_copy(out=qs[:, tg * 512:(tg + 1) * 512], in_=ps[:]))
                    ch.d("sync", lambda e, p=p: e.dma_start(out=q_d.ap()[p, :, :], in_=qs[:]))
                    for kt in range(S // 512):
                        for c in range(8):
                            ch.c("tensor", lambda e, c=c, kt=kt: e.matmul(
                                ps[:], wks[:, c * 128:(c + 1) * 128], xa[:, c * S + kt * 512: c * S + kt * 512 + 512],
                                start=(c == 0), stop=(c == 7)))
                        ch.c("vector", lambda e, kt=kt: e.tensor_copy(out=ks[:, kt * 512:(kt + 1) * 512], in_=ps[:]))
                    ch.d("sync", lambda e, p=p: e.dma_start(out=k_d.ap()[p, :, :], in_=ks[:]))
                    for tc in range(S // 128):
                        for c in range(8):
                            ch.c("tensor", lambda e, c=c, tc=tc: e.matmul(
                                pv[:], xa[:, c * S + tc * 128: c * S + tc * 128 + 128], wvs[:, c * 128:(c + 1) * 128],
                                start=(c == 0), stop=(c == 7)))
                        ch.c("vector", lambda e, tc=tc: e.tensor_copy(out=vs[:, tc * 64:(tc + 1) * 64], in_=pv[:, 0:64]))
                        ch.c("vector", lambda e, tc=tc: e.tensor_copy(out=vs[:, 4096 + tc * 64:4096 + (tc + 1) * 64], in_=pv[:, 64:128]))
                    ch.d("sync", lambda e, p=p: e.dma_start(out=v_d.ap()[2 * p, :, :], in_=vs[:, 0:4096]))
                    ch.d("sync", lambda e, p=p: e.dma_start(out=v_d.ap()[2 * p + 1, :, :], in_=vs[:, 4096:8192]))

            ch.flush(nc, sems)
        if upto >= 2: _SEC2()
        def _SEC3():
            NQT = TPC // 512
            NKC = S // 128
            with _FlushStack(ch, nc, sems) as st0:
                sb0 = lambda n, sh, dt=BF16: st0.enter_context(nc.sbuf_tensor(n, sh, dt))
                ctx = sb0("s3_ctx", [128, 8 * TPC])
                wos = sb0("s3_wo", [128, 8 * 1024])
                for c in range(8):
                    ch.d("gpsimd", lambda e, c=c: e.dma_start(out=wos[:, c * 1024:(c + 1) * 1024], in_=Wo[c * 128:(c + 1) * 128, :]))
                with _FlushStack(ch, nc, sems) as st:
                    sb = lambda n, sh, dt=BF16: st.enter_context(nc.sbuf_tensor(n, sh, dt))
                    ka = sb("s3_ka", [66, S]); vsb = sb("s3_vsb", [128, 4096]); qp = sb("s3_qp", [128, TPC])
                    qa = sb("s3_qa", [66, TPC]); phat = sb("s3_phat", [128, TPC]); dts = sb("s3_dts", [128, 76 * 512])
                    sqk = sb("s3_sqk", [64, 512]); sqq = sb("s3_sqq", [64, 512])
                    kn512 = sb("s3_kn512", [1, 512], F32); kmx = sb("s3_kmx", [1, 1], F32)
                    qn2 = sb("s3_qn2", [1, TPC], F32)
                    khat = sb("s3_khat", [1, 1], F32); vqr = sb("s3_vqr", [1, TPC], F32)
                    stgh = sb("s3_stgh", [1, TPC]); stgl = sb("s3_stgl", [1, TPC])
                    sx = sb("s3_sx", [128, 2048], F32)
                    lsb = sb("s3_lsb", [1, 512], F32); rhi = sb("s3_rhi", [1, 512]); rlo = sb("s3_rlo", [1, 512])
                    rlf = sb("s3_rlf", [1, 512], F32); ctxf = sb("s3_ctxf", [64, 512], F32)
                    o64 = sb("s3_o64", [64, 1]); o128 = sb("s3_o128", [128, 1]); o1x64 = sb("s3_o1x64", [1, 64])
                    spA = st.enter_context(nc.psum_tensor("s3_spA", [128, 1024], F32))
                    spB = st.enter_context(nc.psum_tensor("s3_spB", [128, 1024], F32))
                    cps = st.enter_context(nc.psum_tensor("s3_cps", [66, 2048], F32))
                    spd = (spA, spB)
                    for dd in range(76):
                        ch.d("sync", lambda e, dd=dd: e.dma_start(out=dts[:, dd * 512:(dd + 1) * 512], in_=dtab[dd, :, :]))
                    ch.c("gpsimd", lambda e: e.memset(o64[:], 1.0))
                    ch.c("gpsimd", lambda e: e.memset(o128[:], 1.0))
                    ch.c("gpsimd", lambda e: e.memset(o1x64[:], 1.0))
                    ch.c("gpsimd", lambda e: e.memset(ka[64:66, :], 1.0))
                    for h in range(16):
                        p, r = h // 2, h % 2
                        slope = SLOPES[h]
                        if r == 0:
                            ch.d("sync", lambda e, p=p: e.dma_start(out=qp[:], in_=q_d.ap()[p, :, :]))
                        ch.d("sync", lambda e, p=p, r=r: e.dma_start(out=ka[0:64, :], in_=k_d.ap()[p, r * 64:r * 64 + 64, :]))
                        ch.d("sync", lambda e, h=h: e.dma_start(out=vsb[:], in_=v_d.ap()[h, :, :]))
                        ch.c("gpsimd", lambda e: e.memset(khat[:], 0.0))
                        for kt in range(S // 512):
                            ch.c("scalar", lambda e, kt=kt: e.activation(out=sqk[:], in_=ka[0:64, kt * 512:(kt + 1) * 512], func=AF.Square))
                            ch.c("tensor", lambda e: e.matmul(spA[0:1, 0:512], o64[:], sqk[:], start=True, stop=True))
                            ch.c("vector", lambda e: e.tensor_copy(out=kn512[:], in_=spA[0:1, 0:512]))
                            ch.c("vector", lambda e: e.tensor_reduce(out=kmx[:], in_=kn512[:], axis=AX, op=Alu.max))
                            ch.c("vector", lambda e: e.tensor_tensor(out=khat[:], in0=khat[:], in1=kmx[:], op=Alu.max))
                        ch.c("scalar", lambda e: e.activation(out=khat[:], in_=khat[:], func=AF.Sqrt))
                        for tg in range(NQT):
                            ch.c("scalar", lambda e, r=r, tg=tg: e.activation(out=sqq[:], in_=qp[r * 64:r * 64 + 64, tg * 512:(tg + 1) * 512], func=AF.Square))
                            ch.c("tensor", lambda e: e.matmul(spA[0:1, 0:512], o64[:], sqq[:], start=True, stop=True))
                            ch.c("vector", lambda e, tg=tg: e.tensor_copy(out=qn2[:, tg * 512:(tg + 1) * 512], in_=spA[0:1, 0:512]))
                        ch.c("scalar", lambda e: e.activation(out=qn2[:], in_=qn2[:], func=AF.Sqrt))
                        ch.c("vector", lambda e: e.tensor_scalar(out=vqr[:], in0=qn2[:], scalar1=khat[0:1, 0:1], scalar2=-1.03, op0=Alu.mult, op1=Alu.mult))
                        ch.c("vector", lambda e, r=r: e.tensor_copy(out=qa[0:64, :], in_=qp[r * 64:r * 64 + 64, :]))
                        ch.c("vector", lambda e: e.tensor_copy(out=stgh[:], in_=vqr[:]))
                        ch.c("vector", lambda e: e.tensor_tensor(out=qn2[:], in0=vqr[:], in1=stgh[:], op=Alu.subtract))
                        ch.c("vector", lambda e: e.tensor_copy(out=stgl[:], in_=qn2[:]))
                        ch.d("sync", lambda e: e.dma_start(out=qa[64:65, :], in_=stgh[:]))
                        ch.d("sync", lambda e: e.dma_start(out=qa[65:66, :], in_=stgl[:]))
                        ch.bar("tensor"); ch.bar("vector"); ch.bar("scalar")
                        jbase = pcnt["a"]
                        pend_ctx = None
                        for kc in range(NKC):
                            kb = kc * 128
                            for qh in range(2):
                                jg = pcnt["a"]
                                jl = jg - jbase
                                buf = jl % 2
                                sp = spd[buf]
                                for qq in range(2):
                                    qt = qh * 2 + qq
                                    def mk_sc(jl=jl, jg=jg, sp=sp, kb=kb, qt=qt, qq=qq):
                                        def f(e):
                                            if qq == 0 and jl >= 2:
                                                e.wait_ge(ppv, jg - 1)
                                            ins = e.matmul(sp[:, qq * 512:(qq + 1) * 512], ka[:, kb:kb + 128],
                                                           qa[:, qt * 512:(qt + 1) * 512], start=True, stop=True)
                                            if qq == 1:
                                                ins.then_inc(ppa, 1)
                                        return f
                                    ch.raw("tensor", mk_sc())
                                pcnt["a"] += 1
                                for qq in range(2):
                                    qt = qh * 2 + qq
                                    dd = (qt * 512 - kb + 8064) // 128
                                    def mk_stt(jl=jl, jg=jg, sp=sp, buf=buf, dd=dd, qq=qq, slope=slope):
                                        def f(e):
                                            if qq == 0:
                                                e.wait_ge(ppa, jg + 1)
                                                if jl >= 2:
                                                    e.wait_ge(pps, jg - 1)
                                            ins = e.scalar_tensor_tensor(
                                                out=sx[:, buf * 1024 + qq * 512: buf * 1024 + (qq + 1) * 512],
                                                in0=dts[:, dd * 512:(dd + 1) * 512], scalar=slope,
                                                in1=sp[:, qq * 512:(qq + 1) * 512], op0=Alu.mult, op1=Alu.add)
                                            if qq == 1:
                                                ins.then_inc(ppv, 1)
                                        return f
                                    ch.raw("vector", mk_stt())
                                pcnt["v"] += 1
                                def mk_exp(jl=jl, jg=jg, buf=buf):
                                    def f(e):
                                        e.wait_ge(ppv, jg + 1)
                                        if jl >= 2:
                                            e.wait_ge(ppc, jg - 1)
                                        e.activation(out=phat[:, buf * 1024:(buf + 1) * 1024],
                                                     in_=sx[:, buf * 1024:(buf + 1) * 1024],
                                                     func=AF.Exp, bias=0.0, scale=0.125).then_inc(pps, 1)
                                    return f
                                ch.raw("scalar", mk_exp())
                                pcnt["s"] += 1
                                if pend_ctx is not None:
                                    ch.raw("tensor", pend_ctx)
                                    pcnt["c"] += 1
                                def mk_ctx(jg=jg, buf=buf, kc=kc, qh=qh):
                                    def f(e):
                                        e.wait_ge(pps, jg + 1)
                                        for qq in range(2):
                                            qt = qh * 2 + qq
                                            e.matmul(cps[0:64, qt * 512:(qt + 1) * 512], vsb[:, kc * 64:(kc + 1) * 64],
                                                     phat[:, buf * 1024 + qq * 512: buf * 1024 + (qq + 1) * 512],
                                                     start=(kc == 0), stop=(kc == NKC - 1))
                                            ins = e.matmul(cps[64:65, qt * 512:(qt + 1) * 512], o128[:],
                                                           phat[:, buf * 1024 + qq * 512: buf * 1024 + (qq + 1) * 512],
                                                           start=(kc == 0), stop=(kc == NKC - 1))
                                        ins.then_inc(ppc, 1)
                                    return f
                                pend_ctx = mk_ctx()
                        ch.raw("tensor", pend_ctx)
                        pcnt["c"] += 1
                        def mk_exit(ca=pcnt["a"], cv=pcnt["v"], cs=pcnt["s"], cc=pcnt["c"]):
                            def f(e):
                                e.wait_ge(ppa, ca); e.wait_ge(ppv, cv)
                                e.wait_ge(pps, cs); e.wait_ge(ppc, cc)
                                return e.nop()
                            return f
                        ch.c("tensor", mk_exit())
                        for qt in range(NQT):
                            ch.c("vector", lambda e, qt=qt: e.tensor_copy(out=lsb[:], in_=cps[64:65, qt * 512:(qt + 1) * 512]))
                            ch.c("vector", lambda e: e.reciprocal(out=rlf[:], in_=lsb[:]))
                            ch.c("vector", lambda e: e.tensor_copy(out=rhi[:], in_=rlf[:]))
                            ch.c("vector", lambda e: e.tensor_tensor(out=lsb[:], in0=rlf[:], in1=rhi[:], op=Alu.subtract))
                            ch.c("vector", lambda e: e.tensor_copy(out=rlo[:], in_=lsb[:]))
                            ch.c("tensor", lambda e: e.matmul(spA[0:64, 0:512], o1x64[:], rhi[:], start=True, stop=False))
                            ch.c("tensor", lambda e: e.matmul(spA[0:64, 0:512], o1x64[:], rlo[:], start=False, stop=True))
                            ch.c("vector", lambda e, qt=qt: e.tensor_copy(out=ctxf[:], in_=cps[0:64, qt * 512:(qt + 1) * 512]))
                            ch.c("vector", lambda e, qt=qt, p=p, r=r: e.tensor_tensor(
                                out=ctx[r * 64:r * 64 + 64, p * TPC + qt * 512: p * TPC + (qt + 1) * 512], in0=ctxf[:], in1=spA[0:64, 0:512], op=Alu.mult))
                with _FlushStack(ch, nc, sems) as st:
                    sb = lambda n, sh, dt=F32: st.enter_context(nc.sbuf_tensor(n, sh, dt))
                    outf = sb("s3_outf", [128, 512]); otm = sb("s3_otm", [128, 128])
                    tsf = sb("s3_tsf", [128, 128]); idnf = sb("s3_idnf", [128, 128])
                    pso = st.enter_context(nc.psum_tensor("s3_pso", [128, 512], F32))
                    ptf = st.enter_context(nc.psum_tensor("s3_ptf", [128, 128], F32))
                    ch.c("gpsimd", lambda e: e.memset(tsf[:], 1.0))
                    ch.c("gpsimd", lambda e: e.affine_select(out=idnf[:], in_=tsf[:], pattern=[[-1, 128]], compare_op=Alu.is_equal, fill=0.0, base=0, channel_multiplier=1))
                    for dmc in range(8):
                        for qt in range(NQT):
                            for c in range(8):
                                ch.c("tensor", lambda e, c=c, dmc=dmc, qt=qt: e.matmul(
                                    pso[:], wos[:, c * 1024 + dmc * 128: c * 1024 + dmc * 128 + 128],
                                    ctx[:, c * TPC + qt * 512: c * TPC + (qt + 1) * 512], start=(c == 0), stop=(c == 7)))
                            ch.c("vector", lambda e: e.tensor_copy(out=outf[:], in_=pso[:]))
                            for j in range(4):
                                ch.c("tensor", lambda e, j=j: e.transpose(out=ptf[:], in_=outf[:, j * 128:(j + 1) * 128], identity=idnf[:]))
                                ch.c("vector", lambda e: e.tensor_copy(out=otm[:], in_=ptf[:]))
                                ch.d("sync", lambda e, dmc=dmc, qt=qt, j=j: e.dma_start(
                                    out=at0_d.ap()[qt * 512 + j * 128: qt * 512 + (j + 1) * 128, dmc * 128:(dmc + 1) * 128], in_=otm[:]))

            ch.flush(nc, sems)
        if upto >= 3: _SEC3()
        def _SEC4():
            HTS = [(0, 512), (512, 512), (1024, 512), (1536, 512), (2048, 512), (2560, 256)]
            NTG = TPC // 128
            with _FlushStack(ch, nc, sems) as st:
                sb = lambda n, sh, dt=BF16: st.enter_context(nc.sbuf_tensor(n, sh, dt))
                xqT = sb("s4_xqT", [128, 8 * TPC]); wdT = sb("s4_wdT", [128, 22 * 1024])
                wgs = sb("s4_wgs", [128, 8 * 512]); wus = sb("s4_wus", [128, 8 * 512])
                ld = sb("s4_ld", [128, IH], F32); qb = sb("s4_qb", [128, IH])
                mq = sb("s4_mq", [128, IH // 16], F32)
                uq = sb("s4_uq", [128, IH], F32); gq = sb("s4_gq", [128, IH], F32)
                tq = sb("s4_tq", [128, IH], F32); sgq = sb("s4_sgq", [128, IH], F32)
                hqT = sb("s4_hqT", [128, 22 * 128]); ts0 = sb("s4_ts0", [128, 128]); idn = sb("s4_idn", [128, 128])
                tsf = sb("s4_tsf", [128, 128], F32); idnf = sb("s4_idnf", [128, 128], F32); otm = sb("s4_otm", [128, 128], F32)
                hs = sb("s4_hs", [128, 512], F32); hf = sb("s4_hf", [128, 512], F32)
                xt = sb("s4_xt", [128, D], F32); att = sb("s4_att", [128, D], F32)
                x1t = sb("s4_x1t", [128, D], F32); xn1t = sb("s4_xn1t", [128, D], F32)
                dxs = sb("s4_dxs", [128, D], F32); sqs = sb("s4_sqs", [128, D], F32)
                st1 = sb("s4_st1", [128, 1], F32); st2 = sb("s4_st2", [128, 1], F32)
                pg = st.enter_context(nc.psum_tensor("s4_pg", [128, 512], F32))
                pu = st.enter_context(nc.psum_tensor("s4_pu", [128, 512], F32))
                pd = st.enter_context(nc.psum_tensor("s4_pd", [128, 512], F32))
                pt = st.enter_context(nc.psum_tensor("s4_pt", [128, 128], BF16))
                ptf = st.enter_context(nc.psum_tensor("s4_ptf", [128, 128], F32))
                ch.c("gpsimd", lambda e: e.memset(ts0[:], 1.0))
                ch.c("gpsimd", lambda e: e.affine_select(out=idn[:], in_=ts0[:], pattern=[[-1, 128]], compare_op=Alu.is_equal, fill=0.0, base=0, channel_multiplier=1))
                ch.c("gpsimd", lambda e: e.memset(tsf[:], 1.0))
                ch.c("gpsimd", lambda e: e.affine_select(out=idnf[:], in_=tsf[:], pattern=[[-1, 128]], compare_op=Alu.is_equal, fill=0.0, base=0, channel_multiplier=1))
                for W_in, T_out, nt in ((wg_i, wgT_d, 22), (wu_i, wuT_d, 22)):
                    for ttile in range(nt):
                        ch.d("sync", lambda e, W_in=W_in, tt=ttile: e.dma_start(out=ld[:, 0:D], in_=W_in[tt * 128:(tt + 1) * 128, :]))
                        _quant_tile(ch, ld[:, 0:D], qb[:, 0:D], D, mq, uq, gq, tq, sgq)
                        for fc in range(8):
                            _transp(ch, qb[:, fc * 128:(fc + 1) * 128], ts0[:], pt, idn)
                            ch.d("sync", lambda e, T_out=T_out, fc=fc, tt=ttile: e.dma_start(
                                out=T_out.ap()[fc * 128:(fc + 1) * 128, tt * 128:(tt + 1) * 128], in_=ts0[:]))
                for ttile in range(8):
                    ch.d("sync", lambda e, tt=ttile: e.dma_start(out=ld[:], in_=wd_i[tt * 128:(tt + 1) * 128, :]))
                    _quant_tile(ch, ld[:], qb[:], IH, mq, uq, gq, tq, sgq)
                    for hc in range(22):
                        _transp(ch, qb[:, hc * 128:(hc + 1) * 128], ts0[:], pt, idn)
                        ch.d("sync", lambda e, hc=hc, tt=ttile: e.dma_start(
                            out=wdT_d.ap()[hc * 128:(hc + 1) * 128, tt * 128:(tt + 1) * 128], in_=ts0[:]))
                for tg in range(NTG):
                    ch.d("sync", lambda e, tg=tg: e.dma_start(out=ld[:, 0:D], in_=xn_d.ap()[tg * 128:(tg + 1) * 128, :]))
                    _quant_tile(ch, ld[:, 0:D], qb[:, 0:D], D, mq, uq, gq, tq, sgq)
                    for fc in range(8):
                        _transp(ch, qb[:, fc * 128:(fc + 1) * 128], xqT[:, fc * TPC + tg * 128: fc * TPC + (tg + 1) * 128], pt, idn)
                for hc in range(22):
                    ch.d("sync", lambda e, hc=hc: e.dma_start(out=wdT[:, hc * 1024:(hc + 1) * 1024], in_=wdT_d.ap()[hc * 128:(hc + 1) * 128, :]))
                for tg in range(NTG):
                    ch.d("sync", lambda e, tg=tg: e.dma_start(out=xt[:], in_=x_i[tg * 128:(tg + 1) * 128, :]))
                    ch.d("sync", lambda e, tg=tg: e.dma_start(out=att[:], in_=at0_d.ap()[tg * 128:(tg + 1) * 128, :]))
                    for hti, (h0, hw) in enumerate(HTS):
                        ch.d("sync", lambda e, h0=h0, hw=hw: e.dma_start(
                            out=wgs[:].rearrange("p (f i) -> p f i", i=512)[:, :, 0:hw],
                            in_=bass.AP(wgT_d, h0, [[IH, 128], [128 * IH, 8], [1, hw]])))
                        ch.d("sync", lambda e, h0=h0, hw=hw: e.dma_start(
                            out=wus[:].rearrange("p (f i) -> p f i", i=512)[:, :, 0:hw],
                            in_=bass.AP(wuT_d, h0, [[IH, 128], [128 * IH, 8], [1, hw]])))
                        for fc in range(8):
                            ch.c("tensor", lambda e, fc=fc, tg=tg, hw=hw: e.matmul(
                                pg[:, 0:hw], xqT[:, fc * TPC + tg * 128: fc * TPC + (tg + 1) * 128], wgs[:, fc * 512: fc * 512 + hw],
                                start=(fc == 0), stop=(fc == 7)))
                        for fc in range(8):
                            ch.c("tensor", lambda e, fc=fc, tg=tg, hw=hw: e.matmul(
                                pu[:, 0:hw], xqT[:, fc * TPC + tg * 128: fc * TPC + (tg + 1) * 128], wus[:, fc * 512: fc * 512 + hw],
                                start=(fc == 0), stop=(fc == 7)))
                        ch.c("scalar", lambda e, hw=hw: e.activation(out=hs[:, 0:hw], in_=pg[:, 0:hw], func=AF.Silu))
                        ch.c("vector", lambda e, hw=hw: e.tensor_tensor(out=hf[:, 0:hw], in0=hs[:, 0:hw], in1=pu[:, 0:hw], op=Alu.mult))
                        _quant_tile(ch, hf[:, 0:hw], qb[:, 0:hw], hw, mq, uq, gq, tq, sgq)
                        for sc in range(hw // 128):
                            _transp(ch, qb[:, sc * 128:(sc + 1) * 128],
                                    hqT[:, (h0 // 128 + sc) * 128:(h0 // 128 + sc + 1) * 128], pt, idn)
                    for dmt in range(2):
                        for hc in range(22):
                            ch.c("tensor", lambda e, hc=hc, dmt=dmt: e.matmul(
                                pd[:], hqT[:, hc * 128:(hc + 1) * 128], wdT[:, hc * 1024 + dmt * 512: hc * 1024 + dmt * 512 + 512],
                                start=(hc == 0), stop=(hc == 21)))
                        ch.c("vector", lambda e, dmt=dmt: e.tensor_tensor(out=x1t[:, dmt * 512:(dmt + 1) * 512], in0=att[:, dmt * 512:(dmt + 1) * 512], in1=pd[:], op=Alu.add))
                    ch.c("vector", lambda e: e.tensor_tensor(out=x1t[:], in0=x1t[:], in1=xt[:], op=Alu.add))
                    _ln_tile(ch, x1t[:], xn1t[:], dxs[:], sqs[:], st1[:], st2[:], D)
                    ch.d("sync", lambda e, tg=tg: e.dma_start(out=x1_d.ap()[tg * 128:(tg + 1) * 128, :], in_=x1t[:]))
                    for fc in range(8):
                        ch.c("tensor", lambda e, fc=fc: e.transpose(out=ptf[:], in_=x1t[:, fc * 128:(fc + 1) * 128], identity=idnf[:]))
                        ch.c("vector", lambda e: e.tensor_copy(out=otm[:], in_=ptf[:]))
                        ch.d("sync", lambda e, fc=fc, tg=tg: e.dma_start(out=x1T_d.ap()[fc * 128:(fc + 1) * 128, tg * 128:(tg + 1) * 128], in_=otm[:]))
                    for fc in range(8):
                        ch.c("tensor", lambda e, fc=fc: e.transpose(out=ptf[:], in_=xn1t[:, fc * 128:(fc + 1) * 128], identity=idnf[:]))
                        ch.c("vector", lambda e: e.tensor_copy(out=otm[:], in_=ptf[:]))
                        ch.d("sync", lambda e, fc=fc, tg=tg: e.dma_start(out=e3c.ap()[fc * 128:(fc + 1) * 128, tg * 128:(tg + 1) * 128], in_=otm[:]))
            ch.flush(nc, sems)
        if upto >= 4: _SEC4()
        if upto >= 4.5:
            ch.cc(lambda e: e.collective_compute("AllGather", Alu.bypass, replica_groups=GROUPS,
                                                 ins=[e3c.ap().opt()], outs=[G3.ap().opt()]), ccs3)

        def _SEC5():
            with _FlushStack(ch, nc, sems) as st:
                sb = lambda n, sh, dt=BF16: st.enter_context(nc.sbuf_tensor(n, sh, dt))
                xh = sb("s5_xh", [64, S]); cq = sb("s5_cq", [128, 4 * TPC]); ckv = sb("s5_ckv", [128, 8 * 512])
                kcs = sb("s5_kcs", [64, S]); krs = sb("s5_krs", [64, S])
                vstage = sb("s5_vst", [128, 4096]); vtmp = sb("s5_vtmp", [64, 512])
                qcs = sb("s5_qcs", [64, TPC]); qrs = sb("s5_qrs", [64, TPC])
                w1s = sb("s5_w1", [64, 512]); w2t = sb("s5_w2t", [128, 4 * 64]); w3s = sb("s5_w3", [64, 1024])
                w4s = sb("s5_w4", [128, 8 * 128]); w5s = sb("s5_w5", [64, 64]); w6s = sb("s5_w6", [64, 64])
                qnwc = sb("s5_qnw", [64, 1], F32); knwc = sb("s5_knw", [64, 1], F32)
                rwT = sb("s5_rwT", [128, 8 * 8], F32); xcs = sb("s5_xcs", [128, 8 * 128], F32)
                sqb = sb("s5_sqb", [64, 512])
                rr = sb("s5_rr", [1, 512], F32); rhi2 = sb("s5_rhi2", [1, 512]); rlo2 = sb("s5_rlo2", [1, 512])
                evf = sb("s5_evf", [64, 512], F32)
                o64 = sb("s5_o64", [64, 1]); o1x64 = sb("s5_o1x64", [1, 64]); o1x128 = sb("s5_o1x128", [1, 128])
                kn512 = sb("s5_kn512", [1, 512], F32); kmx = sb("s5_kmx", [1, 1], F32); khat = sb("s5_khat", [1, 1], F32)
                qn2 = sb("s5_qn2", [1, TPC], F32); vqr = sb("s5_vqr", [1, TPC], F32)
                rsf = sb("s5_rsf", [128, 8], F32); rmax = sb("s5_rmax", [128, 1], F32); rsum = sb("s5_rsum", [128, 1], F32)
                scT = sb("s5_scT", [8, S], F32); cmpb = sb("s5_cmpb", [8, S], F32)
                lo = sb("s5_lo", [8, 1], F32); hiv = sb("s5_hiv", [8, 1], F32); mid = sb("s5_mid", [8, 1], F32)
                cnt = sb("s5_cnt", [8, 1], F32); sel = sb("s5_sel", [8, 1], F32); nsel = sb("s5_nsel", [8, 1], F32)
                tv = sb("s5_tv", [8, 1], F32); need = sb("s5_need", [8, 1], F32)
                rlo_i = sb("s5_rloi", [8, 1], F32); rhi_i = sb("s5_rhii", [8, 1], F32)
                thrs = sb("s5_thrs", [8, 4], F32)
                idn_k = sb("s5_idn", [128, 128]); tsk = sb("s5_tsk", [128, 128])
                idnf_k = sb("s5_idnf", [128, 128], F32); tskf = sb("s5_tskf", [128, 128], F32)
                rsbf = sb("s5_rsbf", [128, 8], F32)
                hgs = sb("s5_hgs", [1, 1], mybir.dt.int32)
                ps = st.enter_context(nc.psum_tensor("s5_ps", [128, 512], F32))
                pkv = st.enter_context(nc.psum_tensor("s5_pkv", [128, 512], F32))
                nps = st.enter_context(nc.psum_tensor("s5_nps", [1, 512], F32))
                bps = st.enter_context(nc.psum_tensor("s5_bps", [64, 512], F32))
                pr = st.enter_context(nc.psum_tensor("s5_pr", [128, 8], F32))
                ptv = st.enter_context(nc.psum_tensor("s5_ptv", [128, 64], BF16))
                ptr = st.enter_context(nc.psum_tensor("s5_ptr", [8, 128], F32))
                ch.c("gpsimd", lambda e: e.memset(o64[:], 1.0))
                ch.c("gpsimd", lambda e: e.memset(o1x64[:], 1.0))
                ch.c("gpsimd", lambda e: e.memset(o1x128[:], 1.0))
                ch.c("gpsimd", lambda e: e.memset(tsk[:], 1.0))
                ch.c("gpsimd", lambda e: e.affine_select(out=idn_k[:], in_=tsk[:], pattern=[[-1, 128]], compare_op=Alu.is_equal, fill=0.0, base=0, channel_multiplier=1))
                ch.c("gpsimd", lambda e: e.memset(tskf[:], 1.0))
                ch.c("gpsimd", lambda e: e.affine_select(out=idnf_k[:], in_=tskf[:], pattern=[[-1, 128]], compare_op=Alu.is_equal, fill=0.0, base=0, channel_multiplier=1))
                ch.d("gpsimd", lambda e: e.dma_start(out=w1s[:], in_=wdq[:, :]))
                for lc in range(4):
                    ch.d("gpsimd", lambda e, lc=lc: e.dma_start(out=w2t[:, lc * 64:(lc + 1) * 64], in_=wuq[lc * 128:(lc + 1) * 128, :]))
                ch.d("gpsimd", lambda e: e.dma_start(out=w3s[:], in_=wdkv[:, :]))
                for lc in range(8):
                    ch.d("gpsimd", lambda e, lc=lc: e.dma_start(out=w4s[:, lc * 128:(lc + 1) * 128], in_=wukv[lc * 128:(lc + 1) * 128, :]))
                ch.d("gpsimd", lambda e: e.dma_start(out=w5s[:], in_=wqr[:, :]))
                ch.d("gpsimd", lambda e: e.dma_start(out=w6s[:], in_=wkr[:, :]))
                ch.d("sync", lambda e: e.dma_start(out=qnwc[:], in_=qnw[:, :]))
                ch.d("sync", lambda e: e.dma_start(out=knwc[:], in_=knw[:, :]))
                for fc in range(8):
                    ch.d("sync", lambda e, fc=fc: e.dma_start(out=rwT[:, fc * 8:(fc + 1) * 8], in_=rw[fc * 128:(fc + 1) * 128, :]))
                ch.d("sync", lambda e: e.dma_start(out=hgs[:], in_=hgi[:, :]))
                ch.n("gpsimd", lambda e: e.reg_load(grA, hgs[:1, :1]))
                ch.n("gpsimd", lambda e: e.reg_mul(grA, grA, 256 * TPC))
                for hi in range(4):
                    ch.n("gpsimd", lambda e, o=(hi * 64) * TPC: e.reg_mov(grB, o))
                    ch.n("gpsimd", lambda e: e.reg_add(grB, grB, grA))
                    ch.d("gpsimd", lambda e: e.dma_start(
                        out=xh[:].rearrange("p (r t) -> p r t", t=TPC),
                        in_=bass.AP(G3, grB, [[TPC, 64], [D * TPC, 4], [1, TPC]])))
                    ch.c("gpsimd", lambda e: e.memset(khat[:], 0.0))
                    for tt in range(S // 512):
                        t0 = tt * 512
                        for lc in range(8):
                            ch.c("tensor", lambda e, lc=lc, t0=t0: e.matmul(ps[:], w3s[:, lc * 128:(lc + 1) * 128], xh[:, t0:t0 + 512], start=True, stop=True))
                            ch.c("vector", lambda e, lc=lc: e.tensor_copy(out=ckv[:, lc * 512:(lc + 1) * 512], in_=ps[:]))
                        for lc in range(8):
                            ch.c("tensor", lambda e, lc=lc: e.matmul(pkv[:], w4s[:, lc * 128:(lc + 1) * 128], ckv[:, lc * 512:(lc + 1) * 512], start=(lc == 0), stop=(lc == 7)))
                        _rms_norm(ch, pkv[0:64, 0:512], kcs[0:64, t0:t0 + 512], 512, sqb, nps, rr, rhi2, rlo2, bps, evf, o64, o1x64, knwc[:])
                        ch.c("vector", lambda e: e.tensor_copy(out=vtmp[:], in_=pkv[64:128, 0:512]))
                        for j in range(4):
                            ch.c("tensor", lambda e, j=j: e.transpose(out=ptv[:], in_=vtmp[:, j * 128:(j + 1) * 128], identity=idn_k[0:64, 0:64]))
                            ch.c("vector", lambda e, j=j, tt=tt: e.tensor_copy(out=vstage[:, (tt * 4 + j) * 64:(tt * 4 + j + 1) * 64], in_=ptv[:, 0:64]))
                        ch.c("tensor", lambda e, t0=t0: e.matmul(ps[0:64, 0:512], w6s[:], xh[:, t0:t0 + 512], start=True, stop=True))
                        ch.c("vector", lambda e, t0=t0: e.tensor_copy(out=krs[:, t0:t0 + 512], in_=ps[0:64, 0:512]))
                        ch.c("scalar", lambda e, t0=t0: e.activation(out=sqb[:], in_=kcs[:, t0:t0 + 512], func=AF.Square))
                        ch.c("tensor", lambda e: e.matmul(nps[:], o64[:], sqb[:], start=True, stop=False))
                        ch.c("scalar", lambda e, t0=t0: e.activation(out=sqb[:], in_=krs[:, t0:t0 + 512], func=AF.Square))
                        ch.c("tensor", lambda e: e.matmul(nps[:], o64[:], sqb[:], start=False, stop=True))
                        ch.c("vector", lambda e: e.tensor_copy(out=kn512[:], in_=nps[:]))
                        ch.c("vector", lambda e: e.tensor_reduce(out=kmx[:], in_=kn512[:], axis=AX, op=Alu.max))
                        ch.c("vector", lambda e: e.tensor_tensor(out=khat[:], in0=khat[:], in1=kmx[:], op=Alu.max))
                    ch.c("scalar", lambda e: e.activation(out=khat[:], in_=khat[:], func=AF.Sqrt))
                    ch.d("sync", lambda e, hi=hi: e.dma_start(out=kc_d.ap()[hi, :, :], in_=kcs[:]))
                    ch.d("sync", lambda e, hi=hi: e.dma_start(out=kr_d.ap()[hi, :, :], in_=krs[:]))
                    ch.d("sync", lambda e, hi=hi: e.dma_start(out=v1_d.ap()[hi, :, :], in_=vstage[:]))
                    for tb in range(S // TPC):
                        b0 = tb * TPC
                        for lc in range(4):
                            for ttt in range(4):
                                ch.c("tensor", lambda e, lc=lc, ttt=ttt, b0=b0: e.matmul(
                                    ps[:], w1s[:, lc * 128:(lc + 1) * 128], xh[:, b0 + ttt * 512: b0 + (ttt + 1) * 512], start=True, stop=True))
                                ch.c("vector", lambda e, lc=lc, ttt=ttt: e.tensor_copy(out=cq[:, lc * TPC + ttt * 512: lc * TPC + (ttt + 1) * 512], in_=ps[:]))
                        for ttt in range(4):
                            for lc in range(4):
                                ch.c("tensor", lambda e, lc=lc, ttt=ttt: e.matmul(
                                    pkv[0:64, 0:512], w2t[:, lc * 64:(lc + 1) * 64], cq[:, lc * TPC + ttt * 512: lc * TPC + (ttt + 1) * 512],
                                    start=(lc == 0), stop=(lc == 3)))
                            _rms_norm(ch, pkv[0:64, 0:512], qcs[0:64, ttt * 512:(ttt + 1) * 512], 512, sqb, nps, rr, rhi2, rlo2, bps, evf, o64, o1x64, qnwc[:])
                            ch.c("tensor", lambda e, ttt=ttt: e.matmul(ps[0:64, 0:512], w5s[:], qcs[:, ttt * 512:(ttt + 1) * 512], start=True, stop=True))
                            ch.c("vector", lambda e, ttt=ttt: e.tensor_copy(out=qrs[:, ttt * 512:(ttt + 1) * 512], in_=ps[0:64, 0:512]))
                            ch.c("scalar", lambda e, ttt=ttt: e.activation(out=sqb[:], in_=qcs[:, ttt * 512:(ttt + 1) * 512], func=AF.Square))
                            ch.c("tensor", lambda e: e.matmul(nps[:], o64[:], sqb[:], start=True, stop=False))
                            ch.c("scalar", lambda e, ttt=ttt: e.activation(out=sqb[:], in_=qrs[:, ttt * 512:(ttt + 1) * 512], func=AF.Square))
                            ch.c("tensor", lambda e: e.matmul(nps[:], o64[:], sqb[:], start=False, stop=True))
                            ch.c("vector", lambda e, ttt=ttt: e.tensor_copy(out=qn2[:, ttt * 512:(ttt + 1) * 512], in_=nps[:]))
                        ch.c("scalar", lambda e: e.activation(out=qn2[:], in_=qn2[:], func=AF.Sqrt))
                        ch.c("vector", lambda e: e.tensor_scalar(out=vqr[:], in0=qn2[:], scalar1=khat[0:1, 0:1], scalar2=-1.03, op0=Alu.mult, op1=Alu.mult))
                        ch.d("sync", lambda e, hi=hi, b0=b0: e.dma_start(out=qc_d.ap()[hi, :, b0:b0 + TPC], in_=qcs[:]))
                        ch.d("sync", lambda e, hi=hi, b0=b0: e.dma_start(out=qr_d.ap()[hi, :, b0:b0 + TPC], in_=qrs[:]))
                        ch.d("sync", lambda e, hi=hi, b0=b0: e.dma_start(out=vq_d.ap()[hi, :, b0:b0 + TPC], in_=vqr[:]))
                for tc in range(S // 128):
                    rk, ti = tc // 16, tc % 16
                    for fc in range(8):
                        ch.d("sync", lambda e, fc=fc, rk=rk, ti=ti: e.dma_start(
                            out=xcs[:, fc * 128:(fc + 1) * 128], in_=G3.ap()[rk * D + fc * 128: rk * D + (fc + 1) * 128, ti * 128:(ti + 1) * 128]))
                    for fc in range(8):
                        ch.c("tensor", lambda e, fc=fc: e.matmul(pr[:], xcs[:, fc * 128:(fc + 1) * 128], rwT[:, fc * 8:(fc + 1) * 8], start=(fc == 0), stop=(fc == 7)))
                    ch.c("vector", lambda e: e.tensor_reduce(out=rmax[:], in_=pr[:], axis=AX, op=Alu.max, negate=True))
                    ch.c("scalar", lambda e: e.activation(out=rsf[:], in_=pr[:], func=AF.Exp, bias=rmax[:], scale=1.0))
                    ch.c("vector", lambda e: e.tensor_reduce(out=rsum[:], in_=rsf[:], axis=AX, op=Alu.add))
                    ch.c("vector", lambda e: e.reciprocal(out=rsum[:], in_=rsum[:]))
                    ch.c("vector", lambda e: e.tensor_scalar(out=rsf[:], in0=rsf[:], scalar1=rsum[:], scalar2=None, op0=Alu.mult))
                    ch.c("vector", lambda e: e.tensor_copy(out=rsbf[:, 0:8], in_=rsf[:]))
                    ch.c("tensor", lambda e: e.transpose(out=ptr[:], in_=rsbf[:, 0:8], identity=idnf_k[:]))
                    ch.c("vector", lambda e, tc=tc: e.tensor_copy(out=scT[:, tc * 128:(tc + 1) * 128], in_=ptr[:]))
                KCAP = 2049.0
                ch.c("gpsimd", lambda e: e.memset(lo[:], -1.0))
                ch.c("gpsimd", lambda e: e.memset(hiv[:], 2.0))
                for it in range(64):
                    ch.c("vector", lambda e: e.tensor_tensor(out=mid[:], in0=lo[:], in1=hiv[:], op=Alu.add))
                    ch.c("vector", lambda e: e.tensor_scalar(out=mid[:], in0=mid[:], scalar1=0.5, scalar2=None, op0=Alu.mult))
                    ch.c("vector", lambda e: e.tensor_scalar(out=cmpb[:], in0=scT[:], scalar1=mid[:], scalar2=None, op0=Alu.is_gt))
                    ch.c("vector", lambda e: e.tensor_reduce(out=cnt[:], in_=cmpb[:], axis=AX, op=Alu.add))
                    ch.c("vector", lambda e: e.tensor_scalar(out=sel[:], in0=cnt[:], scalar1=KCAP, scalar2=None, op0=Alu.is_ge))
                    ch.c("vector", lambda e: e.tensor_scalar(out=nsel[:], in0=sel[:], scalar1=-1.0, scalar2=1.0, op0=Alu.mult, op1=Alu.add))
                    ch.c("vector", lambda e: e.tensor_tensor(out=tv[:], in0=mid[:], in1=lo[:], op=Alu.subtract))
                    ch.c("vector", lambda e: e.tensor_tensor(out=tv[:], in0=sel[:], in1=tv[:], op=Alu.mult))
                    ch.c("vector", lambda e: e.tensor_tensor(out=lo[:], in0=lo[:], in1=tv[:], op=Alu.add))
                    ch.c("vector", lambda e: e.tensor_tensor(out=tv[:], in0=mid[:], in1=hiv[:], op=Alu.subtract))
                    ch.c("vector", lambda e: e.tensor_tensor(out=tv[:], in0=nsel[:], in1=tv[:], op=Alu.mult))
                    ch.c("vector", lambda e: e.tensor_tensor(out=hiv[:], in0=hiv[:], in1=tv[:], op=Alu.add))
                ch.c("vector", lambda e: e.tensor_scalar(out=cmpb[:], in0=scT[:], scalar1=hiv[:], scalar2=None, op0=Alu.is_gt))
                ch.c("vector", lambda e: e.tensor_reduce(out=cnt[:], in_=cmpb[:], axis=AX, op=Alu.add))
                ch.c("vector", lambda e: e.tensor_scalar(out=need[:], in0=cnt[:], scalar1=-1.0, scalar2=KCAP, op0=Alu.mult, op1=Alu.add))
                ch.d("sync", lambda e: e.dma_start(out=scT_d.ap()[:, :], in_=scT[:]))
                ch.c("vector", lambda e: e.tensor_scalar(out=scT[:], in0=scT[:], scalar1=hiv[:], scalar2=None, op0=Alu.is_equal))
                ch.c("gpsimd", lambda e: e.memset(rlo_i[:], -1.0))
                ch.c("gpsimd", lambda e: e.memset(rhi_i[:], 8191.0))
                for it in range(30):
                    ch.c("vector", lambda e: e.tensor_tensor(out=mid[:], in0=rlo_i[:], in1=rhi_i[:], op=Alu.add))
                    ch.c("vector", lambda e: e.tensor_scalar(out=mid[:], in0=mid[:], scalar1=0.5, scalar2=None, op0=Alu.mult))
                    ch.c("gpsimd", lambda e: e.iota(cmpb[:], [[1, S]], base=0, channel_multiplier=0, allow_small_or_imprecise_dtypes=True))
                    ch.c("vector", lambda e: e.tensor_scalar(out=cmpb[:], in0=cmpb[:], scalar1=mid[:], scalar2=None, op0=Alu.is_le))
                    ch.c("vector", lambda e: e.tensor_tensor(out=cmpb[:], in0=cmpb[:], in1=scT[:], op=Alu.mult))
                    ch.c("vector", lambda e: e.tensor_reduce(out=cnt[:], in_=cmpb[:], axis=AX, op=Alu.add))
                    ch.c("vector", lambda e: e.tensor_scalar(out=sel[:], in0=cnt[:], scalar1=need[:], scalar2=None, op0=Alu.is_ge))
                    ch.c("vector", lambda e: e.tensor_scalar(out=nsel[:], in0=sel[:], scalar1=-1.0, scalar2=1.0, op0=Alu.mult, op1=Alu.add))
                    ch.c("vector", lambda e: e.tensor_tensor(out=tv[:], in0=mid[:], in1=rhi_i[:], op=Alu.subtract))
                    ch.c("vector", lambda e: e.tensor_tensor(out=tv[:], in0=sel[:], in1=tv[:], op=Alu.mult))
                    ch.c("vector", lambda e: e.tensor_tensor(out=rhi_i[:], in0=rhi_i[:], in1=tv[:], op=Alu.add))
                    ch.c("vector", lambda e: e.tensor_tensor(out=tv[:], in0=mid[:], in1=rlo_i[:], op=Alu.subtract))
                    ch.c("vector", lambda e: e.tensor_tensor(out=tv[:], in0=nsel[:], in1=tv[:], op=Alu.mult))
                    ch.c("vector", lambda e: e.tensor_tensor(out=rlo_i[:], in0=rlo_i[:], in1=tv[:], op=Alu.add))
                ch.c("vector", lambda e: e.tensor_copy(out=thrs[:, 0:1], in_=lo[:]))
                ch.c("vector", lambda e: e.tensor_copy(out=thrs[:, 1:2], in_=hiv[:]))
                ch.c("vector", lambda e: e.tensor_copy(out=thrs[:, 2:3], in_=rhi_i[:]))
                ch.c("vector", lambda e: e.tensor_copy(out=thrs[:, 3:4], in_=need[:]))
                ch.d("sync", lambda e: e.dma_start(out=thr_d.ap()[:, :], in_=thrs[:]))

            ch.flush(nc, sems)
        if upto >= 5: _SEC5()
        def _SEC6():
            NQT6 = S // 512
            with _FlushStack(ch, nc, sems) as st:
                sb = lambda n, sh, dt=BF16: st.enter_context(nc.sbuf_tensor(n, sh, dt))
                kcs = sb("s6_kcs", [64, S]); kra = sb("s6_kra", [66, S])
                vsb = sb("s6_vsb", [128, 4096])
                qcs = sb("s6_qcs", [64, S]); qra = sb("s6_qra", [66, S])
                vqf = sb("s6_vqf", [1, S], F32)
                stgh = sb("s6_stgh", [1, S]); stgl = sb("s6_stgl", [1, S]); losq = sb("s6_losq", [1, S], F32)
                mts = sb("s6_mts", [128, 4 * 512], F32)
                phat = sb("s6_phat", [128, 512]); sx = sb("s6_sx", [128, 512], F32)
                wosb = sb("s6_wos", [64, 64])
                lsb = sb("s6_lsb", [1, 512], F32); rhi = sb("s6_rhi", [1, 512]); rlo = sb("s6_rlo", [1, 512]); rlf = sb("s6_rlf", [1, 512], F32)
                ctxf = sb("s6_ctxf", [64, 512], F32); ctxn = sb("s6_ctxn", [64, 512])
                o128 = sb("s6_o128", [128, 1]); o1x64 = sb("s6_o1x64", [1, 64])
                outf = sb("s6_outf", [64, 512], F32)
                sps = st.enter_context(nc.psum_tensor("s6_sps", [128, 512], F32))
                cps = st.enter_context(nc.psum_tensor("s6_cps", [66, 512], F32))
                bps = st.enter_context(nc.psum_tensor("s6_bps", [64, 512], F32))
                pso = st.enter_context(nc.psum_tensor("s6_pso", [64, 512], F32))
                ch.c("gpsimd", lambda e: e.memset(o128[:], 1.0))
                ch.c("gpsimd", lambda e: e.memset(o1x64[:], 1.0))
                ch.c("gpsimd", lambda e: e.memset(kra[64:66, :], 1.0))
                ch.d("gpsimd", lambda e: e.dma_start(out=wosb[:], in_=wol1[:, :]))
                for v in range(4):
                    ch.d("sync", lambda e, v=v: e.dma_start(out=mts[:, v * 512:(v + 1) * 512], in_=mtab[v, :, :]))
                for hi in range(4):
                    ch.d("sync", lambda e, hi=hi: e.dma_start(out=kcs[:], in_=kc_d.ap()[hi, :, :]))
                    ch.d("sync", lambda e, hi=hi: e.dma_start(out=kra[0:64, :], in_=kr_d.ap()[hi, :, :]))
                    ch.d("sync", lambda e, hi=hi: e.dma_start(out=vsb[:], in_=v1_d.ap()[hi, :, :]))
                    ch.d("sync", lambda e, hi=hi: e.dma_start(out=qcs[:], in_=qc_d.ap()[hi, :, :]))
                    ch.d("sync", lambda e, hi=hi: e.dma_start(out=qra[0:64, :], in_=qr_d.ap()[hi, :, :]))
                    ch.d("sync", lambda e, hi=hi: e.dma_start(out=vqf[:], in_=vq_d.ap()[hi, :, :]))
                    ch.c("vector", lambda e: e.tensor_copy(out=stgh[:], in_=vqf[:]))
                    ch.c("vector", lambda e: e.tensor_tensor(out=losq[:], in0=vqf[:], in1=stgh[:], op=Alu.subtract))
                    ch.c("vector", lambda e: e.tensor_copy(out=stgl[:], in_=losq[:]))
                    ch.d("sync", lambda e: e.dma_start(out=qra[64:65, :], in_=stgh[:]))
                    ch.d("sync", lambda e: e.dma_start(out=qra[65:66, :], in_=stgl[:]))
                    for qt in range(NQT6):
                        q0 = qt * 512
                        nkc = 4 * qt + 4
                        for kc in range(nkc):
                            kb = kc * 128
                            ch.c("tensor", lambda e, kb=kb, q0=q0: e.matmul(sps[:], kcs[:, kb:kb + 128], qcs[:, q0:q0 + 512], start=True, stop=False))
                            ch.c("tensor", lambda e, kb=kb, q0=q0: e.matmul(sps[:], kra[:, kb:kb + 128], qra[:, q0:q0 + 512], start=False, stop=True))
                            if kc >= 4 * qt:
                                v = kc - 4 * qt
                                ch.c("vector", lambda e, v=v: e.tensor_tensor(out=sx[:], in0=mts[:, v * 512:(v + 1) * 512], in1=sps[:], op=Alu.add))
                                ch.c("scalar", lambda e: e.activation(out=phat[:], in_=sx[:], func=AF.Exp, bias=0.0, scale=0.125))
                            else:
                                ch.c("scalar", lambda e: e.activation(out=phat[:], in_=sps[:], func=AF.Exp, bias=0.0, scale=0.125))
                            ch.c("tensor", lambda e, kc=kc, nkc=nkc: e.matmul(cps[0:64, :], vsb[:, kc * 64:(kc + 1) * 64], phat[:], start=(kc == 0), stop=(kc == nkc - 1)))
                            ch.c("tensor", lambda e, kc=kc, nkc=nkc: e.matmul(cps[64:65, :], o128[:], phat[:], start=(kc == 0), stop=(kc == nkc - 1)))
                        ch.c("vector", lambda e: e.tensor_copy(out=lsb[:], in_=cps[64:65, :]))
                        ch.c("vector", lambda e: e.reciprocal(out=rlf[:], in_=lsb[:]))
                        ch.c("vector", lambda e: e.tensor_copy(out=rhi[:], in_=rlf[:]))
                        ch.c("vector", lambda e: e.tensor_tensor(out=lsb[:], in0=rlf[:], in1=rhi[:], op=Alu.subtract))
                        ch.c("vector", lambda e: e.tensor_copy(out=rlo[:], in_=lsb[:]))
                        ch.c("tensor", lambda e: e.matmul(bps[:], o1x64[:], rhi[:], start=True, stop=False))
                        ch.c("tensor", lambda e: e.matmul(bps[:], o1x64[:], rlo[:], start=False, stop=True))
                        ch.c("vector", lambda e: e.tensor_copy(out=ctxf[:], in_=cps[0:64, :]))
                        ch.c("vector", lambda e: e.tensor_tensor(out=ctxn[:], in0=ctxf[:], in1=bps[:], op=Alu.mult))
                        ch.c("tensor", lambda e: e.matmul(pso[:], wosb[:], ctxn[:], start=True, stop=True))
                        ch.c("vector", lambda e: e.tensor_copy(out=outf[:], in_=pso[:]))
                        ch.d("sync", lambda e, hi=hi, q0=q0: e.dma_start(out=e5c.ap()[hi * 64:hi * 64 + 64, q0:q0 + 512], in_=outf[:]))
            ch.flush(nc, sems)
        if upto >= 6: _SEC6()
        if upto >= 6.5:
            ch.cc(lambda e: e.collective_compute("AllGather", Alu.bypass, replica_groups=GROUPS,
                                                 ins=[e5c.ap().opt()], outs=[G5.ap().opt()]), ccs5)

        def _SEC7():
            with _FlushStack(ch, nc, sems) as st:
                sb = lambda n, sh, dt=BF16: st.enter_context(nc.sbuf_tensor(n, sh, dt))
                xb = sb("s7_xb", [128, 8 * TPC])
                w1sb = sb("s7_w1", [128, 8 * 128]); w2sb = sb("s7_w2", [128, 16 * 128])
                b1s = sb("s7_b1", [128, 32], F32); b2s = sb("s7_b2", [128, 8], F32)
                gsb = sb("s7_g", [8, TPC], F32); grow = sb("s7_grow", [1, TPC], F32)
                ghi = sb("s7_ghi", [1, TPC]); glo = sb("s7_glo", [1, TPC]); gtm = sb("s7_gtm", [1, TPC], F32)
                thrs = sb("s7_thrs", [8, 4], F32); o1x128m = sb("s7_o1x128", [1, 128])
                t1 = sb("s7_t1", [128, 512], F32); xs1 = sb("s7_xs1", [128, 512], F32)
                locs7 = sb("s7_loc", [1, 1], mybir.dt.int32)
                ph = st.enter_context(nc.psum_tensor("s7_ph", [128, 512], F32))
                po = st.enter_context(nc.psum_tensor("s7_po", [128, 1024], F32))
                pb = st.enter_context(nc.psum_tensor("s7_pb", [128, 512], F32))
                TH = TPC // 2
                ch.d("sync", lambda e: e.dma_start(out=thrs[:], in_=thr_d.ap()[:, :]))
                for fc in range(8):
                    ch.d("gpsimd", lambda e, fc=fc: e.dma_start(out=xb[:, fc * TPC:(fc + 1) * TPC], in_=e3c.ap()[fc * 128:(fc + 1) * 128, :]))
                ch.c("gpsimd", lambda e: e.memset(o1x128m[:], 1.0))
                ch.d("sync", lambda e: e.dma_start(out=locs7[:], in_=loci[:, :]))
                ch.n("gpsimd", lambda e: e.reg_load(grA, locs7[:1, :1]))
                with _FlushStack(ch, nc, sems) as st2:
                    rss = st2.enter_context(nc.sbuf_tensor("s7_rss", [8, TPC], F32))
                    idxs = st2.enter_context(nc.sbuf_tensor("s7_idxs", [8, TPC], F32))
                    m1 = st2.enter_context(nc.sbuf_tensor("s7_m1", [8, TPC], F32))
                    ch.d("gpsimd", lambda e: e.dma_start(out=rss[:], in_=bass.AP(scT_d, grA, [[S, 8], [1, TPC]])))
                    ch.d("sync", lambda e: e.dma_start(out=idxs[:], in_=idxl[:, :]))
                    ch.c("vector", lambda e: e.tensor_scalar(out=idxs[:], in0=idxs[:], scalar1=thrs[:, 2:3], scalar2=None, op0=Alu.is_le))
                    ch.c("vector", lambda e: e.tensor_scalar(out=m1[:], in0=rss[:], scalar1=thrs[:, 1:2], scalar2=None, op0=Alu.is_equal))
                    ch.c("vector", lambda e: e.tensor_tensor(out=m1[:], in0=m1[:], in1=idxs[:], op=Alu.mult))
                    ch.c("vector", lambda e: e.tensor_scalar(out=gsb[:], in0=rss[:], scalar1=thrs[:, 1:2], scalar2=None, op0=Alu.is_gt))
                    ch.c("vector", lambda e: e.tensor_tensor(out=gsb[:], in0=gsb[:], in1=m1[:], op=Alu.add))
                    ch.c("vector", lambda e: e.tensor_tensor(out=gsb[:], in0=gsb[:], in1=rss[:], op=Alu.mult))
                hsb = sb("s7_hsb", [128, 32 * 1024])
                x2a = sb("s7_x2a", [128, 8 * TPC], F32)
                ch.c("gpsimd", lambda e: e.memset(x2a[:], 0.0))
                for ex in (range(8) if upto >= 7.5 else ()):
                    ch.d("sync", lambda e, ex=ex: e.dma_start(out=b1s[:], in_=b1T[ex, :, :]))
                    ch.d("sync", lambda e, ex=ex: e.dma_start(out=b2s[:], in_=b2T[ex, :, :]))
                    ch.d("sync", lambda e, ex=ex: e.dma_start(out=grow[:], in_=gsb[ex:ex + 1, :]))
                    ch.c("vector", lambda e: e.tensor_copy(out=ghi[:], in_=grow[:]))
                    ch.c("vector", lambda e: e.tensor_tensor(out=gtm[:], in0=grow[:], in1=ghi[:], op=Alu.subtract))
                    ch.c("vector", lambda e: e.tensor_copy(out=glo[:], in_=gtm[:]))
                    for th in range(2):
                        tb = th * TH
                        for hc in range(32):
                            ch.d("gpsimd", lambda e, ex=ex, hc=hc: e.dma_start(
                                out=w1sb[:].rearrange("p (f i) -> p f i", i=128),
                                in_=bass.AP(w1.tensor, ex * D * 4096 + hc * 128, [[4096, 128], [128 * 4096, 8], [1, 128]])))
                            for t5 in range(2):
                                for fc in range(8):
                                    ch.c("tensor", lambda e, fc=fc, tb=tb, t5=t5: e.matmul(
                                        ph[:], w1sb[:, fc * 128:(fc + 1) * 128], xb[:, fc * TPC + tb + t5 * 512: fc * TPC + tb + t5 * 512 + 512],
                                        start=(fc == 0), stop=(fc == 7)))
                                ch.c("scalar", lambda e, hc=hc, t5=t5: e.activation(
                                    out=hsb[:, hc * 1024 + t5 * 512: hc * 1024 + (t5 + 1) * 512], in_=ph[:], func=AF.Gelu,
                                    bias=b1s[:, hc:hc + 1], scale=1.0))
                        for dmc in range(8):
                            for hh in range(2):
                                for g4 in range(2):
                                    ch.d("gpsimd", lambda e, ex=ex, dmc=dmc, hh=hh, g4=g4: e.dma_start(
                                        out=w2sb[:].rearrange("p (f i) -> p f i", i=128)[:, g4 * 8:(g4 + 1) * 8, :],
                                        in_=bass.AP(w2.tensor, ex * 4096 * D + dmc * 128 + (hh * 16 + g4 * 8) * 128 * D, [[D, 128], [128 * D, 8], [1, 128]])))
                                for t5 in range(2):
                                    for hc in range(16):
                                        ch.c("tensor", lambda e, hc=hc, hh=hh, t5=t5: e.matmul(
                                            po[:, t5 * 512:(t5 + 1) * 512], w2sb[:, hc * 128:(hc + 1) * 128],
                                            hsb[:, (hh * 16 + hc) * 1024 + t5 * 512: (hh * 16 + hc) * 1024 + (t5 + 1) * 512],
                                            start=(hh == 0 and hc == 0), stop=(hh == 1 and hc == 15)))
                            for t5 in range(2):
                                ch.c("tensor", lambda e, tb=tb, t5=t5: e.matmul(pb[:], o1x128m[:], ghi[0:1, tb + t5 * 512: tb + (t5 + 1) * 512], start=True, stop=False))
                                ch.c("tensor", lambda e, tb=tb, t5=t5: e.matmul(pb[:], o1x128m[:], glo[0:1, tb + t5 * 512: tb + (t5 + 1) * 512], start=False, stop=True))
                                ch.c("vector", lambda e, dmc=dmc, t5=t5: e.tensor_scalar(out=t1[:], in0=po[:, t5 * 512:(t5 + 1) * 512], scalar1=b2s[:, dmc:dmc + 1], scalar2=None, op0=Alu.add))
                                ch.c("vector", lambda e: e.tensor_tensor(out=t1[:], in0=t1[:], in1=pb[:], op=Alu.mult))
                                ch.c("vector", lambda e, dmc=dmc, tb=tb, t5=t5: e.tensor_tensor(
                                    out=x2a[:, dmc * TPC + tb + t5 * 512: dmc * TPC + tb + (t5 + 1) * 512],
                                    in0=x2a[:, dmc * TPC + tb + t5 * 512: dmc * TPC + tb + (t5 + 1) * 512], in1=t1[:], op=Alu.add))
                ch.n("gpsimd", lambda e: e.reg_load(grB, locs7[:1, :1]))
                at1b = sb("s7_at1b", [128, TPC])
                for fc in (range(8) if upto >= 7.8 else ()):
                    off_static = (fc // 2) * 256 * S + (fc % 2) * 128 * S
                    ch.n("gpsimd", lambda e, o=off_static: e.reg_mov(grA, o))
                    ch.n("gpsimd", lambda e: e.reg_add(grA, grA, grB))
                    ch.d("gpsimd", lambda e: e.dma_start(out=at1b[:], in_=bass.AP(G5, grA, [[S, 128], [1, TPC]])))
                    for t5 in range(4):
                        ch.d("sync", lambda e, fc=fc, t5=t5: e.dma_start(out=xs1[:], in_=x1T_d.ap()[fc * 128:(fc + 1) * 128, t5 * 512:(t5 + 1) * 512]))
                        ch.c("vector", lambda e, t5=t5: e.tensor_tensor(out=xs1[:], in0=xs1[:], in1=at1b[:, t5 * 512:(t5 + 1) * 512], op=Alu.add))
                        ch.c("vector", lambda e, fc=fc, t5=t5: e.tensor_tensor(
                            out=xs1[:], in0=xs1[:], in1=x2a[:, fc * TPC + t5 * 512: fc * TPC + (t5 + 1) * 512], op=Alu.add))
                        ch.d("sync", lambda e, fc=fc, t5=t5: e.dma_start(out=x2T_o[fc * 128:(fc + 1) * 128, t5 * 512:(t5 + 1) * 512], in_=xs1[:]))
            ch.flush(nc, sems)
        if upto >= 7: _SEC7()
        ch.final(nc, sems)
    return nc



_built = {}

def _get(name, fn):
    if name not in _built:
        _built[name] = fn()
    return _built[name]


def _run(nc, in_maps):
    res = run_bass_kernel_spmd(nc, in_maps, core_ids=list(range(NC)))
    return res.results


def _f(a):
    return np.ascontiguousarray(a, dtype=np.float32)


def _mk_fused_maps(ins):
    x = _f(ins["L0_attn_x"]).reshape(B * S, D)
    gam_b = _f(np.broadcast_to(np.asarray(ins["L0_attn_ln_gamma"], np.float32), (128, D)))
    bet_b = _f(np.broadcast_to(np.asarray(ins["L0_attn_ln_beta"], np.float32), (128, D)))
    mtab = np.zeros((4, 128, 512), np.float32)
    pcol = np.arange(128)[:, None]
    ccol = np.arange(512)[None, :]
    for v in range(4):
        mtab[v] = np.where(ccol < 128 * v + pcol, -1e9, 0.0)
    com = {
        "gam": gam_b, "bet": bet_b,
        "Wq": _f(ins["L0_attn_Wq"]), "Wk": _f(ins["L0_attn_Wk"]),
        "Wv": _f(ins["L0_attn_Wv"]), "Wo": _f(ins["L0_attn_Wo"]),
        "wg_i": _f(ins["L0_ffn_w_gate"]), "wu_i": _f(ins["L0_ffn_w_up"]), "wd_i": _f(ins["L0_ffn_w_down"]),
        "wdq": _f(np.asarray(ins["L1_attn_W_DQ"]).T), "wuq": _f(np.asarray(ins["L1_attn_W_UQ"]).T),
        "wdkv": _f(np.asarray(ins["L1_attn_W_DKV"]).T), "wukv": _f(np.asarray(ins["L1_attn_W_UKV"]).T),
        "wqr": _f(np.asarray(ins["L1_attn_W_QR"]).T), "wkr": _f(np.asarray(ins["L1_attn_W_KR"]).T),
        "qnw": _f(np.asarray(ins["L1_attn_q_norm_w"]).reshape(64, 1)),
        "knw": _f(np.asarray(ins["L1_attn_k_norm_w"]).reshape(64, 1)),
        "rw": _f(np.asarray(ins["L1_ffn_router_w"]).T), "mtab": mtab,
        "wol1": _f(np.asarray(ins["L1_attn_W_O"]).T),
        "w1": _f(ins["L1_ffn_w1"]),
        "b1T": _f(np.asarray(ins["L1_ffn_b1"]).reshape(8, 32, 128).transpose(0, 2, 1)),
        "w2": _f(ins["L1_ffn_w2"]),
        "b2T": _f(np.asarray(ins["L1_ffn_b2"]).reshape(8, 8, 128).transpose(0, 2, 1)),
    }
    maps = []
    for c in range(NC):
        hg, loc0 = c % CPB, (c % CPB) * TPC
        dlt = np.arange(76) * 128 - 8064 + loc0
        dtab = (-8.0 * np.abs(dlt[:, None, None] + np.arange(512)[None, None, :]
                              - np.arange(128)[None, :, None])).astype(BF)
        m = dict(com)
        m["x_i"] = np.ascontiguousarray(x[c * TPC:(c + 1) * TPC])
        m["dtab"] = dtab
        m["idxl"] = np.ascontiguousarray(np.broadcast_to(np.arange(loc0, loc0 + TPC, dtype=np.float32), (8, TPC)))
        m["hgi"] = np.array([[hg]], dtype=np.int32)
        m["loci"] = np.array([[loc0]], dtype=np.int32)
        maps.append(m)
    return maps


def kernel(**ins):
    maps = _mk_fused_maps(ins)
    r = _run(_get("fused", build_fused), maps)
    x2 = np.concatenate([np.asarray(r[c]["x2T_o"]).T for c in range(NC)], axis=0)
    return x2.reshape(B, S, D)


def _kernel_multilaunch(**ins):

    x = _f(ins["L0_attn_x"]).reshape(B * S, D)
    gam_b = _f(np.broadcast_to(np.asarray(ins["L0_attn_ln_gamma"], np.float32), (128, D)))
    bet_b = _f(np.broadcast_to(np.asarray(ins["L0_attn_ln_beta"], np.float32), (128, D)))

    r1 = _run(_get("ln0", build_ln0),
              [{"x": x[c * TPC:(c + 1) * TPC], "gam": gam_b, "bet": bet_b} for c in range(NC)])
    xn = np.concatenate([r["xn"] for r in r1], axis=0)
    xnA = np.concatenate([r["xnA"] for r in r1], axis=0)

    xnAT = [np.ascontiguousarray(xnA[b * S:(b + 1) * S].T) for b in range(B)]
    maps = []
    for c in range(NC):
        b, loc0 = c // CPB, (c % CPB) * TPC
        maps.append({"xnAT": xnAT[b], "xnATl": np.ascontiguousarray(xnAT[b][:, loc0:loc0 + TPC]),
                     "Wq": _f(ins["L0_attn_Wq"]), "Wk": _f(ins["L0_attn_Wk"]), "Wv": _f(ins["L0_attn_Wv"])})
    r2 = _run(_get("l0qkv", build_l0qkv), maps)

    maps = []
    for c in range(NC):
        loc0 = (c % CPB) * TPC
        dlt = np.arange(76) * 128 - 8064 + loc0
        dtab = (-8.0 * np.abs(dlt[:, None, None] + np.arange(512)[None, None, :]
                              - np.arange(128)[None, :, None])).astype(BF)
        maps.append({"q_in": np.asarray(r2[c]["q_out"]), "k_in": np.asarray(r2[c]["k_out"]),
                     "v_in": np.asarray(r2[c]["v_out"]), "Wo": _f(ins["L0_attn_Wo"]), "dtab": dtab})
    r3 = _run(_get("l0attn", build_l0attn), maps)
    attn0 = np.concatenate([np.asarray(r3[c]["attn0T"]).T for c in range(NC)], axis=0)

    maps = []
    for c in range(NC):
        sl = slice(c * TPC, (c + 1) * TPC)
        maps.append({"xn_i": np.ascontiguousarray(xn[sl]), "x_i": np.ascontiguousarray(x[sl]),
                     "at_i": np.ascontiguousarray(attn0[sl]),
                     "wg_i": _f(ins["L0_ffn_w_gate"]), "wu_i": _f(ins["L0_ffn_w_up"]),
                     "wd_i": _f(ins["L0_ffn_w_down"])})
    r4 = _run(_get("l0ffn", build_l0ffn), maps)
    x1 = np.concatenate([r["x1_o"] for r in r4], axis=0)
    xn1 = np.concatenate([r["xn1_o"] for r in r4], axis=0)

    xn1T = [np.ascontiguousarray(xn1[b * S:(b + 1) * S].T) for b in range(B)]
    maps = []
    for c in range(NC):
        b, hg = c // CPB, c % CPB
        maps.append({"xn1T": np.ascontiguousarray(np.roll(xn1T[b], -hg * 256, axis=0)),
                     "wdq": _f(ins["L1_attn_W_DQ"].T), "wuq": _f(ins["L1_attn_W_UQ"].T),
                     "wdkv": _f(ins["L1_attn_W_DKV"].T), "wukv": _f(ins["L1_attn_W_UKV"].T),
                     "wqr": _f(ins["L1_attn_W_QR"].T), "wkr": _f(ins["L1_attn_W_KR"].T),
                     "qnw": _f(np.asarray(ins["L1_attn_q_norm_w"]).reshape(64, 1)),
                     "knw": _f(np.asarray(ins["L1_attn_k_norm_w"]).reshape(64, 1)),
                     "rw": _f(ins["L1_ffn_router_w"].T)})
    r5 = _run(_get("l1kqv", build_l1kqv), maps)

    mtab = np.zeros((4, 128, 512), np.float32)
    pcol = np.arange(128)[:, None]
    ccol = np.arange(512)[None, :]
    for v in range(4):
        mtab[v] = np.where(ccol < 128 * v + pcol, -1e9, 0.0)
    maps = []
    for c in range(NC):
        maps.append({"qc_i": np.asarray(r5[c]["qc_o"]), "qr_i": np.asarray(r5[c]["qr_o"]),
                     "vq_i": np.asarray(r5[c]["vq_o"]), "kc_i": np.asarray(r5[c]["kc_o"]),
                     "kr_i": np.asarray(r5[c]["kr_o"]), "v_i": np.asarray(r5[c]["v_o"]),
                     "wo": _f(ins["L1_attn_W_O"].T), "mtab": mtab})
    r6 = _run(_get("l1attn", build_l1attn), maps)
    at1T = [np.zeros((D, S), np.float32) for _ in range(B)]
    for c in range(NC):
        b, hg = c // CPB, c % CPB
        at1T[b][hg * 256:(hg + 1) * 256, :] = np.asarray(r6[c]["at1_o"]).reshape(256, S)

    x1T = [np.ascontiguousarray(x1[b * S:(b + 1) * S].T) for b in range(B)]
    maps = []
    for c in range(NC):
        b, loc0 = c // CPB, (c % CPB) * TPC
        sl = slice(loc0, loc0 + TPC)
        rsc = np.asarray(r5[b * CPB]["rsc_o"])
        maps.append({"xn1T": np.ascontiguousarray(xn1T[b][:, sl]),
                     "x1T": np.ascontiguousarray(x1T[b][:, sl]),
                     "at1T": np.ascontiguousarray(at1T[b][:, sl]),
                     "rscT": np.ascontiguousarray(rsc.T[:, sl]),
                     "thr": np.asarray(r5[b * CPB]["thr_o"]),
                     "idxl": np.ascontiguousarray(np.broadcast_to(
                         np.arange(loc0, loc0 + TPC, dtype=np.float32), (8, TPC))),
                     "w1": _f(ins["L1_ffn_w1"]),
                     "b1T": _f(np.asarray(ins["L1_ffn_b1"]).reshape(8, 32, 128).transpose(0, 2, 1)),
                     "w2": _f(ins["L1_ffn_w2"]),
                     "b2T": _f(np.asarray(ins["L1_ffn_b2"]).reshape(8, 8, 128).transpose(0, 2, 1))})
    r7 = _run(_get("l1moe", build_l1moe), maps)
    x2 = np.concatenate([np.asarray(r7[c]["x2T_o"]).T for c in range(NC)], axis=0)
    return x2.reshape(B, S, D)


# version 09 (final)
# speedup vs baseline: 1.8388x; 1.0468x over previous
import numpy as np
import ml_dtypes
from contextlib import ExitStack
import concourse.bass as bass
import concourse.mybir as mybir
from concourse.bass_utils import run_bass_kernel_spmd
from concourse.alu_op_type import AluOpType as Alu

F32 = mybir.dt.float32
BF16 = mybir.dt.bfloat16
AF = mybir.ActivationFunctionType
AX = mybir.AxisListType.X
BF = ml_dtypes.bfloat16

NC = 8
B, S, D = 2, 8192, 1024
TPC = 2048
CPB = 4



class Ch:
    def __init__(self):
        self.plan = []

    def c(self, eng, fn):
        self.plan.append((eng, 1, fn))

    def d(self, eng, fn):
        self.plan.append((eng, 16, fn))

    SEG = 24000

    def n(self, eng, fn):
        self.plan.append((eng, 0, fn))

    def raw(self, eng, fn):
        self.plan.append((eng, 0, fn, True))

    def bar(self, eng):
        self.plan.append((eng, 0, lambda e: None))

    def cc(self, fn, ccsem):
        self.plan.append(("gpsimd", 0, lambda e, fn=fn, s=ccsem: fn(e).then_inc(s)))
        self.plan.append(("gpsimd", 1, lambda e, s=ccsem: (e.wait_ge(s, 1), e.nop())[1]))

    def flush(self, nc, sems):
        if not hasattr(self, "_emitted"):
            self._emitted = 0
            self._seg = 0
            self._v = 0
            self._prev = (None, 0)
        pend = self.plan[self._emitted:]
        self._emitted = len(self.plan)
        if not pend:
            return
        segs = []
        if not hasattr(self, "_preng"):
            self._preng = None
        for ent in pend:
            eng, inc, fn = ent[0], ent[1], ent[2]
            nowait = len(ent) > 3 and ent[3]
            if self._v + inc > Ch.SEG:
                self._seg += 1
                self._v = 0
            skip = nowait or (self._preng is not None and self._preng[0] == eng
                              and not self._preng[1]
                              and eng in ("tensor", "scalar"))
            if skip:
                segs.append((self._seg, None, 0))
            else:
                segs.append((self._seg, self._prev[0], self._prev[1]))
            self._v += inc
            self._prev = (self._seg, self._v)
            if not nowait:
                self._preng = (eng, inc >= 16)
        assert self._seg < len(sems), "need more chain semaphores"
        with nc.Block() as blk:
            def emit_for(name):
                def f(eng):
                    for ent, (sg, wsg, wv) in zip(pend, segs):
                        en, inc, fn = ent[0], ent[1], ent[2]
                        if en != name:
                            continue
                        if wsg is not None and wv:
                            eng.wait_ge(sems[wsg], wv)
                        ins = fn(eng)
                        if inc:
                            ins.then_inc(sems[sg], inc)
                return f
            blk.sync(emit_for("sync"))
            blk.gpsimd(emit_for("gpsimd"))
            blk.vector(emit_for("vector"))
            blk.scalar(emit_for("scalar"))
            blk.tensor(emit_for("tensor"))

    def final(self, nc, sems):
        self.flush(nc, sems)
        fs, fv = self._prev
        with nc.Block() as blk:
            def f(eng):
                eng.wait_ge(sems[fs], fv)
            blk.sync(f)

    def emit(self, nc, block, sems):
        segs = []
        seg, v = 0, 0
        prev = (None, 0)
        for eng, inc, fn in self.plan:
            if v + inc > Ch.SEG:
                seg, v = seg + 1, 0
            segs.append((seg, prev[0], prev[1]))
            v += inc
            prev = (seg, v)
        nseg = seg + 1
        assert nseg <= len(sems), f"need {nseg} semaphores"
        final_seg, final_val = prev

        def emit_for(name):
            def f(eng):
                for (en, inc, fn), (sg, wsg, wv) in zip(self.plan, segs):
                    if en != name:
                        continue
                    if wsg is not None and wv:
                        eng.wait_ge(sems[wsg], wv)
                    ins = fn(eng)
                    if inc:
                        ins.then_inc(sems[sg], inc)
                if name == "sync":
                    eng.wait_ge(sems[final_seg], final_val)
            return f

        block.sync(emit_for("sync"))
        block.gpsimd(emit_for("gpsimd"))
        block.vector(emit_for("vector"))
        block.scalar(emit_for("scalar"))
        block.tensor(emit_for("tensor"))


class _FlushStack(ExitStack):
    def __init__(self, ch, nc, sems):
        super().__init__()
        self._c = (ch, nc, sems)

    def __exit__(self, *a):
        ch, nc, sems = self._c
        ch.flush(nc, sems)
        return super().__exit__(*a)


def _ln_tile(ch, x_ap, out_ap, tmp_dx, tmp_sq, st1, st2, Dn, eps=1e-5):
    ch.c("vector", lambda e, a=x_ap, o=st1: e.tensor_reduce(out=o, in_=a, axis=AX, op=Alu.add))
    ch.c("vector", lambda e, o=st1: e.tensor_scalar(out=o, in0=o, scalar1=1.0 / Dn, scalar2=None, op0=Alu.mult))
    ch.c("vector", lambda e, a=x_ap, o=tmp_dx, s=st1: e.tensor_scalar(out=o, in0=a, scalar1=s, scalar2=None, op0=Alu.subtract))
    ch.c("scalar", lambda e, a=tmp_dx, o=tmp_sq: e.activation(out=o, in_=a, func=AF.Square))
    ch.c("vector", lambda e, a=tmp_sq, o=st2: e.tensor_reduce(out=o, in_=a, axis=AX, op=Alu.add))
    ch.c("vector", lambda e, o=st2: e.tensor_scalar(out=o, in0=o, scalar1=1.0 / Dn, scalar2=eps, op0=Alu.mult, op1=Alu.add))
    ch.c("scalar", lambda e, o=st2: e.activation(out=o, in_=o, func=AF.Sqrt))
    ch.c("vector", lambda e, o=st2: e.reciprocal(out=o, in_=o))
    ch.c("vector", lambda e, a=tmp_dx, o=out_ap, s=st2: e.tensor_scalar(out=o, in0=a, scalar1=s, scalar2=None, op0=Alu.mult))



def build_ln0():
    nc = bass.Bass("TRN2", target_bir_lowering=False)
    x = nc.dram_tensor("x", [TPC, D], F32, kind="ExternalInput").ap()
    gam = nc.dram_tensor("gam", [128, D], F32, kind="ExternalInput").ap()
    bet = nc.dram_tensor("bet", [128, D], F32, kind="ExternalInput").ap()
    xn = nc.dram_tensor("xn", [TPC, D], F32, kind="ExternalOutput").ap()
    xnA = nc.dram_tensor("xnA", [TPC, D], F32, kind="ExternalOutput").ap()

    ch = Ch()
    with (
        nc.sbuf_tensor("xt", [128, D], F32) as xt,
        nc.sbuf_tensor("dx", [128, D], F32) as dx,
        nc.sbuf_tensor("sq", [128, D], F32) as sq,
        nc.sbuf_tensor("xnt", [128, D], F32) as xnt,
        nc.sbuf_tensor("xat", [128, D], F32) as xat,
        nc.sbuf_tensor("gt", [128, D], F32) as gt,
        nc.sbuf_tensor("bt", [128, D], F32) as bt,
        nc.sbuf_tensor("s1", [128, 1], F32) as s1,
        nc.sbuf_tensor("s2", [128, 1], F32) as s2,
        nc.Block() as block,
    ):
        sems = [nc.semaphore(f"S{i}").__enter__() for i in range(8)]
        ch.d("sync", lambda e: e.dma_start(out=gt[:], in_=gam[:, :]))
        ch.d("sync", lambda e: e.dma_start(out=bt[:], in_=bet[:, :]))
        for t in range(TPC // 128):
            r0 = t * 128
            ch.d("sync", lambda e, r0=r0: e.dma_start(out=xt[:], in_=x[r0:r0 + 128, :]))
            _ln_tile(ch, xt[:], xnt[:], dx[:], sq[:], s1[:], s2[:], D)
            ch.d("sync", lambda e, r0=r0: e.dma_start(out=xn[r0:r0 + 128, :], in_=xnt[:]))
            _ln_tile(ch, xnt[:], xat[:], dx[:], sq[:], s1[:], s2[:], D)
            ch.c("vector", lambda e: e.tensor_tensor(out=xat[:], in0=xat[:], in1=gt[:], op=Alu.mult))
            ch.c("vector", lambda e: e.tensor_tensor(out=xat[:], in0=xat[:], in1=bt[:], op=Alu.add))
            ch.d("sync", lambda e, r0=r0: e.dma_start(out=xnA[r0:r0 + 128, :], in_=xat[:]))
        ch.emit(nc, block, sems)
    return nc



def build_l0qkv():
    nc = bass.Bass("TRN2", target_bir_lowering=False)
    xnAT = nc.dram_tensor("xnAT", [D, S], F32, kind="ExternalInput").ap()
    Wq = nc.dram_tensor("Wq", [D, D], F32, kind="ExternalInput").ap()
    Wk = nc.dram_tensor("Wk", [D, D], F32, kind="ExternalInput").ap()
    Wv = nc.dram_tensor("Wv", [D, D], F32, kind="ExternalInput").ap()
    q_out = nc.dram_tensor("q_out", [8, 128, TPC], BF16, kind="ExternalOutput").ap()
    k_out = nc.dram_tensor("k_out", [8, 128, S], BF16, kind="ExternalOutput").ap()
    v_out = nc.dram_tensor("v_out", [16, 128, 4096], BF16, kind="ExternalOutput").ap()

    xnATl = nc.dram_tensor("xnATl", [D, TPC], F32, kind="ExternalInput").ap()

    ch = Ch()
    with (
        nc.sbuf_tensor("xa", [128, 8 * S], BF16) as xa,
        nc.sbuf_tensor("xl", [128, 8 * TPC], BF16) as xl,
        nc.sbuf_tensor("wq", [128, 8 * 128], BF16) as wq,
        nc.sbuf_tensor("wk", [128, 8 * 128], BF16) as wk,
        nc.sbuf_tensor("wv", [128, 8 * 128], BF16) as wv,
        nc.sbuf_tensor("qs", [128, TPC], BF16) as qs,
        nc.sbuf_tensor("ks", [128, S], BF16) as ks,
        nc.sbuf_tensor("vs", [128, 2 * 4096], BF16) as vs,
        nc.psum_tensor("ps", [128, 512], F32) as ps,
        nc.psum_tensor("pv", [128, 128], F32) as pv,
        nc.Block() as block,
    ):
        sems = [nc.semaphore(f"S{i}").__enter__() for i in range(8)]
        for c in range(8):
            ch.d("gpsimd", lambda e, c=c: e.dma_start(out=xa[:, c * S:(c + 1) * S], in_=xnAT[c * 128:(c + 1) * 128, :]))
            ch.d("gpsimd", lambda e, c=c: e.dma_start(out=xl[:, c * TPC:(c + 1) * TPC], in_=xnATl[c * 128:(c + 1) * 128, :]))
        for p in range(8):
            for c in range(8):
                for W, ws in ((Wq, wq), (Wk, wk), (Wv, wv)):
                    ch.d("gpsimd", lambda e, c=c, p=p, W=W, ws=ws: e.dma_start(
                        out=ws[:, c * 128:(c + 1) * 128], in_=W[c * 128:(c + 1) * 128, p * 128:(p + 1) * 128]))
            for tg in range(TPC // 512):
                for c in range(8):
                    ch.c("tensor", lambda e, c=c, tg=tg: e.matmul(
                        ps[:], wq[:, c * 128:(c + 1) * 128], xl[:, c * TPC + tg * 512: c * TPC + tg * 512 + 512],
                        start=(c == 0), stop=(c == 7)))
                ch.c("vector", lambda e, tg=tg: e.tensor_copy(out=qs[:, tg * 512:(tg + 1) * 512], in_=ps[:]))
            ch.d("sync", lambda e, p=p: e.dma_start(out=q_out[p, :, :], in_=qs[:]))
            for kt in range(S // 512):
                for c in range(8):
                    ch.c("tensor", lambda e, c=c, kt=kt: e.matmul(
                        ps[:], wk[:, c * 128:(c + 1) * 128], xa[:, c * S + kt * 512: c * S + kt * 512 + 512],
                        start=(c == 0), stop=(c == 7)))
                ch.c("vector", lambda e, kt=kt: e.tensor_copy(out=ks[:, kt * 512:(kt + 1) * 512], in_=ps[:]))
            ch.d("sync", lambda e, p=p: e.dma_start(out=k_out[p, :, :], in_=ks[:]))
            for tc in range(S // 128):
                for c in range(8):
                    ch.c("tensor", lambda e, c=c, tc=tc: e.matmul(
                        pv[:], xa[:, c * S + tc * 128: c * S + tc * 128 + 128], wv[:, c * 128:(c + 1) * 128],
                        start=(c == 0), stop=(c == 7)))
                ch.c("vector", lambda e, tc=tc: e.tensor_copy(out=vs[:, tc * 64:(tc + 1) * 64], in_=pv[:, 0:64]))
                ch.c("vector", lambda e, tc=tc: e.tensor_copy(out=vs[:, 4096 + tc * 64:4096 + (tc + 1) * 64], in_=pv[:, 64:128]))
            ch.d("sync", lambda e, p=p: e.dma_start(out=v_out[2 * p, :, :], in_=vs[:, 0:4096]))
            ch.d("sync", lambda e, p=p: e.dma_start(out=v_out[2 * p + 1, :, :], in_=vs[:, 4096:8192]))
        ch.emit(nc, block, sems)
    return nc



SLOPES = [2.0 ** (-8.0 / n) for n in range(16, 0, -1)]
RAMPLEN = 10240

def build_l0attn():
    nc = bass.Bass("TRN2", target_bir_lowering=False)
    q_in = nc.dram_tensor("q_in", [8, 128, TPC], BF16, kind="ExternalInput").ap()
    k_in = nc.dram_tensor("k_in", [8, 128, S], BF16, kind="ExternalInput").ap()
    v_in = nc.dram_tensor("v_in", [16, 128, 4096], BF16, kind="ExternalInput").ap()
    Wo = nc.dram_tensor("Wo", [D, D], F32, kind="ExternalInput").ap()
    dtab = nc.dram_tensor("dtab", [76, 128, 512], BF16, kind="ExternalInput").ap()
    attn0T = nc.dram_tensor("attn0T", [D, TPC], F32, kind="ExternalOutput").ap()

    NQT = TPC // 512
    NKC = S // 128

    ch = Ch()
    with ExitStack() as st:
        sb = lambda n, sh, dt=BF16: st.enter_context(nc.sbuf_tensor(n, sh, dt))
        ka = sb("ka", [66, S]); vsb = sb("vsb", [128, 4096]); qp = sb("qp", [128, TPC])
        qa = sb("qa", [66, TPC]); phat = sb("phat", [128, TPC]); dts = sb("dts", [128, 76 * 512])
        ctx = sb("ctx", [128, 8 * TPC]); wo = sb("wo", [128, 8 * 1024])
        sqk = sb("sqk", [64, 512]); sqq = sb("sqq", [64, 512])
        kn512 = sb("kn512", [1, 512], F32); kmx = sb("kmx", [1, 1], F32); qn2 = sb("qn2", [1, TPC], F32)
        khat = sb("khat", [1, 1], F32); vqr = sb("vqr", [1, TPC], F32)
        stgh = sb("stgh", [1, TPC]); stgl = sb("stgl", [1, TPC])
        sx = sb("sx", [128, 512], F32); lsb = sb("lsb", [1, 512], F32)
        rhi = sb("rhi", [1, 512]); rlo = sb("rlo", [1, 512]); rlf = sb("rlf", [1, 512], F32)
        ctxf = sb("ctxf", [64, 512], F32)
        o64 = sb("o64", [64, 1]); o128 = sb("o128", [128, 1]); o1x64 = sb("o1x64", [1, 64])
        outf = sb("outf", [128, 512], F32)
        sps = st.enter_context(nc.psum_tensor("sps", [128, 2048], F32))
        cps = st.enter_context(nc.psum_tensor("cps", [66, 2048], F32))
        sems = [st.enter_context(nc.semaphore(f"S{i}")) for i in range(8)]
        block = st.enter_context(nc.Block())
        for c in range(8):
            ch.d("gpsimd", lambda e, c=c: e.dma_start(out=wo[:, c * 1024:(c + 1) * 1024], in_=Wo[c * 128:(c + 1) * 128, :]))
        for dd in range(76):
            ch.d("sync", lambda e, dd=dd: e.dma_start(out=dts[:, dd * 512:(dd + 1) * 512], in_=dtab[dd, :, :]))
        ch.c("gpsimd", lambda e: e.memset(o64[:], 1.0))
        ch.c("gpsimd", lambda e: e.memset(o128[:], 1.0))
        ch.c("gpsimd", lambda e: e.memset(o1x64[:], 1.0))
        ch.c("gpsimd", lambda e: e.memset(ka[64:66, :], 1.0))

        for h in range(16):
            p, r = h // 2, h % 2
            slope = SLOPES[h]
            if r == 0:
                ch.d("sync", lambda e, p=p: e.dma_start(out=qp[:], in_=q_in[p, :, :]))
            ch.d("sync", lambda e, p=p, r=r: e.dma_start(out=ka[0:64, :], in_=k_in[p, r * 64:r * 64 + 64, :]))
            ch.d("sync", lambda e, h=h: e.dma_start(out=vsb[:], in_=v_in[h, :, :]))
            ch.c("gpsimd", lambda e: e.memset(khat[:], 0.0))
            for kt in range(S // 512):
                ch.c("scalar", lambda e, kt=kt: e.activation(out=sqk[:], in_=ka[0:64, kt * 512:(kt + 1) * 512], func=AF.Square))
                ch.c("tensor", lambda e: e.matmul(sps[0:1, 0:512], o64[:], sqk[:], start=True, stop=True))
                ch.c("vector", lambda e: e.tensor_copy(out=kn512[:], in_=sps[0:1, 0:512]))
                ch.c("vector", lambda e: e.tensor_reduce(out=kmx[:], in_=kn512[:], axis=AX, op=Alu.max))
                ch.c("vector", lambda e: e.tensor_tensor(out=khat[:], in0=khat[:], in1=kmx[:], op=Alu.max))
            ch.c("scalar", lambda e: e.activation(out=khat[:], in_=khat[:], func=AF.Sqrt))
            for tg in range(NQT):
                ch.c("scalar", lambda e, r=r, tg=tg: e.activation(out=sqq[:], in_=qp[r * 64:r * 64 + 64, tg * 512:(tg + 1) * 512], func=AF.Square))
                ch.c("tensor", lambda e: e.matmul(sps[0:1, 0:512], o64[:], sqq[:], start=True, stop=True))
                ch.c("vector", lambda e, tg=tg: e.tensor_copy(out=qn2[:, tg * 512:(tg + 1) * 512], in_=sps[0:1, 0:512]))
            ch.c("scalar", lambda e: e.activation(out=qn2[:], in_=qn2[:], func=AF.Sqrt))
            ch.c("vector", lambda e: e.tensor_scalar(out=vqr[:], in0=qn2[:], scalar1=khat[0:1, 0:1], scalar2=-1.03, op0=Alu.mult, op1=Alu.mult))
            ch.c("vector", lambda e, r=r: e.tensor_copy(out=qa[0:64, :], in_=qp[r * 64:r * 64 + 64, :]))
            ch.c("vector", lambda e: e.tensor_copy(out=stgh[:], in_=vqr[:]))
            ch.c("vector", lambda e: e.tensor_tensor(out=qn2[:], in0=vqr[:], in1=stgh[:], op=Alu.subtract))
            ch.c("vector", lambda e: e.tensor_copy(out=stgl[:], in_=qn2[:]))
            ch.d("sync", lambda e: e.dma_start(out=qa[64:65, :], in_=stgh[:]))
            ch.d("sync", lambda e: e.dma_start(out=qa[65:66, :], in_=stgl[:]))
            for kc in range(NKC):
                kb = kc * 128
                for qt in range(NQT):
                    ch.c("tensor", lambda e, kb=kb, qt=qt: e.matmul(
                        sps[:, qt * 512:(qt + 1) * 512], ka[:, kb:kb + 128], qa[:, qt * 512:(qt + 1) * 512], start=True, stop=True))
                for qt in range(NQT):
                    dd = (qt * 512 - kb + 8064) // 128
                    ch.c("vector", lambda e, qt=qt, dd=dd, slope=slope: e.scalar_tensor_tensor(
                        out=sx[:], in0=dts[:, dd * 512:(dd + 1) * 512], scalar=slope, in1=sps[:, qt * 512:(qt + 1) * 512],
                        op0=Alu.mult, op1=Alu.add))
                    ch.c("scalar", lambda e, qt=qt: e.activation(
                        out=phat[:, qt * 512:(qt + 1) * 512], in_=sx[:], func=AF.Exp, bias=0.0, scale=0.125))
                for qt in range(NQT):
                    ch.c("tensor", lambda e, kc=kc, qt=qt: e.matmul(
                        cps[0:64, qt * 512:(qt + 1) * 512], vsb[:, kc * 64:(kc + 1) * 64], phat[:, qt * 512:(qt + 1) * 512],
                        start=(kc == 0), stop=(kc == NKC - 1)))
                    ch.c("tensor", lambda e, kc=kc, qt=qt: e.matmul(
                        cps[64:65, qt * 512:(qt + 1) * 512], o128[:], phat[:, qt * 512:(qt + 1) * 512],
                        start=(kc == 0), stop=(kc == NKC - 1)))
            for qt in range(NQT):
                ch.c("vector", lambda e, qt=qt: e.tensor_copy(out=lsb[:], in_=cps[64:65, qt * 512:(qt + 1) * 512]))
                ch.c("vector", lambda e: e.reciprocal(out=rlf[:], in_=lsb[:]))
                ch.c("vector", lambda e: e.tensor_copy(out=rhi[:], in_=rlf[:]))
                ch.c("vector", lambda e: e.tensor_tensor(out=lsb[:], in0=rlf[:], in1=rhi[:], op=Alu.subtract))
                ch.c("vector", lambda e: e.tensor_copy(out=rlo[:], in_=lsb[:]))
                ch.c("tensor", lambda e: e.matmul(sps[0:64, 0:512], o1x64[:], rhi[:], start=True, stop=False))
                ch.c("tensor", lambda e: e.matmul(sps[0:64, 0:512], o1x64[:], rlo[:], start=False, stop=True))
                ch.c("vector", lambda e, qt=qt: e.tensor_copy(out=ctxf[:], in_=cps[0:64, qt * 512:(qt + 1) * 512]))
                ch.c("vector", lambda e, qt=qt, p=p, r=r: e.tensor_tensor(
                    out=ctx[r * 64:r * 64 + 64, p * TPC + qt * 512: p * TPC + (qt + 1) * 512], in0=ctxf[:], in1=sps[0:64, 0:512], op=Alu.mult))
        for dmc in range(8):
            for qt in range(NQT):
                for c in range(8):
                    ch.c("tensor", lambda e, c=c, dmc=dmc, qt=qt: e.matmul(
                        sps[:, 0:512], wo[:, c * 1024 + dmc * 128: c * 1024 + dmc * 128 + 128],
                        ctx[:, c * TPC + qt * 512: c * TPC + (qt + 1) * 512], start=(c == 0), stop=(c == 7)))
                ch.c("vector", lambda e: e.tensor_copy(out=outf[:], in_=sps[:, 0:512]))
                ch.d("sync", lambda e, dmc=dmc, qt=qt: e.dma_start(
                    out=attn0T[dmc * 128:(dmc + 1) * 128, qt * 512:(qt + 1) * 512], in_=outf[:]))
        ch.emit(nc, block, sems)
    return nc



def _quant_tile(ch, xt, outb, W, m, u, g, t, sg):
    nb = W // 16
    x3 = xt.rearrange("p (n b) -> p n b", b=16)
    mw = int(m.shape[1])
    m3 = bass.AP(m, 0, [[mw, 128], [1, nb], [1, 1]])
    mb = bass.AP(m, 0, [[mw, 128], [1, nb], [0, 16]])
    ch.c("vector", lambda e: e.tensor_reduce(out=m3, in_=x3, axis=AX, op=Alu.max, apply_absolute_value=True))
    ch.c("vector", lambda e: e.tensor_scalar(out=m[0:128, 0:nb], in0=m[0:128, 0:nb], scalar1=1.0 / 6.0, scalar2=1e-12, op0=Alu.mult, op1=Alu.max))
    ch.c("vector", lambda e: e.reciprocal(out=g[0:128, 0:nb], in_=m[0:128, 0:nb]))
    gb = bass.AP(g, 0, [[int(g.shape[1]), 128], [1, nb], [0, 16]])
    ch.c("vector", lambda e: e.tensor_tensor(out=u[0:128, 0:W].rearrange("p (n b) -> p n b", b=16), in0=x3, in1=gb, op=Alu.mult))
    ch.c("scalar", lambda e: e.activation(out=sg[0:128, 0:W], in_=u[0:128, 0:W], func=AF.Sign))
    ch.c("vector", lambda e: e.scalar_tensor_tensor(out=t[0:128, 0:W], in0=u[0:128, 0:W], scalar=-1.0, in1=u[0:128, 0:W], op0=Alu.mult, op1=Alu.max))
    ch.c("vector", lambda e: e.tensor_scalar(out=g[0:128, 0:W], in0=t[0:128, 0:W], scalar1=0.25, scalar2=0.5, op0=Alu.is_gt, op1=Alu.mult))
    for thr, w in ((0.75, 0.5), (1.25, 0.5), (1.75, 0.5), (2.5, 1.0), (3.5, 1.0), (5.0, 2.0)):
        ch.c("vector", lambda e, thr=thr, w=w: e.tensor_scalar(out=u[0:128, 0:W], in0=t[0:128, 0:W], scalar1=thr, scalar2=w, op0=Alu.is_gt, op1=Alu.mult))
        ch.c("vector", lambda e: e.tensor_tensor(out=g[0:128, 0:W], in0=g[0:128, 0:W], in1=u[0:128, 0:W], op=Alu.add))
    ch.c("vector", lambda e: e.tensor_tensor(out=g[0:128, 0:W], in0=g[0:128, 0:W], in1=sg[0:128, 0:W], op=Alu.mult))
    ch.c("vector", lambda e: e.tensor_tensor(out=outb.rearrange("p (n b) -> p n b", b=16),
                                             in0=g[0:128, 0:W].rearrange("p (n b) -> p n b", b=16), in1=mb, op=Alu.mult))


def _transp(ch, src_ap, dst_ap, pt, idn):
    ch.c("tensor", lambda e: e.transpose(out=pt[:], in_=src_ap, identity=idn[:]))
    ch.c("vector", lambda e: e.tensor_copy(out=dst_ap, in_=pt[:]))


IH = 2816

def build_l0ffn():
    nc = bass.Bass("TRN2", target_bir_lowering=False)
    xn_i = nc.dram_tensor("xn_i", [TPC, D], F32, kind="ExternalInput").ap()
    x_i = nc.dram_tensor("x_i", [TPC, D], F32, kind="ExternalInput").ap()
    at_i = nc.dram_tensor("at_i", [TPC, D], F32, kind="ExternalInput").ap()
    wg_i = nc.dram_tensor("wg_i", [IH, D], F32, kind="ExternalInput").ap()
    wu_i = nc.dram_tensor("wu_i", [IH, D], F32, kind="ExternalInput").ap()
    wd_i = nc.dram_tensor("wd_i", [D, IH], F32, kind="ExternalInput").ap()
    x1_o = nc.dram_tensor("x1_o", [TPC, D], F32, kind="ExternalOutput").ap()
    xn1_o = nc.dram_tensor("xn1_o", [TPC, D], F32, kind="ExternalOutput").ap()
    wgT_d = nc.dram_tensor("wgT_d", [D, IH], BF16).ap()
    wuT_d = nc.dram_tensor("wuT_d", [D, IH], BF16).ap()
    wdT_d = nc.dram_tensor("wdT_d", [IH, D], BF16).ap()

    NTG = TPC // 128
    HTS = [(0, 512), (512, 512), (1024, 512), (1536, 512), (2048, 512), (2560, 256)]

    ch = Ch()
    with ExitStack() as st:
        sb = lambda n, sh, dt=BF16: st.enter_context(nc.sbuf_tensor(n, sh, dt))
        xqT = sb("xqT", [128, 8 * TPC])
        wdT = sb("wdT", [128, 22 * 1024])
        wgs = sb("wgs", [128, 8 * 512]); wus = sb("wus", [128, 8 * 512])
        ld = sb("ld", [128, IH], F32)
        qb = sb("qb", [128, IH])
        mq = sb("mq", [128, IH // 16], F32)
        uq = sb("uq", [128, IH], F32); gq = sb("gq", [128, IH], F32)
        tq = sb("tq", [128, IH], F32); sgq = sb("sgq", [128, IH], F32)
        hqT = sb("hqT", [128, 22 * 128])
        ts0 = sb("ts0", [128, 128])
        idn = sb("idn", [128, 128])
        hs = sb("hs", [128, 512], F32)
        hf = sb("hf", [128, 512], F32)
        xt = sb("xt", [128, D], F32); att = sb("att", [128, D], F32)
        x1t = sb("x1t", [128, D], F32); xn1t = sb("xn1t", [128, D], F32)
        dxs = sb("dxs", [128, D], F32); sqs = sb("sqs", [128, D], F32)
        st1 = sb("st1", [128, 1], F32); st2 = sb("st2", [128, 1], F32)
        pg = st.enter_context(nc.psum_tensor("pg", [128, 512], F32))
        pu = st.enter_context(nc.psum_tensor("pu", [128, 512], F32))
        pd = st.enter_context(nc.psum_tensor("pd", [128, 512], F32))
        pt = st.enter_context(nc.psum_tensor("pt", [128, 128], BF16))
        sems = [st.enter_context(nc.semaphore(f"S{i}")) for i in range(8)]
        block = st.enter_context(nc.Block())

        ch.c("gpsimd", lambda e: e.memset(ts0[:], 1.0))
        ch.c("gpsimd", lambda e: e.affine_select(out=idn[:], in_=ts0[:], pattern=[[-1, 128]], compare_op=Alu.is_equal, fill=0.0, base=0, channel_multiplier=1))

        for W_in, T_out, nt in ((wg_i, wgT_d, 22), (wu_i, wuT_d, 22)):
            for ttile in range(nt):
                ch.d("sync", lambda e, W_in=W_in, tt=ttile: e.dma_start(out=ld[:, 0:D], in_=W_in[tt * 128:(tt + 1) * 128, :]))
                _quant_tile(ch, ld[:, 0:D], qb[:, 0:D], D, mq, uq, gq, tq, sgq)
                for fc in range(8):
                    _transp(ch, qb[:, fc * 128:(fc + 1) * 128], ts0[:], pt, idn)
                    ch.d("sync", lambda e, T_out=T_out, fc=fc, tt=ttile: e.dma_start(
                        out=T_out[fc * 128:(fc + 1) * 128, tt * 128:(tt + 1) * 128], in_=ts0[:]))
        for ttile in range(8):
            ch.d("sync", lambda e, tt=ttile: e.dma_start(out=ld[:], in_=wd_i[tt * 128:(tt + 1) * 128, :]))
            _quant_tile(ch, ld[:], qb[:], IH, mq, uq, gq, tq, sgq)
            for hc in range(22):
                _transp(ch, qb[:, hc * 128:(hc + 1) * 128], ts0[:], pt, idn)
                ch.d("sync", lambda e, hc=hc, tt=ttile: e.dma_start(
                    out=wdT_d[hc * 128:(hc + 1) * 128, tt * 128:(tt + 1) * 128], in_=ts0[:]))

        for tg in range(NTG):
            ch.d("sync", lambda e, tg=tg: e.dma_start(out=ld[:, 0:D], in_=xn_i[tg * 128:(tg + 1) * 128, :]))
            _quant_tile(ch, ld[:, 0:D], qb[:, 0:D], D, mq, uq, gq, tq, sgq)
            for fc in range(8):
                _transp(ch, qb[:, fc * 128:(fc + 1) * 128], xqT[:, fc * TPC + tg * 128: fc * TPC + (tg + 1) * 128], pt, idn)

        for hc in range(22):
            ch.d("sync", lambda e, hc=hc: e.dma_start(out=wdT[:, hc * 1024:(hc + 1) * 1024], in_=wdT_d[hc * 128:(hc + 1) * 128, :]))

        for tg in range(NTG):
            ch.d("sync", lambda e, tg=tg: e.dma_start(out=xt[:], in_=x_i[tg * 128:(tg + 1) * 128, :]))
            ch.d("sync", lambda e, tg=tg: e.dma_start(out=att[:], in_=at_i[tg * 128:(tg + 1) * 128, :]))
            for hti, (h0, hw) in enumerate(HTS):
                for fc in range(8):
                    ch.d("sync", lambda e, fc=fc, h0=h0, hw=hw: e.dma_start(out=wgs[:, fc * 512: fc * 512 + hw], in_=wgT_d[fc * 128:(fc + 1) * 128, h0:h0 + hw]))
                    ch.d("sync", lambda e, fc=fc, h0=h0, hw=hw: e.dma_start(out=wus[:, fc * 512: fc * 512 + hw], in_=wuT_d[fc * 128:(fc + 1) * 128, h0:h0 + hw]))
                for fc in range(8):
                    ch.c("tensor", lambda e, fc=fc, tg=tg, hw=hw: e.matmul(
                        pg[:, 0:hw], xqT[:, fc * TPC + tg * 128: fc * TPC + (tg + 1) * 128], wgs[:, fc * 512: fc * 512 + hw],
                        start=(fc == 0), stop=(fc == 7)))
                for fc in range(8):
                    ch.c("tensor", lambda e, fc=fc, tg=tg, hw=hw: e.matmul(
                        pu[:, 0:hw], xqT[:, fc * TPC + tg * 128: fc * TPC + (tg + 1) * 128], wus[:, fc * 512: fc * 512 + hw],
                        start=(fc == 0), stop=(fc == 7)))
                ch.c("scalar", lambda e, hw=hw: e.activation(out=hs[:, 0:hw], in_=pg[:, 0:hw], func=AF.Silu))
                ch.c("vector", lambda e, hw=hw: e.tensor_tensor(out=hf[:, 0:hw], in0=hs[:, 0:hw], in1=pu[:, 0:hw], op=Alu.mult))
                _quant_tile(ch, hf[:, 0:hw], qb[:, 0:hw], hw, mq, uq, gq, tq, sgq)
                for sc in range(hw // 128):
                    _transp(ch, qb[:, sc * 128:(sc + 1) * 128],
                            hqT[:, (h0 // 128 + sc) * 128:(h0 // 128 + sc + 1) * 128], pt, idn)
            for dmt in range(2):
                for hc in range(22):
                    ch.c("tensor", lambda e, hc=hc, dmt=dmt: e.matmul(
                        pd[:], hqT[:, hc * 128:(hc + 1) * 128], wdT[:, hc * 1024 + dmt * 512: hc * 1024 + dmt * 512 + 512],
                        start=(hc == 0), stop=(hc == 21)))
                ch.c("vector", lambda e, dmt=dmt: e.tensor_tensor(out=x1t[:, dmt * 512:(dmt + 1) * 512], in0=att[:, dmt * 512:(dmt + 1) * 512], in1=pd[:], op=Alu.add))
            ch.c("vector", lambda e: e.tensor_tensor(out=x1t[:], in0=x1t[:], in1=xt[:], op=Alu.add))
            _ln_tile(ch, x1t[:], xn1t[:], dxs[:], sqs[:], st1[:], st2[:], D)
            ch.d("sync", lambda e, tg=tg: e.dma_start(out=x1_o[tg * 128:(tg + 1) * 128, :], in_=x1t[:]))
            ch.d("sync", lambda e, tg=tg: e.dma_start(out=xn1_o[tg * 128:(tg + 1) * 128, :], in_=xn1t[:]))
        ch.emit(nc, block, sems)
    return nc



def _rms_norm(ch, src_ps, dst_ap, W, sqb, nps, rr, rhi2, rlo2, bps, evf, o64, o1x64, wcol, eps=1.1920929e-07):
    ch.c("scalar", lambda e: e.activation(out=sqb[:, 0:W], in_=src_ps, func=AF.Square))
    ch.c("tensor", lambda e: e.matmul(nps[0:1, 0:W], o64[:], sqb[:, 0:W], start=True, stop=True))
    ch.c("vector", lambda e: e.tensor_copy(out=rr[:, 0:W], in_=nps[0:1, 0:W]))
    ch.c("vector", lambda e: e.tensor_scalar(out=rr[:, 0:W], in0=rr[:, 0:W], scalar1=1.0 / 64.0, scalar2=eps, op0=Alu.mult, op1=Alu.add))
    ch.c("scalar", lambda e: e.activation(out=rr[:, 0:W], in_=rr[:, 0:W], func=AF.Sqrt))
    ch.c("vector", lambda e: e.reciprocal(out=rr[:, 0:W], in_=rr[:, 0:W]))
    ch.c("vector", lambda e: e.tensor_copy(out=rhi2[:, 0:W], in_=rr[:, 0:W]))
    ch.c("vector", lambda e: e.tensor_tensor(out=rr[:, 0:W], in0=rr[:, 0:W], in1=rhi2[:, 0:W], op=Alu.subtract))
    ch.c("vector", lambda e: e.tensor_copy(out=rlo2[:, 0:W], in_=rr[:, 0:W]))
    ch.c("tensor", lambda e: e.matmul(bps[0:64, 0:W], o1x64[:], rhi2[:, 0:W], start=True, stop=False))
    ch.c("tensor", lambda e: e.matmul(bps[0:64, 0:W], o1x64[:], rlo2[:, 0:W], start=False, stop=True))
    ch.c("vector", lambda e: e.tensor_copy(out=evf[:, 0:W], in_=src_ps))
    ch.c("vector", lambda e: e.tensor_tensor(out=evf[:, 0:W], in0=evf[:, 0:W], in1=bps[0:64, 0:W], op=Alu.mult))
    ch.c("vector", lambda e: e.tensor_scalar(out=dst_ap, in0=evf[:, 0:W], scalar1=wcol, scalar2=None, op0=Alu.mult))


def build_l1kqv():
    nc = bass.Bass("TRN2", target_bir_lowering=False)
    xn1T = nc.dram_tensor("xn1T", [D, S], F32, kind="ExternalInput").ap()
    wdq = nc.dram_tensor("wdq", [64, 512], F32, kind="ExternalInput").ap()
    wuq = nc.dram_tensor("wuq", [512, 64], F32, kind="ExternalInput").ap()
    wdkv = nc.dram_tensor("wdkv", [64, 1024], F32, kind="ExternalInput").ap()
    wukv = nc.dram_tensor("wukv", [1024, 128], F32, kind="ExternalInput").ap()
    wqr = nc.dram_tensor("wqr", [64, 64], F32, kind="ExternalInput").ap()
    wkr = nc.dram_tensor("wkr", [64, 64], F32, kind="ExternalInput").ap()
    qnw = nc.dram_tensor("qnw", [64, 1], F32, kind="ExternalInput").ap()
    knw = nc.dram_tensor("knw", [64, 1], F32, kind="ExternalInput").ap()
    rw = nc.dram_tensor("rw", [D, 8], F32, kind="ExternalInput").ap()

    qc_o = nc.dram_tensor("qc_o", [4, 64, S], BF16, kind="ExternalOutput").ap()
    qr_o = nc.dram_tensor("qr_o", [4, 64, S], BF16, kind="ExternalOutput").ap()
    vq_o = nc.dram_tensor("vq_o", [4, 1, S], F32, kind="ExternalOutput").ap()
    kc_o = nc.dram_tensor("kc_o", [4, 64, S], BF16, kind="ExternalOutput").ap()
    kr_o = nc.dram_tensor("kr_o", [4, 64, S], BF16, kind="ExternalOutput").ap()
    v_o = nc.dram_tensor("v_o", [4, 128, 4096], BF16, kind="ExternalOutput").ap()
    rsc_o = nc.dram_tensor("rsc_o", [S, 8], F32, kind="ExternalOutput").ap()
    thr_o = nc.dram_tensor("thr_o", [8, 4], F32, kind="ExternalOutput").ap()


    ch = Ch()
    with ExitStack() as st:
        sb = lambda n, sh, dt=BF16: st.enter_context(nc.sbuf_tensor(n, sh, dt))
        xh = sb("xh", [64, S])
        cq = sb("cq", [128, 4 * TPC])
        ckv = sb("ckv", [128, 8 * 512])
        kcs = sb("kcs", [64, S]); krs = sb("krs", [64, S])
        vstage = sb("vstage", [128, 4096]); vtmp = sb("vtmp", [64, 512])
        qcs = sb("qcs", [64, TPC]); qrs = sb("qrs", [64, TPC])
        w1 = sb("w1", [64, 512])
        w2t = sb("w2t", [128, 4 * 64])
        w3 = sb("w3", [64, 1024])
        w4 = sb("w4", [128, 8 * 128])
        w5 = sb("w5", [64, 64]); w6 = sb("w6", [64, 64])
        qnwc = sb("qnwc", [64, 1], F32); knwc = sb("knwc", [64, 1], F32)
        rwT = sb("rwT", [128, 8 * 8], F32)
        xcs = sb("xcs", [128, 8 * 128], F32)
        sqb = sb("sqb", [64, 512])
        rr = sb("rr", [1, 512], F32); rhi2 = sb("rhi2", [1, 512]); rlo2 = sb("rlo2", [1, 512])
        evf = sb("evf", [64, 512], F32)
        o64 = sb("o64", [64, 1]); o1x64 = sb("o1x64", [1, 64]); o1x128 = sb("o1x128", [1, 128])
        kn512 = sb("kn512", [1, 512], F32); kmx = sb("kmx", [1, 1], F32); khat = sb("khat", [1, 1], F32)
        qn2 = sb("qn2", [1, TPC], F32); vqr = sb("vqr", [1, TPC], F32)
        rsf = sb("rsf", [128, 8], F32); rmax = sb("rmax", [128, 1], F32); rsum = sb("rsum", [128, 1], F32)
        scT = sb("scT", [8, S], F32)
        cmpb = sb("cmpb", [8, S], F32)
        lo = sb("lo", [8, 1], F32); hiv = sb("hiv", [8, 1], F32); mid = sb("mid", [8, 1], F32)
        cnt = sb("cnt", [8, 1], F32); sel = sb("sel", [8, 1], F32); nsel = sb("nsel", [8, 1], F32)
        tv = sb("tv", [8, 1], F32); need = sb("need", [8, 1], F32)
        rlo_i = sb("rlo_i", [8, 1], F32); rhi_i = sb("rhi_i", [8, 1], F32)
        thrs = sb("thrs", [8, 4], F32)
        o1x128m = sb("o1x128m", [1, 128])
        idn_kqv = sb("idn_kqv", [128, 128]); tsk = sb("tsk", [128, 128])
        idnf_kqv = sb("idnf_kqv", [128, 128], F32); tskf = sb("tskf", [128, 128], F32)
        rsbf_kqv = sb("rsbf_kqv", [128, 8], F32)
        ps = st.enter_context(nc.psum_tensor("ps", [128, 512], F32))
        pkv = st.enter_context(nc.psum_tensor("pkv", [128, 512], F32))
        nps = st.enter_context(nc.psum_tensor("nps", [1, 512], F32))
        bps = st.enter_context(nc.psum_tensor("bps", [64, 512], F32))
        pr = st.enter_context(nc.psum_tensor("pr", [128, 8], F32))
        ptv = st.enter_context(nc.psum_tensor("ptv", [128, 64], BF16))
        ptr = st.enter_context(nc.psum_tensor("ptr", [8, 128], F32))
        sems = [st.enter_context(nc.semaphore(f"S{i}")) for i in range(8)]
        block = st.enter_context(nc.Block())

        ch.c("gpsimd", lambda e: e.memset(o64[:], 1.0))
        ch.c("gpsimd", lambda e: e.memset(o1x64[:], 1.0))
        ch.c("gpsimd", lambda e: e.memset(o1x128[:], 1.0))
        ch.c("gpsimd", lambda e: e.memset(tsk[:], 1.0))
        ch.c("gpsimd", lambda e: e.affine_select(out=idn_kqv[:], in_=tsk[:], pattern=[[-1, 128]], compare_op=Alu.is_equal, fill=0.0, base=0, channel_multiplier=1))
        ch.c("gpsimd", lambda e: e.memset(tskf[:], 1.0))
        ch.c("gpsimd", lambda e: e.affine_select(out=idnf_kqv[:], in_=tskf[:], pattern=[[-1, 128]], compare_op=Alu.is_equal, fill=0.0, base=0, channel_multiplier=1))
        ch.d("gpsimd", lambda e: e.dma_start(out=w1[:], in_=wdq[:, :]))
        for lc in range(4):
            ch.d("gpsimd", lambda e, lc=lc: e.dma_start(out=w2t[:, lc * 64:(lc + 1) * 64], in_=wuq[lc * 128:(lc + 1) * 128, :]))
        ch.d("gpsimd", lambda e: e.dma_start(out=w3[:], in_=wdkv[:, :]))
        for lc in range(8):
            ch.d("gpsimd", lambda e, lc=lc: e.dma_start(out=w4[:, lc * 128:(lc + 1) * 128], in_=wukv[lc * 128:(lc + 1) * 128, :]))
        ch.d("gpsimd", lambda e: e.dma_start(out=w5[:], in_=wqr[:, :]))
        ch.d("gpsimd", lambda e: e.dma_start(out=w6[:], in_=wkr[:, :]))
        ch.d("sync", lambda e: e.dma_start(out=qnwc[:], in_=qnw[:, :]))
        ch.d("sync", lambda e: e.dma_start(out=knwc[:], in_=knw[:, :]))
        for fc in range(8):
            ch.d("gpsimd", lambda e, fc=fc: e.dma_start(out=rwT[:, fc * 8:(fc + 1) * 8], in_=rw[fc * 128:(fc + 1) * 128, :]))

        for hi in range(4):
            ch.d("gpsimd", lambda e, hi=hi: e.dma_start(out=xh[:], in_=xn1T[hi * 64:(hi + 1) * 64, :]))
            ch.c("gpsimd", lambda e: e.memset(khat[:], 0.0))
            for tt in range(S // 512):
                t0 = tt * 512
                for lc in range(8):
                    ch.c("tensor", lambda e, lc=lc, t0=t0: e.matmul(ps[:], w3[:, lc * 128:(lc + 1) * 128], xh[:, t0:t0 + 512], start=True, stop=True))
                    ch.c("vector", lambda e, lc=lc: e.tensor_copy(out=ckv[:, lc * 512:(lc + 1) * 512], in_=ps[:]))
                for lc in range(8):
                    ch.c("tensor", lambda e, lc=lc: e.matmul(pkv[:], w4[:, lc * 128:(lc + 1) * 128], ckv[:, lc * 512:(lc + 1) * 512], start=(lc == 0), stop=(lc == 7)))
                _rms_norm(ch, pkv[0:64, 0:512], kcs[0:64, t0:t0 + 512], 512, sqb, nps, rr, rhi2, rlo2, bps, evf, o64, o1x64, knwc[:])
                ch.c("vector", lambda e: e.tensor_copy(out=vtmp[:], in_=pkv[64:128, 0:512]))
                for j in range(4):
                    ch.c("tensor", lambda e, j=j: e.transpose(out=ptv[:], in_=vtmp[:, j * 128:(j + 1) * 128], identity=idn_kqv[0:64, 0:64]))
                    ch.c("vector", lambda e, j=j, tt=tt: e.tensor_copy(out=vstage[:, (tt * 4 + j) * 64:(tt * 4 + j + 1) * 64], in_=ptv[:, 0:64]))
                ch.c("tensor", lambda e, t0=t0: e.matmul(ps[0:64, 0:512], w6[:], xh[:, t0:t0 + 512], start=True, stop=True))
                ch.c("vector", lambda e, t0=t0: e.tensor_copy(out=krs[:, t0:t0 + 512], in_=ps[0:64, 0:512]))
                ch.c("scalar", lambda e, t0=t0: e.activation(out=sqb[:], in_=kcs[:, t0:t0 + 512], func=AF.Square))
                ch.c("tensor", lambda e: e.matmul(nps[:], o64[:], sqb[:], start=True, stop=False))
                ch.c("scalar", lambda e, t0=t0: e.activation(out=sqb[:], in_=krs[:, t0:t0 + 512], func=AF.Square))
                ch.c("tensor", lambda e: e.matmul(nps[:], o64[:], sqb[:], start=False, stop=True))
                ch.c("vector", lambda e: e.tensor_copy(out=kn512[:], in_=nps[:]))
                ch.c("vector", lambda e: e.tensor_reduce(out=kmx[:], in_=kn512[:], axis=AX, op=Alu.max))
                ch.c("vector", lambda e: e.tensor_tensor(out=khat[:], in0=khat[:], in1=kmx[:], op=Alu.max))
            ch.c("scalar", lambda e: e.activation(out=khat[:], in_=khat[:], func=AF.Sqrt))
            ch.d("sync", lambda e, hi=hi: e.dma_start(out=kc_o[hi, :, :], in_=kcs[:]))
            ch.d("sync", lambda e, hi=hi: e.dma_start(out=kr_o[hi, :, :], in_=krs[:]))
            ch.d("sync", lambda e, hi=hi: e.dma_start(out=v_o[hi, :, :], in_=vstage[:]))
            for tb in range(S // TPC):
                b0 = tb * TPC
                for lc in range(4):
                    for ttt in range(4):
                        ch.c("tensor", lambda e, lc=lc, ttt=ttt, b0=b0: e.matmul(
                            ps[:], w1[:, lc * 128:(lc + 1) * 128], xh[:, b0 + ttt * 512: b0 + (ttt + 1) * 512], start=True, stop=True))
                        ch.c("vector", lambda e, lc=lc, ttt=ttt: e.tensor_copy(out=cq[:, lc * TPC + ttt * 512: lc * TPC + (ttt + 1) * 512], in_=ps[:]))
                for ttt in range(4):
                    for lc in range(4):
                        ch.c("tensor", lambda e, lc=lc, ttt=ttt: e.matmul(
                            pkv[0:64, 0:512], w2t[:, lc * 64:(lc + 1) * 64], cq[:, lc * TPC + ttt * 512: lc * TPC + (ttt + 1) * 512],
                            start=(lc == 0), stop=(lc == 3)))
                    _rms_norm(ch, pkv[0:64, 0:512], qcs[0:64, ttt * 512:(ttt + 1) * 512], 512, sqb, nps, rr, rhi2, rlo2, bps, evf, o64, o1x64, qnwc[:])
                    ch.c("tensor", lambda e, ttt=ttt: e.matmul(ps[0:64, 0:512], w5[:], qcs[:, ttt * 512:(ttt + 1) * 512], start=True, stop=True))
                    ch.c("vector", lambda e, ttt=ttt: e.tensor_copy(out=qrs[:, ttt * 512:(ttt + 1) * 512], in_=ps[0:64, 0:512]))
                    ch.c("scalar", lambda e, ttt=ttt: e.activation(out=sqb[:], in_=qcs[:, ttt * 512:(ttt + 1) * 512], func=AF.Square))
                    ch.c("tensor", lambda e: e.matmul(nps[:], o64[:], sqb[:], start=True, stop=False))
                    ch.c("scalar", lambda e, ttt=ttt: e.activation(out=sqb[:], in_=qrs[:, ttt * 512:(ttt + 1) * 512], func=AF.Square))
                    ch.c("tensor", lambda e: e.matmul(nps[:], o64[:], sqb[:], start=False, stop=True))
                    ch.c("vector", lambda e, ttt=ttt: e.tensor_copy(out=qn2[:, ttt * 512:(ttt + 1) * 512], in_=nps[:]))
                ch.c("scalar", lambda e: e.activation(out=qn2[:], in_=qn2[:], func=AF.Sqrt))
                ch.c("vector", lambda e: e.tensor_scalar(out=vqr[:], in0=qn2[:], scalar1=khat[0:1, 0:1], scalar2=-1.03, op0=Alu.mult, op1=Alu.mult))
                ch.d("sync", lambda e, hi=hi, b0=b0: e.dma_start(out=qc_o[hi, :, b0:b0 + TPC], in_=qcs[:]))
                ch.d("sync", lambda e, hi=hi, b0=b0: e.dma_start(out=qr_o[hi, :, b0:b0 + TPC], in_=qrs[:]))
                ch.d("sync", lambda e, hi=hi, b0=b0: e.dma_start(out=vq_o[hi, :, b0:b0 + TPC], in_=vqr[:]))

        for tc in range(S // 128):
            for fc in range(8):
                ch.d("sync", lambda e, fc=fc, tc=tc: e.dma_start(out=xcs[:, fc * 128:(fc + 1) * 128], in_=xn1T[fc * 128:(fc + 1) * 128, tc * 128:(tc + 1) * 128]))
            for fc in range(8):
                ch.c("tensor", lambda e, fc=fc: e.matmul(pr[:], xcs[:, fc * 128:(fc + 1) * 128], rwT[:, fc * 8:(fc + 1) * 8], start=(fc == 0), stop=(fc == 7)))
            ch.c("vector", lambda e: e.tensor_reduce(out=rmax[:], in_=pr[:], axis=AX, op=Alu.max, negate=True))
            ch.c("scalar", lambda e: e.activation(out=rsf[:], in_=pr[:], func=AF.Exp, bias=rmax[:], scale=1.0))
            ch.c("vector", lambda e: e.tensor_reduce(out=rsum[:], in_=rsf[:], axis=AX, op=Alu.add))
            ch.c("vector", lambda e: e.reciprocal(out=rsum[:], in_=rsum[:]))
            ch.c("vector", lambda e: e.tensor_scalar(out=rsf[:], in0=rsf[:], scalar1=rsum[:], scalar2=None, op0=Alu.mult))
            ch.d("sync", lambda e, tc=tc: e.dma_start(out=rsc_o[tc * 128:(tc + 1) * 128, :], in_=rsf[:]))
            ch.c("vector", lambda e: e.tensor_copy(out=rsbf_kqv[:, 0:8], in_=rsf[:]))
            ch.c("tensor", lambda e: e.transpose(out=ptr[:], in_=rsbf_kqv[:, 0:8], identity=idnf_kqv[:]))
            ch.c("vector", lambda e, tc=tc: e.tensor_copy(out=scT[:, tc * 128:(tc + 1) * 128], in_=ptr[:]))

        KCAP = 2049.0
        ch.c("gpsimd", lambda e: e.memset(lo[:], -1.0))
        ch.c("gpsimd", lambda e: e.memset(hiv[:], 2.0))
        for it in range(64):
            ch.c("vector", lambda e: e.tensor_tensor(out=mid[:], in0=lo[:], in1=hiv[:], op=Alu.add))
            ch.c("vector", lambda e: e.tensor_scalar(out=mid[:], in0=mid[:], scalar1=0.5, scalar2=None, op0=Alu.mult))
            ch.c("vector", lambda e: e.tensor_scalar(out=cmpb[:], in0=scT[:], scalar1=mid[:], scalar2=None, op0=Alu.is_gt))
            ch.c("vector", lambda e: e.tensor_reduce(out=cnt[:], in_=cmpb[:], axis=AX, op=Alu.add))
            ch.c("vector", lambda e: e.tensor_scalar(out=sel[:], in0=cnt[:], scalar1=KCAP, scalar2=None, op0=Alu.is_ge))
            ch.c("vector", lambda e: e.tensor_scalar(out=nsel[:], in0=sel[:], scalar1=-1.0, scalar2=1.0, op0=Alu.mult, op1=Alu.add))
            ch.c("vector", lambda e: e.tensor_tensor(out=tv[:], in0=mid[:], in1=lo[:], op=Alu.subtract))
            ch.c("vector", lambda e: e.tensor_tensor(out=tv[:], in0=sel[:], in1=tv[:], op=Alu.mult))
            ch.c("vector", lambda e: e.tensor_tensor(out=lo[:], in0=lo[:], in1=tv[:], op=Alu.add))
            ch.c("vector", lambda e: e.tensor_tensor(out=tv[:], in0=mid[:], in1=hiv[:], op=Alu.subtract))
            ch.c("vector", lambda e: e.tensor_tensor(out=tv[:], in0=nsel[:], in1=tv[:], op=Alu.mult))
            ch.c("vector", lambda e: e.tensor_tensor(out=hiv[:], in0=hiv[:], in1=tv[:], op=Alu.add))
        ch.c("vector", lambda e: e.tensor_scalar(out=cmpb[:], in0=scT[:], scalar1=hiv[:], scalar2=None, op0=Alu.is_gt))
        ch.c("vector", lambda e: e.tensor_reduce(out=cnt[:], in_=cmpb[:], axis=AX, op=Alu.add))
        ch.c("vector", lambda e: e.tensor_scalar(out=need[:], in0=cnt[:], scalar1=-1.0, scalar2=KCAP, op0=Alu.mult, op1=Alu.add))
        ch.c("vector", lambda e: e.tensor_scalar(out=scT[:], in0=scT[:], scalar1=hiv[:], scalar2=None, op0=Alu.is_equal))
        ch.c("gpsimd", lambda e: e.memset(rlo_i[:], -1.0))
        ch.c("gpsimd", lambda e: e.memset(rhi_i[:], 8191.0))
        for it in range(30):
            ch.c("vector", lambda e: e.tensor_tensor(out=mid[:], in0=rlo_i[:], in1=rhi_i[:], op=Alu.add))
            ch.c("vector", lambda e: e.tensor_scalar(out=mid[:], in0=mid[:], scalar1=0.5, scalar2=None, op0=Alu.mult))
            ch.c("gpsimd", lambda e: e.iota(cmpb[:], [[1, S]], base=0, channel_multiplier=0, allow_small_or_imprecise_dtypes=True))
            ch.c("vector", lambda e: e.tensor_scalar(out=cmpb[:], in0=cmpb[:], scalar1=mid[:], scalar2=None, op0=Alu.is_le))
            ch.c("vector", lambda e: e.tensor_tensor(out=cmpb[:], in0=cmpb[:], in1=scT[:], op=Alu.mult))
            ch.c("vector", lambda e: e.tensor_reduce(out=cnt[:], in_=cmpb[:], axis=AX, op=Alu.add))
            ch.c("vector", lambda e: e.tensor_scalar(out=sel[:], in0=cnt[:], scalar1=need[:], scalar2=None, op0=Alu.is_ge))
            ch.c("vector", lambda e: e.tensor_scalar(out=nsel[:], in0=sel[:], scalar1=-1.0, scalar2=1.0, op0=Alu.mult, op1=Alu.add))
            ch.c("vector", lambda e: e.tensor_tensor(out=tv[:], in0=mid[:], in1=rhi_i[:], op=Alu.subtract))
            ch.c("vector", lambda e: e.tensor_tensor(out=tv[:], in0=sel[:], in1=tv[:], op=Alu.mult))
            ch.c("vector", lambda e: e.tensor_tensor(out=rhi_i[:], in0=rhi_i[:], in1=tv[:], op=Alu.add))
            ch.c("vector", lambda e: e.tensor_tensor(out=tv[:], in0=mid[:], in1=rlo_i[:], op=Alu.subtract))
            ch.c("vector", lambda e: e.tensor_tensor(out=tv[:], in0=nsel[:], in1=tv[:], op=Alu.mult))
            ch.c("vector", lambda e: e.tensor_tensor(out=rlo_i[:], in0=rlo_i[:], in1=tv[:], op=Alu.add))
        ch.c("vector", lambda e: e.tensor_copy(out=thrs[:, 0:1], in_=lo[:]))
        ch.c("vector", lambda e: e.tensor_copy(out=thrs[:, 1:2], in_=hiv[:]))
        ch.c("vector", lambda e: e.tensor_copy(out=thrs[:, 2:3], in_=rhi_i[:]))
        ch.c("vector", lambda e: e.tensor_copy(out=thrs[:, 3:4], in_=need[:]))
        ch.d("sync", lambda e: e.dma_start(out=thr_o[:, :], in_=thrs[:]))
        ch.emit(nc, block, sems)
    return nc



def build_l1attn():
    nc = bass.Bass("TRN2", target_bir_lowering=False)
    qc_i = nc.dram_tensor("qc_i", [4, 64, S], BF16, kind="ExternalInput").ap()
    qr_i = nc.dram_tensor("qr_i", [4, 64, S], BF16, kind="ExternalInput").ap()
    vq_i = nc.dram_tensor("vq_i", [4, 1, S], F32, kind="ExternalInput").ap()
    kc_i = nc.dram_tensor("kc_i", [4, 64, S], BF16, kind="ExternalInput").ap()
    kr_i = nc.dram_tensor("kr_i", [4, 64, S], BF16, kind="ExternalInput").ap()
    v_i = nc.dram_tensor("v_i", [4, 128, 4096], BF16, kind="ExternalInput").ap()
    wo = nc.dram_tensor("wo", [64, 64], F32, kind="ExternalInput").ap()
    mtab = nc.dram_tensor("mtab", [4, 128, 512], F32, kind="ExternalInput").ap()
    wol1 = nc.dram_tensor("wol1", [64, 64], F32, kind="ExternalInput").ap()
    at1_o = nc.dram_tensor("at1_o", [4, 64, S], F32, kind="ExternalOutput").ap()

    NQT = S // 512

    ch = Ch()
    with ExitStack() as st:
        sb = lambda n, sh, dt=BF16: st.enter_context(nc.sbuf_tensor(n, sh, dt))
        kcs = sb("kcs", [64, S]); kra = sb("kra", [66, S])
        vsb = sb("vsb", [128, 4096])
        qcs = sb("qcs", [64, S]); qra = sb("qra", [66, S])
        vqf = sb("vqf", [1, S], F32)
        stgh = sb("stgh", [1, S]); stgl = sb("stgl", [1, S]); losq = sb("losq", [1, S], F32)
        mts = sb("mts", [128, 4 * 512], F32)
        phat = sb("phat", [128, 512])
        sx = sb("sx", [128, 512], F32)
        wos = sb("wos", [64, 64])
        lsb = sb("lsb", [1, 512], F32); rhi = sb("rhi", [1, 512]); rlo = sb("rlo", [1, 512]); rlf = sb("rlf", [1, 512], F32)
        ctxf = sb("ctxf", [64, 512], F32)
        ctxn = sb("ctxn", [64, 512])
        o128 = sb("o128", [128, 1]); o1x64 = sb("o1x64", [1, 64])
        outf = sb("outf", [64, 512], F32)
        sps = st.enter_context(nc.psum_tensor("sps", [128, 512], F32))
        cps = st.enter_context(nc.psum_tensor("cps", [66, 512], F32))
        bps = st.enter_context(nc.psum_tensor("bps", [64, 512], F32))
        pso = st.enter_context(nc.psum_tensor("pso", [64, 512], F32))
        sems = [st.enter_context(nc.semaphore(f"S{i}")) for i in range(8)]
        block = st.enter_context(nc.Block())

        ch.c("gpsimd", lambda e: e.memset(o128[:], 1.0))
        ch.c("gpsimd", lambda e: e.memset(o1x64[:], 1.0))
        ch.c("gpsimd", lambda e: e.memset(kra[64:66, :], 1.0))
        ch.d("gpsimd", lambda e: e.dma_start(out=wos[:], in_=wo[:, :]))
        for v in range(4):
            ch.d("sync", lambda e, v=v: e.dma_start(out=mts[:, v * 512:(v + 1) * 512], in_=mtab[v, :, :]))

        for hi in range(4):
            ch.d("sync", lambda e, hi=hi: e.dma_start(out=kcs[:], in_=kc_i[hi, :, :]))
            ch.d("sync", lambda e, hi=hi: e.dma_start(out=kra[0:64, :], in_=kr_i[hi, :, :]))
            ch.d("sync", lambda e, hi=hi: e.dma_start(out=vsb[:], in_=v_i[hi, :, :]))
            ch.d("sync", lambda e, hi=hi: e.dma_start(out=qcs[:], in_=qc_i[hi, :, :]))
            ch.d("sync", lambda e, hi=hi: e.dma_start(out=qra[0:64, :], in_=qr_i[hi, :, :]))
            ch.d("sync", lambda e, hi=hi: e.dma_start(out=vqf[:], in_=vq_i[hi, :, :]))
            ch.c("vector", lambda e: e.tensor_copy(out=stgh[:], in_=vqf[:]))
            ch.c("vector", lambda e: e.tensor_tensor(out=losq[:], in0=vqf[:], in1=stgh[:], op=Alu.subtract))
            ch.c("vector", lambda e: e.tensor_copy(out=stgl[:], in_=losq[:]))
            ch.d("sync", lambda e: e.dma_start(out=qra[64:65, :], in_=stgh[:]))
            ch.d("sync", lambda e: e.dma_start(out=qra[65:66, :], in_=stgl[:]))
            for qt in range(NQT):
                q0 = qt * 512
                nkc = 4 * qt + 4
                for kc in range(nkc):
                    kb = kc * 128
                    ch.c("tensor", lambda e, kb=kb, q0=q0: e.matmul(
                        sps[:], kcs[:, kb:kb + 128], qcs[:, q0:q0 + 512], start=True, stop=False))
                    ch.c("tensor", lambda e, kb=kb, q0=q0: e.matmul(
                        sps[:], kra[:, kb:kb + 128], qra[:, q0:q0 + 512], start=False, stop=True))
                    if kc >= 4 * qt:
                        v = kc - 4 * qt
                        ch.c("vector", lambda e, v=v: e.tensor_tensor(out=sx[:], in0=mts[:, v * 512:(v + 1) * 512], in1=sps[:], op=Alu.add))
                        ch.c("scalar", lambda e: e.activation(out=phat[:], in_=sx[:], func=AF.Exp, bias=0.0, scale=0.125))
                    else:
                        ch.c("scalar", lambda e: e.activation(out=phat[:], in_=sps[:], func=AF.Exp, bias=0.0, scale=0.125))
                    ch.c("tensor", lambda e, kc=kc, nkc=nkc: e.matmul(
                        cps[0:64, :], vsb[:, kc * 64:(kc + 1) * 64], phat[:], start=(kc == 0), stop=(kc == nkc - 1)))
                    ch.c("tensor", lambda e, kc=kc, nkc=nkc: e.matmul(
                        cps[64:65, :], o128[:], phat[:], start=(kc == 0), stop=(kc == nkc - 1)))
                ch.c("vector", lambda e: e.tensor_copy(out=lsb[:], in_=cps[64:65, :]))
                ch.c("vector", lambda e: e.reciprocal(out=rlf[:], in_=lsb[:]))
                ch.c("vector", lambda e: e.tensor_copy(out=rhi[:], in_=rlf[:]))
                ch.c("vector", lambda e: e.tensor_tensor(out=lsb[:], in0=rlf[:], in1=rhi[:], op=Alu.subtract))
                ch.c("vector", lambda e: e.tensor_copy(out=rlo[:], in_=lsb[:]))
                ch.c("tensor", lambda e: e.matmul(bps[:], o1x64[:], rhi[:], start=True, stop=False))
                ch.c("tensor", lambda e: e.matmul(bps[:], o1x64[:], rlo[:], start=False, stop=True))
                ch.c("vector", lambda e: e.tensor_copy(out=ctxf[:], in_=cps[0:64, :]))
                ch.c("vector", lambda e: e.tensor_tensor(out=ctxn[:], in0=ctxf[:], in1=bps[:], op=Alu.mult))
                ch.c("tensor", lambda e: e.matmul(pso[:], wos[:], ctxn[:], start=True, stop=True))
                ch.c("vector", lambda e: e.tensor_copy(out=outf[:], in_=pso[:]))
                ch.d("sync", lambda e, hi=hi, q0=q0: e.dma_start(out=at1_o[hi, :, q0:q0 + 512], in_=outf[:]))
        ch.emit(nc, block, sems)
    return nc



def build_l1moe():
    nc = bass.Bass("TRN2", target_bir_lowering=False)
    xn1T = nc.dram_tensor("xn1T", [D, TPC], F32, kind="ExternalInput").ap()
    x1T = nc.dram_tensor("x1T", [D, TPC], F32, kind="ExternalInput").ap()
    at1T = nc.dram_tensor("at1T", [D, TPC], F32, kind="ExternalInput").ap()
    rscT = nc.dram_tensor("rscT", [8, TPC], F32, kind="ExternalInput").ap()
    thr = nc.dram_tensor("thr", [8, 4], F32, kind="ExternalInput").ap()
    idxl = nc.dram_tensor("idxl", [8, TPC], F32, kind="ExternalInput").ap()
    w1 = nc.dram_tensor("w1", [8, D, 4096], F32, kind="ExternalInput").ap()
    b1T = nc.dram_tensor("b1T", [8, 128, 32], F32, kind="ExternalInput").ap()
    w2 = nc.dram_tensor("w2", [8, 4096, D], F32, kind="ExternalInput").ap()
    b2T = nc.dram_tensor("b2T", [8, 128, 8], F32, kind="ExternalInput").ap()
    x2T_o = nc.dram_tensor("x2T_o", [D, TPC], F32, kind="ExternalOutput").ap()

    ch = Ch()
    with ExitStack() as st:
        sb = lambda n, sh, dt=BF16: st.enter_context(nc.sbuf_tensor(n, sh, dt))
        xb = sb("xb", [128, 8 * TPC])
        w1s = sb("w1s", [128, 8 * 128]); w2s = sb("w2s", [128, 32 * 128])
        b1s = sb("b1s", [128, 32], F32); b2s = sb("b2s", [128, 8], F32)
        gsb = sb("gsb", [8, TPC], F32)
        grow = sb("grow", [1, TPC], F32)
        ghi = sb("ghi", [1, TPC]); glo = sb("glo", [1, TPC]); gtm = sb("gtm", [1, TPC], F32)
        thrs = sb("thrs", [8, 4], F32)
        o1x128m = sb("o1x128m", [1, 128])
        t1 = sb("t1", [128, 512], F32)
        xs1 = sb("xs1", [128, 512], F32)
        ph = st.enter_context(nc.psum_tensor("ph", [128, 512], F32))
        po = st.enter_context(nc.psum_tensor("po", [128, 512], F32))
        pb = st.enter_context(nc.psum_tensor("pb", [128, 512], F32))
        sems = [st.enter_context(nc.semaphore(f"S{i}")) for i in range(8)]
        block = st.enter_context(nc.Block())

        TH = TPC // 2

        ch.d("sync", lambda e: e.dma_start(out=thrs[:], in_=thr[:, :]))
        for fc in range(8):
            ch.d("gpsimd", lambda e, fc=fc: e.dma_start(out=xb[:, fc * TPC:(fc + 1) * TPC], in_=xn1T[fc * 128:(fc + 1) * 128, :]))
        ch.c("gpsimd", lambda e: e.memset(o1x128m[:], 1.0))
        with ExitStack() as st2:
            rss = st2.enter_context(nc.sbuf_tensor("rss", [8, TPC], F32))
            idxs = st2.enter_context(nc.sbuf_tensor("idxs", [8, TPC], F32))
            m1 = st2.enter_context(nc.sbuf_tensor("m1", [8, TPC], F32))
            ch.d("sync", lambda e: e.dma_start(out=rss[:], in_=rscT[:, :]))
            ch.d("sync", lambda e: e.dma_start(out=idxs[:], in_=idxl[:, :]))
            ch.c("vector", lambda e: e.tensor_scalar(out=idxs[:], in0=idxs[:], scalar1=thrs[:, 2:3], scalar2=None, op0=Alu.is_le))
            ch.c("vector", lambda e: e.tensor_scalar(out=m1[:], in0=rss[:], scalar1=thrs[:, 1:2], scalar2=None, op0=Alu.is_equal))
            ch.c("vector", lambda e: e.tensor_tensor(out=m1[:], in0=m1[:], in1=idxs[:], op=Alu.mult))
            ch.c("vector", lambda e: e.tensor_scalar(out=gsb[:], in0=rss[:], scalar1=thrs[:, 1:2], scalar2=None, op0=Alu.is_gt))
            ch.c("vector", lambda e: e.tensor_tensor(out=gsb[:], in0=gsb[:], in1=m1[:], op=Alu.add))
            ch.c("vector", lambda e: e.tensor_tensor(out=gsb[:], in0=gsb[:], in1=rss[:], op=Alu.mult))
        hsb = sb("hsb", [128, 32 * 1024])
        x2a = sb("x2a", [128, 8 * TPC], F32)
        ch.c("gpsimd", lambda e: e.memset(x2a[:], 0.0))

        for ex in range(8):
            ch.d("sync", lambda e, ex=ex: e.dma_start(out=b1s[:], in_=b1T[ex, :, :]))
            ch.d("sync", lambda e, ex=ex: e.dma_start(out=b2s[:], in_=b2T[ex, :, :]))
            ch.d("sync", lambda e, ex=ex: e.dma_start(out=grow[:], in_=gsb[ex:ex + 1, :]))
            ch.c("vector", lambda e: e.tensor_copy(out=ghi[:], in_=grow[:]))
            ch.c("vector", lambda e: e.tensor_tensor(out=gtm[:], in0=grow[:], in1=ghi[:], op=Alu.subtract))
            ch.c("vector", lambda e: e.tensor_copy(out=glo[:], in_=gtm[:]))
            for th in range(2):
                tb = th * TH
                for hc in range(32):
                    for fc in range(8):
                        ch.d("gpsimd", lambda e, ex=ex, hc=hc, fc=fc: e.dma_start(
                            out=w1s[:, fc * 128:(fc + 1) * 128], in_=w1[ex, fc * 128:(fc + 1) * 128, hc * 128:(hc + 1) * 128]))
                    for t5 in range(2):
                        for fc in range(8):
                            ch.c("tensor", lambda e, fc=fc, tb=tb, t5=t5: e.matmul(
                                ph[:], w1s[:, fc * 128:(fc + 1) * 128], xb[:, fc * TPC + tb + t5 * 512: fc * TPC + tb + t5 * 512 + 512],
                                start=(fc == 0), stop=(fc == 7)))
                        ch.c("scalar", lambda e, hc=hc, t5=t5: e.activation(
                            out=hsb[:, hc * 1024 + t5 * 512: hc * 1024 + (t5 + 1) * 512], in_=ph[:], func=AF.Gelu,
                            bias=b1s[:, hc:hc + 1], scale=1.0))
                for dmc in range(8):
                    for hc in range(32):
                        ch.d("gpsimd", lambda e, ex=ex, hc=hc, dmc=dmc: e.dma_start(
                            out=w2s[:, hc * 128:(hc + 1) * 128], in_=w2[ex, hc * 128:(hc + 1) * 128, dmc * 128:(dmc + 1) * 128]))
                    for t5 in range(2):
                        for hc in range(32):
                            ch.c("tensor", lambda e, hc=hc, t5=t5: e.matmul(
                                po[:], w2s[:, hc * 128:(hc + 1) * 128], hsb[:, hc * 1024 + t5 * 512: hc * 1024 + (t5 + 1) * 512],
                                start=(hc == 0), stop=(hc == 31)))
                        ch.c("tensor", lambda e, tb=tb, t5=t5: e.matmul(pb[:], o1x128m[:], ghi[0:1, tb + t5 * 512: tb + (t5 + 1) * 512], start=True, stop=False))
                        ch.c("tensor", lambda e, tb=tb, t5=t5: e.matmul(pb[:], o1x128m[:], glo[0:1, tb + t5 * 512: tb + (t5 + 1) * 512], start=False, stop=True))
                        ch.c("vector", lambda e, dmc=dmc: e.tensor_scalar(out=t1[:], in0=po[:], scalar1=b2s[:, dmc:dmc + 1], scalar2=None, op0=Alu.add))
                        ch.c("vector", lambda e: e.tensor_tensor(out=t1[:], in0=t1[:], in1=pb[:], op=Alu.mult))
                        ch.c("vector", lambda e, dmc=dmc, tb=tb, t5=t5: e.tensor_tensor(
                            out=x2a[:, dmc * TPC + tb + t5 * 512: dmc * TPC + tb + (t5 + 1) * 512],
                            in0=x2a[:, dmc * TPC + tb + t5 * 512: dmc * TPC + tb + (t5 + 1) * 512], in1=t1[:], op=Alu.add))
        for fc in range(8):
            for t5 in range(4):
                ch.d("sync", lambda e, fc=fc, t5=t5: e.dma_start(out=xs1[:], in_=x1T[fc * 128:(fc + 1) * 128, t5 * 512:(t5 + 1) * 512]))
                ch.d("sync", lambda e, fc=fc, t5=t5: e.dma_start(out=t1[:], in_=at1T[fc * 128:(fc + 1) * 128, t5 * 512:(t5 + 1) * 512]))
                ch.c("vector", lambda e: e.tensor_tensor(out=xs1[:], in0=xs1[:], in1=t1[:], op=Alu.add))
                ch.c("vector", lambda e, fc=fc, t5=t5: e.tensor_tensor(
                    out=xs1[:], in0=xs1[:], in1=x2a[:, fc * TPC + t5 * 512: fc * TPC + (t5 + 1) * 512], op=Alu.add))
                ch.d("sync", lambda e, fc=fc, t5=t5: e.dma_start(out=x2T_o[fc * 128:(fc + 1) * 128, t5 * 512:(t5 + 1) * 512], in_=xs1[:]))
        ch.emit(nc, block, sems)
    return nc



def build_fused(upto=99):
    nc = bass.Bass("TRN2", target_bir_lowering=False)
    x_i = nc.dram_tensor("x_i", [TPC, D], F32, kind="ExternalInput").ap()
    gam = nc.dram_tensor("gam", [128, D], F32, kind="ExternalInput").ap()
    bet = nc.dram_tensor("bet", [128, D], F32, kind="ExternalInput").ap()
    Wq = nc.dram_tensor("Wq", [D, D], F32, kind="ExternalInput").ap()
    Wk = nc.dram_tensor("Wk", [D, D], F32, kind="ExternalInput").ap()
    Wv = nc.dram_tensor("Wv", [D, D], F32, kind="ExternalInput").ap()
    Wo = nc.dram_tensor("Wo", [D, D], F32, kind="ExternalInput").ap()
    dtab = nc.dram_tensor("dtab", [76, 128, 512], BF16, kind="ExternalInput").ap()
    wg_i = nc.dram_tensor("wg_i", [IH, D], F32, kind="ExternalInput").ap()
    wu_i = nc.dram_tensor("wu_i", [IH, D], F32, kind="ExternalInput").ap()
    wd_i = nc.dram_tensor("wd_i", [D, IH], F32, kind="ExternalInput").ap()
    wdq = nc.dram_tensor("wdq", [64, 512], F32, kind="ExternalInput").ap()
    wuq = nc.dram_tensor("wuq", [512, 64], F32, kind="ExternalInput").ap()
    wdkv = nc.dram_tensor("wdkv", [64, 1024], F32, kind="ExternalInput").ap()
    wukv = nc.dram_tensor("wukv", [1024, 128], F32, kind="ExternalInput").ap()
    wqr = nc.dram_tensor("wqr", [64, 64], F32, kind="ExternalInput").ap()
    wkr = nc.dram_tensor("wkr", [64, 64], F32, kind="ExternalInput").ap()
    qnw = nc.dram_tensor("qnw", [64, 1], F32, kind="ExternalInput").ap()
    knw = nc.dram_tensor("knw", [64, 1], F32, kind="ExternalInput").ap()
    rw = nc.dram_tensor("rw", [D, 8], F32, kind="ExternalInput").ap()
    mtab = nc.dram_tensor("mtab", [4, 128, 512], F32, kind="ExternalInput").ap()
    wol1 = nc.dram_tensor("wol1", [64, 64], F32, kind="ExternalInput").ap()
    w1 = nc.dram_tensor("w1", [8, D, 4096], F32, kind="ExternalInput").ap()
    b1T = nc.dram_tensor("b1T", [8, 128, 32], F32, kind="ExternalInput").ap()
    w2 = nc.dram_tensor("w2", [8, 4096, D], F32, kind="ExternalInput").ap()
    b2T = nc.dram_tensor("b2T", [8, 128, 8], F32, kind="ExternalInput").ap()
    idxl = nc.dram_tensor("idxl", [8, TPC], F32, kind="ExternalInput").ap()
    hgi = nc.dram_tensor("hgi", [1, 1], mybir.dt.int32, kind="ExternalInput").ap()
    loci = nc.dram_tensor("loci", [1, 1], mybir.dt.int32, kind="ExternalInput").ap()
    x2T_o = nc.dram_tensor("x2T_o", [D, TPC], F32, kind="ExternalOutput").ap()
    xn_d = nc.dram_tensor("xn_d", [TPC, D], F32)
    e1c = nc.dram_tensor("e1c", [D, TPC], BF16)
    G1 = nc.dram_tensor("G1", [4 * D, TPC], BF16)
    q_d = nc.dram_tensor("q_d", [8, 128, TPC], BF16)
    k_d = nc.dram_tensor("k_d", [8, 128, S], BF16)
    v_d = nc.dram_tensor("v_d", [16, 128, 4096], BF16)
    at0_d = nc.dram_tensor("at0_d", [TPC, D], F32)
    x1_d = nc.dram_tensor("x1_d", [TPC, D], F32)
    x1T_d = nc.dram_tensor("x1T_d", [D, TPC], F32)
    e3c = nc.dram_tensor("e3c", [D, TPC], F32)
    G3 = nc.dram_tensor("G3", [4 * D, TPC], F32)
    wgT_d = nc.dram_tensor("wgT_d", [D, IH], BF16)
    wuT_d = nc.dram_tensor("wuT_d", [D, IH], BF16)
    wdT_d = nc.dram_tensor("wdT_d", [IH, D], BF16)
    qc_d = nc.dram_tensor("qc_d", [4, 64, S], BF16)
    qr_d = nc.dram_tensor("qr_d", [4, 64, S], BF16)
    vq_d = nc.dram_tensor("vq_d", [4, 1, S], F32)
    kc_d = nc.dram_tensor("kc_d", [4, 64, S], BF16)
    kr_d = nc.dram_tensor("kr_d", [4, 64, S], BF16)
    v1_d = nc.dram_tensor("v1_d", [4, 128, 4096], BF16)
    scT_d = nc.dram_tensor("scT_d", [8, S], F32)
    thr_d = nc.dram_tensor("thr_d", [8, 4], F32)
    e5c = nc.dram_tensor("e5c", [256, S], F32)
    G5 = nc.dram_tensor("G5", [4 * 256, S], F32)
    GROUPS = [[0, 1, 2, 3], [4, 5, 6, 7]]

    ch = Ch()
    with ExitStack() as topst:
        sems = [topst.enter_context(nc.semaphore(f"S{i}")) for i in range(16)]
        ccs1 = topst.enter_context(nc.semaphore("cc1"))
        ccs3 = topst.enter_context(nc.semaphore("cc3"))
        ccs5 = topst.enter_context(nc.semaphore("cc5"))
        grA = topst.enter_context(nc.gpsimd.register("grA"))
        grB = topst.enter_context(nc.gpsimd.register("grB"))
        ppa = topst.enter_context(nc.semaphore("ppa"))
        ppv = topst.enter_context(nc.semaphore("ppv"))
        pps = topst.enter_context(nc.semaphore("pps"))
        ppc = topst.enter_context(nc.semaphore("ppc"))
        pcnt = {"a": 0, "v": 0, "s": 0, "c": 0}

        def _SEC1():
            with _FlushStack(ch, nc, sems) as st:
                sb = lambda n, sh, dt=F32: st.enter_context(nc.sbuf_tensor(n, sh, dt))
                xt = sb("s1_xt", [128, D]); dx = sb("s1_dx", [128, D]); sq = sb("s1_sq", [128, D])
                xnt = sb("s1_xnt", [128, D]); xat = sb("s1_xat", [128, D])
                gt = sb("s1_gt", [128, D]); bt = sb("s1_bt", [128, D])
                s1v = sb("s1_s1", [128, 1]); s2v = sb("s1_s2", [128, 1])
                xab = sb("s1_xab", [128, D], BF16); sT = sb("s1_sT", [128, 128], BF16)
                ts0 = sb("s1_ts0", [128, 128], BF16); idn = sb("s1_idn", [128, 128], BF16)
                pt = st.enter_context(nc.psum_tensor("s1_pt", [128, 128], BF16))
                ch.c("gpsimd", lambda e: e.memset(ts0[:], 1.0))
                ch.c("gpsimd", lambda e: e.affine_select(out=idn[:], in_=ts0[:], pattern=[[-1, 128]], compare_op=Alu.is_equal, fill=0.0, base=0, channel_multiplier=1))
                ch.d("sync", lambda e: e.dma_start(out=gt[:], in_=gam[:, :]))
                ch.d("sync", lambda e: e.dma_start(out=bt[:], in_=bet[:, :]))
                for tg in range(TPC // 128):
                    r0 = tg * 128
                    ch.d("sync", lambda e, r0=r0: e.dma_start(out=xt[:], in_=x_i[r0:r0 + 128, :]))
                    _ln_tile(ch, xt[:], xnt[:], dx[:], sq[:], s1v[:], s2v[:], D)
                    ch.d("sync", lambda e, r0=r0: e.dma_start(out=xn_d.ap()[r0:r0 + 128, :], in_=xnt[:]))
                    _ln_tile(ch, xnt[:], xat[:], dx[:], sq[:], s1v[:], s2v[:], D)
                    ch.c("vector", lambda e: e.tensor_tensor(out=xat[:], in0=xat[:], in1=gt[:], op=Alu.mult))
                    ch.c("vector", lambda e: e.tensor_tensor(out=xat[:], in0=xat[:], in1=bt[:], op=Alu.add))
                    ch.c("vector", lambda e: e.tensor_copy(out=xab[:], in_=xat[:]))
                    for fc in range(8):
                        ch.c("tensor", lambda e, fc=fc: e.transpose(out=pt[:], in_=xab[:, fc * 128:(fc + 1) * 128], identity=idn[:]))
                        ch.c("vector", lambda e: e.tensor_copy(out=sT[:], in_=pt[:]))
                        ch.d("sync", lambda e, fc=fc, tg=tg: e.dma_start(out=e1c.ap()[fc * 128:(fc + 1) * 128, tg * 128:(tg + 1) * 128], in_=sT[:]))
            ch.flush(nc, sems)
        if upto >= 1: _SEC1()
        if upto >= 1.5:
            ch.cc(lambda e: e.collective_compute("AllGather", Alu.bypass, replica_groups=GROUPS,
                                                 ins=[e1c.ap().opt()], outs=[G1.ap().opt()]), ccs1)

        def _SEC2():
            with _FlushStack(ch, nc, sems) as st:
                sb = lambda n, sh, dt=BF16: st.enter_context(nc.sbuf_tensor(n, sh, dt))
                xa = sb("s2_xa", [128, 8 * S]); xl = sb("s2_xl", [128, 8 * TPC])
                wqs = sb("s2_wq", [128, 8 * 128]); wks = sb("s2_wk", [128, 8 * 128]); wvs = sb("s2_wv", [128, 8 * 128])
                qs = sb("s2_qs", [128, TPC]); ks = sb("s2_ks", [128, S]); vs = sb("s2_vs", [128, 2 * 4096])
                ps = st.enter_context(nc.psum_tensor("s2_ps", [128, 512], F32))
                pv = st.enter_context(nc.psum_tensor("s2_pv", [128, 128], F32))
                for c in range(8):
                    for r in range(4):
                        ch.d("sync", lambda e, c=c, r=r: e.dma_start(
                            out=xa[:, c * S + r * TPC: c * S + (r + 1) * TPC], in_=G1.ap()[r * D + c * 128: r * D + (c + 1) * 128, :]))
                    ch.d("sync", lambda e, c=c: e.dma_start(out=xl[:, c * TPC:(c + 1) * TPC], in_=e1c.ap()[c * 128:(c + 1) * 128, :]))
                for p in range(8):
                    for c in range(8):
                        for W, ws in ((Wq, wqs), (Wk, wks), (Wv, wvs)):
                            ch.d("gpsimd", lambda e, c=c, p=p, W=W, ws=ws: e.dma_start(
                                out=ws[:, c * 128:(c + 1) * 128], in_=W[c * 128:(c + 1) * 128, p * 128:(p + 1) * 128]))
                    for tg in range(TPC // 512):
                        for c in range(8):
                            ch.c("tensor", lambda e, c=c, tg=tg: e.matmul(
                                ps[:], wqs[:, c * 128:(c + 1) * 128], xl[:, c * TPC + tg * 512: c * TPC + tg * 512 + 512],
                                start=(c == 0), stop=(c == 7)))
                        ch.c("vector", lambda e, tg=tg: e.tensor_copy(out=qs[:, tg * 512:(tg + 1) * 512], in_=ps[:]))
                    ch.d("sync", lambda e, p=p: e.dma_start(out=q_d.ap()[p, :, :], in_=qs[:]))
                    for kt in range(S // 512):
                        for c in range(8):
                            ch.c("tensor", lambda e, c=c, kt=kt: e.matmul(
                                ps[:], wks[:, c * 128:(c + 1) * 128], xa[:, c * S + kt * 512: c * S + kt * 512 + 512],
                                start=(c == 0), stop=(c == 7)))
                        ch.c("vector", lambda e, kt=kt: e.tensor_copy(out=ks[:, kt * 512:(kt + 1) * 512], in_=ps[:]))
                    ch.d("sync", lambda e, p=p: e.dma_start(out=k_d.ap()[p, :, :], in_=ks[:]))
                    for tc in range(S // 128):
                        for c in range(8):
                            ch.c("tensor", lambda e, c=c, tc=tc: e.matmul(
                                pv[:], xa[:, c * S + tc * 128: c * S + tc * 128 + 128], wvs[:, c * 128:(c + 1) * 128],
                                start=(c == 0), stop=(c == 7)))
                        ch.c("vector", lambda e, tc=tc: e.tensor_copy(out=vs[:, tc * 64:(tc + 1) * 64], in_=pv[:, 0:64]))
                        ch.c("vector", lambda e, tc=tc: e.tensor_copy(out=vs[:, 4096 + tc * 64:4096 + (tc + 1) * 64], in_=pv[:, 64:128]))
                    ch.d("sync", lambda e, p=p: e.dma_start(out=v_d.ap()[2 * p, :, :], in_=vs[:, 0:4096]))
                    ch.d("sync", lambda e, p=p: e.dma_start(out=v_d.ap()[2 * p + 1, :, :], in_=vs[:, 4096:8192]))

            ch.flush(nc, sems)
        if upto >= 2: _SEC2()
        def _SEC3():
            NQT = TPC // 512
            NKC = S // 128
            with _FlushStack(ch, nc, sems) as st0:
                sb0 = lambda n, sh, dt=BF16: st0.enter_context(nc.sbuf_tensor(n, sh, dt))
                ctx = sb0("s3_ctx", [128, 8 * TPC])
                wos = sb0("s3_wo", [128, 8 * 1024])
                for c in range(8):
                    ch.d("gpsimd", lambda e, c=c: e.dma_start(out=wos[:, c * 1024:(c + 1) * 1024], in_=Wo[c * 128:(c + 1) * 128, :]))
                with _FlushStack(ch, nc, sems) as st:
                    sb = lambda n, sh, dt=BF16: st.enter_context(nc.sbuf_tensor(n, sh, dt))
                    ka = sb("s3_ka", [66, S]); vsb = sb("s3_vsb", [128, 4096]); qp = sb("s3_qp", [128, TPC])
                    qa = sb("s3_qa", [66, TPC]); phat = sb("s3_phat", [128, TPC]); dts = sb("s3_dts", [128, 76 * 512])
                    sqk = sb("s3_sqk", [64, 512]); sqq = sb("s3_sqq", [64, 512])
                    kn512 = sb("s3_kn512", [1, 512], F32); kmx = sb("s3_kmx", [1, 1], F32)
                    qn2 = sb("s3_qn2", [1, TPC], F32)
                    khat = sb("s3_khat", [1, 1], F32); vqr = sb("s3_vqr", [1, TPC], F32)
                    stgh = sb("s3_stgh", [1, TPC]); stgl = sb("s3_stgl", [1, TPC])
                    sx = sb("s3_sx", [128, 2048], F32)
                    lsb = sb("s3_lsb", [1, 512], F32); rhi = sb("s3_rhi", [1, 512]); rlo = sb("s3_rlo", [1, 512])
                    rlf = sb("s3_rlf", [1, 512], F32); ctxf = sb("s3_ctxf", [64, 512], F32)
                    o64 = sb("s3_o64", [64, 1]); o128 = sb("s3_o128", [128, 1]); o1x64 = sb("s3_o1x64", [1, 64])
                    spA = st.enter_context(nc.psum_tensor("s3_spA", [128, 1024], F32))
                    spB = st.enter_context(nc.psum_tensor("s3_spB", [128, 1024], F32))
                    cps = st.enter_context(nc.psum_tensor("s3_cps", [66, 2048], F32))
                    spd = (spA, spB)
                    for dd in range(76):
                        ch.d("sync", lambda e, dd=dd: e.dma_start(out=dts[:, dd * 512:(dd + 1) * 512], in_=dtab[dd, :, :]))
                    ch.c("gpsimd", lambda e: e.memset(o64[:], 1.0))
                    ch.c("gpsimd", lambda e: e.memset(o128[:], 1.0))
                    ch.c("gpsimd", lambda e: e.memset(o1x64[:], 1.0))
                    ch.c("gpsimd", lambda e: e.memset(ka[64:66, :], 1.0))
                    for h in range(16):
                        p, r = h // 2, h % 2
                        slope = SLOPES[h]
                        if r == 0:
                            ch.d("sync", lambda e, p=p: e.dma_start(out=qp[:], in_=q_d.ap()[p, :, :]))
                        ch.d("sync", lambda e, p=p, r=r: e.dma_start(out=ka[0:64, :], in_=k_d.ap()[p, r * 64:r * 64 + 64, :]))
                        ch.d("sync", lambda e, h=h: e.dma_start(out=vsb[:], in_=v_d.ap()[h, :, :]))
                        ch.c("gpsimd", lambda e: e.memset(khat[:], 0.0))
                        for kt in range(S // 512):
                            ch.c("scalar", lambda e, kt=kt: e.activation(out=sqk[:], in_=ka[0:64, kt * 512:(kt + 1) * 512], func=AF.Square))
                            ch.c("tensor", lambda e: e.matmul(spA[0:1, 0:512], o64[:], sqk[:], start=True, stop=True))
                            ch.c("vector", lambda e: e.tensor_copy(out=kn512[:], in_=spA[0:1, 0:512]))
                            ch.c("vector", lambda e: e.tensor_reduce(out=kmx[:], in_=kn512[:], axis=AX, op=Alu.max))
                            ch.c("vector", lambda e: e.tensor_tensor(out=khat[:], in0=khat[:], in1=kmx[:], op=Alu.max))
                        ch.c("scalar", lambda e: e.activation(out=khat[:], in_=khat[:], func=AF.Sqrt))
                        for tg in range(NQT):
                            ch.c("scalar", lambda e, r=r, tg=tg: e.activation(out=sqq[:], in_=qp[r * 64:r * 64 + 64, tg * 512:(tg + 1) * 512], func=AF.Square))
                            ch.c("tensor", lambda e: e.matmul(spA[0:1, 0:512], o64[:], sqq[:], start=True, stop=True))
                            ch.c("vector", lambda e, tg=tg: e.tensor_copy(out=qn2[:, tg * 512:(tg + 1) * 512], in_=spA[0:1, 0:512]))
                        ch.c("scalar", lambda e: e.activation(out=qn2[:], in_=qn2[:], func=AF.Sqrt))
                        ch.c("vector", lambda e: e.tensor_scalar(out=vqr[:], in0=qn2[:], scalar1=khat[0:1, 0:1], scalar2=-1.03, op0=Alu.mult, op1=Alu.mult))
                        ch.c("vector", lambda e, r=r: e.tensor_copy(out=qa[0:64, :], in_=qp[r * 64:r * 64 + 64, :]))
                        ch.c("vector", lambda e: e.tensor_copy(out=stgh[:], in_=vqr[:]))
                        ch.c("vector", lambda e: e.tensor_tensor(out=qn2[:], in0=vqr[:], in1=stgh[:], op=Alu.subtract))
                        ch.c("vector", lambda e: e.tensor_copy(out=stgl[:], in_=qn2[:]))
                        ch.d("sync", lambda e: e.dma_start(out=qa[64:65, :], in_=stgh[:]))
                        ch.d("sync", lambda e: e.dma_start(out=qa[65:66, :], in_=stgl[:]))
                        ch.bar("tensor"); ch.bar("vector"); ch.bar("scalar")
                        jbase = pcnt["a"]
                        pend_ctx = None
                        for kc in range(NKC):
                            kb = kc * 128
                            for qh in range(2):
                                jg = pcnt["a"]
                                jl = jg - jbase
                                buf = jl % 2
                                sp = spd[buf]
                                for qq in range(2):
                                    qt = qh * 2 + qq
                                    def mk_sc(jl=jl, jg=jg, sp=sp, kb=kb, qt=qt, qq=qq):
                                        def f(e):
                                            if qq == 0 and jl >= 2:
                                                e.wait_ge(ppv, jg - 1)
                                            ins = e.matmul(sp[:, qq * 512:(qq + 1) * 512], ka[:, kb:kb + 128],
                                                           qa[:, qt * 512:(qt + 1) * 512], start=True, stop=True)
                                            if qq == 1:
                                                ins.then_inc(ppa, 1)
                                        return f
                                    ch.raw("tensor", mk_sc())
                                pcnt["a"] += 1
                                for qq in range(2):
                                    qt = qh * 2 + qq
                                    dd = (qt * 512 - kb + 8064) // 128
                                    def mk_stt(jl=jl, jg=jg, sp=sp, buf=buf, dd=dd, qq=qq, slope=slope):
                                        def f(e):
                                            if qq == 0:
                                                e.wait_ge(ppa, jg + 1)
                                                if jl >= 2:
                                                    e.wait_ge(pps, jg - 1)
                                            ins = e.scalar_tensor_tensor(
                                                out=sx[:, buf * 1024 + qq * 512: buf * 1024 + (qq + 1) * 512],
                                                in0=dts[:, dd * 512:(dd + 1) * 512], scalar=slope,
                                                in1=sp[:, qq * 512:(qq + 1) * 512], op0=Alu.mult, op1=Alu.add)
                                            if qq == 1:
                                                ins.then_inc(ppv, 1)
                                        return f
                                    ch.raw("vector", mk_stt())
                                pcnt["v"] += 1
                                def mk_exp(jl=jl, jg=jg, buf=buf):
                                    def f(e):
                                        e.wait_ge(ppv, jg + 1)
                                        if jl >= 2:
                                            e.wait_ge(ppc, jg - 1)
                                        e.activation(out=phat[:, buf * 1024:(buf + 1) * 1024],
                                                     in_=sx[:, buf * 1024:(buf + 1) * 1024],
                                                     func=AF.Exp, bias=0.0, scale=0.125).then_inc(pps, 1)
                                    return f
                                ch.raw("scalar", mk_exp())
                                pcnt["s"] += 1
                                if pend_ctx is not None:
                                    ch.raw("tensor", pend_ctx)
                                    pcnt["c"] += 1
                                def mk_ctx(jg=jg, buf=buf, kc=kc, qh=qh):
                                    def f(e):
                                        e.wait_ge(pps, jg + 1)
                                        for qq in range(2):
                                            qt = qh * 2 + qq
                                            e.matmul(cps[0:64, qt * 512:(qt + 1) * 512], vsb[:, kc * 64:(kc + 1) * 64],
                                                     phat[:, buf * 1024 + qq * 512: buf * 1024 + (qq + 1) * 512],
                                                     start=(kc == 0), stop=(kc == NKC - 1))
                                            ins = e.matmul(cps[64:65, qt * 512:(qt + 1) * 512], o128[:],
                                                           phat[:, buf * 1024 + qq * 512: buf * 1024 + (qq + 1) * 512],
                                                           start=(kc == 0), stop=(kc == NKC - 1))
                                        ins.then_inc(ppc, 1)
                                    return f
                                pend_ctx = mk_ctx()
                        ch.raw("tensor", pend_ctx)
                        pcnt["c"] += 1
                        def mk_exit(ca=pcnt["a"], cv=pcnt["v"], cs=pcnt["s"], cc=pcnt["c"]):
                            def f(e):
                                e.wait_ge(ppa, ca); e.wait_ge(ppv, cv)
                                e.wait_ge(pps, cs); e.wait_ge(ppc, cc)
                                return e.nop()
                            return f
                        ch.c("tensor", mk_exit())
                        for qt in range(NQT):
                            ch.c("vector", lambda e, qt=qt: e.tensor_copy(out=lsb[:], in_=cps[64:65, qt * 512:(qt + 1) * 512]))
                            ch.c("vector", lambda e: e.reciprocal(out=rlf[:], in_=lsb[:]))
                            ch.c("vector", lambda e: e.tensor_copy(out=rhi[:], in_=rlf[:]))
                            ch.c("vector", lambda e: e.tensor_tensor(out=lsb[:], in0=rlf[:], in1=rhi[:], op=Alu.subtract))
                            ch.c("vector", lambda e: e.tensor_copy(out=rlo[:], in_=lsb[:]))
                            ch.c("tensor", lambda e: e.matmul(spA[0:64, 0:512], o1x64[:], rhi[:], start=True, stop=False))
                            ch.c("tensor", lambda e: e.matmul(spA[0:64, 0:512], o1x64[:], rlo[:], start=False, stop=True))
                            ch.c("vector", lambda e, qt=qt: e.tensor_copy(out=ctxf[:], in_=cps[0:64, qt * 512:(qt + 1) * 512]))
                            ch.c("vector", lambda e, qt=qt, p=p, r=r: e.tensor_tensor(
                                out=ctx[r * 64:r * 64 + 64, p * TPC + qt * 512: p * TPC + (qt + 1) * 512], in0=ctxf[:], in1=spA[0:64, 0:512], op=Alu.mult))
                with _FlushStack(ch, nc, sems) as st:
                    sb = lambda n, sh, dt=F32: st.enter_context(nc.sbuf_tensor(n, sh, dt))
                    outf = sb("s3_outf", [128, 512]); otm = sb("s3_otm", [128, 128])
                    tsf = sb("s3_tsf", [128, 128]); idnf = sb("s3_idnf", [128, 128])
                    pso = st.enter_context(nc.psum_tensor("s3_pso", [128, 512], F32))
                    ptf = st.enter_context(nc.psum_tensor("s3_ptf", [128, 128], F32))
                    ch.c("gpsimd", lambda e: e.memset(tsf[:], 1.0))
                    ch.c("gpsimd", lambda e: e.affine_select(out=idnf[:], in_=tsf[:], pattern=[[-1, 128]], compare_op=Alu.is_equal, fill=0.0, base=0, channel_multiplier=1))
                    for dmc in range(8):
                        for qt in range(NQT):
                            for c in range(8):
                                ch.c("tensor", lambda e, c=c, dmc=dmc, qt=qt: e.matmul(
                                    pso[:], wos[:, c * 1024 + dmc * 128: c * 1024 + dmc * 128 + 128],
                                    ctx[:, c * TPC + qt * 512: c * TPC + (qt + 1) * 512], start=(c == 0), stop=(c == 7)))
                            ch.c("vector", lambda e: e.tensor_copy(out=outf[:], in_=pso[:]))
                            for j in range(4):
                                ch.c("tensor", lambda e, j=j: e.transpose(out=ptf[:], in_=outf[:, j * 128:(j + 1) * 128], identity=idnf[:]))
                                ch.c("vector", lambda e: e.tensor_copy(out=otm[:], in_=ptf[:]))
                                ch.d("sync", lambda e, dmc=dmc, qt=qt, j=j: e.dma_start(
                                    out=at0_d.ap()[qt * 512 + j * 128: qt * 512 + (j + 1) * 128, dmc * 128:(dmc + 1) * 128], in_=otm[:]))

            ch.flush(nc, sems)
        if upto >= 3: _SEC3()
        def _SEC4():
            HTS = [(0, 512), (512, 512), (1024, 512), (1536, 512), (2048, 512), (2560, 256)]
            NTG = TPC // 128
            with _FlushStack(ch, nc, sems) as st:
                sb = lambda n, sh, dt=BF16: st.enter_context(nc.sbuf_tensor(n, sh, dt))
                xqT = sb("s4_xqT", [128, 8 * TPC]); wdT = sb("s4_wdT", [128, 22 * 1024])
                wgs = sb("s4_wgs", [128, 8 * 512]); wus = sb("s4_wus", [128, 8 * 512])
                ld = sb("s4_ld", [128, IH], F32); qb = sb("s4_qb", [128, IH])
                mq = sb("s4_mq", [128, IH // 16], F32)
                uq = sb("s4_uq", [128, IH], F32); gq = sb("s4_gq", [128, IH], F32)
                tq = sb("s4_tq", [128, IH], F32); sgq = sb("s4_sgq", [128, IH], F32)
                hqT = sb("s4_hqT", [128, 22 * 128]); ts0 = sb("s4_ts0", [128, 128]); idn = sb("s4_idn", [128, 128])
                tsf = sb("s4_tsf", [128, 128], F32); idnf = sb("s4_idnf", [128, 128], F32); otm = sb("s4_otm", [128, 128], F32)
                hs = sb("s4_hs", [128, 512], F32); hf = sb("s4_hf", [128, 512], F32)
                xt = sb("s4_xt", [128, D], F32); att = sb("s4_att", [128, D], F32)
                x1t = sb("s4_x1t", [128, D], F32); xn1t = sb("s4_xn1t", [128, D], F32)
                dxs = sb("s4_dxs", [128, D], F32); sqs = sb("s4_sqs", [128, D], F32)
                st1 = sb("s4_st1", [128, 1], F32); st2 = sb("s4_st2", [128, 1], F32)
                pg = st.enter_context(nc.psum_tensor("s4_pg", [128, 512], F32))
                pu = st.enter_context(nc.psum_tensor("s4_pu", [128, 512], F32))
                pd = st.enter_context(nc.psum_tensor("s4_pd", [128, 512], F32))
                pt = st.enter_context(nc.psum_tensor("s4_pt", [128, 128], BF16))
                ptf = st.enter_context(nc.psum_tensor("s4_ptf", [128, 128], F32))
                ch.c("gpsimd", lambda e: e.memset(ts0[:], 1.0))
                ch.c("gpsimd", lambda e: e.affine_select(out=idn[:], in_=ts0[:], pattern=[[-1, 128]], compare_op=Alu.is_equal, fill=0.0, base=0, channel_multiplier=1))
                ch.c("gpsimd", lambda e: e.memset(tsf[:], 1.0))
                ch.c("gpsimd", lambda e: e.affine_select(out=idnf[:], in_=tsf[:], pattern=[[-1, 128]], compare_op=Alu.is_equal, fill=0.0, base=0, channel_multiplier=1))
                for W_in, T_out, nt in ((wg_i, wgT_d, 22), (wu_i, wuT_d, 22)):
                    for ttile in range(nt):
                        ch.d("sync", lambda e, W_in=W_in, tt=ttile: e.dma_start(out=ld[:, 0:D], in_=W_in[tt * 128:(tt + 1) * 128, :]))
                        _quant_tile(ch, ld[:, 0:D], qb[:, 0:D], D, mq, uq, gq, tq, sgq)
                        for fc in range(8):
                            _transp(ch, qb[:, fc * 128:(fc + 1) * 128], ts0[:], pt, idn)
                            ch.d("sync", lambda e, T_out=T_out, fc=fc, tt=ttile: e.dma_start(
                                out=T_out.ap()[fc * 128:(fc + 1) * 128, tt * 128:(tt + 1) * 128], in_=ts0[:]))
                for ttile in range(8):
                    ch.d("sync", lambda e, tt=ttile: e.dma_start(out=ld[:], in_=wd_i[tt * 128:(tt + 1) * 128, :]))
                    _quant_tile(ch, ld[:], qb[:], IH, mq, uq, gq, tq, sgq)
                    for hc in range(22):
                        _transp(ch, qb[:, hc * 128:(hc + 1) * 128], ts0[:], pt, idn)
                        ch.d("sync", lambda e, hc=hc, tt=ttile: e.dma_start(
                            out=wdT_d.ap()[hc * 128:(hc + 1) * 128, tt * 128:(tt + 1) * 128], in_=ts0[:]))
                for tg in range(NTG):
                    ch.d("sync", lambda e, tg=tg: e.dma_start(out=ld[:, 0:D], in_=xn_d.ap()[tg * 128:(tg + 1) * 128, :]))
                    _quant_tile(ch, ld[:, 0:D], qb[:, 0:D], D, mq, uq, gq, tq, sgq)
                    for fc in range(8):
                        _transp(ch, qb[:, fc * 128:(fc + 1) * 128], xqT[:, fc * TPC + tg * 128: fc * TPC + (tg + 1) * 128], pt, idn)
                for hc in range(22):
                    ch.d("sync", lambda e, hc=hc: e.dma_start(out=wdT[:, hc * 1024:(hc + 1) * 1024], in_=wdT_d.ap()[hc * 128:(hc + 1) * 128, :]))
                for tg in range(NTG):
                    ch.d("sync", lambda e, tg=tg: e.dma_start(out=xt[:], in_=x_i[tg * 128:(tg + 1) * 128, :]))
                    ch.d("sync", lambda e, tg=tg: e.dma_start(out=att[:], in_=at0_d.ap()[tg * 128:(tg + 1) * 128, :]))
                    for hti, (h0, hw) in enumerate(HTS):
                        ch.d("sync", lambda e, h0=h0, hw=hw: e.dma_start(
                            out=wgs[:].rearrange("p (f i) -> p f i", i=512)[:, :, 0:hw],
                            in_=bass.AP(wgT_d, h0, [[IH, 128], [128 * IH, 8], [1, hw]])))
                        ch.d("sync", lambda e, h0=h0, hw=hw: e.dma_start(
                            out=wus[:].rearrange("p (f i) -> p f i", i=512)[:, :, 0:hw],
                            in_=bass.AP(wuT_d, h0, [[IH, 128], [128 * IH, 8], [1, hw]])))
                        for fc in range(8):
                            ch.c("tensor", lambda e, fc=fc, tg=tg, hw=hw: e.matmul(
                                pg[:, 0:hw], xqT[:, fc * TPC + tg * 128: fc * TPC + (tg + 1) * 128], wgs[:, fc * 512: fc * 512 + hw],
                                start=(fc == 0), stop=(fc == 7)))
                        for fc in range(8):
                            ch.c("tensor", lambda e, fc=fc, tg=tg, hw=hw: e.matmul(
                                pu[:, 0:hw], xqT[:, fc * TPC + tg * 128: fc * TPC + (tg + 1) * 128], wus[:, fc * 512: fc * 512 + hw],
                                start=(fc == 0), stop=(fc == 7)))
                        ch.c("scalar", lambda e, hw=hw: e.activation(out=hs[:, 0:hw], in_=pg[:, 0:hw], func=AF.Silu))
                        ch.c("vector", lambda e, hw=hw: e.tensor_tensor(out=hf[:, 0:hw], in0=hs[:, 0:hw], in1=pu[:, 0:hw], op=Alu.mult))
                        _quant_tile(ch, hf[:, 0:hw], qb[:, 0:hw], hw, mq, uq, gq, tq, sgq)
                        for sc in range(hw // 128):
                            _transp(ch, qb[:, sc * 128:(sc + 1) * 128],
                                    hqT[:, (h0 // 128 + sc) * 128:(h0 // 128 + sc + 1) * 128], pt, idn)
                    for dmt in range(2):
                        for hc in range(22):
                            ch.c("tensor", lambda e, hc=hc, dmt=dmt: e.matmul(
                                pd[:], hqT[:, hc * 128:(hc + 1) * 128], wdT[:, hc * 1024 + dmt * 512: hc * 1024 + dmt * 512 + 512],
                                start=(hc == 0), stop=(hc == 21)))
                        ch.c("vector", lambda e, dmt=dmt: e.tensor_tensor(out=x1t[:, dmt * 512:(dmt + 1) * 512], in0=att[:, dmt * 512:(dmt + 1) * 512], in1=pd[:], op=Alu.add))
                    ch.c("vector", lambda e: e.tensor_tensor(out=x1t[:], in0=x1t[:], in1=xt[:], op=Alu.add))
                    _ln_tile(ch, x1t[:], xn1t[:], dxs[:], sqs[:], st1[:], st2[:], D)
                    ch.d("sync", lambda e, tg=tg: e.dma_start(out=x1_d.ap()[tg * 128:(tg + 1) * 128, :], in_=x1t[:]))
                    for fc in range(8):
                        ch.c("tensor", lambda e, fc=fc: e.transpose(out=ptf[:], in_=x1t[:, fc * 128:(fc + 1) * 128], identity=idnf[:]))
                        ch.c("vector", lambda e: e.tensor_copy(out=otm[:], in_=ptf[:]))
                        ch.d("sync", lambda e, fc=fc, tg=tg: e.dma_start(out=x1T_d.ap()[fc * 128:(fc + 1) * 128, tg * 128:(tg + 1) * 128], in_=otm[:]))
                    for fc in range(8):
                        ch.c("tensor", lambda e, fc=fc: e.transpose(out=ptf[:], in_=xn1t[:, fc * 128:(fc + 1) * 128], identity=idnf[:]))
                        ch.c("vector", lambda e: e.tensor_copy(out=otm[:], in_=ptf[:]))
                        ch.d("sync", lambda e, fc=fc, tg=tg: e.dma_start(out=e3c.ap()[fc * 128:(fc + 1) * 128, tg * 128:(tg + 1) * 128], in_=otm[:]))
            ch.flush(nc, sems)
        if upto >= 4: _SEC4()
        if upto >= 4.5:
            ch.cc(lambda e: e.collective_compute("AllGather", Alu.bypass, replica_groups=GROUPS,
                                                 ins=[e3c.ap().opt()], outs=[G3.ap().opt()]), ccs3)

        def _SEC5():
            with _FlushStack(ch, nc, sems) as st:
                sb = lambda n, sh, dt=BF16: st.enter_context(nc.sbuf_tensor(n, sh, dt))
                xh = sb("s5_xh", [64, S]); cq = sb("s5_cq", [128, 4 * TPC]); ckv = sb("s5_ckv", [128, 8 * 512])
                kcs = sb("s5_kcs", [64, S]); krs = sb("s5_krs", [64, S])
                vstage = sb("s5_vst", [128, 4096]); vtmp = sb("s5_vtmp", [64, 512])
                qcs = sb("s5_qcs", [64, TPC]); qrs = sb("s5_qrs", [64, TPC])
                w1s = sb("s5_w1", [64, 512]); w2t = sb("s5_w2t", [128, 4 * 64]); w3s = sb("s5_w3", [64, 1024])
                w4s = sb("s5_w4", [128, 8 * 128]); w5s = sb("s5_w5", [64, 64]); w6s = sb("s5_w6", [64, 64])
                qnwc = sb("s5_qnw", [64, 1], F32); knwc = sb("s5_knw", [64, 1], F32)
                rwT = sb("s5_rwT", [128, 8 * 8], F32); xcs = sb("s5_xcs", [128, 8 * 128], F32)
                sqb = sb("s5_sqb", [64, 512])
                rr = sb("s5_rr", [1, 512], F32); rhi2 = sb("s5_rhi2", [1, 512]); rlo2 = sb("s5_rlo2", [1, 512])
                evf = sb("s5_evf", [64, 512], F32)
                o64 = sb("s5_o64", [64, 1]); o1x64 = sb("s5_o1x64", [1, 64]); o1x128 = sb("s5_o1x128", [1, 128])
                kn512 = sb("s5_kn512", [1, 512], F32); kmx = sb("s5_kmx", [1, 1], F32); khat = sb("s5_khat", [1, 1], F32)
                qn2 = sb("s5_qn2", [1, TPC], F32); vqr = sb("s5_vqr", [1, TPC], F32)
                rsf = sb("s5_rsf", [128, 8], F32); rmax = sb("s5_rmax", [128, 1], F32); rsum = sb("s5_rsum", [128, 1], F32)
                scT = sb("s5_scT", [8, S], F32); cmpb = sb("s5_cmpb", [8, S], F32)
                lo = sb("s5_lo", [8, 1], F32); hiv = sb("s5_hiv", [8, 1], F32); mid = sb("s5_mid", [8, 1], F32)
                cnt = sb("s5_cnt", [8, 1], F32); sel = sb("s5_sel", [8, 1], F32); nsel = sb("s5_nsel", [8, 1], F32)
                tv = sb("s5_tv", [8, 1], F32); need = sb("s5_need", [8, 1], F32)
                rlo_i = sb("s5_rloi", [8, 1], F32); rhi_i = sb("s5_rhii", [8, 1], F32)
                thrs = sb("s5_thrs", [8, 4], F32)
                idn_k = sb("s5_idn", [128, 128]); tsk = sb("s5_tsk", [128, 128])
                idnf_k = sb("s5_idnf", [128, 128], F32); tskf = sb("s5_tskf", [128, 128], F32)
                rsbf = sb("s5_rsbf", [128, 8], F32)
                hgs = sb("s5_hgs", [1, 1], mybir.dt.int32)
                ps = st.enter_context(nc.psum_tensor("s5_ps", [128, 512], F32))
                pkv = st.enter_context(nc.psum_tensor("s5_pkv", [128, 512], F32))
                nps = st.enter_context(nc.psum_tensor("s5_nps", [1, 512], F32))
                bps = st.enter_context(nc.psum_tensor("s5_bps", [64, 512], F32))
                pr = st.enter_context(nc.psum_tensor("s5_pr", [128, 8], F32))
                ptv = st.enter_context(nc.psum_tensor("s5_ptv", [128, 64], BF16))
                ptr = st.enter_context(nc.psum_tensor("s5_ptr", [8, 128], F32))
                ch.c("gpsimd", lambda e: e.memset(o64[:], 1.0))
                ch.c("gpsimd", lambda e: e.memset(o1x64[:], 1.0))
                ch.c("gpsimd", lambda e: e.memset(o1x128[:], 1.0))
                ch.c("gpsimd", lambda e: e.memset(tsk[:], 1.0))
                ch.c("gpsimd", lambda e: e.affine_select(out=idn_k[:], in_=tsk[:], pattern=[[-1, 128]], compare_op=Alu.is_equal, fill=0.0, base=0, channel_multiplier=1))
                ch.c("gpsimd", lambda e: e.memset(tskf[:], 1.0))
                ch.c("gpsimd", lambda e: e.affine_select(out=idnf_k[:], in_=tskf[:], pattern=[[-1, 128]], compare_op=Alu.is_equal, fill=0.0, base=0, channel_multiplier=1))
                ch.d("gpsimd", lambda e: e.dma_start(out=w1s[:], in_=wdq[:, :]))
                for lc in range(4):
                    ch.d("gpsimd", lambda e, lc=lc: e.dma_start(out=w2t[:, lc * 64:(lc + 1) * 64], in_=wuq[lc * 128:(lc + 1) * 128, :]))
                ch.d("gpsimd", lambda e: e.dma_start(out=w3s[:], in_=wdkv[:, :]))
                for lc in range(8):
                    ch.d("gpsimd", lambda e, lc=lc: e.dma_start(out=w4s[:, lc * 128:(lc + 1) * 128], in_=wukv[lc * 128:(lc + 1) * 128, :]))
                ch.d("gpsimd", lambda e: e.dma_start(out=w5s[:], in_=wqr[:, :]))
                ch.d("gpsimd", lambda e: e.dma_start(out=w6s[:], in_=wkr[:, :]))
                ch.d("sync", lambda e: e.dma_start(out=qnwc[:], in_=qnw[:, :]))
                ch.d("sync", lambda e: e.dma_start(out=knwc[:], in_=knw[:, :]))
                for fc in range(8):
                    ch.d("sync", lambda e, fc=fc: e.dma_start(out=rwT[:, fc * 8:(fc + 1) * 8], in_=rw[fc * 128:(fc + 1) * 128, :]))
                ch.d("sync", lambda e: e.dma_start(out=hgs[:], in_=hgi[:, :]))
                ch.n("gpsimd", lambda e: e.reg_load(grA, hgs[:1, :1]))
                ch.n("gpsimd", lambda e: e.reg_mul(grA, grA, 256 * TPC))
                for hi in range(4):
                    ch.n("gpsimd", lambda e, o=(hi * 64) * TPC: e.reg_mov(grB, o))
                    ch.n("gpsimd", lambda e: e.reg_add(grB, grB, grA))
                    ch.d("gpsimd", lambda e: e.dma_start(
                        out=xh[:].rearrange("p (r t) -> p r t", t=TPC),
                        in_=bass.AP(G3, grB, [[TPC, 64], [D * TPC, 4], [1, TPC]])))
                    ch.c("gpsimd", lambda e: e.memset(khat[:], 0.0))
                    for tt in range(S // 512):
                        t0 = tt * 512
                        for lc in range(8):
                            ch.c("tensor", lambda e, lc=lc, t0=t0: e.matmul(ps[:], w3s[:, lc * 128:(lc + 1) * 128], xh[:, t0:t0 + 512], start=True, stop=True))
                            ch.c("vector", lambda e, lc=lc: e.tensor_copy(out=ckv[:, lc * 512:(lc + 1) * 512], in_=ps[:]))
                        for lc in range(8):
                            ch.c("tensor", lambda e, lc=lc: e.matmul(pkv[:], w4s[:, lc * 128:(lc + 1) * 128], ckv[:, lc * 512:(lc + 1) * 512], start=(lc == 0), stop=(lc == 7)))
                        _rms_norm(ch, pkv[0:64, 0:512], kcs[0:64, t0:t0 + 512], 512, sqb, nps, rr, rhi2, rlo2, bps, evf, o64, o1x64, knwc[:])
                        ch.c("vector", lambda e: e.tensor_copy(out=vtmp[:], in_=pkv[64:128, 0:512]))
                        for j in range(4):
                            ch.c("tensor", lambda e, j=j: e.transpose(out=ptv[:], in_=vtmp[:, j * 128:(j + 1) * 128], identity=idn_k[0:64, 0:64]))
                            ch.c("vector", lambda e, j=j, tt=tt: e.tensor_copy(out=vstage[:, (tt * 4 + j) * 64:(tt * 4 + j + 1) * 64], in_=ptv[:, 0:64]))
                        ch.c("tensor", lambda e, t0=t0: e.matmul(ps[0:64, 0:512], w6s[:], xh[:, t0:t0 + 512], start=True, stop=True))
                        ch.c("vector", lambda e, t0=t0: e.tensor_copy(out=krs[:, t0:t0 + 512], in_=ps[0:64, 0:512]))
                        ch.c("scalar", lambda e, t0=t0: e.activation(out=sqb[:], in_=kcs[:, t0:t0 + 512], func=AF.Square))
                        ch.c("tensor", lambda e: e.matmul(nps[:], o64[:], sqb[:], start=True, stop=False))
                        ch.c("scalar", lambda e, t0=t0: e.activation(out=sqb[:], in_=krs[:, t0:t0 + 512], func=AF.Square))
                        ch.c("tensor", lambda e: e.matmul(nps[:], o64[:], sqb[:], start=False, stop=True))
                        ch.c("vector", lambda e: e.tensor_copy(out=kn512[:], in_=nps[:]))
                        ch.c("vector", lambda e: e.tensor_reduce(out=kmx[:], in_=kn512[:], axis=AX, op=Alu.max))
                        ch.c("vector", lambda e: e.tensor_tensor(out=khat[:], in0=khat[:], in1=kmx[:], op=Alu.max))
                    ch.c("scalar", lambda e: e.activation(out=khat[:], in_=khat[:], func=AF.Sqrt))
                    ch.d("sync", lambda e, hi=hi: e.dma_start(out=kc_d.ap()[hi, :, :], in_=kcs[:]))
                    ch.d("sync", lambda e, hi=hi: e.dma_start(out=kr_d.ap()[hi, :, :], in_=krs[:]))
                    ch.d("sync", lambda e, hi=hi: e.dma_start(out=v1_d.ap()[hi, :, :], in_=vstage[:]))
                    for tb in range(S // TPC):
                        b0 = tb * TPC
                        for lc in range(4):
                            for ttt in range(4):
                                ch.c("tensor", lambda e, lc=lc, ttt=ttt, b0=b0: e.matmul(
                                    ps[:], w1s[:, lc * 128:(lc + 1) * 128], xh[:, b0 + ttt * 512: b0 + (ttt + 1) * 512], start=True, stop=True))
                                ch.c("vector", lambda e, lc=lc, ttt=ttt: e.tensor_copy(out=cq[:, lc * TPC + ttt * 512: lc * TPC + (ttt + 1) * 512], in_=ps[:]))
                        for ttt in range(4):
                            for lc in range(4):
                                ch.c("tensor", lambda e, lc=lc, ttt=ttt: e.matmul(
                                    pkv[0:64, 0:512], w2t[:, lc * 64:(lc + 1) * 64], cq[:, lc * TPC + ttt * 512: lc * TPC + (ttt + 1) * 512],
                                    start=(lc == 0), stop=(lc == 3)))
                            _rms_norm(ch, pkv[0:64, 0:512], qcs[0:64, ttt * 512:(ttt + 1) * 512], 512, sqb, nps, rr, rhi2, rlo2, bps, evf, o64, o1x64, qnwc[:])
                            ch.c("tensor", lambda e, ttt=ttt: e.matmul(ps[0:64, 0:512], w5s[:], qcs[:, ttt * 512:(ttt + 1) * 512], start=True, stop=True))
                            ch.c("vector", lambda e, ttt=ttt: e.tensor_copy(out=qrs[:, ttt * 512:(ttt + 1) * 512], in_=ps[0:64, 0:512]))
                            ch.c("scalar", lambda e, ttt=ttt: e.activation(out=sqb[:], in_=qcs[:, ttt * 512:(ttt + 1) * 512], func=AF.Square))
                            ch.c("tensor", lambda e: e.matmul(nps[:], o64[:], sqb[:], start=True, stop=False))
                            ch.c("scalar", lambda e, ttt=ttt: e.activation(out=sqb[:], in_=qrs[:, ttt * 512:(ttt + 1) * 512], func=AF.Square))
                            ch.c("tensor", lambda e: e.matmul(nps[:], o64[:], sqb[:], start=False, stop=True))
                            ch.c("vector", lambda e, ttt=ttt: e.tensor_copy(out=qn2[:, ttt * 512:(ttt + 1) * 512], in_=nps[:]))
                        ch.c("scalar", lambda e: e.activation(out=qn2[:], in_=qn2[:], func=AF.Sqrt))
                        ch.c("vector", lambda e: e.tensor_scalar(out=vqr[:], in0=qn2[:], scalar1=khat[0:1, 0:1], scalar2=-1.03, op0=Alu.mult, op1=Alu.mult))
                        ch.d("sync", lambda e, hi=hi, b0=b0: e.dma_start(out=qc_d.ap()[hi, :, b0:b0 + TPC], in_=qcs[:]))
                        ch.d("sync", lambda e, hi=hi, b0=b0: e.dma_start(out=qr_d.ap()[hi, :, b0:b0 + TPC], in_=qrs[:]))
                        ch.d("sync", lambda e, hi=hi, b0=b0: e.dma_start(out=vq_d.ap()[hi, :, b0:b0 + TPC], in_=vqr[:]))
                for tc in range(S // 128):
                    rk, ti = tc // 16, tc % 16
                    for fc in range(8):
                        ch.d("sync", lambda e, fc=fc, rk=rk, ti=ti: e.dma_start(
                            out=xcs[:, fc * 128:(fc + 1) * 128], in_=G3.ap()[rk * D + fc * 128: rk * D + (fc + 1) * 128, ti * 128:(ti + 1) * 128]))
                    for fc in range(8):
                        ch.c("tensor", lambda e, fc=fc: e.matmul(pr[:], xcs[:, fc * 128:(fc + 1) * 128], rwT[:, fc * 8:(fc + 1) * 8], start=(fc == 0), stop=(fc == 7)))
                    ch.c("vector", lambda e: e.tensor_reduce(out=rmax[:], in_=pr[:], axis=AX, op=Alu.max, negate=True))
                    ch.c("scalar", lambda e: e.activation(out=rsf[:], in_=pr[:], func=AF.Exp, bias=rmax[:], scale=1.0))
                    ch.c("vector", lambda e: e.tensor_reduce(out=rsum[:], in_=rsf[:], axis=AX, op=Alu.add))
                    ch.c("vector", lambda e: e.reciprocal(out=rsum[:], in_=rsum[:]))
                    ch.c("vector", lambda e: e.tensor_scalar(out=rsf[:], in0=rsf[:], scalar1=rsum[:], scalar2=None, op0=Alu.mult))
                    ch.c("vector", lambda e: e.tensor_copy(out=rsbf[:, 0:8], in_=rsf[:]))
                    ch.c("tensor", lambda e: e.transpose(out=ptr[:], in_=rsbf[:, 0:8], identity=idnf_k[:]))
                    ch.c("vector", lambda e, tc=tc: e.tensor_copy(out=scT[:, tc * 128:(tc + 1) * 128], in_=ptr[:]))
                KCAP = 2049.0
                ch.c("gpsimd", lambda e: e.memset(lo[:], -1.0))
                ch.c("gpsimd", lambda e: e.memset(hiv[:], 2.0))
                for it in range(64):
                    ch.c("vector", lambda e: e.tensor_tensor(out=mid[:], in0=lo[:], in1=hiv[:], op=Alu.add))
                    ch.c("vector", lambda e: e.tensor_scalar(out=mid[:], in0=mid[:], scalar1=0.5, scalar2=None, op0=Alu.mult))
                    ch.c("vector", lambda e: e.tensor_scalar(out=cmpb[:], in0=scT[:], scalar1=mid[:], scalar2=None, op0=Alu.is_gt))
                    ch.c("vector", lambda e: e.tensor_reduce(out=cnt[:], in_=cmpb[:], axis=AX, op=Alu.add))
                    ch.c("vector", lambda e: e.tensor_scalar(out=sel[:], in0=cnt[:], scalar1=KCAP, scalar2=None, op0=Alu.is_ge))
                    ch.c("vector", lambda e: e.tensor_scalar(out=nsel[:], in0=sel[:], scalar1=-1.0, scalar2=1.0, op0=Alu.mult, op1=Alu.add))
                    ch.c("vector", lambda e: e.tensor_tensor(out=tv[:], in0=mid[:], in1=lo[:], op=Alu.subtract))
                    ch.c("vector", lambda e: e.tensor_tensor(out=tv[:], in0=sel[:], in1=tv[:], op=Alu.mult))
                    ch.c("vector", lambda e: e.tensor_tensor(out=lo[:], in0=lo[:], in1=tv[:], op=Alu.add))
                    ch.c("vector", lambda e: e.tensor_tensor(out=tv[:], in0=mid[:], in1=hiv[:], op=Alu.subtract))
                    ch.c("vector", lambda e: e.tensor_tensor(out=tv[:], in0=nsel[:], in1=tv[:], op=Alu.mult))
                    ch.c("vector", lambda e: e.tensor_tensor(out=hiv[:], in0=hiv[:], in1=tv[:], op=Alu.add))
                ch.c("vector", lambda e: e.tensor_scalar(out=cmpb[:], in0=scT[:], scalar1=hiv[:], scalar2=None, op0=Alu.is_gt))
                ch.c("vector", lambda e: e.tensor_reduce(out=cnt[:], in_=cmpb[:], axis=AX, op=Alu.add))
                ch.c("vector", lambda e: e.tensor_scalar(out=need[:], in0=cnt[:], scalar1=-1.0, scalar2=KCAP, op0=Alu.mult, op1=Alu.add))
                ch.d("sync", lambda e: e.dma_start(out=scT_d.ap()[:, :], in_=scT[:]))
                ch.c("vector", lambda e: e.tensor_scalar(out=scT[:], in0=scT[:], scalar1=hiv[:], scalar2=None, op0=Alu.is_equal))
                ch.c("gpsimd", lambda e: e.memset(rlo_i[:], -1.0))
                ch.c("gpsimd", lambda e: e.memset(rhi_i[:], 8191.0))
                for it in range(30):
                    ch.c("vector", lambda e: e.tensor_tensor(out=mid[:], in0=rlo_i[:], in1=rhi_i[:], op=Alu.add))
                    ch.c("vector", lambda e: e.tensor_scalar(out=mid[:], in0=mid[:], scalar1=0.5, scalar2=None, op0=Alu.mult))
                    ch.c("gpsimd", lambda e: e.iota(cmpb[:], [[1, S]], base=0, channel_multiplier=0, allow_small_or_imprecise_dtypes=True))
                    ch.c("vector", lambda e: e.tensor_scalar(out=cmpb[:], in0=cmpb[:], scalar1=mid[:], scalar2=None, op0=Alu.is_le))
                    ch.c("vector", lambda e: e.tensor_tensor(out=cmpb[:], in0=cmpb[:], in1=scT[:], op=Alu.mult))
                    ch.c("vector", lambda e: e.tensor_reduce(out=cnt[:], in_=cmpb[:], axis=AX, op=Alu.add))
                    ch.c("vector", lambda e: e.tensor_scalar(out=sel[:], in0=cnt[:], scalar1=need[:], scalar2=None, op0=Alu.is_ge))
                    ch.c("vector", lambda e: e.tensor_scalar(out=nsel[:], in0=sel[:], scalar1=-1.0, scalar2=1.0, op0=Alu.mult, op1=Alu.add))
                    ch.c("vector", lambda e: e.tensor_tensor(out=tv[:], in0=mid[:], in1=rhi_i[:], op=Alu.subtract))
                    ch.c("vector", lambda e: e.tensor_tensor(out=tv[:], in0=sel[:], in1=tv[:], op=Alu.mult))
                    ch.c("vector", lambda e: e.tensor_tensor(out=rhi_i[:], in0=rhi_i[:], in1=tv[:], op=Alu.add))
                    ch.c("vector", lambda e: e.tensor_tensor(out=tv[:], in0=mid[:], in1=rlo_i[:], op=Alu.subtract))
                    ch.c("vector", lambda e: e.tensor_tensor(out=tv[:], in0=nsel[:], in1=tv[:], op=Alu.mult))
                    ch.c("vector", lambda e: e.tensor_tensor(out=rlo_i[:], in0=rlo_i[:], in1=tv[:], op=Alu.add))
                ch.c("vector", lambda e: e.tensor_copy(out=thrs[:, 0:1], in_=lo[:]))
                ch.c("vector", lambda e: e.tensor_copy(out=thrs[:, 1:2], in_=hiv[:]))
                ch.c("vector", lambda e: e.tensor_copy(out=thrs[:, 2:3], in_=rhi_i[:]))
                ch.c("vector", lambda e: e.tensor_copy(out=thrs[:, 3:4], in_=need[:]))
                ch.d("sync", lambda e: e.dma_start(out=thr_d.ap()[:, :], in_=thrs[:]))

            ch.flush(nc, sems)
        if upto >= 5: _SEC5()
        def _SEC6():
            NQT6 = S // 512
            with _FlushStack(ch, nc, sems) as st:
                sb = lambda n, sh, dt=BF16: st.enter_context(nc.sbuf_tensor(n, sh, dt))
                kcs = sb("s6_kcs", [64, S]); kra = sb("s6_kra", [66, S])
                vsb = sb("s6_vsb", [128, 4096])
                qcs = sb("s6_qcs", [64, S]); qra = sb("s6_qra", [66, S])
                vqf = sb("s6_vqf", [1, S], F32)
                stgh = sb("s6_stgh", [1, S]); stgl = sb("s6_stgl", [1, S]); losq = sb("s6_losq", [1, S], F32)
                mts = sb("s6_mts", [128, 4 * 512], F32)
                phat = sb("s6_phat", [128, 1024]); sx = sb("s6_sx", [128, 1024], F32)
                wosb = sb("s6_wos", [64, 64])
                lsb = sb("s6_lsb", [1, 512], F32); rhi = sb("s6_rhi", [1, 512]); rlo = sb("s6_rlo", [1, 512]); rlf = sb("s6_rlf", [1, 512], F32)
                ctxf = sb("s6_ctxf", [64, 512], F32); ctxn = sb("s6_ctxn", [64, 512])
                o128 = sb("s6_o128", [128, 1]); o1x64 = sb("s6_o1x64", [1, 64])
                outf = sb("s6_outf", [64, 512], F32)
                spA6 = st.enter_context(nc.psum_tensor("s6_spA", [128, 512], F32))
                spB6 = st.enter_context(nc.psum_tensor("s6_spB", [128, 512], F32))
                spd6 = (spA6, spB6)
                cps = st.enter_context(nc.psum_tensor("s6_cps", [66, 512], F32))
                bps = st.enter_context(nc.psum_tensor("s6_bps", [64, 512], F32))
                pso = st.enter_context(nc.psum_tensor("s6_pso", [64, 512], F32))
                ch.c("gpsimd", lambda e: e.memset(o128[:], 1.0))
                ch.c("gpsimd", lambda e: e.memset(o1x64[:], 1.0))
                ch.c("gpsimd", lambda e: e.memset(kra[64:66, :], 1.0))
                ch.d("gpsimd", lambda e: e.dma_start(out=wosb[:], in_=wol1[:, :]))
                for v in range(4):
                    ch.d("sync", lambda e, v=v: e.dma_start(out=mts[:, v * 512:(v + 1) * 512], in_=mtab[v, :, :]))
                for hi in range(4):
                    ch.d("sync", lambda e, hi=hi: e.dma_start(out=kcs[:], in_=kc_d.ap()[hi, :, :]))
                    ch.d("sync", lambda e, hi=hi: e.dma_start(out=kra[0:64, :], in_=kr_d.ap()[hi, :, :]))
                    ch.d("sync", lambda e, hi=hi: e.dma_start(out=vsb[:], in_=v1_d.ap()[hi, :, :]))
                    ch.d("sync", lambda e, hi=hi: e.dma_start(out=qcs[:], in_=qc_d.ap()[hi, :, :]))
                    ch.d("sync", lambda e, hi=hi: e.dma_start(out=qra[0:64, :], in_=qr_d.ap()[hi, :, :]))
                    ch.d("sync", lambda e, hi=hi: e.dma_start(out=vqf[:], in_=vq_d.ap()[hi, :, :]))
                    ch.c("vector", lambda e: e.tensor_copy(out=stgh[:], in_=vqf[:]))
                    ch.c("vector", lambda e: e.tensor_tensor(out=losq[:], in0=vqf[:], in1=stgh[:], op=Alu.subtract))
                    ch.c("vector", lambda e: e.tensor_copy(out=stgl[:], in_=losq[:]))
                    ch.d("sync", lambda e: e.dma_start(out=qra[64:65, :], in_=stgh[:]))
                    ch.d("sync", lambda e: e.dma_start(out=qra[65:66, :], in_=stgl[:]))
                    for qt in range(NQT6):
                        q0 = qt * 512
                        nkc = 4 * qt + 4
                        ch.bar("tensor"); ch.bar("vector"); ch.bar("scalar")
                        relhist = []
                        chist = []
                        sxrel = [0, 0]
                        pend_ctx = None
                        for kc in range(nkc):
                            kb = kc * 128
                            jl = kc
                            buf = jl % 2
                            sp = spd6[buf]
                            strad = kc >= 4 * qt
                            def mk_sc(sp=sp, kb=kb, q0=q0, rel=(relhist[jl - 2] if jl >= 2 else None)):
                                def f(e):
                                    if rel is not None:
                                        e.wait_ge(ppv if rel[0] == "v" else pps, rel[1])
                                    e.matmul(sp[:], kcs[:, kb:kb + 128], qcs[:, q0:q0 + 512], start=True, stop=False)
                                    e.matmul(sp[:], kra[:, kb:kb + 128], qra[:, q0:q0 + 512], start=False, stop=True).then_inc(ppa, 1)
                                return f
                            ch.raw("tensor", mk_sc())
                            pcnt["a"] += 1
                            if strad:
                                v = kc - 4 * qt
                                def mk_stt(sp=sp, buf=buf, v=v, ca=pcnt["a"], sxw=sxrel[buf]):
                                    def f(e):
                                        e.wait_ge(ppa, ca)
                                        if sxw:
                                            e.wait_ge(pps, sxw)
                                        e.tensor_tensor(out=sx[:, buf * 512:(buf + 1) * 512], in0=mts[:, v * 512:(v + 1) * 512],
                                                        in1=sp[:], op=Alu.add).then_inc(ppv, 1)
                                    return f
                                ch.raw("vector", mk_stt())
                                pcnt["v"] += 1
                                relhist.append(("v", pcnt["v"]))
                            def mk_exp(sp=sp, buf=buf, strad=strad, ca=pcnt["a"], cv=pcnt["v"],
                                       phw=(chist[jl - 2] if jl >= 2 else 0)):
                                def f(e):
                                    if strad:
                                        e.wait_ge(ppv, cv)
                                    else:
                                        e.wait_ge(ppa, ca)
                                    if phw:
                                        e.wait_ge(ppc, phw)
                                    e.activation(out=phat[:, buf * 512:(buf + 1) * 512],
                                                 in_=(sx[:, buf * 512:(buf + 1) * 512] if strad else sp[:]),
                                                 func=AF.Exp, bias=0.0, scale=0.125).then_inc(pps, 1)
                                return f
                            ch.raw("scalar", mk_exp())
                            pcnt["s"] += 1
                            if strad:
                                sxrel[buf] = pcnt["s"]
                            else:
                                relhist.append(("s", pcnt["s"]))
                            if pend_ctx is not None:
                                ch.raw("tensor", pend_ctx)
                                pcnt["c"] += 1
                                chist.append(pcnt["c"])
                            def mk_ctx(buf=buf, kc=kc, nkc=nkc, cs=pcnt["s"]):
                                def f(e):
                                    e.wait_ge(pps, cs)
                                    e.matmul(cps[0:64, :], vsb[:, kc * 64:(kc + 1) * 64], phat[:, buf * 512:(buf + 1) * 512],
                                             start=(kc == 0), stop=(kc == nkc - 1))
                                    e.matmul(cps[64:65, :], o128[:], phat[:, buf * 512:(buf + 1) * 512],
                                             start=(kc == 0), stop=(kc == nkc - 1)).then_inc(ppc, 1)
                                return f
                            pend_ctx = mk_ctx()
                        ch.raw("tensor", pend_ctx)
                        pcnt["c"] += 1
                        chist.append(pcnt["c"])
                        def mk_exit(ca=pcnt["a"], cv=pcnt["v"], cs=pcnt["s"], cc=pcnt["c"]):
                            def f(e):
                                e.wait_ge(ppa, ca); e.wait_ge(ppv, cv)
                                e.wait_ge(pps, cs); e.wait_ge(ppc, cc)
                                return e.nop()
                            return f
                        ch.c("tensor", mk_exit())
                        ch.c("vector", lambda e: e.tensor_copy(out=lsb[:], in_=cps[64:65, :]))
                        ch.c("vector", lambda e: e.reciprocal(out=rlf[:], in_=lsb[:]))
                        ch.c("vector", lambda e: e.tensor_copy(out=rhi[:], in_=rlf[:]))
                        ch.c("vector", lambda e: e.tensor_tensor(out=lsb[:], in0=rlf[:], in1=rhi[:], op=Alu.subtract))
                        ch.c("vector", lambda e: e.tensor_copy(out=rlo[:], in_=lsb[:]))
                        ch.c("tensor", lambda e: e.matmul(bps[:], o1x64[:], rhi[:], start=True, stop=False))
                        ch.c("tensor", lambda e: e.matmul(bps[:], o1x64[:], rlo[:], start=False, stop=True))
                        ch.c("vector", lambda e: e.tensor_copy(out=ctxf[:], in_=cps[0:64, :]))
                        ch.c("vector", lambda e: e.tensor_tensor(out=ctxn[:], in0=ctxf[:], in1=bps[:], op=Alu.mult))
                        ch.c("tensor", lambda e: e.matmul(pso[:], wosb[:], ctxn[:], start=True, stop=True))
                        ch.c("vector", lambda e: e.tensor_copy(out=outf[:], in_=pso[:]))
                        ch.d("sync", lambda e, hi=hi, q0=q0: e.dma_start(out=e5c.ap()[hi * 64:hi * 64 + 64, q0:q0 + 512], in_=outf[:]))
            ch.flush(nc, sems)
        if upto >= 6: _SEC6()
        if upto >= 6.5:
            ch.cc(lambda e: e.collective_compute("AllGather", Alu.bypass, replica_groups=GROUPS,
                                                 ins=[e5c.ap().opt()], outs=[G5.ap().opt()]), ccs5)

        def _SEC7():
            with _FlushStack(ch, nc, sems) as st:
                sb = lambda n, sh, dt=BF16: st.enter_context(nc.sbuf_tensor(n, sh, dt))
                xb = sb("s7_xb", [128, 8 * TPC])
                w1sb = sb("s7_w1", [128, 8 * 128]); w2sb = sb("s7_w2", [128, 16 * 128])
                b1s = sb("s7_b1", [128, 32], F32); b2s = sb("s7_b2", [128, 8], F32)
                gsb = sb("s7_g", [8, TPC], F32); grow = sb("s7_grow", [1, TPC], F32)
                ghi = sb("s7_ghi", [1, TPC]); glo = sb("s7_glo", [1, TPC]); gtm = sb("s7_gtm", [1, TPC], F32)
                thrs = sb("s7_thrs", [8, 4], F32); o1x128m = sb("s7_o1x128", [1, 128])
                t1 = sb("s7_t1", [128, 512], F32); xs1 = sb("s7_xs1", [128, 512], F32)
                locs7 = sb("s7_loc", [1, 1], mybir.dt.int32)
                ph = st.enter_context(nc.psum_tensor("s7_ph", [128, 512], F32))
                po = st.enter_context(nc.psum_tensor("s7_po", [128, 1024], F32))
                pb = st.enter_context(nc.psum_tensor("s7_pb", [128, 512], F32))
                TH = TPC // 2
                ch.d("sync", lambda e: e.dma_start(out=thrs[:], in_=thr_d.ap()[:, :]))
                for fc in range(8):
                    ch.d("gpsimd", lambda e, fc=fc: e.dma_start(out=xb[:, fc * TPC:(fc + 1) * TPC], in_=e3c.ap()[fc * 128:(fc + 1) * 128, :]))
                ch.c("gpsimd", lambda e: e.memset(o1x128m[:], 1.0))
                ch.d("sync", lambda e: e.dma_start(out=locs7[:], in_=loci[:, :]))
                ch.n("gpsimd", lambda e: e.reg_load(grA, locs7[:1, :1]))
                with _FlushStack(ch, nc, sems) as st2:
                    rss = st2.enter_context(nc.sbuf_tensor("s7_rss", [8, TPC], F32))
                    idxs = st2.enter_context(nc.sbuf_tensor("s7_idxs", [8, TPC], F32))
                    m1 = st2.enter_context(nc.sbuf_tensor("s7_m1", [8, TPC], F32))
                    ch.d("gpsimd", lambda e: e.dma_start(out=rss[:], in_=bass.AP(scT_d, grA, [[S, 8], [1, TPC]])))
                    ch.d("sync", lambda e: e.dma_start(out=idxs[:], in_=idxl[:, :]))
                    ch.c("vector", lambda e: e.tensor_scalar(out=idxs[:], in0=idxs[:], scalar1=thrs[:, 2:3], scalar2=None, op0=Alu.is_le))
                    ch.c("vector", lambda e: e.tensor_scalar(out=m1[:], in0=rss[:], scalar1=thrs[:, 1:2], scalar2=None, op0=Alu.is_equal))
                    ch.c("vector", lambda e: e.tensor_tensor(out=m1[:], in0=m1[:], in1=idxs[:], op=Alu.mult))
                    ch.c("vector", lambda e: e.tensor_scalar(out=gsb[:], in0=rss[:], scalar1=thrs[:, 1:2], scalar2=None, op0=Alu.is_gt))
                    ch.c("vector", lambda e: e.tensor_tensor(out=gsb[:], in0=gsb[:], in1=m1[:], op=Alu.add))
                    ch.c("vector", lambda e: e.tensor_tensor(out=gsb[:], in0=gsb[:], in1=rss[:], op=Alu.mult))
                hsb = sb("s7_hsb", [128, 32 * 1024])
                x2a = sb("s7_x2a", [128, 8 * TPC], F32)
                ch.c("gpsimd", lambda e: e.memset(x2a[:], 0.0))
                for ex in (range(8) if upto >= 7.5 else ()):
                    ch.d("sync", lambda e, ex=ex: e.dma_start(out=b1s[:], in_=b1T[ex, :, :]))
                    ch.d("sync", lambda e, ex=ex: e.dma_start(out=b2s[:], in_=b2T[ex, :, :]))
                    ch.d("sync", lambda e, ex=ex: e.dma_start(out=grow[:], in_=gsb[ex:ex + 1, :]))
                    ch.c("vector", lambda e: e.tensor_copy(out=ghi[:], in_=grow[:]))
                    ch.c("vector", lambda e: e.tensor_tensor(out=gtm[:], in0=grow[:], in1=ghi[:], op=Alu.subtract))
                    ch.c("vector", lambda e: e.tensor_copy(out=glo[:], in_=gtm[:]))
                    for th in range(2):
                        tb = th * TH
                        for hc in range(32):
                            ch.d("gpsimd", lambda e, ex=ex, hc=hc: e.dma_start(
                                out=w1sb[:].rearrange("p (f i) -> p f i", i=128),
                                in_=bass.AP(w1.tensor, ex * D * 4096 + hc * 128, [[4096, 128], [128 * 4096, 8], [1, 128]])))
                            for t5 in range(2):
                                for fc in range(8):
                                    ch.c("tensor", lambda e, fc=fc, tb=tb, t5=t5: e.matmul(
                                        ph[:], w1sb[:, fc * 128:(fc + 1) * 128], xb[:, fc * TPC + tb + t5 * 512: fc * TPC + tb + t5 * 512 + 512],
                                        start=(fc == 0), stop=(fc == 7)))
                                ch.c("scalar", lambda e, hc=hc, t5=t5: e.activation(
                                    out=hsb[:, hc * 1024 + t5 * 512: hc * 1024 + (t5 + 1) * 512], in_=ph[:], func=AF.Gelu,
                                    bias=b1s[:, hc:hc + 1], scale=1.0))
                        for dmc in range(8):
                            for hh in range(2):
                                for g4 in range(2):
                                    ch.d("gpsimd", lambda e, ex=ex, dmc=dmc, hh=hh, g4=g4: e.dma_start(
                                        out=w2sb[:].rearrange("p (f i) -> p f i", i=128)[:, g4 * 8:(g4 + 1) * 8, :],
                                        in_=bass.AP(w2.tensor, ex * 4096 * D + dmc * 128 + (hh * 16 + g4 * 8) * 128 * D, [[D, 128], [128 * D, 8], [1, 128]])))
                                for t5 in range(2):
                                    for hc in range(16):
                                        ch.c("tensor", lambda e, hc=hc, hh=hh, t5=t5: e.matmul(
                                            po[:, t5 * 512:(t5 + 1) * 512], w2sb[:, hc * 128:(hc + 1) * 128],
                                            hsb[:, (hh * 16 + hc) * 1024 + t5 * 512: (hh * 16 + hc) * 1024 + (t5 + 1) * 512],
                                            start=(hh == 0 and hc == 0), stop=(hh == 1 and hc == 15)))
                            for t5 in range(2):
                                ch.c("tensor", lambda e, tb=tb, t5=t5: e.matmul(pb[:], o1x128m[:], ghi[0:1, tb + t5 * 512: tb + (t5 + 1) * 512], start=True, stop=False))
                                ch.c("tensor", lambda e, tb=tb, t5=t5: e.matmul(pb[:], o1x128m[:], glo[0:1, tb + t5 * 512: tb + (t5 + 1) * 512], start=False, stop=True))
                                ch.c("vector", lambda e, dmc=dmc, t5=t5: e.tensor_scalar(out=t1[:], in0=po[:, t5 * 512:(t5 + 1) * 512], scalar1=b2s[:, dmc:dmc + 1], scalar2=None, op0=Alu.add))
                                ch.c("vector", lambda e: e.tensor_tensor(out=t1[:], in0=t1[:], in1=pb[:], op=Alu.mult))
                                ch.c("vector", lambda e, dmc=dmc, tb=tb, t5=t5: e.tensor_tensor(
                                    out=x2a[:, dmc * TPC + tb + t5 * 512: dmc * TPC + tb + (t5 + 1) * 512],
                                    in0=x2a[:, dmc * TPC + tb + t5 * 512: dmc * TPC + tb + (t5 + 1) * 512], in1=t1[:], op=Alu.add))
                ch.n("gpsimd", lambda e: e.reg_load(grB, locs7[:1, :1]))
                at1b = sb("s7_at1b", [128, TPC])
                for fc in (range(8) if upto >= 7.8 else ()):
                    off_static = (fc // 2) * 256 * S + (fc % 2) * 128 * S
                    ch.n("gpsimd", lambda e, o=off_static: e.reg_mov(grA, o))
                    ch.n("gpsimd", lambda e: e.reg_add(grA, grA, grB))
                    ch.d("gpsimd", lambda e: e.dma_start(out=at1b[:], in_=bass.AP(G5, grA, [[S, 128], [1, TPC]])))
                    for t5 in range(4):
                        ch.d("sync", lambda e, fc=fc, t5=t5: e.dma_start(out=xs1[:], in_=x1T_d.ap()[fc * 128:(fc + 1) * 128, t5 * 512:(t5 + 1) * 512]))
                        ch.c("vector", lambda e, t5=t5: e.tensor_tensor(out=xs1[:], in0=xs1[:], in1=at1b[:, t5 * 512:(t5 + 1) * 512], op=Alu.add))
                        ch.c("vector", lambda e, fc=fc, t5=t5: e.tensor_tensor(
                            out=xs1[:], in0=xs1[:], in1=x2a[:, fc * TPC + t5 * 512: fc * TPC + (t5 + 1) * 512], op=Alu.add))
                        ch.d("sync", lambda e, fc=fc, t5=t5: e.dma_start(out=x2T_o[fc * 128:(fc + 1) * 128, t5 * 512:(t5 + 1) * 512], in_=xs1[:]))
            ch.flush(nc, sems)
        if upto >= 7: _SEC7()
        ch.final(nc, sems)
    return nc



_built = {}

def _get(name, fn):
    if name not in _built:
        _built[name] = fn()
    return _built[name]


def _run(nc, in_maps):
    res = run_bass_kernel_spmd(nc, in_maps, core_ids=list(range(NC)))
    return res.results


def _f(a):
    return np.ascontiguousarray(a, dtype=np.float32)


def _mk_fused_maps(ins):
    x = _f(ins["L0_attn_x"]).reshape(B * S, D)
    gam_b = _f(np.broadcast_to(np.asarray(ins["L0_attn_ln_gamma"], np.float32), (128, D)))
    bet_b = _f(np.broadcast_to(np.asarray(ins["L0_attn_ln_beta"], np.float32), (128, D)))
    mtab = np.zeros((4, 128, 512), np.float32)
    pcol = np.arange(128)[:, None]
    ccol = np.arange(512)[None, :]
    for v in range(4):
        mtab[v] = np.where(ccol < 128 * v + pcol, -1e9, 0.0)
    com = {
        "gam": gam_b, "bet": bet_b,
        "Wq": _f(ins["L0_attn_Wq"]), "Wk": _f(ins["L0_attn_Wk"]),
        "Wv": _f(ins["L0_attn_Wv"]), "Wo": _f(ins["L0_attn_Wo"]),
        "wg_i": _f(ins["L0_ffn_w_gate"]), "wu_i": _f(ins["L0_ffn_w_up"]), "wd_i": _f(ins["L0_ffn_w_down"]),
        "wdq": _f(np.asarray(ins["L1_attn_W_DQ"]).T), "wuq": _f(np.asarray(ins["L1_attn_W_UQ"]).T),
        "wdkv": _f(np.asarray(ins["L1_attn_W_DKV"]).T), "wukv": _f(np.asarray(ins["L1_attn_W_UKV"]).T),
        "wqr": _f(np.asarray(ins["L1_attn_W_QR"]).T), "wkr": _f(np.asarray(ins["L1_attn_W_KR"]).T),
        "qnw": _f(np.asarray(ins["L1_attn_q_norm_w"]).reshape(64, 1)),
        "knw": _f(np.asarray(ins["L1_attn_k_norm_w"]).reshape(64, 1)),
        "rw": _f(np.asarray(ins["L1_ffn_router_w"]).T), "mtab": mtab,
        "wol1": _f(np.asarray(ins["L1_attn_W_O"]).T),
        "w1": _f(ins["L1_ffn_w1"]),
        "b1T": _f(np.asarray(ins["L1_ffn_b1"]).reshape(8, 32, 128).transpose(0, 2, 1)),
        "w2": _f(ins["L1_ffn_w2"]),
        "b2T": _f(np.asarray(ins["L1_ffn_b2"]).reshape(8, 8, 128).transpose(0, 2, 1)),
    }
    maps = []
    for c in range(NC):
        hg, loc0 = c % CPB, (c % CPB) * TPC
        dlt = np.arange(76) * 128 - 8064 + loc0
        dtab = (-8.0 * np.abs(dlt[:, None, None] + np.arange(512)[None, None, :]
                              - np.arange(128)[None, :, None])).astype(BF)
        m = dict(com)
        m["x_i"] = np.ascontiguousarray(x[c * TPC:(c + 1) * TPC])
        m["dtab"] = dtab
        m["idxl"] = np.ascontiguousarray(np.broadcast_to(np.arange(loc0, loc0 + TPC, dtype=np.float32), (8, TPC)))
        m["hgi"] = np.array([[hg]], dtype=np.int32)
        m["loci"] = np.array([[loc0]], dtype=np.int32)
        maps.append(m)
    return maps


def kernel(**ins):
    maps = _mk_fused_maps(ins)
    r = _run(_get("fused", build_fused), maps)
    x2 = np.concatenate([np.asarray(r[c]["x2T_o"]).T for c in range(NC)], axis=0)
    return x2.reshape(B, S, D)


def _kernel_multilaunch(**ins):

    x = _f(ins["L0_attn_x"]).reshape(B * S, D)
    gam_b = _f(np.broadcast_to(np.asarray(ins["L0_attn_ln_gamma"], np.float32), (128, D)))
    bet_b = _f(np.broadcast_to(np.asarray(ins["L0_attn_ln_beta"], np.float32), (128, D)))

    r1 = _run(_get("ln0", build_ln0),
              [{"x": x[c * TPC:(c + 1) * TPC], "gam": gam_b, "bet": bet_b} for c in range(NC)])
    xn = np.concatenate([r["xn"] for r in r1], axis=0)
    xnA = np.concatenate([r["xnA"] for r in r1], axis=0)

    xnAT = [np.ascontiguousarray(xnA[b * S:(b + 1) * S].T) for b in range(B)]
    maps = []
    for c in range(NC):
        b, loc0 = c // CPB, (c % CPB) * TPC
        maps.append({"xnAT": xnAT[b], "xnATl": np.ascontiguousarray(xnAT[b][:, loc0:loc0 + TPC]),
                     "Wq": _f(ins["L0_attn_Wq"]), "Wk": _f(ins["L0_attn_Wk"]), "Wv": _f(ins["L0_attn_Wv"])})
    r2 = _run(_get("l0qkv", build_l0qkv), maps)

    maps = []
    for c in range(NC):
        loc0 = (c % CPB) * TPC
        dlt = np.arange(76) * 128 - 8064 + loc0
        dtab = (-8.0 * np.abs(dlt[:, None, None] + np.arange(512)[None, None, :]
                              - np.arange(128)[None, :, None])).astype(BF)
        maps.append({"q_in": np.asarray(r2[c]["q_out"]), "k_in": np.asarray(r2[c]["k_out"]),
                     "v_in": np.asarray(r2[c]["v_out"]), "Wo": _f(ins["L0_attn_Wo"]), "dtab": dtab})
    r3 = _run(_get("l0attn", build_l0attn), maps)
    attn0 = np.concatenate([np.asarray(r3[c]["attn0T"]).T for c in range(NC)], axis=0)

    maps = []
    for c in range(NC):
        sl = slice(c * TPC, (c + 1) * TPC)
        maps.append({"xn_i": np.ascontiguousarray(xn[sl]), "x_i": np.ascontiguousarray(x[sl]),
                     "at_i": np.ascontiguousarray(attn0[sl]),
                     "wg_i": _f(ins["L0_ffn_w_gate"]), "wu_i": _f(ins["L0_ffn_w_up"]),
                     "wd_i": _f(ins["L0_ffn_w_down"])})
    r4 = _run(_get("l0ffn", build_l0ffn), maps)
    x1 = np.concatenate([r["x1_o"] for r in r4], axis=0)
    xn1 = np.concatenate([r["xn1_o"] for r in r4], axis=0)

    xn1T = [np.ascontiguousarray(xn1[b * S:(b + 1) * S].T) for b in range(B)]
    maps = []
    for c in range(NC):
        b, hg = c // CPB, c % CPB
        maps.append({"xn1T": np.ascontiguousarray(np.roll(xn1T[b], -hg * 256, axis=0)),
                     "wdq": _f(ins["L1_attn_W_DQ"].T), "wuq": _f(ins["L1_attn_W_UQ"].T),
                     "wdkv": _f(ins["L1_attn_W_DKV"].T), "wukv": _f(ins["L1_attn_W_UKV"].T),
                     "wqr": _f(ins["L1_attn_W_QR"].T), "wkr": _f(ins["L1_attn_W_KR"].T),
                     "qnw": _f(np.asarray(ins["L1_attn_q_norm_w"]).reshape(64, 1)),
                     "knw": _f(np.asarray(ins["L1_attn_k_norm_w"]).reshape(64, 1)),
                     "rw": _f(ins["L1_ffn_router_w"].T)})
    r5 = _run(_get("l1kqv", build_l1kqv), maps)

    mtab = np.zeros((4, 128, 512), np.float32)
    pcol = np.arange(128)[:, None]
    ccol = np.arange(512)[None, :]
    for v in range(4):
        mtab[v] = np.where(ccol < 128 * v + pcol, -1e9, 0.0)
    maps = []
    for c in range(NC):
        maps.append({"qc_i": np.asarray(r5[c]["qc_o"]), "qr_i": np.asarray(r5[c]["qr_o"]),
                     "vq_i": np.asarray(r5[c]["vq_o"]), "kc_i": np.asarray(r5[c]["kc_o"]),
                     "kr_i": np.asarray(r5[c]["kr_o"]), "v_i": np.asarray(r5[c]["v_o"]),
                     "wo": _f(ins["L1_attn_W_O"].T), "mtab": mtab})
    r6 = _run(_get("l1attn", build_l1attn), maps)
    at1T = [np.zeros((D, S), np.float32) for _ in range(B)]
    for c in range(NC):
        b, hg = c // CPB, c % CPB
        at1T[b][hg * 256:(hg + 1) * 256, :] = np.asarray(r6[c]["at1_o"]).reshape(256, S)

    x1T = [np.ascontiguousarray(x1[b * S:(b + 1) * S].T) for b in range(B)]
    maps = []
    for c in range(NC):
        b, loc0 = c // CPB, (c % CPB) * TPC
        sl = slice(loc0, loc0 + TPC)
        rsc = np.asarray(r5[b * CPB]["rsc_o"])
        maps.append({"xn1T": np.ascontiguousarray(xn1T[b][:, sl]),
                     "x1T": np.ascontiguousarray(x1T[b][:, sl]),
                     "at1T": np.ascontiguousarray(at1T[b][:, sl]),
                     "rscT": np.ascontiguousarray(rsc.T[:, sl]),
                     "thr": np.asarray(r5[b * CPB]["thr_o"]),
                     "idxl": np.ascontiguousarray(np.broadcast_to(
                         np.arange(loc0, loc0 + TPC, dtype=np.float32), (8, TPC))),
                     "w1": _f(ins["L1_ffn_w1"]),
                     "b1T": _f(np.asarray(ins["L1_ffn_b1"]).reshape(8, 32, 128).transpose(0, 2, 1)),
                     "w2": _f(ins["L1_ffn_w2"]),
                     "b2T": _f(np.asarray(ins["L1_ffn_b2"]).reshape(8, 8, 128).transpose(0, 2, 1))})
    r7 = _run(_get("l1moe", build_l1moe), maps)
    x2 = np.concatenate([np.asarray(r7[c]["x2T_o"]).T for c in range(NC)], axis=0)
    return x2.reshape(B, S, D)
